# version 01
# baseline (speedup 1.0000x reference)
import numpy as np
import ml_dtypes
import concourse.bass as bass
import concourse.mybir as mybir

F32 = mybir.dt.float32
BF16 = mybir.dt.bfloat16
I32 = mybir.dt.int32
AF = mybir.ActivationFunctionType
ALU = mybir.AluOpType
AX = mybir.AxisListType

T = 4096
D = 1024
H = 16
DH = 64
CTX = 1536
NI = 3072
NTT = T // 128
NQT = 8
NOC = D // 128
NIC = NI // 128
QMAX = 32767.0
EPS = 1e-5

ENGS = ("s", "t", "v", "a", "g")
KQ = 4
QUEUES = ("sd", "ad", "gd", "vd")
QSEMS = {q: [f"{q}{k}" for k in range(KQ)] for q in QUEUES}
ESEM = {"s": "ss", "t": "ts", "v": "vs", "a": "as_", "g": "gs"}
ENG2Q = {"s": "sd", "a": "ad", "g": "gd", "v": "vd"}
ALLSEMS = [s for q in QUEUES for s in QSEMS[q]] + list(ESEM.values())


class Plan:
    def __init__(self):
        self.cnt = {k: 0 for k in ALLSEMS}
        self.ops = {e: [] for e in ENGS}
        self.qidx = {q: 0 for q in QUEUES}
        self.qgate = {q: False for q in QUEUES}
        self.qwaited = {q: False for q in QUEUES}

    def op(self, eng, fn):
        self.ops[eng].append(("op", fn))

    def inc(self, eng, sem, by, fn, selfwait=None):
        self.cnt[sem] += by
        self.ops[eng].append(("inc", sem, by, fn))
        if selfwait is None:
            selfwait = sem == ESEM.get(eng)
        if selfwait and sem in ESEM.values():
            self.ops[eng].append(("wait", sem, self.cnt[sem]))
        return self.cnt[sem]

    def step(self, eng, fn, selfwait=True):
        return self.inc(eng, ESEM[eng], 1, fn, selfwait=selfwait)

    def newbatch(self, eng):
        q = ENG2Q[eng]
        self.qidx[q] = (self.qidx[q] + 1) % KQ
        self.qgate[q] = True

    def dma(self, eng, out, in_, **kw):
        q = ENG2Q[eng]
        if self.qwaited[q]:
            self.qidx[q] = (self.qidx[q] + 1) % KQ
            self.qgate[q] = True
            self.qwaited[q] = False
        sem = QSEMS[q][self.qidx[q]]
        if self.qgate[q]:
            if self.cnt[sem] > 0:
                self.ops[eng].append(("wait", sem, self.cnt[sem]))
            self.qgate[q] = False
        self.cnt[sem] += 16
        self.ops[eng].append(("inc", sem, 16,
                              lambda e, o=out, i=in_, kw=kw: e.dma_start(out=o, in_=i, **kw)))
        return (sem, self.cnt[sem])

    def wait(self, eng, sem, val=None):
        if sem in QUEUES:
            self.qwaited[sem] = True
            for s in QSEMS[sem]:
                if self.cnt[s] > 0:
                    self.ops[eng].append(("wait", s, self.cnt[s]))
            return
        v = self.cnt[sem] if val is None else val
        if v <= 0:
            return
        self.ops[eng].append(("wait", sem, v))

    def wait_tok(self, eng, tok):
        sem, v = tok
        for q in QUEUES:
            if sem in QSEMS[q]:
                self.qwaited[q] = True
        self.ops[eng].append(("wait", sem, v))

    def waitall(self, eng):
        for q in QUEUES:
            self.wait(eng, q)
        for e, s in ESEM.items():
            if e != eng:
                self.wait(eng, s)

    def fence(self, eng):
        self.wait(eng, ESEM[eng])

    def barrier(self):
        vals = {}
        for e in ENGS:
            vals[e] = self.inc(e, ESEM[e], 1, lambda eng: eng.nop(), selfwait=False)
        for e in ENGS:
            for o in ENGS:
                if o != e:
                    self.wait(e, ESEM[o], vals[o])
            for q in QUEUES:
                self.wait(e, q)

    def emit(self, eng_name, eng, sems):
        for item in self.ops[eng_name]:
            if item[0] == "op":
                item[1](eng)
            elif item[0] == "inc":
                _, sem, by, fn = item
                fn(eng).then_inc(sems[sem], by)
            else:
                _, sem, v = item
                eng.wait_ge(sems[sem], v)


def rope_tables():
    inv_freq = 1.0 / (10000.0 ** (np.arange(0, DH, 2, dtype=np.float64) / DH))
    freqs = np.arange(T, dtype=np.float64)[:, None] * inv_freq[None, :]
    emb = np.concatenate([freqs, freqs], axis=-1)
    cos = np.cos(emb).astype(np.float32).T
    sin = np.sin(emb).astype(np.float32).T
    sinp = sin.copy()
    sinp[:32, :] *= -1.0
    return (np.ascontiguousarray(np.concatenate([cos, cos], axis=0)),
            np.ascontiguousarray(np.concatenate([sinp, sinp], axis=0)))


def mask_consts():
    tri = np.ascontiguousarray(np.triu(np.ones((128, 128), np.float32)).astype(ml_dtypes.bfloat16))
    ems = np.zeros((4, 128, 512), np.float32)
    for m in range(4):
        for q in range(512):
            j = max(q - 128 * m + 1, 0)
            if j <= 127:
                ems[m, j, q] = -32768.0
    return tri, np.ascontiguousarray(ems.astype(ml_dtypes.bfloat16))


def ind2_const():
    a = np.zeros((128, 2), np.float32)
    a[:64, 0] = 1.0
    a[64:, 1] = 1.0
    return np.ascontiguousarray(a.astype(ml_dtypes.bfloat16))


def qtile_chunks(qt):
    if qt < 3:
        return list(range(12)), {}
    chunks = list(range(4 * qt + 4))
    masked = {4 * qt + m: m for m in range(4)}
    return chunks, masked


TPERM = [5, 6, 7, 0, 1, 2, 3, 4]


def declare_io(nc, dbg=True):
    io = {"dbg_on": dbg}
    d = nc.dram_tensor
    io["x"] = d("x", [T, D], F32, kind="ExternalInput").ap()
    io["wqT"] = d("wqT", [D, D], F32, kind="ExternalInput").ap()
    io["wkvT"] = d("wkvT", [D, D], F32, kind="ExternalInput").ap()
    io["woT"] = d("woT", [D, D], F32, kind="ExternalInput").ap()
    io["bo"] = d("bo", [1, D], F32, kind="ExternalInput").ap()
    io["g128"] = d("g128", [128, 1], F32, kind="ExternalInput").ap()
    io["b128"] = d("b128", [128, 1], F32, kind="ExternalInput").ap()
    io["cos_t"] = d("cos_t", [128, T], F32, kind="ExternalInput").ap()
    io["sinp_t"] = d("sinp_t", [128, T], F32, kind="ExternalInput").ap()
    io["tri"] = d("tri", [128, 128], BF16, kind="ExternalInput").ap()
    io["ems"] = d("ems", [4, 128, 512], BF16, kind="ExternalInput").ap()
    io["ind2"] = d("ind2", [128, 2], BF16, kind="ExternalInput").ap()
    io["idb"] = d("idb", [128, 128], BF16, kind="ExternalInput").ap()
    io["idf"] = d("idf", [128, 128], F32, kind="ExternalInput").ap()
    io["w_up"] = d("w_up", [NI, D], F32, kind="ExternalInput").ap()
    io["w_gate"] = d("w_gate", [NI, D], F32, kind="ExternalInput").ap()
    io["w_down"] = d("w_down", [D, NI], F32, kind="ExternalInput").ap()
    io["wupT"] = d("wupT", [D, NI], F32, kind="ExternalInput").ap()
    io["wgateT"] = d("wgateT", [D, NI], F32, kind="ExternalInput").ap()
    io["wdownT"] = d("wdownT", [NI, D], F32, kind="ExternalInput").ap()
    io["phase"] = d("phase", [1, 1], I32, kind="ExternalInput").ap()
    io["peer_mx"] = d("peer_mx", [1, 64], F32, kind="ExternalInput").ap()
    io["out"] = d("out", [T, D], F32, kind="ExternalOutput").ap()
    io["mx_out"] = d("mx_out", [1, 8], F32, kind="ExternalOutput").ap()
    if dbg:
        io["dbg"] = d("dbg", [T, D], F32, kind="ExternalOutput").ap()
    io["y1_d"] = d("y1_d", [T, D], F32, kind="Internal").ap()
    io["up_d"] = d("up_d", [NI, T], F32, kind="Internal").ap()
    io["gate_d"] = d("gate_d", [NI, T], F32, kind="Internal").ap()
    io["o_d"] = d("o_d", [NI, T], F32, kind="Internal").ap()
    io["outn_d"] = d("outn_d", [D, T], BF16, kind="Internal").ap()
    io["wupT_dq"] = d("wupT_dq", [D, NI], BF16, kind="Internal").ap()
    io["wgateT_dq"] = d("wgateT_dq", [D, NI], BF16, kind="Internal").ap()
    io["wdownT_dq"] = d("wdownT_dq", [NI, D], BF16, kind="Internal").ap()
    io["lkvF_d"] = d("lkvF_d", [D, T], BF16, kind="Internal").ap()
    io["lkvT_d"] = d("lkvT_d", [T, D], BF16, kind="Internal").ap()
    io["stats_d1"] = d("stats_d1", [16, 512], F32, kind="Internal").ap()
    io["stats_d2"] = d("stats_d2", [16, 512], F32, kind="Internal").ap()
    io["mx_d"] = d("mx_d", [128, 1], F32, kind="Internal").ap()
    io["scal_d"] = d("scal_d", [1, 16], F32, kind="Internal").ap()
    return io




def build_phase0(nc, P, io, ctx, stages='ABCD', dbg_what=None, c_qts=None, c_pairs=None, c_chunks=True, d_level=3):
    E = ctx.enter_context
    sb = nc.sbuf_tensor
    w_bf = [E(sb(f"p0_w{c}", [128, D], BF16)) for c in range(NOC)]
    g128_sb = E(sb("p0_g", [128, 1], F32))
    b128_sb = E(sb("p0_b", [128, 1], F32))
    ind2_sb = E(sb("p0_ind2", [128, 2], BF16))
    tri_sb = E(sb("p0_tri", [128, 128], BF16))
    ems_sb = E(sb("p0_ems", [128, 4 * 512], BF16))
    idb_sb = E(sb("p0_idb", [128, 128], BF16))
    ones_bf = E(sb("p0_ones", [128, 1], BF16))
    ones_f1 = E(sb("p0_onesf", [1, 128], F32))
    bo_sb = E(sb("p0_bo", [1, D], F32))
    knacc = E(sb("p0_knacc", [2, 8], F32))
    knall = E(sb("p0_knall", [2, 64], F32))
    tmp21 = E(sb("p0_tmp21", [2, 1], F32))
    xtm_f = [E(sb(f"p0_xtf{i}", [128, D], F32)) for i in range(4)]
    xtm_b = [E(sb(f"p0_xtb{i}", [128, D], BF16)) for i in range(4)]
    x_fm = [E(sb(f"p0_xfm{c}", [128, 512], BF16)) for c in range(NOC)]
    kvraw = E(sb("p0_kvr", [128, 512], F32))
    kvrot = E(sb("p0_kvrot", [128, 512], F32))
    kvc = E(sb("p0_kvc", [128, 512], F32))
    kvs = E(sb("p0_kvs", [128, 512], F32))
    kvrope = [E(sb(f"p0_kvro{c}", [128, 512], F32)) for c in range(NOC)]
    cs_sl = E(sb("p0_cs", [128, 512], F32))
    sn_sl = E(sb("p0_sn", [128, 512], F32))
    sq_bf = E(sb("p0_sq", [128, 512], BF16))
    v_bf = E(sb("p0_vbf", [128, 512], BF16))
    t1f = E(sb("p0_t1", [128, 512], F32))
    t2f = E(sb("p0_t2", [128, 512], F32))
    lkv_t = E(sb("p0_lkvt", [128, 512], BF16))
    lkvt_ev = E(sb("p0_lkvtev", [128, 512], BF16))
    stat2 = E(sb("p0_stat2", [2, 1024], F32))
    sums_p = E(sb("p0_sump", [16, 512], F32))
    ssq_p = E(sb("p0_ssqp", [16, 512], F32))
    m_p = E(sb("p0_mp", [16, 512], F32))
    msq_p = E(sb("p0_msqp", [16, 512], F32))
    var_p = E(sb("p0_varp", [16, 512], F32))
    rstd_p = E(sb("p0_rstdp", [16, 512], F32))
    m_b = E(sb("p0_mb", [128, 512], F32))
    rstd_b = E(sb("p0_rstdb", [128, 512], F32))
    q_bf = [E(sb(f"p0_q{c}", [128, 512], BF16)) for c in range(NOC)]
    qnacc = E(sb("p0_qnacc", [2, 8], F32))
    cprod = E(sb("p0_cprod", [2, 8], F32))
    negc = E(sb("p0_negc", [2, 8], F32))
    negc16 = E(sb("p0_negc16", [1, 16], F32))
    bias_all = E(sb("p0_bias", [128, 16], F32))
    pA = E(sb("p0_pA", [128, 1024], BF16))
    pB = E(sb("p0_pB", [128, 1024], BF16))
    lkF = [E(sb(f"p0_lkF{i}", [128, 4096], BF16)) for i in range(2)]
    lkT = [E(sb(f"p0_lkT{i}", [128, 4096], BF16)) for i in range(2)]
    av_sb = [E(sb(f"p0_av{p}", [128, 512], F32)) for p in range(8)]
    sum_ev = E(sb("p0_sumev", [128, 512], F32))
    l2t = E(sb("p0_l2t", [2, 1024], F32))
    l2s = E(sb("p0_l2s", [2, 512], F32))
    l_all = E(sb("p0_lall", [16, 512], F32))
    rl_all = E(sb("p0_rlall", [16, 512], F32))
    rl_b = E(sb("p0_rlb", [128, 512], F32))
    outn_t = E(sb("p0_outn", [128, 512], BF16))
    on_t = [E(sb(f"p0_ont{c}", [128, 128], BF16)) for c in range(NOC)]
    xt_d = E(sb("p0_xtd", [128, D], F32))
    z_sb = E(sb("p0_z", [128, D], F32))
    zsq = E(sb("p0_zsq", [128, D], F32))
    rsum = E(sb("p0_rsum", [128, 1], F32))
    mvec = E(sb("p0_mvec", [128, 1], F32))
    ssum = E(sb("p0_ssum", [128, 1], F32))
    varv = E(sb("p0_varv", [128, 1], F32))
    rstdv = E(sb("p0_rstdv", [128, 1], F32))
    y1t = E(sb("p0_y1t", [128, D], F32))
    mxcol = E(sb("p0_mxcol", [128, NTT], F32))
    mxfin = E(sb("p0_mxfin", [128, 1], F32))
    mxrow = E(sb("p0_mxrow", [1, 128], F32))
    mx8 = E(sb("p0_mx8", [1, 8], F32))
    epsA = E(sb("p0_epsA", [128, 1], F32))
    epsB = E(sb("p0_epsB", [128, 1], F32))
    ps_a = E(nc.psum_tensor("p0_psa", [128, 1024], F32))
    ps_b = E(nc.psum_tensor("p0_psb", [128, 1024], F32))
    ps_av = E(nc.psum_tensor("p0_psav", [128, 512], F32))
    ps_sum = E(nc.psum_tensor("p0_pssum", [128, 512], F32))
    ps_c0 = E(nc.psum_tensor("p0_psc0", [128, 512], F32))
    ps_c1 = E(nc.psum_tensor("p0_psc1", [128, 512], F32))

    dbgv = io.get("dbg")
    dbg_fm = dbgv.rearrange("(r s) c -> r (s c)", s=4) if dbgv is not None else None

    P.step("v", lambda v: v.memset(ones_bf[:], 1.0))
    P.step("v", lambda v: v.memset(epsA[:], EPS))
    P.step("v", lambda v: v.memset(epsB[:], 1e-30))
    P.step("v", lambda v: v.memset(ones_f1[:], 1.0))
    P.step("v", lambda v: v.memset(knacc[:], 0.0))
    P.step("v", lambda v: v.memset(qnacc[:], 0.0))
    for (dst, src) in ((g128_sb, io["g128"]), (b128_sb, io["b128"]),
                       (ind2_sb, io["ind2"]), (tri_sb, io["tri"]),
                       (idb_sb, io["idb"]), (bo_sb, io["bo"])):
        P.dma("s", dst[:], src)
    P.dma("s", ems_sb[:].rearrange("p (m q) -> p m q", m=4), io["ems"].rearrange("m p q -> p m q"))

    def load_weights(name):
        for c in range(NOC):
            P.waitall("s")
            P.dma("s", xtm_f[0][:], io[name][128 * c:128 * (c + 1), :])
            P.wait("v", "sd")
            P.step("v", lambda v, c=c: v.tensor_copy(out=w_bf[c][:], in_=xtm_f[0][:]))

    def load_x_group(tq):
        P.waitall("s")
        for i in range(4):
            P.dma("s", xtm_f[i][:], io["x"][512 * tq + 128 * i:512 * tq + 128 * (i + 1), :])
        P.wait("v", "sd")
        P.waitall("v")
        for i in range(4):
            P.step("v", lambda v, i=i: v.tensor_copy(out=xtm_b[i][:], in_=xtm_f[i][:]))
        for hc in range(NOC):
            bank = hc % 2
            pview = ps_a[:, 512 * bank:512 * (bank + 1)].bitcast(BF16)
            P.wait("t", "vs")
            for i in range(4):
                P.step("t", lambda t, hc=hc, i=i, pv=pview:
                       t.transpose(out=pv[:, 128 * i:128 * (i + 1)],
                                   in_=xtm_b[i][:, 128 * hc:128 * (hc + 1)],
                                   identity=idb_sb[:]))
            P.wait("v", "ts")
            P.step("v", lambda v, hc=hc, pv=pview: v.tensor_copy(out=x_fm[hc][:], in_=pv[:, 0:512]))
        P.wait("t", "vs")

    def proj_rope(oc, outbuf):
        ps = ps_c0 if oc % 2 == 0 else ps_c1
        P.wait("t", "vs")
        for hc in range(NOC):
            P.step("t", lambda t, hc=hc, oc=oc, ps=ps:
                   t.matmul(out=ps[:], lhsT=w_bf[hc][:, 128 * oc:128 * (oc + 1)],
                            rhs=x_fm[hc][:], start=(hc == 0), stop=(hc == NOC - 1)))
        P.wait("v", "ts")
        P.wait("v", "ad")
        P.step("v", lambda v, ps=ps: v.tensor_copy(out=kvraw[:], in_=ps[:]))
        P.wait("a", "vs")
        P.wait("a", "gs")
        for (d0, s0) in ((0, 32), (32, 0), (64, 96), (96, 64)):
            P.dma("a", kvrot[d0:d0 + 32, :], kvraw[s0:s0 + 32, :])
        P.wait("g", "ad")
        P.wait("g", "vs")
        P.step("g", lambda g: g.tensor_mul(out=kvs[:], in0=kvrot[:], in1=sn_sl[:]))
        P.step("v", lambda v: v.tensor_mul(out=kvc[:], in0=kvraw[:], in1=cs_sl[:]))
        P.wait("v", "gs")
        P.wait("v", "as_")
        P.step("v", lambda v, outbuf=outbuf: v.tensor_add(out=outbuf[:], in0=kvc[:], in1=kvs[:]))

    if "B" in stages:
        load_weights("wkvT")
    P.barrier()

    for tq in (range(NQT) if "B" in stages else []):
        load_x_group(tq)
        P.waitall("s")
        P.dma("s", cs_sl[:], io["cos_t"][:, 512 * tq:512 * (tq + 1)])
        P.dma("s", sn_sl[:], io["sinp_t"][:, 512 * tq:512 * (tq + 1)])
        P.wait("v", "sd")
        P.wait("g", "sd")
        for oc in range(NOC):
            proj_rope(oc, kvrope[oc])
            if dbg_what == "kvrope":
                P.wait("g", "vs")
                P.dma("g", dbg_fm[128 * oc:128 * (oc + 1), 512 * tq:512 * (tq + 1)], kvrope[oc][:])
            P.wait("a", "vs")
            P.wait("a", "ts")
            P.inc("a", "as_", 1, lambda a, oc=oc: a.activation(out=sq_bf[:], in_=kvrope[oc][:], func=AF.Square))
            P.step("v", lambda v, oc=oc: v.tensor_copy(out=v_bf[:], in_=kvrope[oc][:]))
            P.wait("t", "as_")
            P.wait("t", "vs")
            P.step("t", lambda t: t.matmul(out=ps_sum[0:2, :], lhsT=ind2_sb[:], rhs=v_bf[:], start=True, stop=True))
            P.step("t", lambda t: t.matmul(out=ps_av[0:2, :], lhsT=ind2_sb[:], rhs=sq_bf[:], start=True, stop=True))
            P.wait("v", "ts")
            P.wait("v", "gd")
            P.step("v", lambda v: v.tensor_copy(out=stat2[:, 0:512], in_=ps_sum[0:2, :]))
            P.step("v", lambda v: v.tensor_copy(out=stat2[:, 512:1024], in_=ps_av[0:2, :]))
            P.wait("g", "vs")
            P.dma("g", io["stats_d1"][2 * oc:2 * oc + 2, :], stat2[:, 0:512])
            P.dma("g", io["stats_d2"][2 * oc:2 * oc + 2, :], stat2[:, 512:1024])
            P.wait("t", "vs")
        P.waitall("s")
        P.dma("s", sums_p[:], io["stats_d1"])
        P.dma("s", ssq_p[:], io["stats_d2"])
        P.wait("v", "sd")
        P.step("v", lambda v: v.tensor_scalar(out=m_p[:], in0=sums_p[:], scalar1=1.0 / DH, scalar2=None, op0=ALU.mult))
        P.step("v", lambda v: v.tensor_mul(out=msq_p[:], in0=m_p[:], in1=m_p[:]))
        P.step("v", lambda v: v.tensor_scalar(out=var_p[:], in0=ssq_p[:], scalar1=1.0 / DH, scalar2=None, op0=ALU.mult))
        P.step("v", lambda v: v.tensor_sub(out=var_p[:], in0=var_p[:], in1=msq_p[:]))
        P.wait("a", "vs")
        P.inc("a", "as_", 1, lambda a: a.activation(out=rstd_p[:], in_=var_p[:], func=AF.Ln, bias=epsA[0:16, 0:1]))
        P.inc("a", "as_", 1, lambda a: a.activation(out=rstd_p[:], in_=rstd_p[:], func=AF.Exp, scale=-0.5))
        for oc in range(NOC):
            P.waitall("s")
            P.wait("s", "as_")
            P.dma("s", m_b[:], m_p[2 * oc:2 * oc + 2, :].unsqueeze(1).broadcast_to([2, 64, 512]))
            P.dma("s", rstd_b[:], rstd_p[2 * oc:2 * oc + 2, :].unsqueeze(1).broadcast_to([2, 64, 512]))
            P.wait("v", "sd")
            P.wait("v", "sd")
            P.step("v", lambda v, oc=oc: v.tensor_sub(out=t1f[:], in0=kvrope[oc][:], in1=m_b[:]))
            P.step("v", lambda v: v.scalar_tensor_tensor(
                out=t2f[:], in0=t1f[:], scalar=g128_sb[:, 0:1], in1=rstd_b[:], op0=ALU.mult, op1=ALU.mult))
            P.wait("v", "gd")
            P.wait("v", "as_")
            P.step("v", lambda v: v.tensor_scalar(
                out=lkv_t[:], in0=t2f[:], scalar1=b128_sb[:, 0:1], scalar2=None, op0=ALU.add))
            P.wait("g", "vs")
            P.dma("g", io["lkvF_d"][128 * oc:128 * (oc + 1), 512 * tq:512 * (tq + 1)], lkv_t[:])
            if dbg_what == "lkv":
                P.dma("g", dbg_fm[128 * oc:128 * (oc + 1), 512 * tq:512 * (tq + 1)], lkv_t[:])
            P.wait("a", "vs")
            P.wait("a", "ts")
            P.inc("a", "as_", 1, lambda a: a.activation(out=sq_bf[:], in_=lkv_t[:], func=AF.Square))
            P.wait("t", "as_")
            P.step("t", lambda t: t.matmul(out=ps_sum[0:2, :], lhsT=ind2_sb[:], rhs=sq_bf[:], start=True, stop=True))
            P.wait("v", "ts")
            P.step("v", lambda v, oc=oc, tq=tq: v.tensor_reduce(
                out=knall[:, 8 * tq + oc:8 * tq + oc + 1], in_=ps_sum[0:2, :], axis=AX.X, op=ALU.max))
            bank = oc % 2
            pview = ps_a[:, 512 * bank:512 * (bank + 1)].bitcast(BF16)
            P.wait("t", "vs")
            P.wait("t", "gd")
            for i in range(4):
                P.step("t", lambda t, i=i, pv=pview:
                       t.transpose(out=pv[:, 128 * i:128 * (i + 1)],
                                   in_=lkv_t[:, 128 * i:128 * (i + 1)],
                                   identity=idb_sb[:]))
            P.wait("v", "ts")
            P.wait("v", "gd")
            P.step("v", lambda v, pv=pview: v.tensor_copy(out=lkvt_ev[:], in_=pv[:, 0:512]))
            P.wait("g", "vs")
            for i in range(4):
                P.dma("g", io["lkvT_d"][512 * tq + 128 * i:512 * tq + 128 * (i + 1), 128 * oc:128 * (oc + 1)],
                      lkvt_ev[:, 128 * i:128 * (i + 1)])
    P.barrier()
    if "C" in stages:
        load_weights("wqT")
    P.barrier()

    if "C" in stages:
        P.step("v", lambda v: v.tensor_reduce(
            out=knacc[:].rearrange("r (one c) -> r one c", one=1),
            in_=knall[:].rearrange("r (t c) -> r c t", t=8), axis=AX.X, op=ALU.max))

    _qts = (c_qts if c_qts is not None else range(NQT)) if "C" in stages else []
    for qt in _qts:
        chunks, masked = qtile_chunks(qt)
        nch = len(chunks)
        G = nch // 2
        load_x_group(qt)
        P.waitall("s")
        P.dma("s", cs_sl[:], io["cos_t"][:, 512 * qt:512 * (qt + 1)])
        P.dma("s", sn_sl[:], io["sinp_t"][:, 512 * qt:512 * (qt + 1)])
        P.wait("v", "sd")
        P.wait("g", "sd")
        for oc in range(NOC):
            proj_rope(oc, t2f)
            P.step("v", lambda v, oc=oc: v.tensor_copy(out=q_bf[oc][:], in_=t2f[:]))
            if dbg_what == "q":
                P.wait("g", "vs")
                P.dma("g", dbg_fm[128 * oc:128 * (oc + 1), 512 * qt:512 * (qt + 1)], t2f[:])
            P.wait("a", "vs")
            P.wait("a", "ts")
            P.inc("a", "as_", 1, lambda a, oc=oc: a.activation(out=sq_bf[:], in_=q_bf[oc][:], func=AF.Square))
            P.wait("t", "as_")
            P.step("t", lambda t: t.matmul(out=ps_sum[0:2, :], lhsT=ind2_sb[:], rhs=sq_bf[:], start=True, stop=True))
            P.wait("v", "ts")
            P.step("v", lambda v, oc=oc: v.tensor_reduce(out=qnacc[:, oc:oc + 1], in_=ps_sum[0:2, :],
                                                         axis=AX.X, op=ALU.max))
            P.wait("t", "vs")
        P.wait("a", "vs")
        P.inc("a", "as_", 1, lambda a: a.activation(out=cprod[:], in_=qnacc[:], func=AF.Ln, bias=epsB[0:2, 0:1]))
        P.inc("a", "as_", 1, lambda a: a.activation(out=negc[:], in_=knacc[:], func=AF.Ln, bias=epsB[0:2, 0:1]))
        P.wait("v", "as_")
        P.step("v", lambda v: v.tensor_add(out=cprod[:], in0=cprod[:], in1=negc[:]))
        P.wait("a", "vs")
        P.inc("a", "as_", 1, lambda a: a.activation(out=negc[:], in_=cprod[:], func=AF.Exp, scale=0.5))
        P.wait("v", "as_")
        P.step("v", lambda v: v.tensor_scalar(out=negc[:], in0=negc[:], scalar1=-0.125, scalar2=None, op0=ALU.mult))
        P.waitall("s")
        P.dma("s", negc16[:].rearrange("p (c r) -> p c r", r=2)[:, :, 0:1], negc[0:1, :].unsqueeze(2),
              allow_slow_non_contiguous=True)
        P.dma("s", negc16[:].rearrange("p (c r) -> p c r", r=2)[:, :, 1:2], negc[1:2, :].unsqueeze(2),
              allow_slow_non_contiguous=True)
        P.wait("s", "sd")
        P.dma("s", bias_all[:], negc16[:].unsqueeze(1).broadcast_to([1, 128, 16]))
        P.wait("a", "sd")

        def load_pair_slices(pair):
            buf = pair % 2
            P.newbatch("s")
            toks = [P.dma("s", lkF[buf][:, 0:128 * nch], io["lkvF_d"][128 * pair:128 * (pair + 1), 0:128 * nch])]
            for c4 in range(0, nch, 4):
                hi = min(c4 + 4, nch)
                toks.append(P.dma("s", lkT[buf][:, 128 * c4:128 * hi].rearrange("p (j c) -> p j c", c=128),
                      io["lkvT_d"][128 * c4:128 * hi, 128 * pair:128 * (pair + 1)]
                      .rearrange("(j p) c -> p j c", p=128)))
            return toks[-1]

        def scores_group(pair, g):
            buf = pair % 2
            for ci in range(2):
                cx = chunks[2 * g + ci]
                stop0 = cx not in masked
                for (base, psx) in ((0, ps_a), (64, ps_b)):
                    P.step("t", lambda t, cx=cx, ci=ci, base=base, psx=psx, pair=pair, stop0=stop0, buf=buf:
                           t.matmul(out=psx[:, 512 * ci:512 * (ci + 1)],
                                    lhsT=lkF[buf][base:base + 64, 128 * cx:128 * (cx + 1)],
                                    rhs=q_bf[pair][base:base + 64, :],
                                    start=True, stop=stop0,
                                    tile_position=(base, 0)))
                    if not stop0:
                        m = masked[cx]
                        P.step("t", lambda t, m=m, ci=ci, psx=psx:
                               t.matmul(out=psx[:, 512 * ci:512 * (ci + 1)],
                                        lhsT=tri_sb[:],
                                        rhs=ems_sb[:, 512 * m:512 * (m + 1)],
                                        start=False, stop=True))

        def av_group(pair, g):
            buf = pair % 2
            for ci in range(2):
                cx = chunks[2 * g + ci]
                st = (g == 0 and ci == 0)
                sp = (g == G - 1 and ci == 1)
                sts = (g == 0)
                sps = (g == G - 1)
                P.step("t", lambda t, cx=cx, ci=ci, st=st, sp=sp, buf=buf:
                       t.matmul(out=ps_av[0:64, :], lhsT=lkT[buf][:, 128 * cx:128 * cx + 64],
                                rhs=pA[:, 512 * ci:512 * (ci + 1)],
                                start=st, stop=sp, tile_position=(0, 0), skip_group_check=True))
                P.step("t", lambda t, cx=cx, ci=ci, st=st, sp=sp, buf=buf:
                       t.matmul(out=ps_av[64:128, :], lhsT=lkT[buf][:, 128 * cx + 64:128 * (cx + 1)],
                                rhs=pB[:, 512 * ci:512 * (ci + 1)],
                                start=st, stop=sp, tile_position=(0, 64), skip_group_check=True))
                P.step("t", lambda t, ci=ci, sts=sts, sps=sps:
                       t.matmul(out=ps_sum[32 * ci:32 * ci + 1, :], lhsT=ones_bf[:],
                                rhs=pA[:, 512 * ci:512 * (ci + 1)],
                                start=sts, stop=sps, tile_position=(0, 32 * ci), skip_group_check=True))
                P.step("t", lambda t, ci=ci, sts=sts, sps=sps:
                       t.matmul(out=ps_sum[64 + 32 * ci:64 + 32 * ci + 1, :], lhsT=ones_bf[:],
                                rhs=pB[:, 512 * ci:512 * (ci + 1)],
                                start=sts, stop=sps, tile_position=(0, 64 + 32 * ci), skip_group_check=True))

        if not c_chunks:
            continue
        _pairs = list(c_pairs if c_pairs is not None else range(8))
        ts_end_pair = []
        load_toks = [load_pair_slices(_pairs[0])]
        for pi, pair in enumerate(_pairs):
            if pi + 1 < len(_pairs):
                if pi >= 1:
                    P.wait("s", "ts", ts_end_pair[pi - 1])
                load_toks.append(load_pair_slices(_pairs[pi + 1]))
            P.step("v", lambda v: v.memset(ps_sum[:], 0.0))
            P.wait_tok("t", load_toks[pi])
            P.wait("t", "vs")
            P.wait("t", "as_")
            scores_group(pair, 0)
            for g in range(G):
                P.wait("a", "ts")
                h0 = 2 * pair
                P.inc("a", "as_", 1, lambda a, h0=h0: a.activation(
                    out=pA[:], in_=ps_a[:], func=AF.Exp, scale=0.125, bias=bias_all[:, h0:h0 + 1]))
                P.inc("a", "as_", 1, lambda a, h0=h0: a.activation(
                    out=pB[:], in_=ps_b[:], func=AF.Exp, scale=0.125, bias=bias_all[:, h0 + 1:h0 + 2]))
                P.wait("t", "as_")
                if g + 1 < G:
                    scores_group(pair, g + 1)
                av_group(pair, g)
            P.wait("v", "ts")
            P.step("v", lambda v: v.tensor_copy(out=sum_ev[:], in_=ps_sum[:]))
            P.step("v", lambda v, pair=pair: v.tensor_copy(out=av_sb[pair][:], in_=ps_av[:]))
            P.wait("g", "vs")
            P.dma("g", l2t[0:1, 0:512], sum_ev[0:1, :])
            P.dma("g", l2t[0:1, 512:1024], sum_ev[32:33, :])
            P.dma("g", l2t[1:2, 0:512], sum_ev[64:65, :])
            P.dma("g", l2t[1:2, 512:1024], sum_ev[96:97, :])
            P.wait("v", "gd")
            P.step("v", lambda v: v.tensor_add(out=l2s[:], in0=l2t[:, 0:512], in1=l2t[:, 512:1024]))
            P.wait("g", "vs")
            P.dma("g", l_all[2 * pair:2 * pair + 2, :], l2s[:])
            P.wait("t", "vs")
            ts_end_pair.append(P.cnt["ts"])
        P.wait("v", "gd")
        P.step("v", lambda v: v.reciprocal(out=rl_all[:], in_=l_all[:]))
        for pair in _pairs:
            P.waitall("s")
            P.dma("s", rl_b[:], rl_all[2 * pair:2 * pair + 2, :].unsqueeze(1).broadcast_to([2, 64, 512]))
            P.wait("v", "sd")
            P.wait("v", "gd")
            P.step("v", lambda v, pair=pair: v.tensor_mul(out=outn_t[:], in0=av_sb[pair][:], in1=rl_b[:]))
            P.wait("g", "vs")
            P.dma("g", io["outn_d"][128 * pair:128 * (pair + 1), 512 * TPERM[qt]:512 * (TPERM[qt] + 1)], outn_t[:])
            if dbg_what == "outn":
                P.dma("g", dbg_fm[128 * pair:128 * (pair + 1), 512 * TPERM[qt]:512 * (TPERM[qt] + 1)], outn_t[:])
    P.barrier()

    if dbg_what == "outnd" and "D" in stages:
        for dc in range(NOC):
            for tq2 in range(8):
                P.waitall("s")
                P.dma("s", outn_t[:], io["outn_d"][128 * dc:128 * (dc + 1), 512 * tq2:512 * (tq2 + 1)])
                P.wait("v", "sd")
                P.step("v", lambda v: v.tensor_copy(out=sum_ev[:], in_=outn_t[:]))
                P.wait("g", "vs")
                P.dma("g", dbg_fm[128 * dc:128 * (dc + 1), 512 * tq2:512 * (tq2 + 1)], sum_ev[:])
    if "D" in stages:
        load_weights("woT")
    P.barrier()
    for tt in (range(NTT) if "D" in stages else []):
        P.waitall("s")
        for dc in range(NOC):
            P.dma("s", on_t[dc][:], io["outn_d"][128 * dc:128 * (dc + 1), 128 * tt:128 * (tt + 1)])
        P.dma("s", xt_d[:], io["x"][128 * tt:128 * (tt + 1), :])
        if dbg_what == "onread":
            P.wait("v", "sd")
            P.wait("v", "gd")
            for dc in range(NOC):
                P.step("v", lambda v, dc=dc: v.tensor_copy(out=y1t[:, 128 * dc:128 * (dc + 1)], in_=on_t[dc][:]))
            P.wait("g", "vs")
            P.dma("g", io["dbg"][128 * tt:128 * (tt + 1), :], y1t[:])
        P.wait("t", "sd")
        P.wait("t", "vs")
        for half in range(2):
            psh = ps_a[:, 512 * half:512 * (half + 1)]
            for dc in range(NOC):
                P.step("t", lambda t, dc=dc, half=half, psh=psh:
                       t.matmul(out=psh[:], lhsT=on_t[dc][:], rhs=w_bf[dc][:, 512 * half:512 * (half + 1)],
                                start=(dc == 0), stop=False))
            P.step("t", lambda t, half=half, psh=psh:
                   t.matmul(out=psh[:], lhsT=ones_f1[:], rhs=bo_sb[0:1, 512 * half:512 * (half + 1)],
                            start=False, stop=True))
        P.wait("v", "ts")
        P.wait("v", "gd")
        P.wait("v", "as_")
        for half in range(2):
            P.step("v", lambda v, half=half: v.tensor_add(
                out=z_sb[:, 512 * half:512 * (half + 1)], in0=ps_a[:, 512 * half:512 * (half + 1)],
                in1=xt_d[:, 512 * half:512 * (half + 1)]))
        if dbg_what == "z":
            P.wait("g", "vs")
            P.dma("g", io["dbg"][128 * tt:128 * (tt + 1), :], z_sb[:])
        if d_level < 2:
            continue
        P.step("v", lambda v: v.tensor_reduce(out=rsum[:], in_=z_sb[:], axis=AX.X, op=ALU.add))
        P.wait("a", "vs")
        P.inc("a", "as_", 1, lambda a: a.activation(out=zsq[:], in_=z_sb[:], func=AF.Square))
        P.wait("v", "as_")
        P.step("v", lambda v: v.tensor_reduce(out=ssum[:], in_=zsq[:], axis=AX.X, op=ALU.add))
        P.wait("g", "vs")
        P.wait("g", "as_")
        P.step("g", lambda g: g.tensor_scalar(out=mvec[:], in0=rsum[:], scalar1=1.0 / D, scalar2=None, op0=ALU.mult))
        P.step("g", lambda g: g.tensor_mul(out=rstdv[:], in0=mvec[:], in1=mvec[:]))
        P.step("g", lambda g: g.tensor_scalar(out=varv[:], in0=ssum[:], scalar1=1.0 / D, scalar2=None, op0=ALU.mult))
        P.step("g", lambda g: g.tensor_sub(out=varv[:], in0=varv[:], in1=rstdv[:]))
        P.wait("a", "gs")
        P.inc("a", "as_", 1, lambda a: a.activation(out=rstdv[:], in_=varv[:], func=AF.Ln, bias=epsA[:, 0:1]))
        P.inc("a", "as_", 1, lambda a: a.activation(out=rstdv[:], in_=rstdv[:], func=AF.Exp, scale=-0.5))
        if dbg_what == "ln1":
            P.wait("g", "vs")
            P.wait("g", "as_")
            P.dma("g", io["dbg"][128 * tt:128 * (tt + 1), 0:1], mvec[:], allow_slow_non_contiguous=True)
            P.dma("g", io["dbg"][128 * tt:128 * (tt + 1), 1:2], varv[:], allow_slow_non_contiguous=True)
            P.dma("g", io["dbg"][128 * tt:128 * (tt + 1), 2:3], rstdv[:], allow_slow_non_contiguous=True)
            P.dma("g", io["dbg"][128 * tt:128 * (tt + 1), 3:4], rsum[:], allow_slow_non_contiguous=True)
            P.dma("g", io["dbg"][128 * tt:128 * (tt + 1), 4:5], ssum[:], allow_slow_non_contiguous=True)
        P.wait("v", "as_")
        P.wait("v", "gs")
        P.step("v", lambda v: v.tensor_scalar(out=y1t[:], in0=z_sb[:], scalar1=mvec[:, 0:1],
                                              scalar2=rstdv[:, 0:1], op0=ALU.subtract, op1=ALU.mult))
        P.step("v", lambda v, tt=tt: v.tensor_reduce(out=mxcol[:, tt:tt + 1], in_=y1t[:], axis=AX.X,
                                                     op=ALU.max, apply_absolute_value=True))
        if d_level < 3:
            continue
        P.wait("g", "vs")
        P.dma("g", io["y1_d"][128 * tt:128 * (tt + 1), :], y1t[:])
        if io["dbg_on"] and dbg_what in (None, "y1"):
            P.dma("g", io["dbg"][128 * tt:128 * (tt + 1), :], y1t[:])
    P.step("v", lambda v: v.tensor_reduce(out=mxfin[:], in_=mxcol[:], axis=AX.X, op=ALU.max))
    P.wait("a", "vs")
    P.dma("a", io["mx_d"], mxfin[:])
    P.wait("a", "ad")
    P.dma("a", mxrow[:], io["mx_d"].rearrange("p one -> one p"))
    P.wait("v", "ad")
    P.step("v", lambda v: v.memset(mx8[:], 0.0))
    P.step("v", lambda v: v.tensor_reduce(out=mx8[:, 0:1], in_=mxrow[:], axis=AX.X, op=ALU.max))
    P.wait("g", "vs")
    P.dma("g", io["mx_out"], mx8[:])
    P.wait("g", "gd")


def peer_reduce(nc, P, io, E, sb, name):
    pm = E(sb(f"{name}_pm", [1, 64], F32))
    gmax = E(sb(f"{name}_gmax", [1, 8], F32))
    P.dma("s", pm[:], io["peer_mx"])
    P.wait("v", "sd")
    P.step("v", lambda v: v.tensor_reduce(
        out=gmax[:].rearrange("p (s one) -> p s one", one=1),
        in_=pm[:].rearrange("p (s r) -> p s r", r=8), axis=AX.X, op=ALU.max))
    return gmax


def build_phase1(nc, P, io, ctx, parts='sdxmer'):
    E = ctx.enter_context
    sb = nc.sbuf_tensor
    gmax = peer_reduce(nc, P, io, E, sb, "p1")
    s1v = E(sb("p1_s1", [1, 1], F32))
    rs1v = E(sb("p1_rs1", [1, 1], F32))
    s1_pp = E(sb("p1_s1pp", [128, 1], F32))
    rs1_pp = E(sb("p1_rs1pp", [128, 1], F32))
    P.step("v", lambda v: v.tensor_scalar(out=s1v[:], in0=gmax[:, 0:1], scalar1=1.0 / QMAX,
                                          scalar2=1e-12, op0=ALU.mult, op1=ALU.max))
    P.wait("a", "vs")
    P.inc("a", "as_", 1, lambda a: a.activation(out=rs1v[:], in_=s1v[:], func=AF.Copy))
    P.wait("v", "as_")
    P.step("v", lambda v: v.reciprocal(out=rs1v[:], in_=rs1v[:]))
    P.waitall("s")
    P.dma("s", s1_pp[:], s1v[:].unsqueeze(1).broadcast_to([1, 128, 1]))
    P.dma("s", rs1_pp[:], rs1v[:].unsqueeze(1).broadcast_to([1, 128, 1]))

    nat = E(sb("p1_nat", [128, NI], F32))
    bm = E(sb("p1_bm", [128, 96], F32))
    s_c = E(sb("p1_sc", [128, 96], F32))
    rs_c = E(sb("p1_rsc", [128, 96], F32))
    sT_up = E(sb("p1_sTup", [32, NI], F32))
    rT_up = E(sb("p1_rTup", [32, NI], F32))
    sT_gt = E(sb("p1_sTgt", [32, NI], F32))
    rT_gt = E(sb("p1_rTgt", [32, NI], F32))
    sT_dn = E(sb("p1_sTdn", [96, D], F32))
    rT_dn = E(sb("p1_rTdn", [96, D], F32))
    wt_f = [E(sb(f"p1_wtf{i}", [128, 512], F32)) for i in range(2)]
    sb_b = E(sb("p1_sbb", [128, 512], F32))
    rb_b = E(sb("p1_rbb", [128, 512], F32))
    ti32 = E(sb("p1_ti32", [128, 512], I32))
    tf32 = E(sb("p1_tf32", [128, 512], F32))
    wq_b = E(sb("p1_wqb", [128, 512], BF16))
    ps_t = E(nc.psum_tensor("p1_pst", [128, 512], F32))
    ps_c0 = E(nc.psum_tensor("p1_psc0", [128, 512], F32))
    ps_c1 = E(nc.psum_tensor("p1_psc1", [128, 512], F32))

    def dequant(natname, tname, dqname, K, O, sT, rT):
        nblk = K // 32
        for c in range(O // 128):
            P.waitall("s")
            P.dma("s", nat[:, 0:K], io[natname][128 * c:128 * (c + 1), :])
            P.wait("v", "sd")
            P.step("v", lambda v, K=K, nblk=nblk: v.tensor_reduce(
                out=bm[:, 0:nblk].rearrange("p (b one) -> p b one", one=1),
                in_=nat[:, 0:K].rearrange("p (b w) -> p b w", w=32),
                axis=AX.X, op=ALU.max, apply_absolute_value=True))
            P.wait("g", "vs")
            P.step("g", lambda g, nblk=nblk: g.tensor_scalar(
                out=s_c[:, 0:nblk], in0=bm[:, 0:nblk], scalar1=1.0 / 7.0, scalar2=1e-12,
                op0=ALU.mult, op1=ALU.max))
            P.wait("v", "gs")
            P.step("v", lambda v, nblk=nblk: v.reciprocal(out=rs_c[:, 0:nblk], in_=s_c[:, 0:nblk]))
            P.wait("t", "vs")
            P.wait("t", "gs")
            P.step("t", lambda t, nblk=nblk: t.transpose(out=ps_t[0:nblk, 0:128], in_=s_c[:, 0:nblk],
                                                         identity=idf_sb[:]))
            P.wait("v", "ts")
            P.step("v", lambda v, c=c, nblk=nblk, sT=sT: v.tensor_copy(
                out=sT[0:nblk, 128 * c:128 * (c + 1)], in_=ps_t[0:nblk, 0:128]))
            P.wait("t", "vs")
            P.step("t", lambda t, nblk=nblk: t.transpose(out=ps_t[0:nblk, 0:128], in_=rs_c[:, 0:nblk],
                                                         identity=idf_sb[:]))
            P.wait("v", "ts")
            P.step("v", lambda v, c=c, nblk=nblk, rT=rT: v.tensor_copy(
                out=rT[0:nblk, 128 * c:128 * (c + 1)], in_=ps_t[0:nblk, 0:128]))
            P.wait("t", "vs")
        for hc in range(K // 128):
            for os_ in range(O // 512):
                buf = (hc * (O // 512) + os_) % 2
                P.waitall("s")
                P.dma("s", wt_f[buf][:], io[tname][128 * hc:128 * (hc + 1), 512 * os_:512 * (os_ + 1)])
                P.dma("s", sb_b[:], sT[4 * hc:4 * hc + 4, 512 * os_:512 * (os_ + 1)]
                      .unsqueeze(1).broadcast_to([4, 32, 512]))
                P.dma("s", rb_b[:], rT[4 * hc:4 * hc + 4, 512 * os_:512 * (os_ + 1)]
                      .unsqueeze(1).broadcast_to([4, 32, 512]))
                P.wait("v", "sd")
                P.wait("v", "gd")
                P.step("v", lambda v, buf=buf: v.tensor_mul(out=tf32[:], in0=wt_f[buf][:], in1=rb_b[:]))
                P.step("v", lambda v: v.tensor_copy(out=ti32[:], in_=tf32[:]))
                P.wait("v", "gd")
                P.step("v", lambda v: v.tensor_mul(out=wq_b[:], in0=ti32[:], in1=sb_b[:]))
                P.wait("g", "vs")
                P.dma("g", io[dqname][128 * hc:128 * (hc + 1), 512 * os_:512 * (os_ + 1)], wq_b[:])

    idf_sb = E(sb("p1_idf", [128, 128], F32))
    P.dma("s", idf_sb[:], io["idf"])
    if "d" in parts:
        dequant("w_up", "wupT", "wupT_dq", D, NI, sT_up, rT_up)
        dequant("w_gate", "wgateT", "wgateT_dq", D, NI, sT_gt, rT_gt)
        dequant("w_down", "wdownT", "wdownT_dq", NI, D, sT_dn, rT_dn)
    P.barrier()

    y1t = E(sb("p1_y1t", [128, D], F32))
    xqi = E(sb("p1_xqi", [128, D], I32))
    xqf = E(sb("p1_xqf", [128, D], F32))
    xqb = E(sb("p1_xqb", [128, D], BF16))
    xq_fm = [E(sb(f"p1_xqfm{c}", [128, T], BF16)) for c in range(NOC)]
    idb_sb = E(sb("p1_idb", [128, 128], BF16))
    P.dma("s", idb_sb[:], io["idb"])
    for tt in (range(NTT) if "x" in parts else []):
        P.waitall("s")
        P.dma("s", y1t[:], io["y1_d"][128 * tt:128 * (tt + 1), :])
        P.wait("v", "sd")
        P.wait("v", "gs")
        P.step("v", lambda v: v.tensor_scalar(out=xqf[:], in0=y1t[:], scalar1=rs1_pp[:, 0:1],
                                              scalar2=None, op0=ALU.mult))
        P.step("v", lambda v: v.tensor_copy(out=xqi[:], in_=xqf[:]))
        P.wait("v", "ts")
        P.step("v", lambda v: v.tensor_scalar(out=xqb[:], in0=xqi[:], scalar1=s1_pp[:, 0:1],
                                              scalar2=None, op0=ALU.mult))
        P.wait("t", "vs")
        for hc in range(NOC):
            bank = hc % 2
            pview = (ps_c0 if bank == 0 else ps_c1)[:, 0:128].bitcast(BF16)
            P.wait("t", "vs")
            P.step("t", lambda t, hc=hc, pv=pview:
                   t.transpose(out=pv[:, 0:128], in_=xqb[:, 128 * hc:128 * (hc + 1)], identity=idb_sb[:]))
            P.wait("v", "ts")
            P.step("v", lambda v, hc=hc, tt=tt, pv=pview:
                   v.tensor_copy(out=xq_fm[hc][:, 128 * tt:128 * (tt + 1)], in_=pv[:, 0:128]))
    P.barrier()

    wq_l = [E(sb(f"p1_wql{h}", [128, 128], BF16)) for h in range(NOC)]
    upt = E(sb("p1_upt", [128, 512], F32))
    ucol = E(sb("p1_ucol", [128, 2 * NIC * NQT], F32))
    gcol = E(sb("p1_gcol", [128, NIC * NQT], F32))
    mxf = E(sb("p1_mxf", [128, 4], F32))
    mxrow = E(sb("p1_mxrow", [1, 128], F32))
    mx8 = E(sb("p1_mx8", [1, 8], F32))
    _wl = ()
    if "m" in parts:
        _wl = ((0, "wupT_dq", "up_d"), (1, "wgateT_dq", "gate_d"))
    elif "G" in parts:
        _wl = ((1, "wgateT_dq", "gate_d"),)
    elif "u" in parts:
        _wl = ((0, "wupT_dq", "up_d"),)
    _noc_lim = NIC if "u" not in parts or "u4" not in parts else 4
    for wi, dqname, dram in _wl:
        for oc in range(4 if "u4" in parts else NIC):
            P.waitall("s")
            for hc in range(NOC):
                P.dma("s", wq_l[hc][:], io[dqname][128 * hc:128 * (hc + 1), 128 * oc:128 * (oc + 1)])
            for tc in range(NQT):
                ps = ps_c0 if tc % 2 == 0 else ps_c1
                P.wait("t", "sd")
                P.wait("t", "as_")
                P.wait("t", "vs")
                for hc in range(NOC):
                    P.step("t", lambda t, hc=hc, tc=tc, ps=ps:
                           t.matmul(out=ps[:], lhsT=wq_l[hc][:], rhs=xq_fm[hc][:, 512 * tc:512 * (tc + 1)],
                                    start=(hc == 0), stop=(hc == NOC - 1)))
                P.wait("a", "ts")
                P.wait("a", "gd")
                P.wait("a", "vs")
                P.inc("a", "as_", 1, lambda a, ps=ps: a.activation(out=upt[:], in_=ps[:], func=AF.Copy))
                P.wait("g", "as_")
                P.dma("g", io[dram][128 * oc:128 * (oc + 1), 512 * tc:512 * (tc + 1)], upt[:])
                idx = oc * NQT + tc
                P.wait("v", "as_")
                P.step("v", lambda v, idx=idx, wi=wi: v.tensor_reduce(
                    out=ucol[:, (wi * NIC * NQT + idx):(wi * NIC * NQT + idx) + 1],
                    in_=upt[:], axis=AX.X, op=ALU.max, apply_absolute_value=True))
                if wi == 1:
                    P.step("v", lambda v, idx=idx: v.tensor_reduce(
                        out=gcol[:, idx:idx + 1], in_=upt[:], axis=AX.X, op=ALU.max))
    P.barrier()
    P.step("v", lambda v: v.tensor_reduce(out=mxf[:, 0:1], in_=ucol[:, 0:NIC * NQT], axis=AX.X, op=ALU.max))
    P.step("v", lambda v: v.tensor_reduce(out=mxf[:, 1:2], in_=ucol[:, NIC * NQT:], axis=AX.X, op=ALU.max))
    P.step("v", lambda v: v.tensor_reduce(out=mxf[:, 2:3], in_=gcol[:], axis=AX.X, op=ALU.max))
    P.wait("a", "vs")
    P.dma("a", io["mx_d"].rearrange("p one -> p one"), mxf[:, 0:1])
    P.wait("a", "ad")
    P.dma("a", mxrow[:], io["mx_d"].rearrange("p one -> one p"))
    P.wait("v", "ad")
    P.step("v", lambda v: v.memset(mx8[:], 0.0))
    P.step("v", lambda v: v.tensor_reduce(out=mx8[:, 1:2], in_=mxrow[:], axis=AX.X, op=ALU.max))
    P.wait("a", "vs")
    P.dma("a", io["mx_d"], mxf[:, 1:2])
    P.wait("a", "ad")
    P.dma("a", mxrow[:], io["mx_d"].rearrange("p one -> one p"))
    P.wait("v", "ad")
    P.step("v", lambda v: v.tensor_reduce(out=mx8[:, 2:3], in_=mxrow[:], axis=AX.X, op=ALU.max))
    P.wait("a", "vs")
    P.dma("a", io["mx_d"], mxf[:, 2:3])
    P.wait("a", "ad")
    P.dma("a", mxrow[:], io["mx_d"].rearrange("p one -> one p"))
    P.wait("v", "ad")
    P.step("v", lambda v: v.tensor_reduce(out=mx8[:, 3:4], in_=mxrow[:], axis=AX.X, op=ALU.max))
    P.wait("g", "vs")
    P.dma("g", io["mx_out"], mx8[:])
    P.wait("g", "gd")


def build_phase2(nc, P, io, ctx):
    E = ctx.enter_context
    sb = nc.sbuf_tensor
    gmax = peer_reduce(nc, P, io, E, sb, "p2")
    scl = E(sb("p2_scl", [1, 8], F32))
    rscl = E(sb("p2_rscl", [1, 8], F32))
    t11a = E(sb("p2_t11a", [1, 1], F32))
    t11b = E(sb("p2_t11b", [1, 1], I32))
    t11c = E(sb("p2_t11c", [1, 1], I32))
    t11d = E(sb("p2_t11d", [1, 1], F32))
    t11e = E(sb("p2_t11e", [1, 1], F32))
    sb16 = E(sb("p2_sb16", [1, 1], F32))
    scraw = E(sb("p2_scraw", [1, 1], F32))
    scv = E(sb("p2_scv", [1, 1], F32))
    rscv = E(sb("p2_rscv", [1, 1], F32))
    sbrc = E(sb("p2_sbrc", [1, 1], F32))
    csa = E(sb("p2_csa", [1, 1], F32))
    rsa_pp = E(sb("p2_rsapp", [128, 1], F32))
    rsb_pp = E(sb("p2_rsbpp", [128, 1], F32))
    sb_pp = E(sb("p2_sbpp", [128, 1], F32))
    sbrc_pp = E(sb("p2_sbrcpp", [128, 1], F32))
    P.step("v", lambda v: v.tensor_scalar(out=scl[:], in0=gmax[:], scalar1=1.0 / QMAX,
                                          scalar2=1e-12, op0=ALU.mult, op1=ALU.max))
    P.wait("a", "vs")
    P.inc("a", "as_", 1, lambda a: a.activation(out=rscl[:], in_=scl[:], func=AF.Copy))
    P.wait("v", "as_")
    P.step("v", lambda v: v.reciprocal(out=rscl[:], in_=rscl[:]))
    P.wait("a", "vs")
    P.inc("a", "as_", 1, lambda a: a.activation(out=t11a[:], in_=gmax[:, 3:4], func=AF.Copy))
    P.wait("v", "as_")
    P.step("v", lambda v: v.tensor_scalar(out=t11d[:], in0=t11a[:], scalar1=rscl[0:1, 2:3],
                                          scalar2=None, op0=ALU.mult))
    P.step("v", lambda v: v.tensor_copy(out=t11b[:], in_=t11d[:]))
    P.wait("a", "vs")
    P.inc("a", "as_", 1, lambda a: a.activation(out=t11d[:], in_=t11b[:], func=AF.Sigmoid,
                                                scale=scl[0:1, 2:3]))
    P.wait("v", "as_")
    P.step("v", lambda v: v.tensor_scalar(out=t11e[:], in0=t11d[:], scalar1=65536.0,
                                          scalar2=None, op0=ALU.mult))
    P.step("v", lambda v: v.tensor_copy(out=t11c[:], in_=t11e[:]))
    P.wait("g", "vs")
    P.step("g", lambda g: g.tensor_scalar(out=t11c[:], in0=t11c[:], scalar1=65535.0,
                                          scalar2=None, op0=ALU.min)) if False else None
    P.wait("a", "vs")
    P.inc("a", "as_", 1, lambda a: a.activation(out=t11d[:], in_=t11c[:], func=AF.Copy))
    P.wait("v", "as_")
    P.step("v", lambda v: v.tensor_scalar(out=t11c[:], in0=t11d[:], scalar1=65535.0,
                                          scalar2=None, op0=ALU.min))
    P.step("v", lambda v: v.tensor_mul(out=scraw[:], in0=t11b[:], in1=t11c[:]))
    P.step("v", lambda v: v.tensor_scalar(out=sb16[:], in0=scl[0:1, 2:3], scalar1=1.0 / 65536.0,
                                          scalar2=None, op0=ALU.mult))
    P.wait("a", "vs")
    P.inc("a", "as_", 1, lambda a: a.activation(out=scraw[:], in_=scraw[:], func=AF.Copy,
                                                scale=sb16[0:1, 0:1]))
    P.wait("v", "as_")
    P.step("v", lambda v: v.tensor_scalar(out=scv[:], in0=scraw[:], scalar1=1.0 / QMAX,
                                          scalar2=1e-12, op0=ALU.mult, op1=ALU.max))
    P.wait("a", "vs")
    P.inc("a", "as_", 1, lambda a: a.activation(out=rscv[:], in_=scv[:], func=AF.Copy))
    P.wait("v", "as_")
    P.step("v", lambda v: v.reciprocal(out=rscv[:], in_=rscv[:]))
    P.step("v", lambda v: v.tensor_scalar(out=sbrc[:], in0=rscv[:], scalar1=sb16[0:1, 0:1],
                                          scalar2=None, op0=ALU.mult))
    P.step("v", lambda v: v.tensor_scalar(out=csa[:], in0=scv[:], scalar1=scl[0:1, 1:2],
                                          scalar2=None, op0=ALU.mult))
    scrow = E(sb("p2_scrow", [1, 16], F32))
    P.step("v", lambda v: v.memset(scrow[:], 0.0))
    P.step("v", lambda v: v.tensor_copy(out=scrow[:, 0:1], in_=scl[:, 1:2]))
    P.step("v", lambda v: v.tensor_copy(out=scrow[:, 1:2], in_=scl[:, 2:3]))
    P.step("v", lambda v: v.tensor_copy(out=scrow[:, 2:3], in_=scv[:]))
    P.step("v", lambda v: v.tensor_copy(out=scrow[:, 3:4], in_=csa[:]))
    P.step("v", lambda v: v.tensor_copy(out=scrow[:, 4:5], in_=sbrc[:]))
    P.wait("g", "vs")
    P.dma("g", io["scal_d"], scrow[:])
    P.waitall("s")
    P.dma("s", rsa_pp[:], rscl[:, 1:2].unsqueeze(1).broadcast_to([1, 128, 1]))
    P.dma("s", rsb_pp[:], rscl[:, 2:3].unsqueeze(1).broadcast_to([1, 128, 1]))
    P.dma("s", sb_pp[:], scl[:, 2:3].unsqueeze(1).broadcast_to([1, 128, 1]))
    P.dma("s", sbrc_pp[:], sbrc[:].unsqueeze(1).broadcast_to([1, 128, 1]))
    P.barrier()

    up_t = [E(sb(f"p2_upt{i}", [128, 512], F32)) for i in range(2)]
    gt_t = [E(sb(f"p2_gtt{i}", [128, 512], F32)) for i in range(2)]
    upi = E(sb("p2_upi", [128, 512], I32))
    tmpf = E(sb("p2_tmpf", [128, 512], F32))
    gti = E(sb("p2_gti", [128, 512], I32))
    sig = E(sb("p2_sig", [128, 512], F32))
    sqi = E(sb("p2_sqi", [128, 512], I32))
    sq2 = E(sb("p2_sq2", [128, 512], I32))
    g2p = E(sb("p2_g2p", [128, 512], F32))
    g2i = E(sb("p2_g2i", [128, 512], I32))
    opr = E(sb("p2_opr", [128, 512], F32))
    ocol = E(sb("p2_ocol", [128, NIC * NQT], F32))
    mxrow = E(sb("p2_mxrow", [1, 128], F32))
    mx8 = E(sb("p2_mx8", [1, 8], F32))
    mxf = E(sb("p2_mxf", [128, 1], F32))
    ntile = NIC * NQT
    for idx in range(ntile):
        oc, tc = divmod(idx, NQT)
        buf = idx % 2
        P.wait("s", "vs")
        P.wait("s", "gs")
        P.wait("s", "as_")
        P.dma("s", up_t[buf][:], io["up_d"][128 * oc:128 * (oc + 1), 512 * tc:512 * (tc + 1)])
        P.dma("s", gt_t[buf][:], io["gate_d"][128 * oc:128 * (oc + 1), 512 * tc:512 * (tc + 1)])
        P.wait("v", "sd")
        P.wait("v", "gs")
        P.wait("v", "as_")
        P.step("v", lambda v, buf=buf: v.tensor_scalar(out=tmpf[:], in0=up_t[buf][:], scalar1=rsa_pp[:, 0:1],
                                                       scalar2=None, op0=ALU.mult))
        P.step("v", lambda v: v.tensor_copy(out=upi[:], in_=tmpf[:]))
        P.step("v", lambda v, buf=buf: v.tensor_scalar(out=tmpf[:], in0=gt_t[buf][:], scalar1=rsb_pp[:, 0:1],
                                                       scalar2=None, op0=ALU.mult))
        P.step("v", lambda v: v.tensor_copy(out=gti[:], in_=tmpf[:]))
        P.wait("a", "vs")
        P.inc("a", "as_", 1, lambda a: a.activation(out=sig[:], in_=gti[:], func=AF.Sigmoid,
                                                    scale=sb_pp[:, 0:1]))
        P.wait("v", "as_")
        P.step("v", lambda v: v.tensor_scalar(out=tmpf[:], in0=sig[:], scalar1=65536.0,
                                              scalar2=None, op0=ALU.mult))
        P.step("v", lambda v: v.tensor_copy(out=sqi[:], in_=tmpf[:]))
        P.step("v", lambda v: v.tensor_scalar(out=sq2[:], in0=sqi[:], scalar1=65535.0,
                                              scalar2=None, op0=ALU.min))
        P.step("v", lambda v: v.tensor_mul(out=g2p[:], in0=gti[:], in1=sq2[:]))
        P.step("v", lambda v: v.tensor_scalar(out=tmpf[:], in0=g2p[:], scalar1=sbrc_pp[:, 0:1],
                                              scalar2=None, op0=ALU.mult))
        P.step("v", lambda v: v.tensor_copy(out=g2i[:], in_=tmpf[:]))
        P.wait("v", "ad")
        P.step("v", lambda v: v.tensor_mul(out=opr[:], in0=g2i[:], in1=upi[:]))
        P.step("v", lambda v, idx=idx: v.tensor_reduce(out=ocol[:, idx:idx + 1], in_=opr[:],
                                                       axis=AX.X, op=ALU.max, apply_absolute_value=True))
        P.wait("a", "vs")
        P.wait("a", "ad")
        P.dma("a", io["o_d"][128 * oc:128 * (oc + 1), 512 * tc:512 * (tc + 1)], opr[:])
    P.barrier()
    P.step("v", lambda v: v.tensor_reduce(out=mxf[:], in_=ocol[:], axis=AX.X, op=ALU.max))
    P.wait("a", "vs")
    P.dma("a", io["mx_d"], mxf[:])
    P.wait("a", "ad")
    P.dma("a", mxrow[:], io["mx_d"].rearrange("p one -> one p"))
    P.wait("v", "ad")
    P.step("v", lambda v: v.memset(mx8[:], 0.0))
    P.step("v", lambda v: v.tensor_reduce(out=mx8[:, 4:5], in_=mxrow[:], axis=AX.X, op=ALU.max))
    P.wait("g", "vs")
    P.dma("g", io["mx_out"], mx8[:])
    P.wait("g", "gd")


def build_phase3(nc, P, io, ctx, dbg_what=None):
    E = ctx.enter_context
    sb = nc.sbuf_tensor
    gmax = peer_reduce(nc, P, io, E, sb, "p3")
    scrow = E(sb("p3_scrow", [1, 16], F32))
    sdv = E(sb("p3_sdv", [1, 1], F32))
    rsdv = E(sb("p3_rsdv", [1, 1], F32))
    crd = E(sb("p3_crd", [1, 1], F32))
    crd_pp = E(sb("p3_crdpp", [128, 1], F32))
    sd_pp = E(sb("p3_sdpp", [128, 1], F32))
    P.wait("s", "vs")
    P.dma("s", scrow[:], io["scal_d"])
    P.wait("v", "sd")
    P.step("v", lambda v: v.tensor_scalar(out=sdv[:], in0=gmax[:, 4:5], scalar1=scrow[0:1, 3:4],
                                          scalar2=None, op0=ALU.mult))
    P.wait("a", "vs")
    P.inc("a", "as_", 1, lambda a: a.activation(out=rsdv[:], in_=sdv[:], func=AF.Copy))
    P.wait("v", "as_")
    P.step("v", lambda v: v.tensor_scalar(out=sdv[:], in0=rsdv[:], scalar1=1.0 / QMAX,
                                          scalar2=1e-12, op0=ALU.mult, op1=ALU.max))
    P.wait("a", "vs")
    P.inc("a", "as_", 1, lambda a: a.activation(out=rsdv[:], in_=sdv[:], func=AF.Copy))
    P.wait("v", "as_")
    P.step("v", lambda v: v.reciprocal(out=rsdv[:], in_=rsdv[:]))
    P.step("v", lambda v: v.tensor_scalar(out=crd[:], in0=rsdv[:], scalar1=scrow[0:1, 3:4],
                                          scalar2=None, op0=ALU.mult))
    P.waitall("s")
    P.dma("s", crd_pp[:], crd[:].unsqueeze(1).broadcast_to([1, 128, 1]))
    P.dma("s", sd_pp[:], sdv[:].unsqueeze(1).broadcast_to([1, 128, 1]))
    P.barrier()

    opr = [E(sb(f"p3_opr{i}", [128, 512], F32)) for i in range(2)]
    oqi = E(sb("p3_oqi", [128, 512], I32))
    oqf = E(sb("p3_oqf", [128, 512], F32))
    oqb = E(sb("p3_oqb", [128, 512], BF16))
    for idx in range(NIC * NQT):
        oc, tc = divmod(idx, NQT)
        buf = idx % 2
        P.wait("s", "vs")
        P.dma("s", opr[buf][:], io["o_d"][128 * oc:128 * (oc + 1), 512 * tc:512 * (tc + 1)])
        P.wait("v", "sd")
        P.wait("v", "gd")
        P.step("v", lambda v, buf=buf: v.tensor_scalar(out=oqf[:], in0=opr[buf][:], scalar1=crd_pp[:, 0:1],
                                                       scalar2=None, op0=ALU.mult))
        P.step("v", lambda v: v.tensor_copy(out=oqi[:], in_=oqf[:]))
        P.wait("v", "gd")
        P.step("v", lambda v: v.tensor_scalar(out=oqb[:], in0=oqi[:], scalar1=sd_pp[:, 0:1],
                                              scalar2=None, op0=ALU.mult))
        P.wait("g", "vs")
        P.dma("g", io["gate_d"][128 * oc:128 * (oc + 1), 512 * tc:512 * (tc + 1)].bitcast(BF16)[:, 0:512],
              oqb[:])
    P.barrier()

    wd_l = [E(sb(f"p3_wdl{c}", [128, D], BF16)) for c in range(NIC)]
    for ic in range(NIC):
        P.dma("s", wd_l[ic][:], io["wdownT_dq"][128 * ic:128 * (ic + 1), :])
    on_t = [E(sb(f"p3_ont{c}", [128, 128], BF16)) for c in range(NIC)]
    y1t = E(sb("p3_y1t", [128, D], F32))
    z_sb = E(sb("p3_z", [128, D], F32))
    zsq = E(sb("p3_zsq", [128, D], F32))
    rsum = E(sb("p3_rsum", [128, 1], F32))
    mvec = E(sb("p3_mvec", [128, 1], F32))
    ssum = E(sb("p3_ssum", [128, 1], F32))
    varv = E(sb("p3_varv", [128, 1], F32))
    rstdv = E(sb("p3_rstdv", [128, 1], F32))
    y2t = E(sb("p3_y2t", [128, D], F32))
    epsA = E(sb("p3_epsA", [128, 1], F32))
    P.step("v", lambda v: v.memset(epsA[:], EPS))
    ps_a = E(nc.psum_tensor("p3_psa", [128, 1024], F32))
    for tt in range(NTT):
        P.waitall("s")
        for ic in range(NIC):
            P.dma("s", on_t[ic][:],
                  io["gate_d"][128 * ic:128 * (ic + 1), 512 * (tt // 4):512 * (tt // 4) + 512]
                  .bitcast(BF16)[:, 128 * (tt % 4):128 * (tt % 4) + 128])
        P.dma("s", y1t[:], io["y1_d"][128 * tt:128 * (tt + 1), :])
        P.wait("t", "sd")
        P.wait("t", "vs")
        for half in range(2):
            psh = ps_a[:, 512 * half:512 * (half + 1)]
            for ic in range(NIC):
                P.step("t", lambda t, ic=ic, half=half, psh=psh:
                       t.matmul(out=psh[:], lhsT=on_t[ic][:], rhs=wd_l[ic][:, 512 * half:512 * (half + 1)],
                                start=(ic == 0), stop=(ic == NIC - 1)))
        P.wait("v", "ts")
        P.wait("v", "gd")
        P.wait("v", "as_")
        for half in range(2):
            P.step("v", lambda v, half=half: v.tensor_add(
                out=z_sb[:, 512 * half:512 * (half + 1)], in0=ps_a[:, 512 * half:512 * (half + 1)],
                in1=y1t[:, 512 * half:512 * (half + 1)]))
        P.step("v", lambda v: v.tensor_reduce(out=rsum[:], in_=z_sb[:], axis=AX.X, op=ALU.add))
        P.wait("a", "vs")
        P.inc("a", "as_", 1, lambda a: a.activation(out=zsq[:], in_=z_sb[:], func=AF.Square))
        P.wait("v", "as_")
        P.step("v", lambda v: v.tensor_reduce(out=ssum[:], in_=zsq[:], axis=AX.X, op=ALU.add))
        P.wait("g", "vs")
        P.wait("g", "as_")
        P.step("g", lambda g: g.tensor_scalar(out=mvec[:], in0=rsum[:], scalar1=1.0 / D, scalar2=None, op0=ALU.mult))
        P.step("g", lambda g: g.tensor_mul(out=rstdv[:], in0=mvec[:], in1=mvec[:]))
        P.step("g", lambda g: g.tensor_scalar(out=varv[:], in0=ssum[:], scalar1=1.0 / D, scalar2=None, op0=ALU.mult))
        P.step("g", lambda g: g.tensor_sub(out=varv[:], in0=varv[:], in1=rstdv[:]))
        P.wait("a", "gs")
        P.inc("a", "as_", 1, lambda a: a.activation(out=rstdv[:], in_=varv[:], func=AF.Ln, bias=epsA[:, 0:1]))
        P.inc("a", "as_", 1, lambda a: a.activation(out=rstdv[:], in_=rstdv[:], func=AF.Exp, scale=-0.5))
        P.wait("v", "as_")
        P.wait("v", "gs")
        P.step("v", lambda v: v.tensor_scalar(out=y2t[:], in0=z_sb[:], scalar1=mvec[:, 0:1],
                                              scalar2=rstdv[:, 0:1], op0=ALU.subtract, op1=ALU.mult))
        P.wait("g", "vs")
        P.dma("g", io["out"][128 * tt:128 * (tt + 1), :], y2t[:])
    P.wait("g", "gd")


ALLSEMS2 = ALLSEMS + ["ph"]


def build_full(dbg=False, only=None, p1_parts="sdxmer"):
    nc = bass.Bass("TRN2", target_bir_lowering=False)
    io = declare_io(nc, dbg=dbg)
    plans = []
    from contextlib import ExitStack
    with ExitStack() as octx:
        sems = {k: octx.enter_context(nc.semaphore(k)) for k in ALLSEMS2}
        ph_sb = octx.enter_context(nc.sbuf_tensor("ph_sb", [1, 1], I32))
        builders = [build_phase0,
                    lambda nc, P, io, c: build_phase1(nc, P, io, c, parts=p1_parts),
                    build_phase2, build_phase3]
        for i, bf in enumerate(builders):
            P = Plan()
            if only is None or i == only:
                with ExitStack() as pctx:
                    bf(nc, P, io, pctx)
            plans.append(P)
        block = octx.enter_context(nc.Block())

        def mk_emit(engname):
            def f(eng):
                if engname == "s":
                    eng.dma_start(out=ph_sb[:], in_=io["phase"]).then_inc(sems["ph"], 16)
                eng.wait_ge(sems["ph"], 16)
                p = eng.value_load(ph_sb[0:1, 0:1])
                with nc.If(p == 0):
                    plans[0].emit(engname, eng, sems)
                with nc.Else():
                    with nc.If(p == 1):
                        plans[1].emit(engname, eng, sems)
                    with nc.Else():
                        with nc.If(p == 2):
                            plans[2].emit(engname, eng, sems)
                        with nc.Else():
                            plans[3].emit(engname, eng, sems)
            return f

        block.sync(mk_emit("s"))
        block.tensor(mk_emit("t"))
        block.vector(mk_emit("v"))
        block.scalar(mk_emit("a"))
        block.gpsimd(mk_emit("g"))
    return nc, io


def host_inputs(x_core, Wq, Wkv, Wo, bo, ln_g, ln_b, w_up, w_gate, w_down):
    cos_t, sinp_t = rope_tables()
    tri, ems = mask_consts()
    return {
        "x": np.ascontiguousarray(x_core),
        "wqT": np.ascontiguousarray(Wq.T), "wkvT": np.ascontiguousarray(Wkv.T),
        "woT": np.ascontiguousarray(Wo.T), "bo": np.ascontiguousarray(bo.reshape(1, -1)),
        "g128": np.ascontiguousarray(np.tile(ln_g, 2).reshape(128, 1)),
        "b128": np.ascontiguousarray(np.tile(ln_b, 2).reshape(128, 1)),
        "cos_t": cos_t, "sinp_t": sinp_t, "tri": tri, "ems": ems,
        "ind2": ind2_const(),
        "idb": np.ascontiguousarray(np.eye(128).astype(ml_dtypes.bfloat16)),
        "idf": np.ascontiguousarray(np.eye(128, dtype=np.float32)),
        "w_up": np.ascontiguousarray(w_up), "w_gate": np.ascontiguousarray(w_gate),
        "w_down": np.ascontiguousarray(w_down),
        "wupT": np.ascontiguousarray(w_up.T), "wgateT": np.ascontiguousarray(w_gate.T),
        "wdownT": np.ascontiguousarray(w_down.T),
        "phase": np.array([[0]], np.int32),
        "peer_mx": np.zeros((1, 64), np.float32),
    }


_CACHE = {}


def kernel(L0_attn_x, L0_attn_Wq, L0_attn_Wkv, L0_attn_Wo, L0_attn_bo,
           L0_attn_ln_g, L0_attn_ln_b, L0_ffn_w_gate, L0_ffn_w_up, L0_ffn_w_down):
    from concourse.bass_utils import run_bass_kernel_spmd
    n = 8
    if "nc" not in _CACHE:
        _CACHE["nc"], _CACHE["io"] = build_full(dbg=False)
    nc = _CACHE["nc"]
    x = np.asarray(L0_attn_x)
    base = [host_inputs(x[c], np.asarray(L0_attn_Wq), np.asarray(L0_attn_Wkv),
                        np.asarray(L0_attn_Wo), np.asarray(L0_attn_bo),
                        np.asarray(L0_attn_ln_g), np.asarray(L0_attn_ln_b),
                        np.asarray(L0_ffn_w_up), np.asarray(L0_ffn_w_gate),
                        np.asarray(L0_ffn_w_down)) for c in range(n)]
    peer = np.zeros((1, 64), np.float32)
    slot_of_phase = {0: [0], 1: [1, 2, 3], 2: [4], 3: []}
    outs = None
    for phase in range(4):
        for c in range(n):
            base[c]["phase"] = np.array([[phase]], np.int32)
            base[c]["peer_mx"] = peer.copy()
        res = run_bass_kernel_spmd(nc, base, core_ids=list(range(n)))
        mx = np.stack([np.asarray(res.results[c]["mx_out"])[0] for c in range(n)])
        for s in slot_of_phase[phase]:
            for r in range(n):
                peer[0, s * 8 + r] = mx[r, s]
        outs = res
    return np.stack([np.asarray(outs.results[c]["out"]) for c in range(n)]).astype(np.float32)


# version 02
# speedup vs baseline: 1.5304x; 1.5304x over previous
import numpy as np
import ml_dtypes
import concourse.bass as bass
import concourse.mybir as mybir

F32 = mybir.dt.float32
BF16 = mybir.dt.bfloat16
I32 = mybir.dt.int32
AF = mybir.ActivationFunctionType
ALU = mybir.AluOpType
AX = mybir.AxisListType

T = 4096
D = 1024
H = 16
DH = 64
CTX = 1536
NI = 3072
NTT = T // 128
NQT = 8
NOC = D // 128
NIC = NI // 128
QMAX = 32767.0
EPS = 1e-5

ENGS = ("s", "t", "v", "a", "g")
KQ = 4
QUEUES = ("sd", "ad", "gd", "vd")
QSEMS = {q: [f"{q}{k}" for k in range(KQ)] for q in QUEUES}
ESEM = {"s": "ss", "t": "ts", "v": "vs", "a": "as_", "g": "gs"}
ENG2Q = {"s": "sd", "a": "ad", "g": "gd", "v": "vd"}
ALLSEMS = [s for q in QUEUES for s in QSEMS[q]] + list(ESEM.values())


class Plan:
    def __init__(self):
        self.cnt = {k: 0 for k in ALLSEMS}
        self.ops = {e: [] for e in ENGS}
        self.qidx = {q: 0 for q in QUEUES}
        self.qgate = {q: False for q in QUEUES}
        self.qwaited = {q: False for q in QUEUES}

    def op(self, eng, fn):
        self.ops[eng].append(("op", fn))

    def inc(self, eng, sem, by, fn, selfwait=None):
        self.cnt[sem] += by
        self.ops[eng].append(("inc", sem, by, fn))
        if selfwait is None:
            selfwait = sem == ESEM.get(eng)
        if selfwait and sem in ESEM.values() and eng != "t":
            self.ops[eng].append(("wait", sem, self.cnt[sem]))
        return self.cnt[sem]

    def step(self, eng, fn, selfwait=True):
        return self.inc(eng, ESEM[eng], 1, fn, selfwait=selfwait)

    def newbatch(self, eng):
        q = ENG2Q[eng]
        self.qidx[q] = (self.qidx[q] + 1) % KQ
        self.qgate[q] = True

    def dma(self, eng, out, in_, **kw):
        q = ENG2Q[eng]
        if self.qwaited[q]:
            self.qidx[q] = (self.qidx[q] + 1) % KQ
            self.qgate[q] = True
            self.qwaited[q] = False
        sem = QSEMS[q][self.qidx[q]]
        if self.qgate[q]:
            if self.cnt[sem] > 0:
                self.ops[eng].append(("wait", sem, self.cnt[sem]))
            self.qgate[q] = False
        self.cnt[sem] += 16
        self.ops[eng].append(("inc", sem, 16,
                              lambda e, o=out, i=in_, kw=kw: e.dma_start(out=o, in_=i, **kw)))
        return (sem, self.cnt[sem])

    def wait(self, eng, sem, val=None):
        if sem in QUEUES:
            self.qwaited[sem] = True
            for s in QSEMS[sem]:
                if self.cnt[s] > 0:
                    self.ops[eng].append(("wait", s, self.cnt[s]))
            return
        v = self.cnt[sem] if val is None else val
        if v <= 0:
            return
        self.ops[eng].append(("wait", sem, v))

    def wait_tok(self, eng, tok):
        sem, v = tok
        for q in QUEUES:
            if sem in QSEMS[q]:
                self.qwaited[q] = True
        self.ops[eng].append(("wait", sem, v))

    def waitall(self, eng):
        for q in QUEUES:
            self.wait(eng, q)
        for e, s in ESEM.items():
            if e != eng:
                self.wait(eng, s)

    def fence(self, eng):
        self.wait(eng, ESEM[eng])

    def barrier(self):
        vals = {}
        for e in ENGS:
            vals[e] = self.inc(e, ESEM[e], 1, lambda eng: eng.nop(), selfwait=False)
        for e in ENGS:
            for o in ENGS:
                if o != e:
                    self.wait(e, ESEM[o], vals[o])
            for q in QUEUES:
                self.wait(e, q)

    def emit(self, eng_name, eng, sems):
        for item in self.ops[eng_name]:
            if item[0] == "op":
                item[1](eng)
            elif item[0] == "inc":
                _, sem, by, fn = item
                fn(eng).then_inc(sems[sem], by)
            else:
                _, sem, v = item
                eng.wait_ge(sems[sem], v)


def rope_tables():
    inv_freq = 1.0 / (10000.0 ** (np.arange(0, DH, 2, dtype=np.float64) / DH))
    freqs = np.arange(T, dtype=np.float64)[:, None] * inv_freq[None, :]
    emb = np.concatenate([freqs, freqs], axis=-1)
    cos = np.cos(emb).astype(np.float32).T
    sin = np.sin(emb).astype(np.float32).T
    sinp = sin.copy()
    sinp[:32, :] *= -1.0
    return (np.ascontiguousarray(np.concatenate([cos, cos], axis=0)),
            np.ascontiguousarray(np.concatenate([sinp, sinp], axis=0)))


def mask_consts():
    tri = np.ascontiguousarray(np.triu(np.ones((128, 128), np.float32)).astype(ml_dtypes.bfloat16))
    ems = np.zeros((4, 128, 512), np.float32)
    for m in range(4):
        for q in range(512):
            j = max(q - 128 * m + 1, 0)
            if j <= 127:
                ems[m, j, q] = -32768.0
    return tri, np.ascontiguousarray(ems.astype(ml_dtypes.bfloat16))


def ind2_const():
    a = np.zeros((128, 2), np.float32)
    a[:64, 0] = 1.0
    a[64:, 1] = 1.0
    return np.ascontiguousarray(a.astype(ml_dtypes.bfloat16))


def qtile_chunks(qt):
    if qt < 3:
        return list(range(12)), {}
    chunks = list(range(4 * qt + 4))
    masked = {4 * qt + m: m for m in range(4)}
    return chunks, masked


TPERM = [5, 6, 7, 0, 1, 2, 3, 4]


def declare_io(nc, dbg=True):
    io = {"dbg_on": dbg}
    d = nc.dram_tensor
    io["x"] = d("x", [T, D], F32, kind="ExternalInput").ap()
    io["wqT"] = d("wqT", [D, D], F32, kind="ExternalInput").ap()
    io["wkvT"] = d("wkvT", [D, D], F32, kind="ExternalInput").ap()
    io["woT"] = d("woT", [D, D], F32, kind="ExternalInput").ap()
    io["bo"] = d("bo", [1, D], F32, kind="ExternalInput").ap()
    io["g128"] = d("g128", [128, 1], F32, kind="ExternalInput").ap()
    io["b128"] = d("b128", [128, 1], F32, kind="ExternalInput").ap()
    io["cos_t"] = d("cos_t", [128, T], F32, kind="ExternalInput").ap()
    io["sinp_t"] = d("sinp_t", [128, T], F32, kind="ExternalInput").ap()
    io["tri"] = d("tri", [128, 128], BF16, kind="ExternalInput").ap()
    io["ems"] = d("ems", [4, 128, 512], BF16, kind="ExternalInput").ap()
    io["ind2"] = d("ind2", [128, 2], BF16, kind="ExternalInput").ap()
    io["idb"] = d("idb", [128, 128], BF16, kind="ExternalInput").ap()
    io["idf"] = d("idf", [128, 128], F32, kind="ExternalInput").ap()
    io["w_up"] = d("w_up", [NI, D], F32, kind="ExternalInput").ap()
    io["w_gate"] = d("w_gate", [NI, D], F32, kind="ExternalInput").ap()
    io["w_down"] = d("w_down", [D, NI], F32, kind="ExternalInput").ap()
    io["wupT"] = d("wupT", [D, NI], F32, kind="ExternalInput").ap()
    io["wgateT"] = d("wgateT", [D, NI], F32, kind="ExternalInput").ap()
    io["wdownT"] = d("wdownT", [NI, D], F32, kind="ExternalInput").ap()
    io["phase"] = d("phase", [1, 1], I32, kind="ExternalInput").ap()
    io["peer_mx"] = d("peer_mx", [1, 64], F32, kind="ExternalInput").ap()
    io["out"] = d("out", [T, D], F32, kind="ExternalOutput").ap()
    io["mx_out"] = d("mx_out", [1, 8], F32, kind="ExternalOutput").ap()
    if dbg:
        io["dbg"] = d("dbg", [T, D], F32, kind="ExternalOutput").ap()
    io["y1_d"] = d("y1_d", [T, D], F32, kind="Internal").ap()
    io["up_d"] = d("up_d", [NI, T], F32, kind="Internal").ap()
    io["gate_d"] = d("gate_d", [NI, T], F32, kind="Internal").ap()
    io["o_d"] = d("o_d", [NI, T], F32, kind="Internal").ap()
    io["outn_d"] = d("outn_d", [D, T], BF16, kind="Internal").ap()
    io["wupT_dq"] = d("wupT_dq", [D, NI], BF16, kind="Internal").ap()
    io["wgateT_dq"] = d("wgateT_dq", [D, NI], BF16, kind="Internal").ap()
    io["wdownT_dq"] = d("wdownT_dq", [NI, D], BF16, kind="Internal").ap()
    io["lkvF_d"] = d("lkvF_d", [D, T], BF16, kind="Internal").ap()
    io["lkvT_d"] = d("lkvT_d", [T, D], BF16, kind="Internal").ap()
    io["stats_d1"] = d("stats_d1", [16, 512], F32, kind="Internal").ap()
    io["stats_d2"] = d("stats_d2", [16, 512], F32, kind="Internal").ap()
    io["mx_d"] = d("mx_d", [128, 1], F32, kind="Internal").ap()
    io["scal_d"] = d("scal_d", [1, 16], F32, kind="Internal").ap()
    return io




def build_phase0(nc, P, io, ctx, stages='ABCD', dbg_what=None, c_qts=None, c_pairs=None, c_chunks=True, d_level=3):
    E = ctx.enter_context
    sb = nc.sbuf_tensor
    w_bf = [E(sb(f"p0_w{c}", [128, D], BF16)) for c in range(NOC)]
    g128_sb = E(sb("p0_g", [128, 1], F32))
    b128_sb = E(sb("p0_b", [128, 1], F32))
    ind2_sb = E(sb("p0_ind2", [128, 2], BF16))
    tri_sb = E(sb("p0_tri", [128, 128], BF16))
    ems_sb = E(sb("p0_ems", [128, 4 * 512], BF16))
    idb_sb = E(sb("p0_idb", [128, 128], BF16))
    ones_bf = E(sb("p0_ones", [128, 1], BF16))
    ones_f1 = E(sb("p0_onesf", [1, 128], F32))
    bo_sb = E(sb("p0_bo", [1, D], F32))
    knacc = E(sb("p0_knacc", [2, 8], F32))
    knall = E(sb("p0_knall", [2, 64], F32))
    tmp21 = E(sb("p0_tmp21", [2, 1], F32))
    xtm_f = [E(sb(f"p0_xtf{i}", [128, D], F32)) for i in range(4)]
    xtm_b = [E(sb(f"p0_xtb{i}", [128, D], BF16)) for i in range(4)]
    x_fm = [E(sb(f"p0_xfm{c}", [128, 512], BF16)) for c in range(NOC)]
    kvraw = E(sb("p0_kvr", [128, 512], F32))
    kvrot = E(sb("p0_kvrot", [128, 512], F32))
    kvc = E(sb("p0_kvc", [128, 512], F32))
    kvs = E(sb("p0_kvs", [128, 512], F32))
    kvrope = [E(sb(f"p0_kvro{c}", [128, 512], F32)) for c in range(NOC)]
    cs_sl = E(sb("p0_cs", [128, 512], F32))
    sn_sl = E(sb("p0_sn", [128, 512], F32))
    sq_bf = E(sb("p0_sq", [128, 512], BF16))
    v_bf = E(sb("p0_vbf", [128, 512], BF16))
    t1f = E(sb("p0_t1", [128, 512], F32))
    t2f = E(sb("p0_t2", [128, 512], F32))
    lkv_t = E(sb("p0_lkvt", [128, 512], BF16))
    lkvt_ev = E(sb("p0_lkvtev", [128, 512], BF16))
    stat2 = E(sb("p0_stat2", [2, 1024], F32))
    sums_p = E(sb("p0_sump", [16, 512], F32))
    ssq_p = E(sb("p0_ssqp", [16, 512], F32))
    m_p = E(sb("p0_mp", [16, 512], F32))
    msq_p = E(sb("p0_msqp", [16, 512], F32))
    var_p = E(sb("p0_varp", [16, 512], F32))
    rstd_p = E(sb("p0_rstdp", [16, 512], F32))
    m_b = E(sb("p0_mb", [128, 512], F32))
    rstd_b = E(sb("p0_rstdb", [128, 512], F32))
    q_bf = [E(sb(f"p0_q{c}", [128, 512], BF16)) for c in range(NOC)]
    qnacc = E(sb("p0_qnacc", [2, 8], F32))
    cprod = E(sb("p0_cprod", [2, 8], F32))
    negc = E(sb("p0_negc", [2, 8], F32))
    negc16 = E(sb("p0_negc16", [1, 16], F32))
    bias_all = E(sb("p0_bias", [128, 16], F32))
    pA = E(sb("p0_pA", [128, 1024], BF16))
    pB = E(sb("p0_pB", [128, 1024], BF16))
    lkF = [E(sb(f"p0_lkF{i}", [128, 4096], BF16)) for i in range(2)]
    lkT = [E(sb(f"p0_lkT{i}", [128, 4096], BF16)) for i in range(2)]
    av_sb = [E(sb(f"p0_av{p}", [128, 512], F32)) for p in range(8)]
    sum_ev = E(sb("p0_sumev", [128, 512], F32))
    l2t = E(sb("p0_l2t", [2, 1024], F32))
    l2s = E(sb("p0_l2s", [2, 512], F32))
    l_all = E(sb("p0_lall", [16, 512], F32))
    rl_all = E(sb("p0_rlall", [16, 512], F32))
    rl_b = E(sb("p0_rlb", [128, 512], F32))
    outn_t = E(sb("p0_outn", [128, 512], BF16))
    on_t = [E(sb(f"p0_ont{c}", [128, 128], BF16)) for c in range(NOC)]
    xt_d = E(sb("p0_xtd", [128, D], F32))
    z_sb = E(sb("p0_z", [128, D], F32))
    zsq = E(sb("p0_zsq", [128, D], F32))
    rsum = E(sb("p0_rsum", [128, 1], F32))
    mvec = E(sb("p0_mvec", [128, 1], F32))
    ssum = E(sb("p0_ssum", [128, 1], F32))
    varv = E(sb("p0_varv", [128, 1], F32))
    rstdv = E(sb("p0_rstdv", [128, 1], F32))
    y1t = E(sb("p0_y1t", [128, D], F32))
    mxcol = E(sb("p0_mxcol", [128, NTT], F32))
    mxfin = E(sb("p0_mxfin", [128, 1], F32))
    mxrow = E(sb("p0_mxrow", [1, 128], F32))
    mx8 = E(sb("p0_mx8", [1, 8], F32))
    epsA = E(sb("p0_epsA", [128, 1], F32))
    epsB = E(sb("p0_epsB", [128, 1], F32))
    ps_a = E(nc.psum_tensor("p0_psa", [128, 1024], F32))
    ps_b = E(nc.psum_tensor("p0_psb", [128, 1024], F32))
    ps_av = E(nc.psum_tensor("p0_psav", [128, 512], F32))
    ps_sum = E(nc.psum_tensor("p0_pssum", [128, 512], F32))
    ps_c0 = E(nc.psum_tensor("p0_psc0", [128, 512], F32))
    ps_c1 = E(nc.psum_tensor("p0_psc1", [128, 512], F32))

    dbgv = io.get("dbg")
    dbg_fm = dbgv.rearrange("(r s) c -> r (s c)", s=4) if dbgv is not None else None

    P.step("v", lambda v: v.memset(ones_bf[:], 1.0))
    P.step("v", lambda v: v.memset(epsA[:], EPS))
    P.step("v", lambda v: v.memset(epsB[:], 1e-30))
    P.step("v", lambda v: v.memset(ones_f1[:], 1.0))
    P.step("v", lambda v: v.memset(knacc[:], 0.0))
    P.step("v", lambda v: v.memset(qnacc[:], 0.0))
    for (dst, src) in ((g128_sb, io["g128"]), (b128_sb, io["b128"]),
                       (ind2_sb, io["ind2"]), (tri_sb, io["tri"]),
                       (idb_sb, io["idb"]), (bo_sb, io["bo"])):
        P.dma("s", dst[:], src)
    P.dma("s", ems_sb[:].rearrange("p (m q) -> p m q", m=4), io["ems"].rearrange("m p q -> p m q"))

    def load_weights(name):
        for c in range(NOC):
            P.waitall("s")
            P.dma("s", xtm_f[0][:], io[name][128 * c:128 * (c + 1), :])
            P.wait("v", "sd")
            P.step("v", lambda v, c=c: v.tensor_copy(out=w_bf[c][:], in_=xtm_f[0][:]))

    def load_x_group(tq):
        P.waitall("s")
        for i in range(4):
            P.dma("s", xtm_f[i][:], io["x"][512 * tq + 128 * i:512 * tq + 128 * (i + 1), :])
        P.wait("v", "sd")
        P.waitall("v")
        for i in range(4):
            P.step("v", lambda v, i=i: v.tensor_copy(out=xtm_b[i][:], in_=xtm_f[i][:]), selfwait=False)
        for hc in range(NOC):
            bank = hc % 2
            pview = ps_a[:, 512 * bank:512 * (bank + 1)].bitcast(BF16)
            P.wait("t", "vs")
            for i in range(4):
                P.step("t", lambda t, hc=hc, i=i, pv=pview:
                       t.transpose(out=pv[:, 128 * i:128 * (i + 1)],
                                   in_=xtm_b[i][:, 128 * hc:128 * (hc + 1)],
                                   identity=idb_sb[:]))
            P.wait("v", "ts")
            P.step("v", lambda v, hc=hc, pv=pview: v.tensor_copy(out=x_fm[hc][:], in_=pv[:, 0:512]),
                   selfwait=False)
        P.wait("t", "vs")

    def proj_rope(oc, outbuf):
        ps = ps_c0 if oc % 2 == 0 else ps_c1
        P.wait("t", "vs")
        for hc in range(NOC):
            P.step("t", lambda t, hc=hc, oc=oc, ps=ps:
                   t.matmul(out=ps[:], lhsT=w_bf[hc][:, 128 * oc:128 * (oc + 1)],
                            rhs=x_fm[hc][:], start=(hc == 0), stop=(hc == NOC - 1)))
        P.wait("v", "ts")
        P.wait("v", "ad")
        P.step("v", lambda v, ps=ps: v.tensor_copy(out=kvraw[:], in_=ps[:]), selfwait=False)
        P.wait("a", "vs")
        P.wait("a", "gs")
        for (d0, s0) in ((0, 32), (32, 0), (64, 96), (96, 64)):
            P.dma("a", kvrot[d0:d0 + 32, :], kvraw[s0:s0 + 32, :])
        P.wait("g", "ad")
        P.wait("g", "vs")
        P.step("g", lambda g: g.tensor_mul(out=kvs[:], in0=kvrot[:], in1=sn_sl[:]), selfwait=False)
        P.step("v", lambda v: v.tensor_mul(out=kvc[:], in0=kvraw[:], in1=cs_sl[:]), selfwait=False)
        P.wait("v", "gs")
        P.wait("v", "as_")
        P.step("v", lambda v, outbuf=outbuf: v.tensor_add(out=outbuf[:], in0=kvc[:], in1=kvs[:]),
               selfwait=False)

    if "B" in stages:
        load_weights("wkvT")
    P.barrier()

    for tq in (range(NQT) if "B" in stages else []):
        load_x_group(tq)
        P.waitall("s")
        P.dma("s", cs_sl[:], io["cos_t"][:, 512 * tq:512 * (tq + 1)])
        P.dma("s", sn_sl[:], io["sinp_t"][:, 512 * tq:512 * (tq + 1)])
        P.wait("v", "sd")
        P.wait("g", "sd")
        for oc in range(NOC):
            proj_rope(oc, kvrope[oc])
            if dbg_what == "kvrope":
                P.wait("g", "vs")
                P.dma("g", dbg_fm[128 * oc:128 * (oc + 1), 512 * tq:512 * (tq + 1)], kvrope[oc][:])
            P.wait("a", "vs")
            P.wait("a", "ts")
            P.inc("a", "as_", 1, lambda a, oc=oc: a.activation(out=sq_bf[:], in_=kvrope[oc][:], func=AF.Square))
            P.step("v", lambda v, oc=oc: v.tensor_copy(out=v_bf[:], in_=kvrope[oc][:]), selfwait=False)
            P.wait("t", "as_")
            P.wait("t", "vs")
            P.step("t", lambda t: t.matmul(out=ps_sum[0:2, :], lhsT=ind2_sb[:], rhs=v_bf[:], start=True, stop=True))
            P.step("t", lambda t: t.matmul(out=ps_av[0:2, :], lhsT=ind2_sb[:], rhs=sq_bf[:], start=True, stop=True))
            P.wait("v", "ts")
            P.wait("v", "gd")
            P.step("v", lambda v: v.tensor_copy(out=stat2[:, 0:512], in_=ps_sum[0:2, :]), selfwait=False)
            P.step("v", lambda v: v.tensor_copy(out=stat2[:, 512:1024], in_=ps_av[0:2, :]), selfwait=False)
            P.wait("g", "vs")
            P.dma("g", io["stats_d1"][2 * oc:2 * oc + 2, :], stat2[:, 0:512])
            P.dma("g", io["stats_d2"][2 * oc:2 * oc + 2, :], stat2[:, 512:1024])
            P.wait("t", "vs")
        P.waitall("s")
        P.dma("s", sums_p[:], io["stats_d1"])
        P.dma("s", ssq_p[:], io["stats_d2"])
        P.wait("v", "sd")
        P.step("v", lambda v: v.tensor_scalar(out=m_p[:], in0=sums_p[:], scalar1=1.0 / DH, scalar2=None,
                                              op0=ALU.mult), selfwait=False)
        P.step("v", lambda v: v.tensor_mul(out=msq_p[:], in0=m_p[:], in1=m_p[:]), selfwait=False)
        P.step("v", lambda v: v.tensor_scalar(out=var_p[:], in0=ssq_p[:], scalar1=1.0 / DH, scalar2=None,
                                              op0=ALU.mult), selfwait=False)
        P.step("v", lambda v: v.tensor_sub(out=var_p[:], in0=var_p[:], in1=msq_p[:]), selfwait=False)
        P.wait("a", "vs")
        P.inc("a", "as_", 1, lambda a: a.activation(out=rstd_p[:], in_=var_p[:], func=AF.Ln, bias=epsA[0:16, 0:1]))
        P.inc("a", "as_", 1, lambda a: a.activation(out=rstd_p[:], in_=rstd_p[:], func=AF.Exp, scale=-0.5))
        for oc in range(NOC):
            P.waitall("s")
            P.wait("s", "as_")
            P.dma("s", m_b[:], m_p[2 * oc:2 * oc + 2, :].unsqueeze(1).broadcast_to([2, 64, 512]))
            P.dma("s", rstd_b[:], rstd_p[2 * oc:2 * oc + 2, :].unsqueeze(1).broadcast_to([2, 64, 512]))
            P.wait("v", "sd")
            P.wait("v", "sd")
            P.step("v", lambda v, oc=oc: v.tensor_sub(out=t1f[:], in0=kvrope[oc][:], in1=m_b[:]),
                   selfwait=False)
            P.step("v", lambda v: v.scalar_tensor_tensor(
                out=t2f[:], in0=t1f[:], scalar=g128_sb[:, 0:1], in1=rstd_b[:], op0=ALU.mult, op1=ALU.mult),
                   selfwait=False)
            P.wait("v", "gd")
            P.wait("v", "as_")
            P.step("v", lambda v: v.tensor_scalar(
                out=lkv_t[:], in0=t2f[:], scalar1=b128_sb[:, 0:1], scalar2=None, op0=ALU.add),
                   selfwait=False)
            P.wait("g", "vs")
            P.dma("g", io["lkvF_d"][128 * oc:128 * (oc + 1), 512 * tq:512 * (tq + 1)], lkv_t[:])
            if dbg_what == "lkv":
                P.dma("g", dbg_fm[128 * oc:128 * (oc + 1), 512 * tq:512 * (tq + 1)], lkv_t[:])
            P.wait("a", "vs")
            P.wait("a", "ts")
            P.inc("a", "as_", 1, lambda a: a.activation(out=sq_bf[:], in_=lkv_t[:], func=AF.Square))
            P.wait("t", "as_")
            P.step("t", lambda t: t.matmul(out=ps_sum[0:2, :], lhsT=ind2_sb[:], rhs=sq_bf[:], start=True, stop=True))
            P.wait("v", "ts")
            P.step("v", lambda v, oc=oc, tq=tq: v.tensor_reduce(
                out=knall[:, 8 * tq + oc:8 * tq + oc + 1], in_=ps_sum[0:2, :], axis=AX.X, op=ALU.max))
            bank = oc % 2
            pview = ps_a[:, 512 * bank:512 * (bank + 1)].bitcast(BF16)
            P.wait("t", "vs")
            P.wait("t", "gd")
            for i in range(4):
                P.step("t", lambda t, i=i, pv=pview:
                       t.transpose(out=pv[:, 128 * i:128 * (i + 1)],
                                   in_=lkv_t[:, 128 * i:128 * (i + 1)],
                                   identity=idb_sb[:]))
            P.wait("v", "ts")
            P.wait("v", "gd")
            P.step("v", lambda v, pv=pview: v.tensor_copy(out=lkvt_ev[:], in_=pv[:, 0:512]), selfwait=False)
            P.wait("g", "vs")
            for i in range(4):
                P.dma("g", io["lkvT_d"][512 * tq + 128 * i:512 * tq + 128 * (i + 1), 128 * oc:128 * (oc + 1)],
                      lkvt_ev[:, 128 * i:128 * (i + 1)])
    P.barrier()
    if "C" in stages:
        load_weights("wqT")
    P.barrier()

    if "C" in stages:
        P.step("v", lambda v: v.tensor_reduce(
            out=knacc[:].rearrange("r (one c) -> r one c", one=1),
            in_=knall[:].rearrange("r (t c) -> r c t", t=8), axis=AX.X, op=ALU.max))

    _qts = (c_qts if c_qts is not None else range(NQT)) if "C" in stages else []
    for qt in _qts:
        chunks, masked = qtile_chunks(qt)
        nch = len(chunks)
        G = nch // 2
        load_x_group(qt)
        P.waitall("s")
        P.dma("s", cs_sl[:], io["cos_t"][:, 512 * qt:512 * (qt + 1)])
        P.dma("s", sn_sl[:], io["sinp_t"][:, 512 * qt:512 * (qt + 1)])
        P.wait("v", "sd")
        P.wait("g", "sd")
        for oc in range(NOC):
            proj_rope(oc, t2f)
            P.step("v", lambda v, oc=oc: v.tensor_copy(out=q_bf[oc][:], in_=t2f[:]), selfwait=False)
            if dbg_what == "q":
                P.wait("g", "vs")
                P.dma("g", dbg_fm[128 * oc:128 * (oc + 1), 512 * qt:512 * (qt + 1)], t2f[:])
            P.wait("a", "vs")
            P.wait("a", "ts")
            P.inc("a", "as_", 1, lambda a, oc=oc: a.activation(out=sq_bf[:], in_=q_bf[oc][:], func=AF.Square))
            P.wait("t", "as_")
            P.step("t", lambda t: t.matmul(out=ps_sum[0:2, :], lhsT=ind2_sb[:], rhs=sq_bf[:], start=True, stop=True))
            P.wait("v", "ts")
            P.step("v", lambda v, oc=oc: v.tensor_reduce(out=qnacc[:, oc:oc + 1], in_=ps_sum[0:2, :],
                                                         axis=AX.X, op=ALU.max))
            P.wait("t", "vs")
        P.wait("a", "vs")
        P.inc("a", "as_", 1, lambda a: a.activation(out=cprod[:], in_=qnacc[:], func=AF.Ln, bias=epsB[0:2, 0:1]))
        P.inc("a", "as_", 1, lambda a: a.activation(out=negc[:], in_=knacc[:], func=AF.Ln, bias=epsB[0:2, 0:1]))
        P.wait("v", "as_")
        P.step("v", lambda v: v.tensor_add(out=cprod[:], in0=cprod[:], in1=negc[:]))
        P.wait("a", "vs")
        P.inc("a", "as_", 1, lambda a: a.activation(out=negc[:], in_=cprod[:], func=AF.Exp, scale=0.5))
        P.wait("v", "as_")
        P.step("v", lambda v: v.tensor_scalar(out=negc[:], in0=negc[:], scalar1=-0.125, scalar2=None, op0=ALU.mult))
        P.waitall("s")
        P.dma("s", negc16[:].rearrange("p (c r) -> p c r", r=2)[:, :, 0:1], negc[0:1, :].unsqueeze(2),
              allow_slow_non_contiguous=True)
        P.dma("s", negc16[:].rearrange("p (c r) -> p c r", r=2)[:, :, 1:2], negc[1:2, :].unsqueeze(2),
              allow_slow_non_contiguous=True)
        P.wait("s", "sd")
        P.dma("s", bias_all[:], negc16[:].unsqueeze(1).broadcast_to([1, 128, 16]))
        P.wait("a", "sd")

        def load_pair_slices(pair):
            buf = pair % 2
            P.newbatch("s")
            toks = [P.dma("s", lkF[buf][:, 0:128 * nch], io["lkvF_d"][128 * pair:128 * (pair + 1), 0:128 * nch])]
            for c4 in range(0, nch, 4):
                hi = min(c4 + 4, nch)
                toks.append(P.dma("s", lkT[buf][:, 128 * c4:128 * hi].rearrange("p (j c) -> p j c", c=128),
                      io["lkvT_d"][128 * c4:128 * hi, 128 * pair:128 * (pair + 1)]
                      .rearrange("(j p) c -> p j c", p=128)))
            return toks[-1]

        def scores_group(pair, g):
            buf = pair % 2
            for ci in range(2):
                cx = chunks[2 * g + ci]
                stop0 = cx not in masked
                for (base, psx) in ((0, ps_a), (64, ps_b)):
                    P.step("t", lambda t, cx=cx, ci=ci, base=base, psx=psx, pair=pair, stop0=stop0, buf=buf:
                           t.matmul(out=psx[:, 512 * ci:512 * (ci + 1)],
                                    lhsT=lkF[buf][base:base + 64, 128 * cx:128 * (cx + 1)],
                                    rhs=q_bf[pair][base:base + 64, :],
                                    start=True, stop=stop0,
                                    tile_position=(base, 0)))
                    if not stop0:
                        m = masked[cx]
                        P.step("t", lambda t, m=m, ci=ci, psx=psx:
                               t.matmul(out=psx[:, 512 * ci:512 * (ci + 1)],
                                        lhsT=tri_sb[:],
                                        rhs=ems_sb[:, 512 * m:512 * (m + 1)],
                                        start=False, stop=True))

        def av_group(pair, g):
            buf = pair % 2
            for ci in range(2):
                cx = chunks[2 * g + ci]
                st = (g == 0 and ci == 0)
                sp = (g == G - 1 and ci == 1)
                sts = (g == 0)
                sps = (g == G - 1)
                P.step("t", lambda t, cx=cx, ci=ci, st=st, sp=sp, buf=buf:
                       t.matmul(out=ps_av[0:64, :], lhsT=lkT[buf][:, 128 * cx:128 * cx + 64],
                                rhs=pA[:, 512 * ci:512 * (ci + 1)],
                                start=st, stop=sp, tile_position=(0, 0), skip_group_check=True))
                P.step("t", lambda t, cx=cx, ci=ci, st=st, sp=sp, buf=buf:
                       t.matmul(out=ps_av[64:128, :], lhsT=lkT[buf][:, 128 * cx + 64:128 * (cx + 1)],
                                rhs=pB[:, 512 * ci:512 * (ci + 1)],
                                start=st, stop=sp, tile_position=(0, 64), skip_group_check=True))
                P.step("t", lambda t, ci=ci, sts=sts, sps=sps:
                       t.matmul(out=ps_sum[32 * ci:32 * ci + 1, :], lhsT=ones_bf[:],
                                rhs=pA[:, 512 * ci:512 * (ci + 1)],
                                start=sts, stop=sps, tile_position=(0, 32 * ci), skip_group_check=True))
                P.step("t", lambda t, ci=ci, sts=sts, sps=sps:
                       t.matmul(out=ps_sum[64 + 32 * ci:64 + 32 * ci + 1, :], lhsT=ones_bf[:],
                                rhs=pB[:, 512 * ci:512 * (ci + 1)],
                                start=sts, stop=sps, tile_position=(0, 64 + 32 * ci), skip_group_check=True))

        if not c_chunks:
            continue
        _pairs = list(c_pairs if c_pairs is not None else range(8))
        ts_end_pair = []
        load_toks = [load_pair_slices(_pairs[0])]
        for pi, pair in enumerate(_pairs):
            if pi + 1 < len(_pairs):
                if pi >= 1:
                    P.wait("s", "ts", ts_end_pair[pi - 1])
                load_toks.append(load_pair_slices(_pairs[pi + 1]))
            P.step("v", lambda v: v.memset(ps_sum[:], 0.0), selfwait=False)
            P.wait_tok("t", load_toks[pi])
            P.wait("t", "vs")
            P.wait("t", "as_")
            scores_group(pair, 0)
            for g in range(G):
                P.wait("a", "ts")
                h0 = 2 * pair
                P.inc("a", "as_", 1, lambda a, h0=h0: a.activation(
                    out=pA[:], in_=ps_a[:], func=AF.Exp, scale=0.125, bias=bias_all[:, h0:h0 + 1]),
                      selfwait=False)
                P.inc("a", "as_", 1, lambda a, h0=h0: a.activation(
                    out=pB[:], in_=ps_b[:], func=AF.Exp, scale=0.125, bias=bias_all[:, h0 + 1:h0 + 2]),
                      selfwait=False)
                P.wait("t", "as_")
                if g + 1 < G:
                    scores_group(pair, g + 1)
                av_group(pair, g)
            P.wait("v", "ts")
            P.step("v", lambda v: v.tensor_copy(out=sum_ev[:], in_=ps_sum[:]), selfwait=False)
            P.step("v", lambda v, pair=pair: v.tensor_copy(out=av_sb[pair][:], in_=ps_av[:]), selfwait=False)
            P.wait("g", "vs")
            P.dma("g", l2t[0:1, 0:512], sum_ev[0:1, :])
            P.dma("g", l2t[0:1, 512:1024], sum_ev[32:33, :])
            P.dma("g", l2t[1:2, 0:512], sum_ev[64:65, :])
            P.dma("g", l2t[1:2, 512:1024], sum_ev[96:97, :])
            P.wait("v", "gd")
            P.step("v", lambda v: v.tensor_add(out=l2s[:], in0=l2t[:, 0:512], in1=l2t[:, 512:1024]),
                   selfwait=False)
            P.wait("g", "vs")
            P.dma("g", l_all[2 * pair:2 * pair + 2, :], l2s[:])
            P.wait("t", "vs")
            ts_end_pair.append(P.cnt["ts"])
        P.wait("v", "gd")
        P.step("v", lambda v: v.reciprocal(out=rl_all[:], in_=l_all[:]))
        for pair in _pairs:
            P.waitall("s")
            P.dma("s", rl_b[:], rl_all[2 * pair:2 * pair + 2, :].unsqueeze(1).broadcast_to([2, 64, 512]))
            P.wait("v", "sd")
            P.wait("v", "gd")
            P.step("v", lambda v, pair=pair: v.tensor_mul(out=outn_t[:], in0=av_sb[pair][:], in1=rl_b[:]),
                   selfwait=False)
            P.wait("g", "vs")
            P.dma("g", io["outn_d"][128 * pair:128 * (pair + 1), 512 * TPERM[qt]:512 * (TPERM[qt] + 1)], outn_t[:])
            if dbg_what == "outn":
                P.dma("g", dbg_fm[128 * pair:128 * (pair + 1), 512 * TPERM[qt]:512 * (TPERM[qt] + 1)], outn_t[:])
    P.barrier()

    if dbg_what == "outnd" and "D" in stages:
        for dc in range(NOC):
            for tq2 in range(8):
                P.waitall("s")
                P.dma("s", outn_t[:], io["outn_d"][128 * dc:128 * (dc + 1), 512 * tq2:512 * (tq2 + 1)])
                P.wait("v", "sd")
                P.step("v", lambda v: v.tensor_copy(out=sum_ev[:], in_=outn_t[:]))
                P.wait("g", "vs")
                P.dma("g", dbg_fm[128 * dc:128 * (dc + 1), 512 * tq2:512 * (tq2 + 1)], sum_ev[:])
    if "D" in stages:
        load_weights("woT")
    P.barrier()
    d_mm_tok = 0
    d_z_tok = 0
    for tt in (range(NTT) if "D" in stages else []):
        if tt == 0:
            P.waitall("s")
        else:
            P.wait("s", "ts", d_mm_tok)
            P.wait("s", "vs", d_z_tok)
        P.newbatch("s")
        for dc in range(NOC):
            P.dma("s", on_t[dc][:], io["outn_d"][128 * dc:128 * (dc + 1), 128 * tt:128 * (tt + 1)])
        d_ld_tok = P.dma("s", xt_d[:], io["x"][128 * tt:128 * (tt + 1), :])
        if dbg_what == "onread":
            P.wait("v", "sd")
            P.wait("v", "gd")
            for dc in range(NOC):
                P.step("v", lambda v, dc=dc: v.tensor_copy(out=y1t[:, 128 * dc:128 * (dc + 1)], in_=on_t[dc][:]))
            P.wait("g", "vs")
            P.dma("g", io["dbg"][128 * tt:128 * (tt + 1), :], y1t[:])
        P.wait_tok("t", d_ld_tok)
        if d_z_tok > 0:
            P.wait("t", "vs", d_z_tok)
        for half in range(2):
            psh = ps_a[:, 512 * half:512 * (half + 1)]
            for dc in range(NOC):
                P.step("t", lambda t, dc=dc, half=half, psh=psh:
                       t.matmul(out=psh[:], lhsT=on_t[dc][:], rhs=w_bf[dc][:, 512 * half:512 * (half + 1)],
                                start=(dc == 0), stop=False))
            P.step("t", lambda t, half=half, psh=psh:
                   t.matmul(out=psh[:], lhsT=ones_f1[:], rhs=bo_sb[0:1, 512 * half:512 * (half + 1)],
                            start=False, stop=True))
        d_mm_tok = P.cnt["ts"]
        P.wait("v", "ts", d_mm_tok)
        P.wait("v", "gd")
        P.wait("v", "as_")
        for half in range(2):
            P.step("v", lambda v, half=half: v.tensor_add(
                out=z_sb[:, 512 * half:512 * (half + 1)], in0=ps_a[:, 512 * half:512 * (half + 1)],
                in1=xt_d[:, 512 * half:512 * (half + 1)]), selfwait=False)
        d_z_tok = P.cnt["vs"]
        if dbg_what == "z":
            P.wait("g", "vs")
            P.dma("g", io["dbg"][128 * tt:128 * (tt + 1), :], z_sb[:])
        if d_level < 2:
            continue
        P.step("v", lambda v: v.tensor_reduce(out=rsum[:], in_=z_sb[:], axis=AX.X, op=ALU.add))
        P.wait("a", "vs")
        P.inc("a", "as_", 1, lambda a: a.activation(out=zsq[:], in_=z_sb[:], func=AF.Square), selfwait=False)
        P.wait("v", "as_")
        P.step("v", lambda v: v.tensor_reduce(out=ssum[:], in_=zsq[:], axis=AX.X, op=ALU.add))
        P.wait("g", "vs")
        P.wait("g", "as_")
        P.step("g", lambda g: g.tensor_scalar(out=mvec[:], in0=rsum[:], scalar1=1.0 / D, scalar2=None, op0=ALU.mult))
        P.step("g", lambda g: g.tensor_mul(out=rstdv[:], in0=mvec[:], in1=mvec[:]))
        P.step("g", lambda g: g.tensor_scalar(out=varv[:], in0=ssum[:], scalar1=1.0 / D, scalar2=None, op0=ALU.mult))
        P.step("g", lambda g: g.tensor_sub(out=varv[:], in0=varv[:], in1=rstdv[:]))
        P.wait("a", "gs")
        P.inc("a", "as_", 1, lambda a: a.activation(out=rstdv[:], in_=varv[:], func=AF.Ln, bias=epsA[:, 0:1]))
        P.inc("a", "as_", 1, lambda a: a.activation(out=rstdv[:], in_=rstdv[:], func=AF.Exp, scale=-0.5))
        if dbg_what == "ln1":
            P.wait("g", "vs")
            P.wait("g", "as_")
            P.dma("g", io["dbg"][128 * tt:128 * (tt + 1), 0:1], mvec[:], allow_slow_non_contiguous=True)
            P.dma("g", io["dbg"][128 * tt:128 * (tt + 1), 1:2], varv[:], allow_slow_non_contiguous=True)
            P.dma("g", io["dbg"][128 * tt:128 * (tt + 1), 2:3], rstdv[:], allow_slow_non_contiguous=True)
            P.dma("g", io["dbg"][128 * tt:128 * (tt + 1), 3:4], rsum[:], allow_slow_non_contiguous=True)
            P.dma("g", io["dbg"][128 * tt:128 * (tt + 1), 4:5], ssum[:], allow_slow_non_contiguous=True)
        P.wait("v", "as_")
        P.wait("v", "gs")
        P.step("v", lambda v: v.tensor_scalar(out=y1t[:], in0=z_sb[:], scalar1=mvec[:, 0:1],
                                              scalar2=rstdv[:, 0:1], op0=ALU.subtract, op1=ALU.mult),
               selfwait=False)
        P.step("v", lambda v, tt=tt: v.tensor_reduce(out=mxcol[:, tt:tt + 1], in_=y1t[:], axis=AX.X,
                                                     op=ALU.max, apply_absolute_value=True))
        if d_level < 3:
            continue
        P.wait("g", "vs")
        P.dma("g", io["y1_d"][128 * tt:128 * (tt + 1), :], y1t[:])
        if io["dbg_on"] and dbg_what in (None, "y1"):
            P.dma("g", io["dbg"][128 * tt:128 * (tt + 1), :], y1t[:])
    P.step("v", lambda v: v.tensor_reduce(out=mxfin[:], in_=mxcol[:], axis=AX.X, op=ALU.max))
    P.wait("a", "vs")
    P.dma("a", io["mx_d"], mxfin[:])
    P.wait("a", "ad")
    P.dma("a", mxrow[:], io["mx_d"].rearrange("p one -> one p"))
    P.wait("v", "ad")
    P.step("v", lambda v: v.memset(mx8[:], 0.0))
    P.step("v", lambda v: v.tensor_reduce(out=mx8[:, 0:1], in_=mxrow[:], axis=AX.X, op=ALU.max))
    P.wait("g", "vs")
    P.dma("g", io["mx_out"], mx8[:])
    P.wait("g", "gd")


def peer_reduce(nc, P, io, E, sb, name):
    pm = E(sb(f"{name}_pm", [1, 64], F32))
    gmax = E(sb(f"{name}_gmax", [1, 8], F32))
    P.dma("s", pm[:], io["peer_mx"])
    P.wait("v", "sd")
    P.step("v", lambda v: v.tensor_reduce(
        out=gmax[:].rearrange("p (s one) -> p s one", one=1),
        in_=pm[:].rearrange("p (s r) -> p s r", r=8), axis=AX.X, op=ALU.max))
    return gmax


def build_phase1(nc, P, io, ctx, parts='sdxmer'):
    E = ctx.enter_context
    sb = nc.sbuf_tensor
    gmax = peer_reduce(nc, P, io, E, sb, "p1")
    s1v = E(sb("p1_s1", [1, 1], F32))
    rs1v = E(sb("p1_rs1", [1, 1], F32))
    s1_pp = E(sb("p1_s1pp", [128, 1], F32))
    rs1_pp = E(sb("p1_rs1pp", [128, 1], F32))
    P.step("v", lambda v: v.tensor_scalar(out=s1v[:], in0=gmax[:, 0:1], scalar1=1.0 / QMAX,
                                          scalar2=1e-12, op0=ALU.mult, op1=ALU.max))
    P.wait("a", "vs")
    P.inc("a", "as_", 1, lambda a: a.activation(out=rs1v[:], in_=s1v[:], func=AF.Copy))
    P.wait("v", "as_")
    P.step("v", lambda v: v.reciprocal(out=rs1v[:], in_=rs1v[:]))
    P.waitall("s")
    P.dma("s", s1_pp[:], s1v[:].unsqueeze(1).broadcast_to([1, 128, 1]))
    P.dma("s", rs1_pp[:], rs1v[:].unsqueeze(1).broadcast_to([1, 128, 1]))

    nat = E(sb("p1_nat", [128, NI], F32))
    bm = E(sb("p1_bm", [128, 96], F32))
    s_c = E(sb("p1_sc", [128, 96], F32))
    rs_c = E(sb("p1_rsc", [128, 96], F32))
    sT_up = E(sb("p1_sTup", [32, NI], F32))
    rT_up = E(sb("p1_rTup", [32, NI], F32))
    sT_gt = E(sb("p1_sTgt", [32, NI], F32))
    rT_gt = E(sb("p1_rTgt", [32, NI], F32))
    sT_dn = E(sb("p1_sTdn", [96, D], F32))
    rT_dn = E(sb("p1_rTdn", [96, D], F32))
    wt_f = [E(sb(f"p1_wtf{i}", [128, 512], F32)) for i in range(2)]
    sb_b = E(sb("p1_sbb", [128, 512], F32))
    rb_b = E(sb("p1_rbb", [128, 512], F32))
    ti32 = E(sb("p1_ti32", [128, 512], I32))
    tf32 = E(sb("p1_tf32", [128, 512], F32))
    wq_b = E(sb("p1_wqb", [128, 512], BF16))
    ps_t = E(nc.psum_tensor("p1_pst", [128, 512], F32))
    ps_c0 = E(nc.psum_tensor("p1_psc0", [128, 512], F32))
    ps_c1 = E(nc.psum_tensor("p1_psc1", [128, 512], F32))

    def dequant(natname, tname, dqname, K, O, sT, rT):
        nblk = K // 32
        for c in range(O // 128):
            P.waitall("s")
            P.dma("s", nat[:, 0:K], io[natname][128 * c:128 * (c + 1), :])
            P.wait("v", "sd")
            P.step("v", lambda v, K=K, nblk=nblk: v.tensor_reduce(
                out=bm[:, 0:nblk].rearrange("p (b one) -> p b one", one=1),
                in_=nat[:, 0:K].rearrange("p (b w) -> p b w", w=32),
                axis=AX.X, op=ALU.max, apply_absolute_value=True))
            P.wait("g", "vs")
            P.step("g", lambda g, nblk=nblk: g.tensor_scalar(
                out=s_c[:, 0:nblk], in0=bm[:, 0:nblk], scalar1=1.0 / 7.0, scalar2=1e-12,
                op0=ALU.mult, op1=ALU.max))
            P.wait("v", "gs")
            P.step("v", lambda v, nblk=nblk: v.reciprocal(out=rs_c[:, 0:nblk], in_=s_c[:, 0:nblk]))
            P.wait("t", "vs")
            P.wait("t", "gs")
            P.step("t", lambda t, nblk=nblk: t.transpose(out=ps_t[0:nblk, 0:128], in_=s_c[:, 0:nblk],
                                                         identity=idf_sb[:]))
            P.wait("v", "ts")
            P.step("v", lambda v, c=c, nblk=nblk, sT=sT: v.tensor_copy(
                out=sT[0:nblk, 128 * c:128 * (c + 1)], in_=ps_t[0:nblk, 0:128]))
            P.wait("t", "vs")
            P.step("t", lambda t, nblk=nblk: t.transpose(out=ps_t[0:nblk, 0:128], in_=rs_c[:, 0:nblk],
                                                         identity=idf_sb[:]))
            P.wait("v", "ts")
            P.step("v", lambda v, c=c, nblk=nblk, rT=rT: v.tensor_copy(
                out=rT[0:nblk, 128 * c:128 * (c + 1)], in_=ps_t[0:nblk, 0:128]))
            P.wait("t", "vs")
        for hc in range(K // 128):
            for os_ in range(O // 512):
                buf = (hc * (O // 512) + os_) % 2
                P.waitall("s")
                P.dma("s", wt_f[buf][:], io[tname][128 * hc:128 * (hc + 1), 512 * os_:512 * (os_ + 1)])
                P.dma("s", sb_b[:], sT[4 * hc:4 * hc + 4, 512 * os_:512 * (os_ + 1)]
                      .unsqueeze(1).broadcast_to([4, 32, 512]))
                P.dma("s", rb_b[:], rT[4 * hc:4 * hc + 4, 512 * os_:512 * (os_ + 1)]
                      .unsqueeze(1).broadcast_to([4, 32, 512]))
                P.wait("v", "sd")
                P.wait("v", "gd")
                P.step("v", lambda v, buf=buf: v.tensor_mul(out=tf32[:], in0=wt_f[buf][:], in1=rb_b[:]),
                       selfwait=False)
                P.step("v", lambda v: v.tensor_copy(out=ti32[:], in_=tf32[:]), selfwait=False)
                P.wait("v", "gd")
                P.step("v", lambda v: v.tensor_mul(out=wq_b[:], in0=ti32[:], in1=sb_b[:]), selfwait=False)
                P.wait("g", "vs")
                P.dma("g", io[dqname][128 * hc:128 * (hc + 1), 512 * os_:512 * (os_ + 1)], wq_b[:])

    idf_sb = E(sb("p1_idf", [128, 128], F32))
    P.dma("s", idf_sb[:], io["idf"])
    if "d" in parts:
        dequant("w_up", "wupT", "wupT_dq", D, NI, sT_up, rT_up)
        dequant("w_gate", "wgateT", "wgateT_dq", D, NI, sT_gt, rT_gt)
        dequant("w_down", "wdownT", "wdownT_dq", NI, D, sT_dn, rT_dn)
    P.barrier()

    y1t = E(sb("p1_y1t", [128, D], F32))
    xqi = E(sb("p1_xqi", [128, D], I32))
    xqf = E(sb("p1_xqf", [128, D], F32))
    xqb = E(sb("p1_xqb", [128, D], BF16))
    xq_fm = [E(sb(f"p1_xqfm{c}", [128, T], BF16)) for c in range(NOC)]
    idb_sb = E(sb("p1_idb", [128, 128], BF16))
    P.dma("s", idb_sb[:], io["idb"])
    for tt in (range(NTT) if "x" in parts else []):
        P.waitall("s")
        P.dma("s", y1t[:], io["y1_d"][128 * tt:128 * (tt + 1), :])
        P.wait("v", "sd")
        P.wait("v", "gs")
        P.step("v", lambda v: v.tensor_scalar(out=xqf[:], in0=y1t[:], scalar1=rs1_pp[:, 0:1],
                                              scalar2=None, op0=ALU.mult), selfwait=False)
        P.step("v", lambda v: v.tensor_copy(out=xqi[:], in_=xqf[:]), selfwait=False)
        P.wait("v", "ts")
        P.step("v", lambda v: v.tensor_scalar(out=xqb[:], in0=xqi[:], scalar1=s1_pp[:, 0:1],
                                              scalar2=None, op0=ALU.mult), selfwait=False)
        P.wait("t", "vs")
        for hc in range(NOC):
            bank = hc % 2
            pview = (ps_c0 if bank == 0 else ps_c1)[:, 0:128].bitcast(BF16)
            P.wait("t", "vs")
            P.step("t", lambda t, hc=hc, pv=pview:
                   t.transpose(out=pv[:, 0:128], in_=xqb[:, 128 * hc:128 * (hc + 1)], identity=idb_sb[:]))
            P.wait("v", "ts")
            P.step("v", lambda v, hc=hc, tt=tt, pv=pview:
                   v.tensor_copy(out=xq_fm[hc][:, 128 * tt:128 * (tt + 1)], in_=pv[:, 0:128]),
                   selfwait=False)
    P.barrier()

    wq_l = [[E(sb(f"p1_wql{b}_{h}", [128, 128], BF16)) for h in range(NOC)] for b in range(2)]
    upt = [E(sb(f"p1_upt{b}", [128, 512], F32)) for b in range(2)]
    ucol = E(sb("p1_ucol", [128, 2 * NIC * NQT], F32))
    gcol = E(sb("p1_gcol", [128, NIC * NQT], F32))
    mxf = E(sb("p1_mxf", [128, 4], F32))
    mxrow = E(sb("p1_mxrow", [1, 128], F32))
    mx8 = E(sb("p1_mx8", [1, 8], F32))
    _wl = ()
    if "m" in parts:
        _wl = ((0, "wupT_dq", "up_d"), (1, "wgateT_dq", "gate_d"))
    elif "G" in parts:
        _wl = ((1, "wgateT_dq", "gate_d"),)
    elif "u" in parts:
        _wl = ((0, "wupT_dq", "up_d"),)
    _iters = [(wi, dqname, dram, oc) for (wi, dqname, dram) in _wl
              for oc in range(4 if "u4" in parts else NIC)]
    mm_tok = {}
    evac_tok = {0: 0, 1: 0}
    vred_tok = {0: 0, 1: 0}
    gdma_tok = {0: None, 1: None}
    for k, (wi, dqname, dram, oc) in enumerate(_iters):
        wb = k % 2
        if k - 2 in mm_tok:
            P.wait("s", "ts", mm_tok[k - 2])
        P.newbatch("s")
        ld_tok = None
        for hc in range(NOC):
            ld_tok = P.dma("s", wq_l[wb][hc][:],
                           io[dqname][128 * hc:128 * (hc + 1), 128 * oc:128 * (oc + 1)])
        P.wait_tok("t", ld_tok)
        for tc in range(NQT):
            pb = tc % 2
            ps = ps_c0 if pb == 0 else ps_c1
            if evac_tok[pb] > 0:
                P.wait("t", "as_", evac_tok[pb])
            for hc in range(NOC):
                P.step("t", lambda t, hc=hc, tc=tc, ps=ps, wb=wb:
                       t.matmul(out=ps[:], lhsT=wq_l[wb][hc][:], rhs=xq_fm[hc][:, 512 * tc:512 * (tc + 1)],
                                start=(hc == 0), stop=(hc == NOC - 1)))
            P.wait("a", "ts", P.cnt["ts"])
            if vred_tok[pb] > 0:
                P.wait("a", "vs", vred_tok[pb])
            if gdma_tok[pb] is not None:
                P.wait_tok("a", gdma_tok[pb])
            P.inc("a", "as_", 1, lambda a, ps=ps, pb=pb:
                  a.activation(out=upt[pb][:], in_=ps[:], func=AF.Copy), selfwait=False)
            evac_tok[pb] = P.cnt["as_"]
            P.wait("g", "as_", evac_tok[pb])
            gdma_tok[pb] = P.dma("g", io[dram][128 * oc:128 * (oc + 1), 512 * tc:512 * (tc + 1)],
                                 upt[pb][:])
            idx = oc * NQT + tc
            P.wait("v", "as_", evac_tok[pb])
            P.step("v", lambda v, idx=idx, wi=wi, pb=pb: v.tensor_reduce(
                out=ucol[:, (wi * NIC * NQT + idx):(wi * NIC * NQT + idx) + 1],
                in_=upt[pb][:], axis=AX.X, op=ALU.max, apply_absolute_value=True))
            if wi == 1:
                P.step("v", lambda v, idx=idx, pb=pb: v.tensor_reduce(
                    out=gcol[:, idx:idx + 1], in_=upt[pb][:], axis=AX.X, op=ALU.max))
            vred_tok[pb] = P.cnt["vs"]
        mm_tok[k] = P.cnt["ts"]
    P.barrier()
    P.step("v", lambda v: v.tensor_reduce(out=mxf[:, 0:1], in_=ucol[:, 0:NIC * NQT], axis=AX.X, op=ALU.max))
    P.step("v", lambda v: v.tensor_reduce(out=mxf[:, 1:2], in_=ucol[:, NIC * NQT:], axis=AX.X, op=ALU.max))
    P.step("v", lambda v: v.tensor_reduce(out=mxf[:, 2:3], in_=gcol[:], axis=AX.X, op=ALU.max))
    P.wait("a", "vs")
    P.dma("a", io["mx_d"].rearrange("p one -> p one"), mxf[:, 0:1])
    P.wait("a", "ad")
    P.dma("a", mxrow[:], io["mx_d"].rearrange("p one -> one p"))
    P.wait("v", "ad")
    P.step("v", lambda v: v.memset(mx8[:], 0.0))
    P.step("v", lambda v: v.tensor_reduce(out=mx8[:, 1:2], in_=mxrow[:], axis=AX.X, op=ALU.max))
    P.wait("a", "vs")
    P.dma("a", io["mx_d"], mxf[:, 1:2])
    P.wait("a", "ad")
    P.dma("a", mxrow[:], io["mx_d"].rearrange("p one -> one p"))
    P.wait("v", "ad")
    P.step("v", lambda v: v.tensor_reduce(out=mx8[:, 2:3], in_=mxrow[:], axis=AX.X, op=ALU.max))
    P.wait("a", "vs")
    P.dma("a", io["mx_d"], mxf[:, 2:3])
    P.wait("a", "ad")
    P.dma("a", mxrow[:], io["mx_d"].rearrange("p one -> one p"))
    P.wait("v", "ad")
    P.step("v", lambda v: v.tensor_reduce(out=mx8[:, 3:4], in_=mxrow[:], axis=AX.X, op=ALU.max))
    P.wait("g", "vs")
    P.dma("g", io["mx_out"], mx8[:])
    P.wait("g", "gd")


def build_phase2(nc, P, io, ctx):
    E = ctx.enter_context
    sb = nc.sbuf_tensor
    gmax = peer_reduce(nc, P, io, E, sb, "p2")
    scl = E(sb("p2_scl", [1, 8], F32))
    rscl = E(sb("p2_rscl", [1, 8], F32))
    t11a = E(sb("p2_t11a", [1, 1], F32))
    t11b = E(sb("p2_t11b", [1, 1], I32))
    t11c = E(sb("p2_t11c", [1, 1], I32))
    t11d = E(sb("p2_t11d", [1, 1], F32))
    t11e = E(sb("p2_t11e", [1, 1], F32))
    sb16 = E(sb("p2_sb16", [1, 1], F32))
    scraw = E(sb("p2_scraw", [1, 1], F32))
    scv = E(sb("p2_scv", [1, 1], F32))
    rscv = E(sb("p2_rscv", [1, 1], F32))
    sbrc = E(sb("p2_sbrc", [1, 1], F32))
    csa = E(sb("p2_csa", [1, 1], F32))
    rsa_pp = E(sb("p2_rsapp", [128, 1], F32))
    rsb_pp = E(sb("p2_rsbpp", [128, 1], F32))
    sb_pp = E(sb("p2_sbpp", [128, 1], F32))
    sbrc_pp = E(sb("p2_sbrcpp", [128, 1], F32))
    P.step("v", lambda v: v.tensor_scalar(out=scl[:], in0=gmax[:], scalar1=1.0 / QMAX,
                                          scalar2=1e-12, op0=ALU.mult, op1=ALU.max))
    P.wait("a", "vs")
    P.inc("a", "as_", 1, lambda a: a.activation(out=rscl[:], in_=scl[:], func=AF.Copy))
    P.wait("v", "as_")
    P.step("v", lambda v: v.reciprocal(out=rscl[:], in_=rscl[:]))
    P.wait("a", "vs")
    P.inc("a", "as_", 1, lambda a: a.activation(out=t11a[:], in_=gmax[:, 3:4], func=AF.Copy))
    P.wait("v", "as_")
    P.step("v", lambda v: v.tensor_scalar(out=t11d[:], in0=t11a[:], scalar1=rscl[0:1, 2:3],
                                          scalar2=None, op0=ALU.mult))
    P.step("v", lambda v: v.tensor_copy(out=t11b[:], in_=t11d[:]))
    P.wait("a", "vs")
    P.inc("a", "as_", 1, lambda a: a.activation(out=t11d[:], in_=t11b[:], func=AF.Sigmoid,
                                                scale=scl[0:1, 2:3]))
    P.wait("v", "as_")
    P.step("v", lambda v: v.tensor_scalar(out=t11e[:], in0=t11d[:], scalar1=65536.0,
                                          scalar2=None, op0=ALU.mult))
    P.step("v", lambda v: v.tensor_copy(out=t11c[:], in_=t11e[:]))
    P.wait("g", "vs")
    P.step("g", lambda g: g.tensor_scalar(out=t11c[:], in0=t11c[:], scalar1=65535.0,
                                          scalar2=None, op0=ALU.min)) if False else None
    P.wait("a", "vs")
    P.inc("a", "as_", 1, lambda a: a.activation(out=t11d[:], in_=t11c[:], func=AF.Copy))
    P.wait("v", "as_")
    P.step("v", lambda v: v.tensor_scalar(out=t11c[:], in0=t11d[:], scalar1=65535.0,
                                          scalar2=None, op0=ALU.min))
    P.step("v", lambda v: v.tensor_mul(out=scraw[:], in0=t11b[:], in1=t11c[:]))
    P.step("v", lambda v: v.tensor_scalar(out=sb16[:], in0=scl[0:1, 2:3], scalar1=1.0 / 65536.0,
                                          scalar2=None, op0=ALU.mult))
    P.wait("a", "vs")
    P.inc("a", "as_", 1, lambda a: a.activation(out=scraw[:], in_=scraw[:], func=AF.Copy,
                                                scale=sb16[0:1, 0:1]))
    P.wait("v", "as_")
    P.step("v", lambda v: v.tensor_scalar(out=scv[:], in0=scraw[:], scalar1=1.0 / QMAX,
                                          scalar2=1e-12, op0=ALU.mult, op1=ALU.max))
    P.wait("a", "vs")
    P.inc("a", "as_", 1, lambda a: a.activation(out=rscv[:], in_=scv[:], func=AF.Copy))
    P.wait("v", "as_")
    P.step("v", lambda v: v.reciprocal(out=rscv[:], in_=rscv[:]))
    P.step("v", lambda v: v.tensor_scalar(out=sbrc[:], in0=rscv[:], scalar1=sb16[0:1, 0:1],
                                          scalar2=None, op0=ALU.mult))
    P.step("v", lambda v: v.tensor_scalar(out=csa[:], in0=scv[:], scalar1=scl[0:1, 1:2],
                                          scalar2=None, op0=ALU.mult))
    scrow = E(sb("p2_scrow", [1, 16], F32))
    P.step("v", lambda v: v.memset(scrow[:], 0.0))
    P.step("v", lambda v: v.tensor_copy(out=scrow[:, 0:1], in_=scl[:, 1:2]))
    P.step("v", lambda v: v.tensor_copy(out=scrow[:, 1:2], in_=scl[:, 2:3]))
    P.step("v", lambda v: v.tensor_copy(out=scrow[:, 2:3], in_=scv[:]))
    P.step("v", lambda v: v.tensor_copy(out=scrow[:, 3:4], in_=csa[:]))
    P.step("v", lambda v: v.tensor_copy(out=scrow[:, 4:5], in_=sbrc[:]))
    P.wait("g", "vs")
    P.dma("g", io["scal_d"], scrow[:])
    P.waitall("s")
    P.dma("s", rsa_pp[:], rscl[:, 1:2].unsqueeze(1).broadcast_to([1, 128, 1]))
    P.dma("s", rsb_pp[:], rscl[:, 2:3].unsqueeze(1).broadcast_to([1, 128, 1]))
    P.dma("s", sb_pp[:], scl[:, 2:3].unsqueeze(1).broadcast_to([1, 128, 1]))
    P.dma("s", sbrc_pp[:], sbrc[:].unsqueeze(1).broadcast_to([1, 128, 1]))
    P.barrier()

    up_t = [E(sb(f"p2_upt{i}", [128, 512], F32)) for i in range(2)]
    gt_t = [E(sb(f"p2_gtt{i}", [128, 512], F32)) for i in range(2)]
    upi = E(sb("p2_upi", [128, 512], I32))
    tmpf = E(sb("p2_tmpf", [128, 512], F32))
    gti = E(sb("p2_gti", [128, 512], I32))
    sig = E(sb("p2_sig", [128, 512], F32))
    sqi = E(sb("p2_sqi", [128, 512], I32))
    sq2 = E(sb("p2_sq2", [128, 512], I32))
    g2p = E(sb("p2_g2p", [128, 512], F32))
    g2i = E(sb("p2_g2i", [128, 512], I32))
    opr = E(sb("p2_opr", [128, 512], F32))
    ocol = E(sb("p2_ocol", [128, NIC * NQT], F32))
    mxrow = E(sb("p2_mxrow", [1, 128], F32))
    mx8 = E(sb("p2_mx8", [1, 8], F32))
    mxf = E(sb("p2_mxf", [128, 1], F32))
    ntile = NIC * NQT
    rd_tok = {0: 0, 1: 0}
    for idx in range(ntile):
        oc, tc = divmod(idx, NQT)
        buf = idx % 2
        if rd_tok[buf] > 0:
            P.wait("s", "vs", rd_tok[buf])
        P.newbatch("s")
        P.dma("s", up_t[buf][:], io["up_d"][128 * oc:128 * (oc + 1), 512 * tc:512 * (tc + 1)])
        ld_tok = P.dma("s", gt_t[buf][:], io["gate_d"][128 * oc:128 * (oc + 1), 512 * tc:512 * (tc + 1)])
        P.wait_tok("v", ld_tok)
        P.wait("v", "as_")
        P.step("v", lambda v, buf=buf: v.tensor_scalar(out=tmpf[:], in0=up_t[buf][:], scalar1=rsa_pp[:, 0:1],
                                                       scalar2=None, op0=ALU.mult), selfwait=False)
        P.step("v", lambda v: v.tensor_copy(out=upi[:], in_=tmpf[:]), selfwait=False)
        P.step("v", lambda v, buf=buf: v.tensor_scalar(out=tmpf[:], in0=gt_t[buf][:], scalar1=rsb_pp[:, 0:1],
                                                       scalar2=None, op0=ALU.mult), selfwait=False)
        P.step("v", lambda v: v.tensor_copy(out=gti[:], in_=tmpf[:]), selfwait=False)
        rd_tok[buf] = P.cnt["vs"]
        P.wait("a", "vs")
        P.inc("a", "as_", 1, lambda a: a.activation(out=sig[:], in_=gti[:], func=AF.Sigmoid,
                                                    scale=sb_pp[:, 0:1]), selfwait=False)
        P.wait("v", "as_")
        P.step("v", lambda v: v.tensor_scalar(out=tmpf[:], in0=sig[:], scalar1=65536.0,
                                              scalar2=None, op0=ALU.mult), selfwait=False)
        P.step("v", lambda v: v.tensor_copy(out=sqi[:], in_=tmpf[:]), selfwait=False)
        P.step("v", lambda v: v.scalar_tensor_tensor(out=g2p[:], in0=sqi[:], scalar=65535.0,
                                                     in1=gti[:], op0=ALU.min, op1=ALU.mult),
               selfwait=False)
        P.step("v", lambda v: v.tensor_scalar(out=tmpf[:], in0=g2p[:], scalar1=sbrc_pp[:, 0:1],
                                              scalar2=None, op0=ALU.mult), selfwait=False)
        P.step("v", lambda v: v.tensor_copy(out=g2i[:], in_=tmpf[:]), selfwait=False)
        P.wait("v", "ad")
        P.step("v", lambda v: v.tensor_mul(out=opr[:], in0=g2i[:], in1=upi[:]), selfwait=False)
        P.step("v", lambda v, idx=idx: v.tensor_reduce(out=ocol[:, idx:idx + 1], in_=opr[:],
                                                       axis=AX.X, op=ALU.max, apply_absolute_value=True))
        P.wait("a", "vs")
        P.wait("a", "ad")
        P.dma("a", io["o_d"][128 * oc:128 * (oc + 1), 512 * tc:512 * (tc + 1)], opr[:])
    P.barrier()
    P.step("v", lambda v: v.tensor_reduce(out=mxf[:], in_=ocol[:], axis=AX.X, op=ALU.max))
    P.wait("a", "vs")
    P.dma("a", io["mx_d"], mxf[:])
    P.wait("a", "ad")
    P.dma("a", mxrow[:], io["mx_d"].rearrange("p one -> one p"))
    P.wait("v", "ad")
    P.step("v", lambda v: v.memset(mx8[:], 0.0))
    P.step("v", lambda v: v.tensor_reduce(out=mx8[:, 4:5], in_=mxrow[:], axis=AX.X, op=ALU.max))
    P.wait("g", "vs")
    P.dma("g", io["mx_out"], mx8[:])
    P.wait("g", "gd")


def build_phase3(nc, P, io, ctx, dbg_what=None):
    E = ctx.enter_context
    sb = nc.sbuf_tensor
    gmax = peer_reduce(nc, P, io, E, sb, "p3")
    scrow = E(sb("p3_scrow", [1, 16], F32))
    sdv = E(sb("p3_sdv", [1, 1], F32))
    rsdv = E(sb("p3_rsdv", [1, 1], F32))
    crd = E(sb("p3_crd", [1, 1], F32))
    crd_pp = E(sb("p3_crdpp", [128, 1], F32))
    sd_pp = E(sb("p3_sdpp", [128, 1], F32))
    P.wait("s", "vs")
    P.dma("s", scrow[:], io["scal_d"])
    P.wait("v", "sd")
    P.step("v", lambda v: v.tensor_scalar(out=sdv[:], in0=gmax[:, 4:5], scalar1=scrow[0:1, 3:4],
                                          scalar2=None, op0=ALU.mult))
    P.wait("a", "vs")
    P.inc("a", "as_", 1, lambda a: a.activation(out=rsdv[:], in_=sdv[:], func=AF.Copy))
    P.wait("v", "as_")
    P.step("v", lambda v: v.tensor_scalar(out=sdv[:], in0=rsdv[:], scalar1=1.0 / QMAX,
                                          scalar2=1e-12, op0=ALU.mult, op1=ALU.max))
    P.wait("a", "vs")
    P.inc("a", "as_", 1, lambda a: a.activation(out=rsdv[:], in_=sdv[:], func=AF.Copy))
    P.wait("v", "as_")
    P.step("v", lambda v: v.reciprocal(out=rsdv[:], in_=rsdv[:]))
    P.step("v", lambda v: v.tensor_scalar(out=crd[:], in0=rsdv[:], scalar1=scrow[0:1, 3:4],
                                          scalar2=None, op0=ALU.mult))
    P.waitall("s")
    P.dma("s", crd_pp[:], crd[:].unsqueeze(1).broadcast_to([1, 128, 1]))
    P.dma("s", sd_pp[:], sdv[:].unsqueeze(1).broadcast_to([1, 128, 1]))
    P.barrier()

    opr = [E(sb(f"p3_opr{i}", [128, 512], F32)) for i in range(2)]
    oqi = E(sb("p3_oqi", [128, 512], I32))
    oqf = E(sb("p3_oqf", [128, 512], F32))
    oqb = E(sb("p3_oqb", [128, 512], BF16))
    oq_tok = {0: 0, 1: 0}
    for idx in range(NIC * NQT):
        oc, tc = divmod(idx, NQT)
        buf = idx % 2
        if oq_tok[buf] > 0:
            P.wait("s", "vs", oq_tok[buf])
        P.newbatch("s")
        ld_tok = P.dma("s", opr[buf][:], io["o_d"][128 * oc:128 * (oc + 1), 512 * tc:512 * (tc + 1)])
        P.wait_tok("v", ld_tok)
        P.wait("v", "gd")
        P.step("v", lambda v, buf=buf: v.tensor_scalar(out=oqf[:], in0=opr[buf][:], scalar1=crd_pp[:, 0:1],
                                                       scalar2=None, op0=ALU.mult), selfwait=False)
        oq_tok[buf] = P.cnt["vs"]
        P.step("v", lambda v: v.tensor_copy(out=oqi[:], in_=oqf[:]), selfwait=False)
        P.step("v", lambda v: v.tensor_scalar(out=oqb[:], in0=oqi[:], scalar1=sd_pp[:, 0:1],
                                              scalar2=None, op0=ALU.mult), selfwait=False)
        P.wait("g", "vs")
        P.dma("g", io["gate_d"][128 * oc:128 * (oc + 1), 512 * tc:512 * (tc + 1)].bitcast(BF16)[:, 0:512],
              oqb[:])
    P.barrier()

    wd_l = [E(sb(f"p3_wdl{c}", [128, D], BF16)) for c in range(NIC)]
    wd_tok = None
    for ic in range(NIC):
        wd_tok = P.dma("s", wd_l[ic][:], io["wdownT_dq"][128 * ic:128 * (ic + 1), :])
    on_t = [E(sb(f"p3_ont{c}", [128, 128], BF16)) for c in range(NIC)]
    y1t = E(sb("p3_y1t", [128, D], F32))
    z_sb = E(sb("p3_z", [128, D], F32))
    zsq = E(sb("p3_zsq", [128, D], F32))
    rsum = E(sb("p3_rsum", [128, 1], F32))
    mvec = E(sb("p3_mvec", [128, 1], F32))
    ssum = E(sb("p3_ssum", [128, 1], F32))
    varv = E(sb("p3_varv", [128, 1], F32))
    rstdv = E(sb("p3_rstdv", [128, 1], F32))
    y2t = E(sb("p3_y2t", [128, D], F32))
    epsA = E(sb("p3_epsA", [128, 1], F32))
    P.step("v", lambda v: v.memset(epsA[:], EPS))
    ps_a = E(nc.psum_tensor("p3_psa", [128, 1024], F32))
    P.wait_tok("t", wd_tok)
    d_mm_tok = 0
    d_z_tok = 0
    for tt in range(NTT):
        if tt == 0:
            P.waitall("s")
        else:
            P.wait("s", "ts", d_mm_tok)
            P.wait("s", "vs", d_z_tok)
        P.newbatch("s")
        for ic in range(NIC):
            P.dma("s", on_t[ic][:],
                  io["gate_d"][128 * ic:128 * (ic + 1), 512 * (tt // 4):512 * (tt // 4) + 512]
                  .bitcast(BF16)[:, 128 * (tt % 4):128 * (tt % 4) + 128])
        d_ld_tok = P.dma("s", y1t[:], io["y1_d"][128 * tt:128 * (tt + 1), :])
        P.wait_tok("t", d_ld_tok)
        if d_z_tok > 0:
            P.wait("t", "vs", d_z_tok)
        for half in range(2):
            psh = ps_a[:, 512 * half:512 * (half + 1)]
            for ic in range(NIC):
                P.step("t", lambda t, ic=ic, half=half, psh=psh:
                       t.matmul(out=psh[:], lhsT=on_t[ic][:], rhs=wd_l[ic][:, 512 * half:512 * (half + 1)],
                                start=(ic == 0), stop=(ic == NIC - 1)))
        d_mm_tok = P.cnt["ts"]
        P.wait("v", "ts", d_mm_tok)
        P.wait("v", "gd")
        P.wait("v", "as_")
        for half in range(2):
            P.step("v", lambda v, half=half: v.tensor_add(
                out=z_sb[:, 512 * half:512 * (half + 1)], in0=ps_a[:, 512 * half:512 * (half + 1)],
                in1=y1t[:, 512 * half:512 * (half + 1)]), selfwait=False)
        d_z_tok = P.cnt["vs"]
        P.step("v", lambda v: v.tensor_reduce(out=rsum[:], in_=z_sb[:], axis=AX.X, op=ALU.add))
        P.wait("a", "vs")
        P.inc("a", "as_", 1, lambda a: a.activation(out=zsq[:], in_=z_sb[:], func=AF.Square), selfwait=False)
        P.wait("v", "as_")
        P.step("v", lambda v: v.tensor_reduce(out=ssum[:], in_=zsq[:], axis=AX.X, op=ALU.add))
        P.wait("g", "vs")
        P.wait("g", "as_")
        P.step("g", lambda g: g.tensor_scalar(out=mvec[:], in0=rsum[:], scalar1=1.0 / D, scalar2=None, op0=ALU.mult))
        P.step("g", lambda g: g.tensor_mul(out=rstdv[:], in0=mvec[:], in1=mvec[:]))
        P.step("g", lambda g: g.tensor_scalar(out=varv[:], in0=ssum[:], scalar1=1.0 / D, scalar2=None, op0=ALU.mult))
        P.step("g", lambda g: g.tensor_sub(out=varv[:], in0=varv[:], in1=rstdv[:]))
        P.wait("a", "gs")
        P.inc("a", "as_", 1, lambda a: a.activation(out=rstdv[:], in_=varv[:], func=AF.Ln, bias=epsA[:, 0:1]))
        P.inc("a", "as_", 1, lambda a: a.activation(out=rstdv[:], in_=rstdv[:], func=AF.Exp, scale=-0.5))
        P.wait("v", "as_")
        P.wait("v", "gs")
        P.step("v", lambda v: v.tensor_scalar(out=y2t[:], in0=z_sb[:], scalar1=mvec[:, 0:1],
                                              scalar2=rstdv[:, 0:1], op0=ALU.subtract, op1=ALU.mult))
        P.wait("g", "vs")
        P.dma("g", io["out"][128 * tt:128 * (tt + 1), :], y2t[:])
    P.wait("g", "gd")


ALLSEMS2 = ALLSEMS + ["ph"]


def build_full(dbg=False, only=None, p1_parts="sdxmer"):
    nc = bass.Bass("TRN2", target_bir_lowering=False)
    io = declare_io(nc, dbg=dbg)
    plans = []
    from contextlib import ExitStack
    with ExitStack() as octx:
        sems = {k: octx.enter_context(nc.semaphore(k)) for k in ALLSEMS2}
        ph_sb = octx.enter_context(nc.sbuf_tensor("ph_sb", [1, 1], I32))
        builders = [build_phase0,
                    lambda nc, P, io, c: build_phase1(nc, P, io, c, parts=p1_parts),
                    build_phase2, build_phase3]
        for i, bf in enumerate(builders):
            P = Plan()
            if only is None or i == only:
                with ExitStack() as pctx:
                    bf(nc, P, io, pctx)
            plans.append(P)
        block = octx.enter_context(nc.Block())

        def mk_emit(engname):
            def f(eng):
                if engname == "s":
                    eng.dma_start(out=ph_sb[:], in_=io["phase"]).then_inc(sems["ph"], 16)
                eng.wait_ge(sems["ph"], 16)
                p = eng.value_load(ph_sb[0:1, 0:1])
                with nc.If(p == 0):
                    plans[0].emit(engname, eng, sems)
                with nc.Else():
                    with nc.If(p == 1):
                        plans[1].emit(engname, eng, sems)
                    with nc.Else():
                        with nc.If(p == 2):
                            plans[2].emit(engname, eng, sems)
                        with nc.Else():
                            plans[3].emit(engname, eng, sems)
            return f

        block.sync(mk_emit("s"))
        block.tensor(mk_emit("t"))
        block.vector(mk_emit("v"))
        block.scalar(mk_emit("a"))
        block.gpsimd(mk_emit("g"))
    return nc, io


def host_inputs(x_core, Wq, Wkv, Wo, bo, ln_g, ln_b, w_up, w_gate, w_down):
    cos_t, sinp_t = rope_tables()
    tri, ems = mask_consts()
    return {
        "x": np.ascontiguousarray(x_core),
        "wqT": np.ascontiguousarray(Wq.T), "wkvT": np.ascontiguousarray(Wkv.T),
        "woT": np.ascontiguousarray(Wo.T), "bo": np.ascontiguousarray(bo.reshape(1, -1)),
        "g128": np.ascontiguousarray(np.tile(ln_g, 2).reshape(128, 1)),
        "b128": np.ascontiguousarray(np.tile(ln_b, 2).reshape(128, 1)),
        "cos_t": cos_t, "sinp_t": sinp_t, "tri": tri, "ems": ems,
        "ind2": ind2_const(),
        "idb": np.ascontiguousarray(np.eye(128).astype(ml_dtypes.bfloat16)),
        "idf": np.ascontiguousarray(np.eye(128, dtype=np.float32)),
        "w_up": np.ascontiguousarray(w_up), "w_gate": np.ascontiguousarray(w_gate),
        "w_down": np.ascontiguousarray(w_down),
        "wupT": np.ascontiguousarray(w_up.T), "wgateT": np.ascontiguousarray(w_gate.T),
        "wdownT": np.ascontiguousarray(w_down.T),
        "phase": np.array([[0]], np.int32),
        "peer_mx": np.zeros((1, 64), np.float32),
    }
_CACHE = {}


def kernel(L0_attn_x, L0_attn_Wq, L0_attn_Wkv, L0_attn_Wo, L0_attn_bo,
           L0_attn_ln_g, L0_attn_ln_b, L0_ffn_w_gate, L0_ffn_w_up, L0_ffn_w_down):
    from concourse.bass_utils import run_bass_kernel_spmd
    n = 8
    if "nc" not in _CACHE:
        _CACHE["nc"], _CACHE["io"] = build_full(dbg=False)
    nc = _CACHE["nc"]
    x = np.asarray(L0_attn_x)
    base = [host_inputs(x[c], np.asarray(L0_attn_Wq), np.asarray(L0_attn_Wkv),
                        np.asarray(L0_attn_Wo), np.asarray(L0_attn_bo),
                        np.asarray(L0_attn_ln_g), np.asarray(L0_attn_ln_b),
                        np.asarray(L0_ffn_w_up), np.asarray(L0_ffn_w_gate),
                        np.asarray(L0_ffn_w_down)) for c in range(n)]
    peer = np.zeros((1, 64), np.float32)
    slot_of_phase = {0: [0], 1: [1, 2, 3], 2: [4], 3: []}
    outs = None
    for phase in range(4):
        for c in range(n):
            base[c]["phase"] = np.array([[phase]], np.int32)
            base[c]["peer_mx"] = peer.copy()
        res = run_bass_kernel_spmd(nc, base, core_ids=list(range(n)))
        mx = np.stack([np.asarray(res.results[c]["mx_out"])[0] for c in range(n)])
        for s in slot_of_phase[phase]:
            for r in range(n):
                peer[0, s * 8 + r] = mx[r, s]
        outs = res
    return np.stack([np.asarray(outs.results[c]["out"]) for c in range(n)]).astype(np.float32)


# version 03
# speedup vs baseline: 1.6938x; 1.1068x over previous
import numpy as np
import ml_dtypes
import concourse.bass as bass
import concourse.mybir as mybir

F32 = mybir.dt.float32
BF16 = mybir.dt.bfloat16
I32 = mybir.dt.int32
AF = mybir.ActivationFunctionType
ALU = mybir.AluOpType
AX = mybir.AxisListType

T = 4096
D = 1024
H = 16
DH = 64
CTX = 1536
NI = 3072
NTT = T // 128
NQT = 8
NOC = D // 128
NIC = NI // 128
QMAX = 32767.0
EPS = 1e-5

ENGS = ("s", "t", "v", "a", "g")
KQ = 4
QUEUES = ("sd", "ad", "gd", "vd")
QSEMS = {q: [f"{q}{k}" for k in range(KQ)] for q in QUEUES}
ESEM = {"s": "ss", "t": "ts", "v": "vs", "a": "as_", "g": "gs"}
ENG2Q = {"s": "sd", "a": "ad", "g": "gd", "v": "vd"}
ALLSEMS = [s for q in QUEUES for s in QSEMS[q]] + list(ESEM.values())


class Plan:
    def __init__(self):
        self.cnt = {k: 0 for k in ALLSEMS}
        self.ops = {e: [] for e in ENGS}
        self.qidx = {q: 0 for q in QUEUES}
        self.qgate = {q: False for q in QUEUES}
        self.qwaited = {q: False for q in QUEUES}

    def op(self, eng, fn):
        self.ops[eng].append(("op", fn))

    def inc(self, eng, sem, by, fn, selfwait=None):
        self.cnt[sem] += by
        self.ops[eng].append(("inc", sem, by, fn))
        if selfwait is None:
            selfwait = sem == ESEM.get(eng)
        if selfwait and sem in ESEM.values() and eng != "t":
            self.ops[eng].append(("wait", sem, self.cnt[sem]))
        return self.cnt[sem]

    def step(self, eng, fn, selfwait=True):
        return self.inc(eng, ESEM[eng], 1, fn, selfwait=selfwait)

    def newbatch(self, eng):
        q = ENG2Q[eng]
        self.qidx[q] = (self.qidx[q] + 1) % KQ
        self.qgate[q] = True

    def dma(self, eng, out, in_, **kw):
        q = ENG2Q[eng]
        if self.qwaited[q]:
            self.qidx[q] = (self.qidx[q] + 1) % KQ
            self.qgate[q] = True
            self.qwaited[q] = False
        sem = QSEMS[q][self.qidx[q]]
        if self.qgate[q]:
            if self.cnt[sem] > 0:
                self.ops[eng].append(("wait", sem, self.cnt[sem]))
            self.qgate[q] = False
        self.cnt[sem] += 16
        self.ops[eng].append(("inc", sem, 16,
                              lambda e, o=out, i=in_, kw=kw: e.dma_start(out=o, in_=i, **kw)))
        return (sem, self.cnt[sem])

    def wait(self, eng, sem, val=None):
        if sem in QUEUES:
            self.qwaited[sem] = True
            for s in QSEMS[sem]:
                if self.cnt[s] > 0:
                    self.ops[eng].append(("wait", s, self.cnt[s]))
            return
        v = self.cnt[sem] if val is None else val
        if v <= 0:
            return
        self.ops[eng].append(("wait", sem, v))

    def wait_tok(self, eng, tok):
        sem, v = tok
        for q in QUEUES:
            if sem in QSEMS[q]:
                self.qwaited[q] = True
        self.ops[eng].append(("wait", sem, v))

    def waitall(self, eng):
        for q in QUEUES:
            self.wait(eng, q)
        for e, s in ESEM.items():
            if e != eng:
                self.wait(eng, s)

    def fence(self, eng):
        self.wait(eng, ESEM[eng])

    def barrier(self):
        vals = {}
        for e in ENGS:
            vals[e] = self.inc(e, ESEM[e], 1, lambda eng: eng.nop(), selfwait=False)
        for e in ENGS:
            for o in ENGS:
                if o != e:
                    self.wait(e, ESEM[o], vals[o])
            for q in QUEUES:
                self.wait(e, q)

    def emit(self, eng_name, eng, sems):
        for item in self.ops[eng_name]:
            if item[0] == "op":
                item[1](eng)
            elif item[0] == "inc":
                _, sem, by, fn = item
                fn(eng).then_inc(sems[sem], by)
            else:
                _, sem, v = item
                eng.wait_ge(sems[sem], v)


def rope_tables():
    inv_freq = 1.0 / (10000.0 ** (np.arange(0, DH, 2, dtype=np.float64) / DH))
    freqs = np.arange(T, dtype=np.float64)[:, None] * inv_freq[None, :]
    emb = np.concatenate([freqs, freqs], axis=-1)
    cos = np.cos(emb).astype(np.float32).T
    sin = np.sin(emb).astype(np.float32).T
    sinp = sin.copy()
    sinp[:32, :] *= -1.0
    return (np.ascontiguousarray(np.concatenate([cos, cos], axis=0)),
            np.ascontiguousarray(np.concatenate([sinp, sinp], axis=0)))


def mask_consts():
    tri = np.ascontiguousarray(np.triu(np.ones((128, 128), np.float32)).astype(ml_dtypes.bfloat16))
    ems = np.zeros((4, 128, 512), np.float32)
    for m in range(4):
        for q in range(512):
            j = max(q - 128 * m + 1, 0)
            if j <= 127:
                ems[m, j, q] = -32768.0
    return tri, np.ascontiguousarray(ems.astype(ml_dtypes.bfloat16))


def ind2_const():
    a = np.zeros((128, 2), np.float32)
    a[:64, 0] = 1.0
    a[64:, 1] = 1.0
    return np.ascontiguousarray(a.astype(ml_dtypes.bfloat16))


def qtile_chunks(qt):
    if qt < 3:
        return list(range(12)), {}
    chunks = list(range(4 * qt + 4))
    masked = {4 * qt + m: m for m in range(4)}
    return chunks, masked


TPERM = [5, 6, 7, 0, 1, 2, 3, 4]


def declare_io(nc, dbg=True):
    io = {"dbg_on": dbg}
    d = nc.dram_tensor
    io["x"] = d("x", [T, D], F32, kind="ExternalInput").ap()
    io["wqT"] = d("wqT", [D, D], F32, kind="ExternalInput").ap()
    io["wkvT"] = d("wkvT", [D, D], F32, kind="ExternalInput").ap()
    io["woT"] = d("woT", [D, D], F32, kind="ExternalInput").ap()
    io["bo"] = d("bo", [1, D], F32, kind="ExternalInput").ap()
    io["g128"] = d("g128", [128, 1], F32, kind="ExternalInput").ap()
    io["b128"] = d("b128", [128, 1], F32, kind="ExternalInput").ap()
    io["cos_t"] = d("cos_t", [128, T], F32, kind="ExternalInput").ap()
    io["sinp_t"] = d("sinp_t", [128, T], F32, kind="ExternalInput").ap()
    io["tri"] = d("tri", [128, 128], BF16, kind="ExternalInput").ap()
    io["ems"] = d("ems", [4, 128, 512], BF16, kind="ExternalInput").ap()
    io["ind2"] = d("ind2", [128, 2], BF16, kind="ExternalInput").ap()
    io["idb"] = d("idb", [128, 128], BF16, kind="ExternalInput").ap()
    io["idf"] = d("idf", [128, 128], F32, kind="ExternalInput").ap()
    io["w_up"] = d("w_up", [NI, D], F32, kind="ExternalInput").ap()
    io["w_gate"] = d("w_gate", [NI, D], F32, kind="ExternalInput").ap()
    io["w_down"] = d("w_down", [D, NI], F32, kind="ExternalInput").ap()
    io["wupT"] = d("wupT", [D, NI], F32, kind="ExternalInput").ap()
    io["wgateT"] = d("wgateT", [D, NI], F32, kind="ExternalInput").ap()
    io["wdownT"] = d("wdownT", [NI, D], F32, kind="ExternalInput").ap()
    io["phase"] = d("phase", [1, 1], I32, kind="ExternalInput").ap()
    io["peer_mx"] = d("peer_mx", [1, 64], F32, kind="ExternalInput").ap()
    io["out"] = d("out", [T, D], F32, kind="ExternalOutput").ap()
    io["mx_out"] = d("mx_out", [1, 8], F32, kind="ExternalOutput").ap()
    if dbg:
        io["dbg"] = d("dbg", [T, D], F32, kind="ExternalOutput").ap()
    io["y1_d"] = d("y1_d", [T, D], F32, kind="Internal").ap()
    io["up_d"] = d("up_d", [NI, T], F32, kind="Internal").ap()
    io["gate_d"] = d("gate_d", [NI, T], F32, kind="Internal").ap()
    io["o_d"] = d("o_d", [NI, T], F32, kind="Internal").ap()
    io["outn_d"] = d("outn_d", [D, T], BF16, kind="Internal").ap()
    io["wupT_dq"] = d("wupT_dq", [D, NI], BF16, kind="Internal").ap()
    io["wgateT_dq"] = d("wgateT_dq", [D, NI], BF16, kind="Internal").ap()
    io["wdownT_dq"] = d("wdownT_dq", [NI, D], BF16, kind="Internal").ap()
    io["lkvF_d"] = d("lkvF_d", [D, T], BF16, kind="Internal").ap()
    io["lkvT_d"] = d("lkvT_d", [T, D], BF16, kind="Internal").ap()
    io["stats_d1"] = d("stats_d1", [16, 512], F32, kind="Internal").ap()
    io["stats_d2"] = d("stats_d2", [16, 512], F32, kind="Internal").ap()
    io["mx_d"] = d("mx_d", [128, 1], F32, kind="Internal").ap()
    io["scal_d"] = d("scal_d", [1, 16], F32, kind="Internal").ap()
    return io




def build_phase0(nc, P, io, ctx, stages='ABCD', dbg_what=None, c_qts=None, c_pairs=None, c_chunks=True, d_level=3):
    E = ctx.enter_context
    sb = nc.sbuf_tensor
    w_bf = [E(sb(f"p0_w{c}", [128, D], BF16)) for c in range(NOC)]
    g128_sb = E(sb("p0_g", [128, 1], F32))
    b128_sb = E(sb("p0_b", [128, 1], F32))
    ind2_sb = E(sb("p0_ind2", [128, 2], BF16))
    tri_sb = E(sb("p0_tri", [128, 128], BF16))
    ems_sb = E(sb("p0_ems", [128, 4 * 512], BF16))
    idb_sb = E(sb("p0_idb", [128, 128], BF16))
    ones_bf = E(sb("p0_ones", [128, 1], BF16))
    ones_f1 = E(sb("p0_onesf", [1, 128], F32))
    bo_sb = E(sb("p0_bo", [1, D], F32))
    knacc = E(sb("p0_knacc", [2, 8], F32))
    knall = E(sb("p0_knall", [2, 64], F32))
    tmp21 = E(sb("p0_tmp21", [2, 1], F32))
    xtm_f = [E(sb(f"p0_xtf{i}", [128, D], F32)) for i in range(4)]
    xtm_b = [E(sb(f"p0_xtb{i}", [128, D], BF16)) for i in range(4)]
    x_fm = [E(sb(f"p0_xfm{c}", [128, 512], BF16)) for c in range(NOC)]
    kvraw = E(sb("p0_kvr", [128, 512], F32))
    kvrot = E(sb("p0_kvrot", [128, 512], F32))
    kvc = E(sb("p0_kvc", [128, 512], F32))
    kvs = E(sb("p0_kvs", [128, 512], F32))
    kvrope = [E(sb(f"p0_kvro{c}", [128, 512], F32)) for c in range(NOC)]
    cs_sl = E(sb("p0_cs", [128, 512], F32))
    sn_sl = E(sb("p0_sn", [128, 512], F32))
    sq_bf = E(sb("p0_sq", [128, 512], BF16))
    v_bf = E(sb("p0_vbf", [128, 512], BF16))
    t1f = E(sb("p0_t1", [128, 512], F32))
    t2f = E(sb("p0_t2", [128, 512], F32))
    lkv_t = E(sb("p0_lkvt", [128, 512], BF16))
    lkvt_ev = E(sb("p0_lkvtev", [128, 512], BF16))
    stat2 = E(sb("p0_stat2", [2, 1024], F32))
    sums_p = E(sb("p0_sump", [16, 512], F32))
    ssq_p = E(sb("p0_ssqp", [16, 512], F32))
    m_p = E(sb("p0_mp", [16, 512], F32))
    msq_p = E(sb("p0_msqp", [16, 512], F32))
    var_p = E(sb("p0_varp", [16, 512], F32))
    rstd_p = E(sb("p0_rstdp", [16, 512], F32))
    m_b = E(sb("p0_mb", [128, 512], F32))
    rstd_b = E(sb("p0_rstdb", [128, 512], F32))
    q_bf = [E(sb(f"p0_q{c}", [128, 512], BF16)) for c in range(NOC)]
    qnacc = E(sb("p0_qnacc", [2, 8], F32))
    cprod = E(sb("p0_cprod", [2, 8], F32))
    negc = E(sb("p0_negc", [2, 8], F32))
    negc16 = E(sb("p0_negc16", [1, 16], F32))
    bias_all = E(sb("p0_bias", [128, 16], F32))
    pA = E(sb("p0_pA", [128, 1024], BF16))
    pB = E(sb("p0_pB", [128, 1024], BF16))
    lkF = [E(sb(f"p0_lkF{i}", [128, 4096], BF16)) for i in range(2)]
    lkT = [E(sb(f"p0_lkT{i}", [128, 4096], BF16)) for i in range(2)]
    av_sb = [E(sb(f"p0_av{p}", [128, 512], F32)) for p in range(8)]
    sum_ev = E(sb("p0_sumev", [128, 512], F32))
    l_all = E(sb("p0_lall", [16, 512], F32))
    rl_all = E(sb("p0_rlall", [16, 512], F32))
    rl_b = E(sb("p0_rlb", [128, 512], F32))
    outn_t = E(sb("p0_outn", [128, 512], BF16))
    on_t = [E(sb(f"p0_ont{c}", [128, 128], BF16)) for c in range(NOC)]
    xt_d = E(sb("p0_xtd", [128, D], F32))
    z_sb = E(sb("p0_z", [128, D], F32))
    zsq = E(sb("p0_zsq", [128, D], F32))
    rsum = E(sb("p0_rsum", [128, 1], F32))
    mvec = E(sb("p0_mvec", [128, 1], F32))
    ssum = E(sb("p0_ssum", [128, 1], F32))
    varv = E(sb("p0_varv", [128, 1], F32))
    rstdv = E(sb("p0_rstdv", [128, 1], F32))
    y1t = E(sb("p0_y1t", [128, D], F32))
    mxcol = E(sb("p0_mxcol", [128, NTT], F32))
    mxfin = E(sb("p0_mxfin", [128, 1], F32))
    mxrow = E(sb("p0_mxrow", [1, 128], F32))
    mx8 = E(sb("p0_mx8", [1, 8], F32))
    epsA = E(sb("p0_epsA", [128, 1], F32))
    epsB = E(sb("p0_epsB", [128, 1], F32))
    ps_a = E(nc.psum_tensor("p0_psa", [128, 1024], F32))
    ps_b = E(nc.psum_tensor("p0_psb", [128, 1024], F32))
    ps_av = E(nc.psum_tensor("p0_psav", [128, 512], F32))
    ps_sum = E(nc.psum_tensor("p0_pssum", [128, 512], F32))
    ps_c0 = E(nc.psum_tensor("p0_psc0", [128, 512], F32))
    ps_c1 = E(nc.psum_tensor("p0_psc1", [128, 512], F32))

    dbgv = io.get("dbg")
    dbg_fm = dbgv.rearrange("(r s) c -> r (s c)", s=4) if dbgv is not None else None

    P.step("v", lambda v: v.memset(ones_bf[:], 1.0))
    P.step("v", lambda v: v.memset(epsA[:], EPS))
    P.step("v", lambda v: v.memset(epsB[:], 1e-30))
    P.step("v", lambda v: v.memset(ones_f1[:], 1.0))
    P.step("v", lambda v: v.memset(knacc[:], 0.0))
    P.step("v", lambda v: v.memset(qnacc[:], 0.0))
    for (dst, src) in ((g128_sb, io["g128"]), (b128_sb, io["b128"]),
                       (ind2_sb, io["ind2"]), (tri_sb, io["tri"]),
                       (idb_sb, io["idb"]), (bo_sb, io["bo"])):
        P.dma("s", dst[:], src)
    P.dma("s", ems_sb[:].rearrange("p (m q) -> p m q", m=4), io["ems"].rearrange("m p q -> p m q"))

    def load_weights(name):
        for c in range(NOC):
            P.waitall("s")
            P.dma("s", xtm_f[0][:], io[name][128 * c:128 * (c + 1), :])
            P.wait("v", "sd")
            P.step("v", lambda v, c=c: v.tensor_copy(out=w_bf[c][:], in_=xtm_f[0][:]))

    def load_x_group(tq):
        P.waitall("s")
        for i in range(4):
            P.dma("s", xtm_f[i][:], io["x"][512 * tq + 128 * i:512 * tq + 128 * (i + 1), :])
        P.wait("v", "sd")
        P.waitall("v")
        for i in range(4):
            P.step("v", lambda v, i=i: v.tensor_copy(out=xtm_b[i][:], in_=xtm_f[i][:]), selfwait=False)
        for hc in range(NOC):
            bank = hc % 2
            pview = ps_a[:, 512 * bank:512 * (bank + 1)].bitcast(BF16)
            P.wait("t", "vs")
            for i in range(4):
                P.step("t", lambda t, hc=hc, i=i, pv=pview:
                       t.transpose(out=pv[:, 128 * i:128 * (i + 1)],
                                   in_=xtm_b[i][:, 128 * hc:128 * (hc + 1)],
                                   identity=idb_sb[:]))
            P.wait("v", "ts")
            P.step("v", lambda v, hc=hc, pv=pview: v.tensor_copy(out=x_fm[hc][:], in_=pv[:, 0:512]),
                   selfwait=False)
        P.wait("t", "vs")

    def proj_rope(oc, outbuf):
        ps = ps_c0 if oc % 2 == 0 else ps_c1
        P.wait("t", "vs")
        for hc in range(NOC):
            P.step("t", lambda t, hc=hc, oc=oc, ps=ps:
                   t.matmul(out=ps[:], lhsT=w_bf[hc][:, 128 * oc:128 * (oc + 1)],
                            rhs=x_fm[hc][:], start=(hc == 0), stop=(hc == NOC - 1)))
        P.wait("v", "ts")
        P.wait("v", "ad")
        P.step("v", lambda v, ps=ps: v.tensor_copy(out=kvraw[:], in_=ps[:]), selfwait=False)
        P.wait("a", "vs")
        P.wait("a", "gs")
        for (d0, s0) in ((0, 32), (32, 0), (64, 96), (96, 64)):
            P.dma("a", kvrot[d0:d0 + 32, :], kvraw[s0:s0 + 32, :])
        P.wait("g", "ad")
        P.wait("g", "vs")
        P.step("g", lambda g: g.tensor_mul(out=kvs[:], in0=kvrot[:], in1=sn_sl[:]), selfwait=False)
        P.step("v", lambda v: v.tensor_mul(out=kvc[:], in0=kvraw[:], in1=cs_sl[:]), selfwait=False)
        P.wait("v", "gs")
        P.wait("v", "as_")
        P.step("v", lambda v, outbuf=outbuf: v.tensor_add(out=outbuf[:], in0=kvc[:], in1=kvs[:]),
               selfwait=False)

    if "B" in stages:
        load_weights("wkvT")
    P.barrier()

    for tq in (range(NQT) if "B" in stages else []):
        load_x_group(tq)
        P.waitall("s")
        P.dma("s", cs_sl[:], io["cos_t"][:, 512 * tq:512 * (tq + 1)])
        P.dma("s", sn_sl[:], io["sinp_t"][:, 512 * tq:512 * (tq + 1)])
        P.wait("v", "sd")
        P.wait("g", "sd")
        for oc in range(NOC):
            proj_rope(oc, kvrope[oc])
            if dbg_what == "kvrope":
                P.wait("g", "vs")
                P.dma("g", dbg_fm[128 * oc:128 * (oc + 1), 512 * tq:512 * (tq + 1)], kvrope[oc][:])
            P.wait("a", "vs")
            P.wait("a", "ts")
            P.inc("a", "as_", 1, lambda a, oc=oc: a.activation(out=sq_bf[:], in_=kvrope[oc][:], func=AF.Square))
            P.step("v", lambda v, oc=oc: v.tensor_copy(out=v_bf[:], in_=kvrope[oc][:]), selfwait=False)
            P.wait("t", "as_")
            P.wait("t", "vs")
            P.step("t", lambda t: t.matmul(out=ps_sum[0:2, :], lhsT=ind2_sb[:], rhs=v_bf[:], start=True, stop=True))
            P.step("t", lambda t: t.matmul(out=ps_av[0:2, :], lhsT=ind2_sb[:], rhs=sq_bf[:], start=True, stop=True))
            P.wait("v", "ts")
            P.wait("v", "gd")
            P.step("v", lambda v: v.tensor_copy(out=stat2[:, 0:512], in_=ps_sum[0:2, :]), selfwait=False)
            P.step("v", lambda v: v.tensor_copy(out=stat2[:, 512:1024], in_=ps_av[0:2, :]), selfwait=False)
            P.wait("g", "vs")
            P.dma("g", io["stats_d1"][2 * oc:2 * oc + 2, :], stat2[:, 0:512])
            P.dma("g", io["stats_d2"][2 * oc:2 * oc + 2, :], stat2[:, 512:1024])
            P.wait("t", "vs")
        P.waitall("s")
        P.dma("s", sums_p[:], io["stats_d1"])
        P.dma("s", ssq_p[:], io["stats_d2"])
        P.wait("v", "sd")
        P.step("v", lambda v: v.tensor_scalar(out=m_p[:], in0=sums_p[:], scalar1=1.0 / DH, scalar2=None,
                                              op0=ALU.mult), selfwait=False)
        P.step("v", lambda v: v.tensor_mul(out=msq_p[:], in0=m_p[:], in1=m_p[:]), selfwait=False)
        P.step("v", lambda v: v.tensor_scalar(out=var_p[:], in0=ssq_p[:], scalar1=1.0 / DH, scalar2=None,
                                              op0=ALU.mult), selfwait=False)
        P.step("v", lambda v: v.tensor_sub(out=var_p[:], in0=var_p[:], in1=msq_p[:]), selfwait=False)
        P.wait("a", "vs")
        P.inc("a", "as_", 1, lambda a: a.activation(out=rstd_p[:], in_=var_p[:], func=AF.Ln, bias=epsA[0:16, 0:1]))
        P.inc("a", "as_", 1, lambda a: a.activation(out=rstd_p[:], in_=rstd_p[:], func=AF.Exp, scale=-0.5))
        for oc in range(NOC):
            P.waitall("s")
            P.wait("s", "as_")
            P.dma("s", m_b[:], m_p[2 * oc:2 * oc + 2, :].unsqueeze(1).broadcast_to([2, 64, 512]))
            P.dma("s", rstd_b[:], rstd_p[2 * oc:2 * oc + 2, :].unsqueeze(1).broadcast_to([2, 64, 512]))
            P.wait("v", "sd")
            P.wait("v", "sd")
            P.step("v", lambda v, oc=oc: v.tensor_sub(out=t1f[:], in0=kvrope[oc][:], in1=m_b[:]),
                   selfwait=False)
            P.step("v", lambda v: v.scalar_tensor_tensor(
                out=t2f[:], in0=t1f[:], scalar=g128_sb[:, 0:1], in1=rstd_b[:], op0=ALU.mult, op1=ALU.mult),
                   selfwait=False)
            P.wait("v", "gd")
            P.wait("v", "as_")
            P.step("v", lambda v: v.tensor_scalar(
                out=lkv_t[:], in0=t2f[:], scalar1=b128_sb[:, 0:1], scalar2=None, op0=ALU.add),
                   selfwait=False)
            P.wait("g", "vs")
            P.dma("g", io["lkvF_d"][128 * oc:128 * (oc + 1), 512 * tq:512 * (tq + 1)], lkv_t[:])
            if dbg_what == "lkv":
                P.dma("g", dbg_fm[128 * oc:128 * (oc + 1), 512 * tq:512 * (tq + 1)], lkv_t[:])
            P.wait("a", "vs")
            P.wait("a", "ts")
            P.inc("a", "as_", 1, lambda a: a.activation(out=sq_bf[:], in_=lkv_t[:], func=AF.Square))
            P.wait("t", "as_")
            P.step("t", lambda t: t.matmul(out=ps_sum[0:2, :], lhsT=ind2_sb[:], rhs=sq_bf[:], start=True, stop=True))
            P.wait("v", "ts")
            P.step("v", lambda v, oc=oc, tq=tq: v.tensor_reduce(
                out=knall[:, 8 * tq + oc:8 * tq + oc + 1], in_=ps_sum[0:2, :], axis=AX.X, op=ALU.max))
            bank = oc % 2
            pview = ps_a[:, 512 * bank:512 * (bank + 1)].bitcast(BF16)
            P.wait("t", "vs")
            P.wait("t", "gd")
            for i in range(4):
                P.step("t", lambda t, i=i, pv=pview:
                       t.transpose(out=pv[:, 128 * i:128 * (i + 1)],
                                   in_=lkv_t[:, 128 * i:128 * (i + 1)],
                                   identity=idb_sb[:]))
            P.wait("v", "ts")
            P.wait("v", "gd")
            P.step("v", lambda v, pv=pview: v.tensor_copy(out=lkvt_ev[:], in_=pv[:, 0:512]), selfwait=False)
            P.wait("g", "vs")
            for i in range(4):
                P.dma("g", io["lkvT_d"][512 * tq + 128 * i:512 * tq + 128 * (i + 1), 128 * oc:128 * (oc + 1)],
                      lkvt_ev[:, 128 * i:128 * (i + 1)])
    P.barrier()
    if "C" in stages:
        load_weights("wqT")
    P.barrier()

    if "C" in stages:
        P.step("v", lambda v: v.tensor_reduce(
            out=knacc[:].rearrange("r (one c) -> r one c", one=1),
            in_=knall[:].rearrange("r (t c) -> r c t", t=8), axis=AX.X, op=ALU.max))

    _qts = (c_qts if c_qts is not None else range(NQT)) if "C" in stages else []
    exp_tok = {0: 0, 1: 0}
    for qt in _qts:
        chunks, masked = qtile_chunks(qt)
        nch = len(chunks)
        G = nch // 2
        load_x_group(qt)
        P.waitall("s")
        P.dma("s", cs_sl[:], io["cos_t"][:, 512 * qt:512 * (qt + 1)])
        P.dma("s", sn_sl[:], io["sinp_t"][:, 512 * qt:512 * (qt + 1)])
        P.wait("v", "sd")
        P.wait("g", "sd")
        for oc in range(NOC):
            proj_rope(oc, t2f)
            P.step("v", lambda v, oc=oc: v.tensor_copy(out=q_bf[oc][:], in_=t2f[:]), selfwait=False)
            if dbg_what == "q":
                P.wait("g", "vs")
                P.dma("g", dbg_fm[128 * oc:128 * (oc + 1), 512 * qt:512 * (qt + 1)], t2f[:])
            P.wait("a", "vs")
            P.wait("a", "ts")
            P.inc("a", "as_", 1, lambda a, oc=oc: a.activation(out=sq_bf[:], in_=q_bf[oc][:], func=AF.Square))
            P.wait("t", "as_")
            P.step("t", lambda t: t.matmul(out=ps_sum[0:2, :], lhsT=ind2_sb[:], rhs=sq_bf[:], start=True, stop=True))
            P.wait("v", "ts")
            P.step("v", lambda v, oc=oc: v.tensor_reduce(out=qnacc[:, oc:oc + 1], in_=ps_sum[0:2, :],
                                                         axis=AX.X, op=ALU.max))
            P.wait("t", "vs")
        P.wait("a", "vs")
        P.inc("a", "as_", 1, lambda a: a.activation(out=cprod[:], in_=qnacc[:], func=AF.Ln, bias=epsB[0:2, 0:1]))
        P.inc("a", "as_", 1, lambda a: a.activation(out=negc[:], in_=knacc[:], func=AF.Ln, bias=epsB[0:2, 0:1]))
        P.wait("v", "as_")
        P.step("v", lambda v: v.tensor_add(out=cprod[:], in0=cprod[:], in1=negc[:]))
        P.wait("a", "vs")
        P.inc("a", "as_", 1, lambda a: a.activation(out=negc[:], in_=cprod[:], func=AF.Exp, scale=0.5))
        P.wait("v", "as_")
        P.step("v", lambda v: v.tensor_scalar(out=negc[:], in0=negc[:], scalar1=-0.125, scalar2=None, op0=ALU.mult))
        P.waitall("s")
        P.dma("s", negc16[:].rearrange("p (c r) -> p c r", r=2)[:, :, 0:1], negc[0:1, :].unsqueeze(2),
              allow_slow_non_contiguous=True)
        P.dma("s", negc16[:].rearrange("p (c r) -> p c r", r=2)[:, :, 1:2], negc[1:2, :].unsqueeze(2),
              allow_slow_non_contiguous=True)
        P.wait("s", "sd")
        P.dma("s", bias_all[:], negc16[:].unsqueeze(1).broadcast_to([1, 128, 16]))
        P.wait("a", "sd")

        def load_pair_slices(pair):
            buf = pair % 2
            P.newbatch("s")
            toks = [P.dma("s", lkF[buf][:, 0:128 * nch], io["lkvF_d"][128 * pair:128 * (pair + 1), 0:128 * nch])]
            for c4 in range(0, nch, 4):
                hi = min(c4 + 4, nch)
                toks.append(P.dma("s", lkT[buf][:, 128 * c4:128 * hi].rearrange("p (j c) -> p j c", c=128),
                      io["lkvT_d"][128 * c4:128 * hi, 128 * pair:128 * (pair + 1)]
                      .rearrange("(j p) c -> p j c", p=128)))
            return toks[-1]

        def scores_chunk(pair, cx, c_i):
            buf = pair % 2
            pss = ps_a if c_i % 2 == 0 else ps_b
            stop0 = cx not in masked
            for (base, half) in ((0, 0), (64, 1)):
                P.step("t", lambda t, cx=cx, base=base, half=half, pss=pss, pair=pair, stop0=stop0, buf=buf:
                       t.matmul(out=pss[:, 512 * half:512 * (half + 1)],
                                lhsT=lkF[buf][base:base + 64, 128 * cx:128 * (cx + 1)],
                                rhs=q_bf[pair][base:base + 64, :],
                                start=True, stop=stop0,
                                tile_position=(base, 0)))
                if not stop0:
                    m = masked[cx]
                    P.step("t", lambda t, m=m, half=half, pss=pss:
                           t.matmul(out=pss[:, 512 * half:512 * (half + 1)],
                                    lhsT=tri_sb[:],
                                    rhs=ems_sb[:, 512 * m:512 * (m + 1)],
                                    start=False, stop=True))

        def exp_chunk(pair, c_i):
            sl = c_i % 2
            pss = ps_a if sl == 0 else ps_b
            h0 = 2 * pair
            P.inc("a", "as_", 1, lambda a, h0=h0, sl=sl, pss=pss: a.activation(
                out=pA[:, 512 * sl:512 * (sl + 1)], in_=pss[:, 0:512],
                func=AF.Exp, scale=0.125, bias=bias_all[:, h0:h0 + 1]), selfwait=False)
            P.inc("a", "as_", 1, lambda a, h0=h0, sl=sl, pss=pss: a.activation(
                out=pB[:, 512 * sl:512 * (sl + 1)], in_=pss[:, 512:1024],
                func=AF.Exp, scale=0.125, bias=bias_all[:, h0 + 1:h0 + 2]), selfwait=False)

        def av_chunk(pair, cx, c_i, st, sp):
            buf = pair % 2
            sl = c_i % 2
            P.step("t", lambda t, cx=cx, sl=sl, st=st, sp=sp, buf=buf:
                   t.matmul(out=ps_av[0:64, :], lhsT=lkT[buf][:, 128 * cx:128 * cx + 64],
                            rhs=pA[:, 512 * sl:512 * (sl + 1)],
                            start=st, stop=sp, tile_position=(0, 0), skip_group_check=True))
            P.step("t", lambda t, cx=cx, sl=sl, st=st, sp=sp, buf=buf:
                   t.matmul(out=ps_av[64:128, :], lhsT=lkT[buf][:, 128 * cx + 64:128 * (cx + 1)],
                            rhs=pB[:, 512 * sl:512 * (sl + 1)],
                            start=st, stop=sp, tile_position=(0, 64), skip_group_check=True))
            P.step("t", lambda t, sl=sl, st=st, sp=sp:
                   t.matmul(out=ps_sum[0:1, :], lhsT=ones_bf[:],
                            rhs=pA[:, 512 * sl:512 * (sl + 1)],
                            start=st, stop=sp, tile_position=(0, 0), skip_group_check=True))
            P.step("t", lambda t, sl=sl, st=st, sp=sp:
                   t.matmul(out=ps_sum[64:65, :], lhsT=ones_bf[:],
                            rhs=pB[:, 512 * sl:512 * (sl + 1)],
                            start=st, stop=sp, tile_position=(0, 64), skip_group_check=True))

        if not c_chunks:
            continue
        _pairs = list(c_pairs if c_pairs is not None else range(8))
        ts_end_pair = []
        load_toks = [load_pair_slices(_pairs[0])]
        for pi, pair in enumerate(_pairs):
            if pi + 1 < len(_pairs):
                if pi >= 1:
                    P.wait("s", "ts", ts_end_pair[pi - 1])
                load_toks.append(load_pair_slices(_pairs[pi + 1]))
            P.wait_tok("t", load_toks[pi])
            P.wait("t", "vs")
            P.wait("t", "as_")
            score_tok = {}
            scores_chunk(pair, chunks[0], 0)
            score_tok[0] = P.cnt["ts"]
            for c_i in range(nch):
                P.wait("a", "ts", score_tok[c_i])
                exp_chunk(pair, c_i)
                exp_tok[c_i % 2] = P.cnt["as_"]
                if c_i + 1 < nch:
                    if exp_tok[(c_i + 1) % 2] > 0:
                        P.wait("t", "as_", exp_tok[(c_i + 1) % 2])
                    scores_chunk(pair, chunks[c_i + 1], c_i + 1)
                    score_tok[c_i + 1] = P.cnt["ts"]
                P.wait("t", "as_", exp_tok[c_i % 2])
                av_chunk(pair, chunks[c_i], c_i, st=(c_i == 0), sp=(c_i == nch - 1))
            P.wait("v", "ts")
            P.step("v", lambda v: v.tensor_copy(out=sum_ev[0:1, :], in_=ps_sum[0:1, :]), selfwait=False)
            P.step("v", lambda v: v.tensor_copy(out=sum_ev[64:65, :], in_=ps_sum[64:65, :]), selfwait=False)
            P.step("v", lambda v, pair=pair: v.tensor_copy(out=av_sb[pair][:], in_=ps_av[:]), selfwait=False)
            P.wait("g", "vs")
            P.dma("g", l_all[2 * pair:2 * pair + 1, :], sum_ev[0:1, :])
            P.dma("g", l_all[2 * pair + 1:2 * pair + 2, :], sum_ev[64:65, :])
            P.wait("t", "vs")
            ts_end_pair.append(P.cnt["ts"])
        P.wait("v", "gd")
        P.step("v", lambda v: v.reciprocal(out=rl_all[:], in_=l_all[:]))
        for pair in _pairs:
            P.waitall("s")
            P.dma("s", rl_b[:], rl_all[2 * pair:2 * pair + 2, :].unsqueeze(1).broadcast_to([2, 64, 512]))
            P.wait("v", "sd")
            P.wait("v", "gd")
            P.step("v", lambda v, pair=pair: v.tensor_mul(out=outn_t[:], in0=av_sb[pair][:], in1=rl_b[:]),
                   selfwait=False)
            P.wait("g", "vs")
            P.dma("g", io["outn_d"][128 * pair:128 * (pair + 1), 512 * TPERM[qt]:512 * (TPERM[qt] + 1)], outn_t[:])
            if dbg_what == "outn":
                P.dma("g", dbg_fm[128 * pair:128 * (pair + 1), 512 * TPERM[qt]:512 * (TPERM[qt] + 1)], outn_t[:])
    P.barrier()

    if dbg_what == "outnd" and "D" in stages:
        for dc in range(NOC):
            for tq2 in range(8):
                P.waitall("s")
                P.dma("s", outn_t[:], io["outn_d"][128 * dc:128 * (dc + 1), 512 * tq2:512 * (tq2 + 1)])
                P.wait("v", "sd")
                P.step("v", lambda v: v.tensor_copy(out=sum_ev[:], in_=outn_t[:]))
                P.wait("g", "vs")
                P.dma("g", dbg_fm[128 * dc:128 * (dc + 1), 512 * tq2:512 * (tq2 + 1)], sum_ev[:])
    if "D" in stages:
        load_weights("woT")
    P.barrier()
    d_mm_tok = 0
    d_z_tok = 0
    for tt in (range(NTT) if "D" in stages else []):
        if tt == 0:
            P.waitall("s")
        else:
            P.wait("s", "ts", d_mm_tok)
            P.wait("s", "vs", d_z_tok)
        P.newbatch("s")
        for dc in range(NOC):
            P.dma("s", on_t[dc][:], io["outn_d"][128 * dc:128 * (dc + 1), 128 * tt:128 * (tt + 1)])
        d_ld_tok = P.dma("s", xt_d[:], io["x"][128 * tt:128 * (tt + 1), :])
        if dbg_what == "onread":
            P.wait("v", "sd")
            P.wait("v", "gd")
            for dc in range(NOC):
                P.step("v", lambda v, dc=dc: v.tensor_copy(out=y1t[:, 128 * dc:128 * (dc + 1)], in_=on_t[dc][:]))
            P.wait("g", "vs")
            P.dma("g", io["dbg"][128 * tt:128 * (tt + 1), :], y1t[:])
        P.wait_tok("t", d_ld_tok)
        if d_z_tok > 0:
            P.wait("t", "vs", d_z_tok)
        for half in range(2):
            psh = ps_a[:, 512 * half:512 * (half + 1)]
            for dc in range(NOC):
                P.step("t", lambda t, dc=dc, half=half, psh=psh:
                       t.matmul(out=psh[:], lhsT=on_t[dc][:], rhs=w_bf[dc][:, 512 * half:512 * (half + 1)],
                                start=(dc == 0), stop=False))
            P.step("t", lambda t, half=half, psh=psh:
                   t.matmul(out=psh[:], lhsT=ones_f1[:], rhs=bo_sb[0:1, 512 * half:512 * (half + 1)],
                            start=False, stop=True))
        d_mm_tok = P.cnt["ts"]
        P.wait("v", "ts", d_mm_tok)
        P.wait("v", "gd")
        P.wait("v", "as_")
        for half in range(2):
            P.step("v", lambda v, half=half: v.tensor_add(
                out=z_sb[:, 512 * half:512 * (half + 1)], in0=ps_a[:, 512 * half:512 * (half + 1)],
                in1=xt_d[:, 512 * half:512 * (half + 1)]), selfwait=False)
        d_z_tok = P.cnt["vs"]
        if dbg_what == "z":
            P.wait("g", "vs")
            P.dma("g", io["dbg"][128 * tt:128 * (tt + 1), :], z_sb[:])
        if d_level < 2:
            continue
        P.step("v", lambda v: v.tensor_reduce(out=rsum[:], in_=z_sb[:], axis=AX.X, op=ALU.add))
        P.wait("a", "vs")
        P.inc("a", "as_", 1, lambda a: a.activation(out=zsq[:], in_=z_sb[:], func=AF.Square), selfwait=False)
        P.wait("v", "as_")
        P.step("v", lambda v: v.tensor_reduce(out=ssum[:], in_=zsq[:], axis=AX.X, op=ALU.add))
        P.wait("g", "vs")
        P.wait("g", "as_")
        P.step("g", lambda g: g.tensor_scalar(out=mvec[:], in0=rsum[:], scalar1=1.0 / D, scalar2=None, op0=ALU.mult))
        P.step("g", lambda g: g.tensor_mul(out=rstdv[:], in0=mvec[:], in1=mvec[:]))
        P.step("g", lambda g: g.tensor_scalar(out=varv[:], in0=ssum[:], scalar1=1.0 / D, scalar2=None, op0=ALU.mult))
        P.step("g", lambda g: g.tensor_sub(out=varv[:], in0=varv[:], in1=rstdv[:]))
        P.wait("a", "gs")
        P.inc("a", "as_", 1, lambda a: a.activation(out=rstdv[:], in_=varv[:], func=AF.Ln, bias=epsA[:, 0:1]))
        P.inc("a", "as_", 1, lambda a: a.activation(out=rstdv[:], in_=rstdv[:], func=AF.Exp, scale=-0.5))
        if dbg_what == "ln1":
            P.wait("g", "vs")
            P.wait("g", "as_")
            P.dma("g", io["dbg"][128 * tt:128 * (tt + 1), 0:1], mvec[:], allow_slow_non_contiguous=True)
            P.dma("g", io["dbg"][128 * tt:128 * (tt + 1), 1:2], varv[:], allow_slow_non_contiguous=True)
            P.dma("g", io["dbg"][128 * tt:128 * (tt + 1), 2:3], rstdv[:], allow_slow_non_contiguous=True)
            P.dma("g", io["dbg"][128 * tt:128 * (tt + 1), 3:4], rsum[:], allow_slow_non_contiguous=True)
            P.dma("g", io["dbg"][128 * tt:128 * (tt + 1), 4:5], ssum[:], allow_slow_non_contiguous=True)
        P.wait("v", "as_")
        P.wait("v", "gs")
        P.step("v", lambda v: v.tensor_scalar(out=y1t[:], in0=z_sb[:], scalar1=mvec[:, 0:1],
                                              scalar2=rstdv[:, 0:1], op0=ALU.subtract, op1=ALU.mult),
               selfwait=False)
        P.step("v", lambda v, tt=tt: v.tensor_reduce(out=mxcol[:, tt:tt + 1], in_=y1t[:], axis=AX.X,
                                                     op=ALU.max, apply_absolute_value=True))
        if d_level < 3:
            continue
        P.wait("g", "vs")
        P.dma("g", io["y1_d"][128 * tt:128 * (tt + 1), :], y1t[:])
        if io["dbg_on"] and dbg_what in (None, "y1"):
            P.dma("g", io["dbg"][128 * tt:128 * (tt + 1), :], y1t[:])
    P.step("v", lambda v: v.tensor_reduce(out=mxfin[:], in_=mxcol[:], axis=AX.X, op=ALU.max))
    P.wait("a", "vs")
    P.dma("a", io["mx_d"], mxfin[:])
    P.wait("a", "ad")
    P.dma("a", mxrow[:], io["mx_d"].rearrange("p one -> one p"))
    P.wait("v", "ad")
    P.step("v", lambda v: v.memset(mx8[:], 0.0))
    P.step("v", lambda v: v.tensor_reduce(out=mx8[:, 0:1], in_=mxrow[:], axis=AX.X, op=ALU.max))
    P.wait("g", "vs")
    P.dma("g", io["mx_out"], mx8[:])
    P.wait("g", "gd")


def peer_reduce(nc, P, io, E, sb, name):
    pm = E(sb(f"{name}_pm", [1, 64], F32))
    gmax = E(sb(f"{name}_gmax", [1, 8], F32))
    P.dma("s", pm[:], io["peer_mx"])
    P.wait("v", "sd")
    P.step("v", lambda v: v.tensor_reduce(
        out=gmax[:].rearrange("p (s one) -> p s one", one=1),
        in_=pm[:].rearrange("p (s r) -> p s r", r=8), axis=AX.X, op=ALU.max))
    return gmax


def build_phase1(nc, P, io, ctx, parts='sdxmer'):
    E = ctx.enter_context
    sb = nc.sbuf_tensor
    gmax = peer_reduce(nc, P, io, E, sb, "p1")
    s1v = E(sb("p1_s1", [1, 1], F32))
    rs1v = E(sb("p1_rs1", [1, 1], F32))
    s1_pp = E(sb("p1_s1pp", [128, 1], F32))
    rs1_pp = E(sb("p1_rs1pp", [128, 1], F32))
    P.step("v", lambda v: v.tensor_scalar(out=s1v[:], in0=gmax[:, 0:1], scalar1=1.0 / QMAX,
                                          scalar2=1e-12, op0=ALU.mult, op1=ALU.max))
    P.wait("a", "vs")
    P.inc("a", "as_", 1, lambda a: a.activation(out=rs1v[:], in_=s1v[:], func=AF.Copy))
    P.wait("v", "as_")
    P.step("v", lambda v: v.reciprocal(out=rs1v[:], in_=rs1v[:]))
    P.waitall("s")
    P.dma("s", s1_pp[:], s1v[:].unsqueeze(1).broadcast_to([1, 128, 1]))
    P.dma("s", rs1_pp[:], rs1v[:].unsqueeze(1).broadcast_to([1, 128, 1]))

    nat = E(sb("p1_nat", [128, NI], F32))
    bm = E(sb("p1_bm", [128, 96], F32))
    s_c = E(sb("p1_sc", [128, 96], F32))
    rs_c = E(sb("p1_rsc", [128, 96], F32))
    sT_up = E(sb("p1_sTup", [32, NI], F32))
    rT_up = E(sb("p1_rTup", [32, NI], F32))
    sT_gt = E(sb("p1_sTgt", [32, NI], F32))
    rT_gt = E(sb("p1_rTgt", [32, NI], F32))
    sT_dn = E(sb("p1_sTdn", [96, D], F32))
    rT_dn = E(sb("p1_rTdn", [96, D], F32))
    wt_f = [E(sb(f"p1_wtf{i}", [128, 512], F32)) for i in range(2)]
    sb_b = E(sb("p1_sbb", [128, 512], F32))
    rb_b = E(sb("p1_rbb", [128, 512], F32))
    ti32 = E(sb("p1_ti32", [128, 512], I32))
    tf32 = E(sb("p1_tf32", [128, 512], F32))
    wq_b = E(sb("p1_wqb", [128, 512], BF16))
    ps_t = E(nc.psum_tensor("p1_pst", [128, 512], F32))
    ps_c0 = E(nc.psum_tensor("p1_psc0", [128, 512], F32))
    ps_c1 = E(nc.psum_tensor("p1_psc1", [128, 512], F32))

    def dequant(natname, tname, dqname, K, O, sT, rT):
        nblk = K // 32
        for c in range(O // 128):
            P.waitall("s")
            P.dma("s", nat[:, 0:K], io[natname][128 * c:128 * (c + 1), :])
            P.wait("v", "sd")
            P.step("v", lambda v, K=K, nblk=nblk: v.tensor_reduce(
                out=bm[:, 0:nblk].rearrange("p (b one) -> p b one", one=1),
                in_=nat[:, 0:K].rearrange("p (b w) -> p b w", w=32),
                axis=AX.X, op=ALU.max, apply_absolute_value=True))
            P.wait("g", "vs")
            P.step("g", lambda g, nblk=nblk: g.tensor_scalar(
                out=s_c[:, 0:nblk], in0=bm[:, 0:nblk], scalar1=1.0 / 7.0, scalar2=1e-12,
                op0=ALU.mult, op1=ALU.max))
            P.wait("v", "gs")
            P.step("v", lambda v, nblk=nblk: v.reciprocal(out=rs_c[:, 0:nblk], in_=s_c[:, 0:nblk]))
            P.wait("t", "vs")
            P.wait("t", "gs")
            P.step("t", lambda t, nblk=nblk: t.transpose(out=ps_t[0:nblk, 0:128], in_=s_c[:, 0:nblk],
                                                         identity=idf_sb[:]))
            P.wait("v", "ts")
            P.step("v", lambda v, c=c, nblk=nblk, sT=sT: v.tensor_copy(
                out=sT[0:nblk, 128 * c:128 * (c + 1)], in_=ps_t[0:nblk, 0:128]))
            P.wait("t", "vs")
            P.step("t", lambda t, nblk=nblk: t.transpose(out=ps_t[0:nblk, 0:128], in_=rs_c[:, 0:nblk],
                                                         identity=idf_sb[:]))
            P.wait("v", "ts")
            P.step("v", lambda v, c=c, nblk=nblk, rT=rT: v.tensor_copy(
                out=rT[0:nblk, 128 * c:128 * (c + 1)], in_=ps_t[0:nblk, 0:128]))
            P.wait("t", "vs")
        for hc in range(K // 128):
            for os_ in range(O // 512):
                buf = (hc * (O // 512) + os_) % 2
                P.waitall("s")
                P.dma("s", wt_f[buf][:], io[tname][128 * hc:128 * (hc + 1), 512 * os_:512 * (os_ + 1)])
                P.dma("s", sb_b[:], sT[4 * hc:4 * hc + 4, 512 * os_:512 * (os_ + 1)]
                      .unsqueeze(1).broadcast_to([4, 32, 512]))
                P.dma("s", rb_b[:], rT[4 * hc:4 * hc + 4, 512 * os_:512 * (os_ + 1)]
                      .unsqueeze(1).broadcast_to([4, 32, 512]))
                P.wait("v", "sd")
                P.wait("v", "gd")
                P.step("v", lambda v, buf=buf: v.tensor_mul(out=tf32[:], in0=wt_f[buf][:], in1=rb_b[:]),
                       selfwait=False)
                P.step("v", lambda v: v.tensor_copy(out=ti32[:], in_=tf32[:]), selfwait=False)
                P.wait("v", "gd")
                P.step("v", lambda v: v.tensor_mul(out=wq_b[:], in0=ti32[:], in1=sb_b[:]), selfwait=False)
                P.wait("g", "vs")
                P.dma("g", io[dqname][128 * hc:128 * (hc + 1), 512 * os_:512 * (os_ + 1)], wq_b[:])

    idf_sb = E(sb("p1_idf", [128, 128], F32))
    P.dma("s", idf_sb[:], io["idf"])
    if "d" in parts:
        dequant("w_up", "wupT", "wupT_dq", D, NI, sT_up, rT_up)
        dequant("w_gate", "wgateT", "wgateT_dq", D, NI, sT_gt, rT_gt)
        dequant("w_down", "wdownT", "wdownT_dq", NI, D, sT_dn, rT_dn)
    P.barrier()

    y1t = E(sb("p1_y1t", [128, D], F32))
    xqi = E(sb("p1_xqi", [128, D], I32))
    xqf = E(sb("p1_xqf", [128, D], F32))
    xqb = E(sb("p1_xqb", [128, D], BF16))
    xq_fm = [E(sb(f"p1_xqfm{c}", [128, T], BF16)) for c in range(NOC)]
    idb_sb = E(sb("p1_idb", [128, 128], BF16))
    P.dma("s", idb_sb[:], io["idb"])
    for tt in (range(NTT) if "x" in parts else []):
        P.waitall("s")
        P.dma("s", y1t[:], io["y1_d"][128 * tt:128 * (tt + 1), :])
        P.wait("v", "sd")
        P.wait("v", "gs")
        P.step("v", lambda v: v.tensor_scalar(out=xqf[:], in0=y1t[:], scalar1=rs1_pp[:, 0:1],
                                              scalar2=None, op0=ALU.mult), selfwait=False)
        P.step("v", lambda v: v.tensor_copy(out=xqi[:], in_=xqf[:]), selfwait=False)
        P.wait("v", "ts")
        P.step("v", lambda v: v.tensor_scalar(out=xqb[:], in0=xqi[:], scalar1=s1_pp[:, 0:1],
                                              scalar2=None, op0=ALU.mult), selfwait=False)
        P.wait("t", "vs")
        for hc in range(NOC):
            bank = hc % 2
            pview = (ps_c0 if bank == 0 else ps_c1)[:, 0:128].bitcast(BF16)
            P.wait("t", "vs")
            P.step("t", lambda t, hc=hc, pv=pview:
                   t.transpose(out=pv[:, 0:128], in_=xqb[:, 128 * hc:128 * (hc + 1)], identity=idb_sb[:]))
            P.wait("v", "ts")
            P.step("v", lambda v, hc=hc, tt=tt, pv=pview:
                   v.tensor_copy(out=xq_fm[hc][:, 128 * tt:128 * (tt + 1)], in_=pv[:, 0:128]),
                   selfwait=False)
    P.barrier()

    wq_l = [[E(sb(f"p1_wql{b}_{h}", [128, 128], BF16)) for h in range(NOC)] for b in range(2)]
    upt = [E(sb(f"p1_upt{b}", [128, 512], F32)) for b in range(2)]
    ucol = E(sb("p1_ucol", [128, 2 * NIC * NQT], F32))
    gcol = E(sb("p1_gcol", [128, NIC * NQT], F32))
    mxf = E(sb("p1_mxf", [128, 4], F32))
    mxrow = E(sb("p1_mxrow", [1, 128], F32))
    mx8 = E(sb("p1_mx8", [1, 8], F32))
    _wl = ()
    if "m" in parts:
        _wl = ((0, "wupT_dq", "up_d"), (1, "wgateT_dq", "gate_d"))
    elif "G" in parts:
        _wl = ((1, "wgateT_dq", "gate_d"),)
    elif "u" in parts:
        _wl = ((0, "wupT_dq", "up_d"),)
    _iters = [(wi, dqname, dram, oc) for (wi, dqname, dram) in _wl
              for oc in range(4 if "u4" in parts else NIC)]
    mm_tok = {}
    evac_tok = {0: 0, 1: 0}
    vred_tok = {0: 0, 1: 0}
    gdma_tok = {0: None, 1: None}
    for k, (wi, dqname, dram, oc) in enumerate(_iters):
        wb = k % 2
        if k - 2 in mm_tok:
            P.wait("s", "ts", mm_tok[k - 2])
        P.newbatch("s")
        ld_tok = None
        for hc in range(NOC):
            ld_tok = P.dma("s", wq_l[wb][hc][:],
                           io[dqname][128 * hc:128 * (hc + 1), 128 * oc:128 * (oc + 1)])
        P.wait_tok("t", ld_tok)
        for tc in range(NQT):
            pb = tc % 2
            ps = ps_c0 if pb == 0 else ps_c1
            if evac_tok[pb] > 0:
                P.wait("t", "as_", evac_tok[pb])
            for hc in range(NOC):
                P.step("t", lambda t, hc=hc, tc=tc, ps=ps, wb=wb:
                       t.matmul(out=ps[:], lhsT=wq_l[wb][hc][:], rhs=xq_fm[hc][:, 512 * tc:512 * (tc + 1)],
                                start=(hc == 0), stop=(hc == NOC - 1)))
            P.wait("a", "ts", P.cnt["ts"])
            if vred_tok[pb] > 0:
                P.wait("a", "vs", vred_tok[pb])
            if gdma_tok[pb] is not None:
                P.wait_tok("a", gdma_tok[pb])
            P.inc("a", "as_", 1, lambda a, ps=ps, pb=pb:
                  a.activation(out=upt[pb][:], in_=ps[:], func=AF.Copy), selfwait=False)
            evac_tok[pb] = P.cnt["as_"]
            P.wait("g", "as_", evac_tok[pb])
            gdma_tok[pb] = P.dma("g", io[dram][128 * oc:128 * (oc + 1), 512 * tc:512 * (tc + 1)],
                                 upt[pb][:])
            idx = oc * NQT + tc
            P.wait("v", "as_", evac_tok[pb])
            P.step("v", lambda v, idx=idx, wi=wi, pb=pb: v.tensor_reduce(
                out=ucol[:, (wi * NIC * NQT + idx):(wi * NIC * NQT + idx) + 1],
                in_=upt[pb][:], axis=AX.X, op=ALU.max, apply_absolute_value=True))
            if wi == 1:
                P.step("v", lambda v, idx=idx, pb=pb: v.tensor_reduce(
                    out=gcol[:, idx:idx + 1], in_=upt[pb][:], axis=AX.X, op=ALU.max))
            vred_tok[pb] = P.cnt["vs"]
        mm_tok[k] = P.cnt["ts"]
    P.barrier()
    P.step("v", lambda v: v.tensor_reduce(out=mxf[:, 0:1], in_=ucol[:, 0:NIC * NQT], axis=AX.X, op=ALU.max))
    P.step("v", lambda v: v.tensor_reduce(out=mxf[:, 1:2], in_=ucol[:, NIC * NQT:], axis=AX.X, op=ALU.max))
    P.step("v", lambda v: v.tensor_reduce(out=mxf[:, 2:3], in_=gcol[:], axis=AX.X, op=ALU.max))
    P.wait("a", "vs")
    P.dma("a", io["mx_d"].rearrange("p one -> p one"), mxf[:, 0:1])
    P.wait("a", "ad")
    P.dma("a", mxrow[:], io["mx_d"].rearrange("p one -> one p"))
    P.wait("v", "ad")
    P.step("v", lambda v: v.memset(mx8[:], 0.0))
    P.step("v", lambda v: v.tensor_reduce(out=mx8[:, 1:2], in_=mxrow[:], axis=AX.X, op=ALU.max))
    P.wait("a", "vs")
    P.dma("a", io["mx_d"], mxf[:, 1:2])
    P.wait("a", "ad")
    P.dma("a", mxrow[:], io["mx_d"].rearrange("p one -> one p"))
    P.wait("v", "ad")
    P.step("v", lambda v: v.tensor_reduce(out=mx8[:, 2:3], in_=mxrow[:], axis=AX.X, op=ALU.max))
    P.wait("a", "vs")
    P.dma("a", io["mx_d"], mxf[:, 2:3])
    P.wait("a", "ad")
    P.dma("a", mxrow[:], io["mx_d"].rearrange("p one -> one p"))
    P.wait("v", "ad")
    P.step("v", lambda v: v.tensor_reduce(out=mx8[:, 3:4], in_=mxrow[:], axis=AX.X, op=ALU.max))
    P.wait("g", "vs")
    P.dma("g", io["mx_out"], mx8[:])
    P.wait("g", "gd")


def build_phase2(nc, P, io, ctx):
    E = ctx.enter_context
    sb = nc.sbuf_tensor
    gmax = peer_reduce(nc, P, io, E, sb, "p2")
    scl = E(sb("p2_scl", [1, 8], F32))
    rscl = E(sb("p2_rscl", [1, 8], F32))
    t11a = E(sb("p2_t11a", [1, 1], F32))
    t11b = E(sb("p2_t11b", [1, 1], I32))
    t11c = E(sb("p2_t11c", [1, 1], I32))
    t11d = E(sb("p2_t11d", [1, 1], F32))
    t11e = E(sb("p2_t11e", [1, 1], F32))
    sb16 = E(sb("p2_sb16", [1, 1], F32))
    scraw = E(sb("p2_scraw", [1, 1], F32))
    scv = E(sb("p2_scv", [1, 1], F32))
    rscv = E(sb("p2_rscv", [1, 1], F32))
    sbrc = E(sb("p2_sbrc", [1, 1], F32))
    csa = E(sb("p2_csa", [1, 1], F32))
    rsa_pp = E(sb("p2_rsapp", [128, 1], F32))
    rsb_pp = E(sb("p2_rsbpp", [128, 1], F32))
    sb_pp = E(sb("p2_sbpp", [128, 1], F32))
    sbrc_pp = E(sb("p2_sbrcpp", [128, 1], F32))
    P.step("v", lambda v: v.tensor_scalar(out=scl[:], in0=gmax[:], scalar1=1.0 / QMAX,
                                          scalar2=1e-12, op0=ALU.mult, op1=ALU.max))
    P.wait("a", "vs")
    P.inc("a", "as_", 1, lambda a: a.activation(out=rscl[:], in_=scl[:], func=AF.Copy))
    P.wait("v", "as_")
    P.step("v", lambda v: v.reciprocal(out=rscl[:], in_=rscl[:]))
    P.wait("a", "vs")
    P.inc("a", "as_", 1, lambda a: a.activation(out=t11a[:], in_=gmax[:, 3:4], func=AF.Copy))
    P.wait("v", "as_")
    P.step("v", lambda v: v.tensor_scalar(out=t11d[:], in0=t11a[:], scalar1=rscl[0:1, 2:3],
                                          scalar2=None, op0=ALU.mult))
    P.step("v", lambda v: v.tensor_copy(out=t11b[:], in_=t11d[:]))
    P.wait("a", "vs")
    P.inc("a", "as_", 1, lambda a: a.activation(out=t11d[:], in_=t11b[:], func=AF.Sigmoid,
                                                scale=scl[0:1, 2:3]))
    P.wait("v", "as_")
    P.step("v", lambda v: v.tensor_scalar(out=t11e[:], in0=t11d[:], scalar1=65536.0,
                                          scalar2=None, op0=ALU.mult))
    P.step("v", lambda v: v.tensor_copy(out=t11c[:], in_=t11e[:]))
    P.wait("g", "vs")
    P.step("g", lambda g: g.tensor_scalar(out=t11c[:], in0=t11c[:], scalar1=65535.0,
                                          scalar2=None, op0=ALU.min)) if False else None
    P.wait("a", "vs")
    P.inc("a", "as_", 1, lambda a: a.activation(out=t11d[:], in_=t11c[:], func=AF.Copy))
    P.wait("v", "as_")
    P.step("v", lambda v: v.tensor_scalar(out=t11c[:], in0=t11d[:], scalar1=65535.0,
                                          scalar2=None, op0=ALU.min))
    P.step("v", lambda v: v.tensor_mul(out=scraw[:], in0=t11b[:], in1=t11c[:]))
    P.step("v", lambda v: v.tensor_scalar(out=sb16[:], in0=scl[0:1, 2:3], scalar1=1.0 / 65536.0,
                                          scalar2=None, op0=ALU.mult))
    P.wait("a", "vs")
    P.inc("a", "as_", 1, lambda a: a.activation(out=scraw[:], in_=scraw[:], func=AF.Copy,
                                                scale=sb16[0:1, 0:1]))
    P.wait("v", "as_")
    P.step("v", lambda v: v.tensor_scalar(out=scv[:], in0=scraw[:], scalar1=1.0 / QMAX,
                                          scalar2=1e-12, op0=ALU.mult, op1=ALU.max))
    P.wait("a", "vs")
    P.inc("a", "as_", 1, lambda a: a.activation(out=rscv[:], in_=scv[:], func=AF.Copy))
    P.wait("v", "as_")
    P.step("v", lambda v: v.reciprocal(out=rscv[:], in_=rscv[:]))
    P.step("v", lambda v: v.tensor_scalar(out=sbrc[:], in0=rscv[:], scalar1=sb16[0:1, 0:1],
                                          scalar2=None, op0=ALU.mult))
    P.step("v", lambda v: v.tensor_scalar(out=csa[:], in0=scv[:], scalar1=scl[0:1, 1:2],
                                          scalar2=None, op0=ALU.mult))
    scrow = E(sb("p2_scrow", [1, 16], F32))
    P.step("v", lambda v: v.memset(scrow[:], 0.0))
    P.step("v", lambda v: v.tensor_copy(out=scrow[:, 0:1], in_=scl[:, 1:2]))
    P.step("v", lambda v: v.tensor_copy(out=scrow[:, 1:2], in_=scl[:, 2:3]))
    P.step("v", lambda v: v.tensor_copy(out=scrow[:, 2:3], in_=scv[:]))
    P.step("v", lambda v: v.tensor_copy(out=scrow[:, 3:4], in_=csa[:]))
    P.step("v", lambda v: v.tensor_copy(out=scrow[:, 4:5], in_=sbrc[:]))
    P.wait("g", "vs")
    P.dma("g", io["scal_d"], scrow[:])
    P.waitall("s")
    P.dma("s", rsa_pp[:], rscl[:, 1:2].unsqueeze(1).broadcast_to([1, 128, 1]))
    P.dma("s", rsb_pp[:], rscl[:, 2:3].unsqueeze(1).broadcast_to([1, 128, 1]))
    P.dma("s", sb_pp[:], scl[:, 2:3].unsqueeze(1).broadcast_to([1, 128, 1]))
    P.dma("s", sbrc_pp[:], sbrc[:].unsqueeze(1).broadcast_to([1, 128, 1]))
    P.barrier()

    up_t = [E(sb(f"p2_upt{i}", [128, 512], F32)) for i in range(2)]
    gt_t = [E(sb(f"p2_gtt{i}", [128, 512], F32)) for i in range(2)]
    upi = E(sb("p2_upi", [128, 512], I32))
    tmpf = E(sb("p2_tmpf", [128, 512], F32))
    gti = E(sb("p2_gti", [128, 512], I32))
    sig = E(sb("p2_sig", [128, 512], F32))
    sqi = E(sb("p2_sqi", [128, 512], I32))
    sq2 = E(sb("p2_sq2", [128, 512], I32))
    g2p = E(sb("p2_g2p", [128, 512], F32))
    g2i = E(sb("p2_g2i", [128, 512], I32))
    opr = E(sb("p2_opr", [128, 512], F32))
    ocol = E(sb("p2_ocol", [128, NIC * NQT], F32))
    mxrow = E(sb("p2_mxrow", [1, 128], F32))
    mx8 = E(sb("p2_mx8", [1, 8], F32))
    mxf = E(sb("p2_mxf", [128, 1], F32))
    ntile = NIC * NQT
    rd_tok = {0: 0, 1: 0}
    for idx in range(ntile):
        oc, tc = divmod(idx, NQT)
        buf = idx % 2
        if rd_tok[buf] > 0:
            P.wait("s", "vs", rd_tok[buf])
        P.newbatch("s")
        P.dma("s", up_t[buf][:], io["up_d"][128 * oc:128 * (oc + 1), 512 * tc:512 * (tc + 1)])
        ld_tok = P.dma("s", gt_t[buf][:], io["gate_d"][128 * oc:128 * (oc + 1), 512 * tc:512 * (tc + 1)])
        P.wait_tok("v", ld_tok)
        P.wait("v", "as_")
        P.step("v", lambda v, buf=buf: v.tensor_scalar(out=tmpf[:], in0=up_t[buf][:], scalar1=rsa_pp[:, 0:1],
                                                       scalar2=None, op0=ALU.mult), selfwait=False)
        P.step("v", lambda v: v.tensor_copy(out=upi[:], in_=tmpf[:]), selfwait=False)
        P.step("v", lambda v, buf=buf: v.tensor_scalar(out=tmpf[:], in0=gt_t[buf][:], scalar1=rsb_pp[:, 0:1],
                                                       scalar2=None, op0=ALU.mult), selfwait=False)
        P.step("v", lambda v: v.tensor_copy(out=gti[:], in_=tmpf[:]), selfwait=False)
        rd_tok[buf] = P.cnt["vs"]
        P.wait("a", "vs")
        P.inc("a", "as_", 1, lambda a: a.activation(out=sig[:], in_=gti[:], func=AF.Sigmoid,
                                                    scale=sb_pp[:, 0:1]), selfwait=False)
        P.wait("v", "as_")
        P.step("v", lambda v: v.tensor_scalar(out=tmpf[:], in0=sig[:], scalar1=65536.0,
                                              scalar2=None, op0=ALU.mult), selfwait=False)
        P.step("v", lambda v: v.tensor_copy(out=sqi[:], in_=tmpf[:]), selfwait=False)
        P.step("v", lambda v: v.scalar_tensor_tensor(out=g2p[:], in0=sqi[:], scalar=65535.0,
                                                     in1=gti[:], op0=ALU.min, op1=ALU.mult),
               selfwait=False)
        P.step("v", lambda v: v.tensor_scalar(out=tmpf[:], in0=g2p[:], scalar1=sbrc_pp[:, 0:1],
                                              scalar2=None, op0=ALU.mult), selfwait=False)
        P.step("v", lambda v: v.tensor_copy(out=g2i[:], in_=tmpf[:]), selfwait=False)
        P.wait("v", "ad")
        P.step("v", lambda v: v.tensor_mul(out=opr[:], in0=g2i[:], in1=upi[:]), selfwait=False)
        P.step("v", lambda v, idx=idx: v.tensor_reduce(out=ocol[:, idx:idx + 1], in_=opr[:],
                                                       axis=AX.X, op=ALU.max, apply_absolute_value=True))
        P.wait("a", "vs")
        P.wait("a", "ad")
        P.dma("a", io["o_d"][128 * oc:128 * (oc + 1), 512 * tc:512 * (tc + 1)], opr[:])
    P.barrier()
    P.step("v", lambda v: v.tensor_reduce(out=mxf[:], in_=ocol[:], axis=AX.X, op=ALU.max))
    P.wait("a", "vs")
    P.dma("a", io["mx_d"], mxf[:])
    P.wait("a", "ad")
    P.dma("a", mxrow[:], io["mx_d"].rearrange("p one -> one p"))
    P.wait("v", "ad")
    P.step("v", lambda v: v.memset(mx8[:], 0.0))
    P.step("v", lambda v: v.tensor_reduce(out=mx8[:, 4:5], in_=mxrow[:], axis=AX.X, op=ALU.max))
    P.wait("g", "vs")
    P.dma("g", io["mx_out"], mx8[:])
    P.wait("g", "gd")


def build_phase3(nc, P, io, ctx, dbg_what=None):
    E = ctx.enter_context
    sb = nc.sbuf_tensor
    gmax = peer_reduce(nc, P, io, E, sb, "p3")
    scrow = E(sb("p3_scrow", [1, 16], F32))
    sdv = E(sb("p3_sdv", [1, 1], F32))
    rsdv = E(sb("p3_rsdv", [1, 1], F32))
    crd = E(sb("p3_crd", [1, 1], F32))
    crd_pp = E(sb("p3_crdpp", [128, 1], F32))
    sd_pp = E(sb("p3_sdpp", [128, 1], F32))
    P.wait("s", "vs")
    P.dma("s", scrow[:], io["scal_d"])
    P.wait("v", "sd")
    P.step("v", lambda v: v.tensor_scalar(out=sdv[:], in0=gmax[:, 4:5], scalar1=scrow[0:1, 3:4],
                                          scalar2=None, op0=ALU.mult))
    P.wait("a", "vs")
    P.inc("a", "as_", 1, lambda a: a.activation(out=rsdv[:], in_=sdv[:], func=AF.Copy))
    P.wait("v", "as_")
    P.step("v", lambda v: v.tensor_scalar(out=sdv[:], in0=rsdv[:], scalar1=1.0 / QMAX,
                                          scalar2=1e-12, op0=ALU.mult, op1=ALU.max))
    P.wait("a", "vs")
    P.inc("a", "as_", 1, lambda a: a.activation(out=rsdv[:], in_=sdv[:], func=AF.Copy))
    P.wait("v", "as_")
    P.step("v", lambda v: v.reciprocal(out=rsdv[:], in_=rsdv[:]))
    P.step("v", lambda v: v.tensor_scalar(out=crd[:], in0=rsdv[:], scalar1=scrow[0:1, 3:4],
                                          scalar2=None, op0=ALU.mult))
    P.waitall("s")
    P.dma("s", crd_pp[:], crd[:].unsqueeze(1).broadcast_to([1, 128, 1]))
    P.dma("s", sd_pp[:], sdv[:].unsqueeze(1).broadcast_to([1, 128, 1]))
    P.barrier()

    opr = [E(sb(f"p3_opr{i}", [128, 512], F32)) for i in range(2)]
    oqi = E(sb("p3_oqi", [128, 512], I32))
    oqf = E(sb("p3_oqf", [128, 512], F32))
    oqb = E(sb("p3_oqb", [128, 512], BF16))
    oq_tok = {0: 0, 1: 0}
    for idx in range(NIC * NQT):
        oc, tc = divmod(idx, NQT)
        buf = idx % 2
        if oq_tok[buf] > 0:
            P.wait("s", "vs", oq_tok[buf])
        P.newbatch("s")
        ld_tok = P.dma("s", opr[buf][:], io["o_d"][128 * oc:128 * (oc + 1), 512 * tc:512 * (tc + 1)])
        P.wait_tok("v", ld_tok)
        P.wait("v", "gd")
        P.step("v", lambda v, buf=buf: v.tensor_scalar(out=oqf[:], in0=opr[buf][:], scalar1=crd_pp[:, 0:1],
                                                       scalar2=None, op0=ALU.mult), selfwait=False)
        oq_tok[buf] = P.cnt["vs"]
        P.step("v", lambda v: v.tensor_copy(out=oqi[:], in_=oqf[:]), selfwait=False)
        P.step("v", lambda v: v.tensor_scalar(out=oqb[:], in0=oqi[:], scalar1=sd_pp[:, 0:1],
                                              scalar2=None, op0=ALU.mult), selfwait=False)
        P.wait("g", "vs")
        P.dma("g", io["gate_d"][128 * oc:128 * (oc + 1), 512 * tc:512 * (tc + 1)].bitcast(BF16)[:, 0:512],
              oqb[:])
    P.barrier()

    wd_l = [E(sb(f"p3_wdl{c}", [128, D], BF16)) for c in range(NIC)]
    wd_tok = None
    for ic in range(NIC):
        wd_tok = P.dma("s", wd_l[ic][:], io["wdownT_dq"][128 * ic:128 * (ic + 1), :])
    on_t = [E(sb(f"p3_ont{c}", [128, 128], BF16)) for c in range(NIC)]
    y1t = E(sb("p3_y1t", [128, D], F32))
    z_sb = E(sb("p3_z", [128, D], F32))
    zsq = E(sb("p3_zsq", [128, D], F32))
    rsum = E(sb("p3_rsum", [128, 1], F32))
    mvec = E(sb("p3_mvec", [128, 1], F32))
    ssum = E(sb("p3_ssum", [128, 1], F32))
    varv = E(sb("p3_varv", [128, 1], F32))
    rstdv = E(sb("p3_rstdv", [128, 1], F32))
    y2t = E(sb("p3_y2t", [128, D], F32))
    epsA = E(sb("p3_epsA", [128, 1], F32))
    P.step("v", lambda v: v.memset(epsA[:], EPS))
    ps_a = E(nc.psum_tensor("p3_psa", [128, 1024], F32))
    P.wait_tok("t", wd_tok)
    d_mm_tok = 0
    d_z_tok = 0
    for tt in range(NTT):
        if tt == 0:
            P.waitall("s")
        else:
            P.wait("s", "ts", d_mm_tok)
            P.wait("s", "vs", d_z_tok)
        P.newbatch("s")
        for ic in range(NIC):
            P.dma("s", on_t[ic][:],
                  io["gate_d"][128 * ic:128 * (ic + 1), 512 * (tt // 4):512 * (tt // 4) + 512]
                  .bitcast(BF16)[:, 128 * (tt % 4):128 * (tt % 4) + 128])
        d_ld_tok = P.dma("s", y1t[:], io["y1_d"][128 * tt:128 * (tt + 1), :])
        P.wait_tok("t", d_ld_tok)
        if d_z_tok > 0:
            P.wait("t", "vs", d_z_tok)
        for half in range(2):
            psh = ps_a[:, 512 * half:512 * (half + 1)]
            for ic in range(NIC):
                P.step("t", lambda t, ic=ic, half=half, psh=psh:
                       t.matmul(out=psh[:], lhsT=on_t[ic][:], rhs=wd_l[ic][:, 512 * half:512 * (half + 1)],
                                start=(ic == 0), stop=(ic == NIC - 1)))
        d_mm_tok = P.cnt["ts"]
        P.wait("v", "ts", d_mm_tok)
        P.wait("v", "gd")
        P.wait("v", "as_")
        for half in range(2):
            P.step("v", lambda v, half=half: v.tensor_add(
                out=z_sb[:, 512 * half:512 * (half + 1)], in0=ps_a[:, 512 * half:512 * (half + 1)],
                in1=y1t[:, 512 * half:512 * (half + 1)]), selfwait=False)
        d_z_tok = P.cnt["vs"]
        P.step("v", lambda v: v.tensor_reduce(out=rsum[:], in_=z_sb[:], axis=AX.X, op=ALU.add))
        P.wait("a", "vs")
        P.inc("a", "as_", 1, lambda a: a.activation(out=zsq[:], in_=z_sb[:], func=AF.Square), selfwait=False)
        P.wait("v", "as_")
        P.step("v", lambda v: v.tensor_reduce(out=ssum[:], in_=zsq[:], axis=AX.X, op=ALU.add))
        P.wait("g", "vs")
        P.wait("g", "as_")
        P.step("g", lambda g: g.tensor_scalar(out=mvec[:], in0=rsum[:], scalar1=1.0 / D, scalar2=None, op0=ALU.mult))
        P.step("g", lambda g: g.tensor_mul(out=rstdv[:], in0=mvec[:], in1=mvec[:]))
        P.step("g", lambda g: g.tensor_scalar(out=varv[:], in0=ssum[:], scalar1=1.0 / D, scalar2=None, op0=ALU.mult))
        P.step("g", lambda g: g.tensor_sub(out=varv[:], in0=varv[:], in1=rstdv[:]))
        P.wait("a", "gs")
        P.inc("a", "as_", 1, lambda a: a.activation(out=rstdv[:], in_=varv[:], func=AF.Ln, bias=epsA[:, 0:1]))
        P.inc("a", "as_", 1, lambda a: a.activation(out=rstdv[:], in_=rstdv[:], func=AF.Exp, scale=-0.5))
        P.wait("v", "as_")
        P.wait("v", "gs")
        P.step("v", lambda v: v.tensor_scalar(out=y2t[:], in0=z_sb[:], scalar1=mvec[:, 0:1],
                                              scalar2=rstdv[:, 0:1], op0=ALU.subtract, op1=ALU.mult))
        P.wait("g", "vs")
        P.dma("g", io["out"][128 * tt:128 * (tt + 1), :], y2t[:])
    P.wait("g", "gd")


ALLSEMS2 = ALLSEMS + ["ph"]


def build_full(dbg=False, only=None, p1_parts="sdxmer"):
    nc = bass.Bass("TRN2", target_bir_lowering=False)
    io = declare_io(nc, dbg=dbg)
    plans = []
    from contextlib import ExitStack
    with ExitStack() as octx:
        sems = {k: octx.enter_context(nc.semaphore(k)) for k in ALLSEMS2}
        ph_sb = octx.enter_context(nc.sbuf_tensor("ph_sb", [1, 1], I32))
        builders = [build_phase0,
                    lambda nc, P, io, c: build_phase1(nc, P, io, c, parts=p1_parts),
                    build_phase2, build_phase3]
        for i, bf in enumerate(builders):
            P = Plan()
            if only is None or i == only:
                with ExitStack() as pctx:
                    bf(nc, P, io, pctx)
            plans.append(P)
        block = octx.enter_context(nc.Block())

        def mk_emit(engname):
            def f(eng):
                if engname == "s":
                    eng.dma_start(out=ph_sb[:], in_=io["phase"]).then_inc(sems["ph"], 16)
                eng.wait_ge(sems["ph"], 16)
                p = eng.value_load(ph_sb[0:1, 0:1])
                with nc.If(p == 0):
                    plans[0].emit(engname, eng, sems)
                with nc.Else():
                    with nc.If(p == 1):
                        plans[1].emit(engname, eng, sems)
                    with nc.Else():
                        with nc.If(p == 2):
                            plans[2].emit(engname, eng, sems)
                        with nc.Else():
                            plans[3].emit(engname, eng, sems)
            return f

        block.sync(mk_emit("s"))
        block.tensor(mk_emit("t"))
        block.vector(mk_emit("v"))
        block.scalar(mk_emit("a"))
        block.gpsimd(mk_emit("g"))
    return nc, io


def host_inputs(x_core, Wq, Wkv, Wo, bo, ln_g, ln_b, w_up, w_gate, w_down):
    cos_t, sinp_t = rope_tables()
    tri, ems = mask_consts()
    return {
        "x": np.ascontiguousarray(x_core),
        "wqT": np.ascontiguousarray(Wq.T), "wkvT": np.ascontiguousarray(Wkv.T),
        "woT": np.ascontiguousarray(Wo.T), "bo": np.ascontiguousarray(bo.reshape(1, -1)),
        "g128": np.ascontiguousarray(np.tile(ln_g, 2).reshape(128, 1)),
        "b128": np.ascontiguousarray(np.tile(ln_b, 2).reshape(128, 1)),
        "cos_t": cos_t, "sinp_t": sinp_t, "tri": tri, "ems": ems,
        "ind2": ind2_const(),
        "idb": np.ascontiguousarray(np.eye(128).astype(ml_dtypes.bfloat16)),
        "idf": np.ascontiguousarray(np.eye(128, dtype=np.float32)),
        "w_up": np.ascontiguousarray(w_up), "w_gate": np.ascontiguousarray(w_gate),
        "w_down": np.ascontiguousarray(w_down),
        "wupT": np.ascontiguousarray(w_up.T), "wgateT": np.ascontiguousarray(w_gate.T),
        "wdownT": np.ascontiguousarray(w_down.T),
        "phase": np.array([[0]], np.int32),
        "peer_mx": np.zeros((1, 64), np.float32),
    }
_CACHE = {}


def kernel(L0_attn_x, L0_attn_Wq, L0_attn_Wkv, L0_attn_Wo, L0_attn_bo,
           L0_attn_ln_g, L0_attn_ln_b, L0_ffn_w_gate, L0_ffn_w_up, L0_ffn_w_down):
    from concourse.bass_utils import run_bass_kernel_spmd
    n = 8
    if "nc" not in _CACHE:
        _CACHE["nc"], _CACHE["io"] = build_full(dbg=False)
    nc = _CACHE["nc"]
    x = np.asarray(L0_attn_x)
    base = [host_inputs(x[c], np.asarray(L0_attn_Wq), np.asarray(L0_attn_Wkv),
                        np.asarray(L0_attn_Wo), np.asarray(L0_attn_bo),
                        np.asarray(L0_attn_ln_g), np.asarray(L0_attn_ln_b),
                        np.asarray(L0_ffn_w_up), np.asarray(L0_ffn_w_gate),
                        np.asarray(L0_ffn_w_down)) for c in range(n)]
    peer = np.zeros((1, 64), np.float32)
    slot_of_phase = {0: [0], 1: [1, 2, 3], 2: [4], 3: []}
    outs = None
    for phase in range(4):
        for c in range(n):
            base[c]["phase"] = np.array([[phase]], np.int32)
            base[c]["peer_mx"] = peer.copy()
        res = run_bass_kernel_spmd(nc, base, core_ids=list(range(n)))
        mx = np.stack([np.asarray(res.results[c]["mx_out"])[0] for c in range(n)])
        for s in slot_of_phase[phase]:
            for r in range(n):
                peer[0, s * 8 + r] = mx[r, s]
        outs = res
    return np.stack([np.asarray(outs.results[c]["out"]) for c in range(n)]).astype(np.float32)


# version 04
# speedup vs baseline: 1.7382x; 1.0262x over previous
import numpy as np
import ml_dtypes
import concourse.bass as bass
import concourse.mybir as mybir

F32 = mybir.dt.float32
BF16 = mybir.dt.bfloat16
I32 = mybir.dt.int32
AF = mybir.ActivationFunctionType
ALU = mybir.AluOpType
AX = mybir.AxisListType

T = 4096
D = 1024
H = 16
DH = 64
CTX = 1536
NI = 3072
NTT = T // 128
NQT = 8
NOC = D // 128
NIC = NI // 128
QMAX = 32767.0
EPS = 1e-5

ENGS = ("s", "t", "v", "a", "g")
KQ = 4
QUEUES = ("sd", "ad", "gd", "vd")
QSEMS = {q: [f"{q}{k}" for k in range(KQ)] for q in QUEUES}
ESEM = {"s": "ss", "t": "ts", "v": "vs", "a": "as_", "g": "gs"}
ENG2Q = {"s": "sd", "a": "ad", "g": "gd", "v": "vd"}
ALLSEMS = [s for q in QUEUES for s in QSEMS[q]] + list(ESEM.values())


class Plan:
    def __init__(self):
        self.cnt = {k: 0 for k in ALLSEMS}
        self.ops = {e: [] for e in ENGS}
        self.qidx = {q: 0 for q in QUEUES}
        self.qgate = {q: False for q in QUEUES}
        self.qwaited = {q: False for q in QUEUES}

    def op(self, eng, fn):
        self.ops[eng].append(("op", fn))

    def inc(self, eng, sem, by, fn, selfwait=None):
        self.cnt[sem] += by
        self.ops[eng].append(("inc", sem, by, fn))
        if selfwait is None:
            selfwait = sem == ESEM.get(eng)
        if selfwait and sem in ESEM.values() and eng != "t":
            self.ops[eng].append(("wait", sem, self.cnt[sem]))
        return self.cnt[sem]

    def step(self, eng, fn, selfwait=True):
        return self.inc(eng, ESEM[eng], 1, fn, selfwait=selfwait)

    def newbatch(self, eng):
        q = ENG2Q[eng]
        self.qidx[q] = (self.qidx[q] + 1) % KQ
        self.qgate[q] = True

    def dma(self, eng, out, in_, **kw):
        q = ENG2Q[eng]
        if self.qwaited[q]:
            self.qidx[q] = (self.qidx[q] + 1) % KQ
            self.qgate[q] = True
            self.qwaited[q] = False
        sem = QSEMS[q][self.qidx[q]]
        if self.qgate[q]:
            if self.cnt[sem] > 0:
                self.ops[eng].append(("wait", sem, self.cnt[sem]))
            self.qgate[q] = False
        self.cnt[sem] += 16
        self.ops[eng].append(("inc", sem, 16,
                              lambda e, o=out, i=in_, kw=kw: e.dma_start(out=o, in_=i, **kw)))
        return (sem, self.cnt[sem])

    def wait(self, eng, sem, val=None):
        if sem in QUEUES:
            self.qwaited[sem] = True
            for s in QSEMS[sem]:
                if self.cnt[s] > 0:
                    self.ops[eng].append(("wait", s, self.cnt[s]))
            return
        v = self.cnt[sem] if val is None else val
        if v <= 0:
            return
        self.ops[eng].append(("wait", sem, v))

    def wait_tok(self, eng, tok):
        sem, v = tok
        for q in QUEUES:
            if sem in QSEMS[q]:
                self.qwaited[q] = True
        self.ops[eng].append(("wait", sem, v))

    def waitall(self, eng):
        for q in QUEUES:
            self.wait(eng, q)
        for e, s in ESEM.items():
            if e != eng:
                self.wait(eng, s)

    def fence(self, eng):
        self.wait(eng, ESEM[eng])

    def barrier(self):
        vals = {}
        for e in ENGS:
            vals[e] = self.inc(e, ESEM[e], 1, lambda eng: eng.nop(), selfwait=False)
        for e in ENGS:
            for o in ENGS:
                if o != e:
                    self.wait(e, ESEM[o], vals[o])
            for q in QUEUES:
                self.wait(e, q)

    def emit(self, eng_name, eng, sems):
        for item in self.ops[eng_name]:
            if item[0] == "op":
                item[1](eng)
            elif item[0] == "inc":
                _, sem, by, fn = item
                fn(eng).then_inc(sems[sem], by)
            else:
                _, sem, v = item
                eng.wait_ge(sems[sem], v)


def rope_tables():
    inv_freq = 1.0 / (10000.0 ** (np.arange(0, DH, 2, dtype=np.float64) / DH))
    freqs = np.arange(T, dtype=np.float64)[:, None] * inv_freq[None, :]
    emb = np.concatenate([freqs, freqs], axis=-1)
    cos = np.cos(emb).astype(np.float32).T
    sin = np.sin(emb).astype(np.float32).T
    sinp = sin.copy()
    sinp[:32, :] *= -1.0
    return (np.ascontiguousarray(np.concatenate([cos, cos], axis=0)),
            np.ascontiguousarray(np.concatenate([sinp, sinp], axis=0)))


def mask_consts():
    tri = np.ascontiguousarray(np.triu(np.ones((128, 128), np.float32)).astype(ml_dtypes.bfloat16))
    ems = np.zeros((4, 128, 512), np.float32)
    for m in range(4):
        for q in range(512):
            j = max(q - 128 * m + 1, 0)
            if j <= 127:
                ems[m, j, q] = -32768.0
    return tri, np.ascontiguousarray(ems.astype(ml_dtypes.bfloat16))


def ind2_const():
    a = np.zeros((128, 2), np.float32)
    a[:64, 0] = 1.0
    a[64:, 1] = 1.0
    return np.ascontiguousarray(a.astype(ml_dtypes.bfloat16))


def qtile_chunks(qt):
    if qt < 3:
        return list(range(12)), {}
    chunks = list(range(4 * qt + 4))
    masked = {4 * qt + m: m for m in range(4)}
    return chunks, masked


TPERM = [5, 6, 7, 0, 1, 2, 3, 4]


def declare_io(nc, dbg=True):
    io = {"dbg_on": dbg}
    d = nc.dram_tensor
    io["x"] = d("x", [T, D], F32, kind="ExternalInput").ap()
    io["wqT"] = d("wqT", [D, D], F32, kind="ExternalInput").ap()
    io["wkvT"] = d("wkvT", [D, D], F32, kind="ExternalInput").ap()
    io["woT"] = d("woT", [D, D], F32, kind="ExternalInput").ap()
    io["bo"] = d("bo", [1, D], F32, kind="ExternalInput").ap()
    io["g128"] = d("g128", [128, 1], F32, kind="ExternalInput").ap()
    io["b128"] = d("b128", [128, 1], F32, kind="ExternalInput").ap()
    io["cos_t"] = d("cos_t", [128, T], F32, kind="ExternalInput").ap()
    io["sinp_t"] = d("sinp_t", [128, T], F32, kind="ExternalInput").ap()
    io["tri"] = d("tri", [128, 128], BF16, kind="ExternalInput").ap()
    io["ems"] = d("ems", [4, 128, 512], BF16, kind="ExternalInput").ap()
    io["ind2"] = d("ind2", [128, 2], BF16, kind="ExternalInput").ap()
    io["idb"] = d("idb", [128, 128], BF16, kind="ExternalInput").ap()
    io["idf"] = d("idf", [128, 128], F32, kind="ExternalInput").ap()
    io["w_up"] = d("w_up", [NI, D], F32, kind="ExternalInput").ap()
    io["w_gate"] = d("w_gate", [NI, D], F32, kind="ExternalInput").ap()
    io["w_down"] = d("w_down", [D, NI], F32, kind="ExternalInput").ap()
    io["wupT"] = d("wupT", [D, NI], F32, kind="ExternalInput").ap()
    io["wgateT"] = d("wgateT", [D, NI], F32, kind="ExternalInput").ap()
    io["wdownT"] = d("wdownT", [NI, D], F32, kind="ExternalInput").ap()
    io["phase"] = d("phase", [1, 1], I32, kind="ExternalInput").ap()
    io["peer_mx"] = d("peer_mx", [1, 64], F32, kind="ExternalInput").ap()
    io["out"] = d("out", [T, D], F32, kind="ExternalOutput").ap()
    io["mx_out"] = d("mx_out", [1, 8], F32, kind="ExternalOutput").ap()
    if dbg:
        io["dbg"] = d("dbg", [T, D], F32, kind="ExternalOutput").ap()
    io["y1_d"] = d("y1_d", [T, D], F32, kind="Internal").ap()
    io["up_d"] = d("up_d", [NI, T], F32, kind="Internal").ap()
    io["gate_d"] = d("gate_d", [NI, T], F32, kind="Internal").ap()
    io["o_d"] = d("o_d", [NI, T], F32, kind="Internal").ap()
    io["outn_d"] = d("outn_d", [D, T], BF16, kind="Internal").ap()
    io["wupT_dq"] = d("wupT_dq", [D, NI], BF16, kind="Internal").ap()
    io["wgateT_dq"] = d("wgateT_dq", [D, NI], BF16, kind="Internal").ap()
    io["wdownT_dq"] = d("wdownT_dq", [NI, D], BF16, kind="Internal").ap()
    io["lkvF_d"] = d("lkvF_d", [D, T], BF16, kind="Internal").ap()
    io["lkvT_d"] = d("lkvT_d", [T, D], BF16, kind="Internal").ap()
    io["stats_d1"] = d("stats_d1", [16, 512], F32, kind="Internal").ap()
    io["stats_d2"] = d("stats_d2", [16, 512], F32, kind="Internal").ap()
    io["mx_d"] = d("mx_d", [128, 1], F32, kind="Internal").ap()
    io["scal_d"] = d("scal_d", [1, 16], F32, kind="Internal").ap()
    return io




def build_phase0(nc, P, io, ctx, stages='ABCD', dbg_what=None, c_qts=None, c_pairs=None, c_chunks=True, d_level=3):
    E = ctx.enter_context
    sb = nc.sbuf_tensor
    w_bf = [E(sb(f"p0_w{c}", [128, D], BF16)) for c in range(NOC)]
    g128_sb = E(sb("p0_g", [128, 1], F32))
    b128_sb = E(sb("p0_b", [128, 1], F32))
    ind2_sb = E(sb("p0_ind2", [128, 2], BF16))
    tri_sb = E(sb("p0_tri", [128, 128], BF16))
    ems_sb = E(sb("p0_ems", [128, 4 * 512], BF16))
    idb_sb = E(sb("p0_idb", [128, 128], BF16))
    ones_bf = E(sb("p0_ones", [128, 1], BF16))
    ones_f1 = E(sb("p0_onesf", [1, 128], F32))
    bo_sb = E(sb("p0_bo", [1, D], F32))
    knacc = E(sb("p0_knacc", [2, 8], F32))
    knall = E(sb("p0_knall", [2, 64], F32))
    tmp21 = E(sb("p0_tmp21", [2, 1], F32))
    xtm_f = [E(sb(f"p0_xtf{i}", [128, D], F32)) for i in range(4)]
    xtm_b = [E(sb(f"p0_xtb{i}", [128, D], BF16)) for i in range(4)]
    x_fm = [E(sb(f"p0_xfm{c}", [128, 512], BF16)) for c in range(NOC)]
    kvraw = E(sb("p0_kvr", [128, 512], F32))
    kvrot = E(sb("p0_kvrot", [128, 512], F32))
    kvc = E(sb("p0_kvc", [128, 512], F32))
    kvs = E(sb("p0_kvs", [128, 512], F32))
    kvrope = [E(sb(f"p0_kvro{c}", [128, 512], F32)) for c in range(NOC)]
    cs_sl = E(sb("p0_cs", [128, 512], F32))
    sn_sl = E(sb("p0_sn", [128, 512], F32))
    sq_bf = E(sb("p0_sq", [128, 512], BF16))
    v_bf = E(sb("p0_vbf", [128, 512], BF16))
    t1f = E(sb("p0_t1", [128, 512], F32))
    t2f = E(sb("p0_t2", [128, 512], F32))
    lkv_t = E(sb("p0_lkvt", [128, 512], BF16))
    lkvt_ev = E(sb("p0_lkvtev", [128, 512], BF16))
    stat2 = E(sb("p0_stat2", [2, 1024], F32))
    sums_p = E(sb("p0_sump", [16, 512], F32))
    ssq_p = E(sb("p0_ssqp", [16, 512], F32))
    m_p = E(sb("p0_mp", [16, 512], F32))
    msq_p = E(sb("p0_msqp", [16, 512], F32))
    var_p = E(sb("p0_varp", [16, 512], F32))
    rstd_p = E(sb("p0_rstdp", [16, 512], F32))
    m_b = E(sb("p0_mb", [128, 512], F32))
    rstd_b = E(sb("p0_rstdb", [128, 512], F32))
    q_bf = [E(sb(f"p0_q{c}", [128, 512], BF16)) for c in range(NOC)]
    qnacc = E(sb("p0_qnacc", [2, 8], F32))
    cprod = E(sb("p0_cprod", [2, 8], F32))
    negc = E(sb("p0_negc", [2, 8], F32))
    negc16 = E(sb("p0_negc16", [1, 16], F32))
    bias_all = E(sb("p0_bias", [128, 16], F32))
    pA = E(sb("p0_pA", [128, 1024], BF16))
    pB = E(sb("p0_pB", [128, 1024], BF16))
    lkF = [E(sb(f"p0_lkF{i}", [128, 4096], BF16)) for i in range(2)]
    lkT = [E(sb(f"p0_lkT{i}", [128, 4096], BF16)) for i in range(2)]
    av_sb = [E(sb(f"p0_av{p}", [128, 512], F32)) for p in range(8)]
    sum_ev = E(sb("p0_sumev", [128, 512], F32))
    l_all = E(sb("p0_lall", [16, 512], F32))
    rl_all = E(sb("p0_rlall", [16, 512], F32))
    rl_b = E(sb("p0_rlb", [128, 512], F32))
    outn_t = E(sb("p0_outn", [128, 512], BF16))
    on_t = [E(sb(f"p0_ont{c}", [128, 128], BF16)) for c in range(NOC)]
    xt_d = E(sb("p0_xtd", [128, D], F32))
    z_sb = E(sb("p0_z", [128, D], F32))
    zsq = E(sb("p0_zsq", [128, D], F32))
    rsum = E(sb("p0_rsum", [128, 1], F32))
    mvec = E(sb("p0_mvec", [128, 1], F32))
    ssum = E(sb("p0_ssum", [128, 1], F32))
    varv = E(sb("p0_varv", [128, 1], F32))
    rstdv = E(sb("p0_rstdv", [128, 1], F32))
    y1t = E(sb("p0_y1t", [128, D], F32))
    mxcol = E(sb("p0_mxcol", [128, NTT], F32))
    mxfin = E(sb("p0_mxfin", [128, 1], F32))
    mxrow = E(sb("p0_mxrow", [1, 128], F32))
    mx8 = E(sb("p0_mx8", [1, 8], F32))
    epsA = E(sb("p0_epsA", [128, 1], F32))
    epsB = E(sb("p0_epsB", [128, 1], F32))
    ps_a = E(nc.psum_tensor("p0_psa", [128, 1024], F32))
    ps_b = E(nc.psum_tensor("p0_psb", [128, 1024], F32))
    ps_av = E(nc.psum_tensor("p0_psav", [128, 512], F32))
    ps_sum = E(nc.psum_tensor("p0_pssum", [128, 512], F32))
    ps_c0 = E(nc.psum_tensor("p0_psc0", [128, 512], F32))
    ps_c1 = E(nc.psum_tensor("p0_psc1", [128, 512], F32))

    dbgv = io.get("dbg")
    dbg_fm = dbgv.rearrange("(r s) c -> r (s c)", s=4) if dbgv is not None else None

    P.step("v", lambda v: v.memset(ones_bf[:], 1.0))
    P.step("v", lambda v: v.memset(epsA[:], EPS))
    P.step("v", lambda v: v.memset(epsB[:], 1e-30))
    P.step("v", lambda v: v.memset(ones_f1[:], 1.0))
    P.step("v", lambda v: v.memset(knacc[:], 0.0))
    P.step("v", lambda v: v.memset(qnacc[:], 0.0))
    for (dst, src) in ((g128_sb, io["g128"]), (b128_sb, io["b128"]),
                       (ind2_sb, io["ind2"]), (tri_sb, io["tri"]),
                       (idb_sb, io["idb"]), (bo_sb, io["bo"])):
        P.dma("s", dst[:], src)
    P.dma("s", ems_sb[:].rearrange("p (m q) -> p m q", m=4), io["ems"].rearrange("m p q -> p m q"))

    def load_weights(name):
        for c in range(NOC):
            P.wait("s", "vs")
            P.newbatch("s")
            P.dma("s", xtm_f[0][:], io[name][128 * c:128 * (c + 1), :])
            P.wait("v", "sd")
            P.step("v", lambda v, c=c: v.tensor_copy(out=w_bf[c][:], in_=xtm_f[0][:]), selfwait=False)

    def load_x_group(tq):
        P.waitall("s")
        for i in range(4):
            P.dma("s", xtm_f[i][:], io["x"][512 * tq + 128 * i:512 * tq + 128 * (i + 1), :])
        P.wait("v", "sd")
        P.waitall("v")
        for i in range(4):
            P.step("v", lambda v, i=i: v.tensor_copy(out=xtm_b[i][:], in_=xtm_f[i][:]), selfwait=False)
        for hc in range(NOC):
            bank = hc % 2
            pview = ps_a[:, 512 * bank:512 * (bank + 1)].bitcast(BF16)
            P.wait("t", "vs")
            for i in range(4):
                P.step("t", lambda t, hc=hc, i=i, pv=pview:
                       t.transpose(out=pv[:, 128 * i:128 * (i + 1)],
                                   in_=xtm_b[i][:, 128 * hc:128 * (hc + 1)],
                                   identity=idb_sb[:]))
            P.wait("v", "ts")
            P.step("v", lambda v, hc=hc, pv=pview: v.tensor_copy(out=x_fm[hc][:], in_=pv[:, 0:512]),
                   selfwait=False)
        P.wait("t", "vs")

    def proj_rope(oc, outbuf):
        ps = ps_c0 if oc % 2 == 0 else ps_c1
        P.wait("t", "vs")
        for hc in range(NOC):
            P.step("t", lambda t, hc=hc, oc=oc, ps=ps:
                   t.matmul(out=ps[:], lhsT=w_bf[hc][:, 128 * oc:128 * (oc + 1)],
                            rhs=x_fm[hc][:], start=(hc == 0), stop=(hc == NOC - 1)))
        P.wait("v", "ts")
        P.wait("v", "ad")
        P.step("v", lambda v, ps=ps: v.tensor_copy(out=kvraw[:], in_=ps[:]), selfwait=False)
        P.wait("a", "vs")
        P.wait("a", "gs")
        for (d0, s0) in ((0, 32), (32, 0), (64, 96), (96, 64)):
            P.dma("a", kvrot[d0:d0 + 32, :], kvraw[s0:s0 + 32, :])
        P.wait("g", "ad")
        P.wait("g", "vs")
        P.step("g", lambda g: g.tensor_mul(out=kvs[:], in0=kvrot[:], in1=sn_sl[:]), selfwait=False)
        P.step("v", lambda v: v.tensor_mul(out=kvc[:], in0=kvraw[:], in1=cs_sl[:]), selfwait=False)
        P.wait("v", "gs")
        P.wait("v", "as_")
        P.step("v", lambda v, outbuf=outbuf: v.tensor_add(out=outbuf[:], in0=kvc[:], in1=kvs[:]),
               selfwait=False)

    if "B" in stages:
        load_weights("wkvT")
    P.barrier()

    for tq in (range(NQT) if "B" in stages else []):
        load_x_group(tq)
        P.waitall("s")
        P.dma("s", cs_sl[:], io["cos_t"][:, 512 * tq:512 * (tq + 1)])
        P.dma("s", sn_sl[:], io["sinp_t"][:, 512 * tq:512 * (tq + 1)])
        P.wait("v", "sd")
        P.wait("g", "sd")
        for oc in range(NOC):
            proj_rope(oc, kvrope[oc])
            if dbg_what == "kvrope":
                P.wait("g", "vs")
                P.dma("g", dbg_fm[128 * oc:128 * (oc + 1), 512 * tq:512 * (tq + 1)], kvrope[oc][:])
            P.wait("a", "vs")
            P.wait("a", "ts")
            P.inc("a", "as_", 1, lambda a, oc=oc: a.activation(out=sq_bf[:], in_=kvrope[oc][:], func=AF.Square))
            P.step("v", lambda v, oc=oc: v.tensor_copy(out=v_bf[:], in_=kvrope[oc][:]), selfwait=False)
            P.wait("t", "as_")
            P.wait("t", "vs")
            P.step("t", lambda t: t.matmul(out=ps_sum[0:2, :], lhsT=ind2_sb[:], rhs=v_bf[:], start=True, stop=True))
            P.step("t", lambda t: t.matmul(out=ps_av[0:2, :], lhsT=ind2_sb[:], rhs=sq_bf[:], start=True, stop=True))
            P.wait("v", "ts")
            P.wait("v", "gd")
            P.step("v", lambda v: v.tensor_copy(out=stat2[:, 0:512], in_=ps_sum[0:2, :]), selfwait=False)
            P.step("v", lambda v: v.tensor_copy(out=stat2[:, 512:1024], in_=ps_av[0:2, :]), selfwait=False)
            P.wait("g", "vs")
            P.dma("g", io["stats_d1"][2 * oc:2 * oc + 2, :], stat2[:, 0:512])
            P.dma("g", io["stats_d2"][2 * oc:2 * oc + 2, :], stat2[:, 512:1024])
            P.wait("t", "vs")
        P.wait("s", "gd")
        P.wait("s", "vs")
        P.newbatch("s")
        P.dma("s", sums_p[:], io["stats_d1"])
        P.dma("s", ssq_p[:], io["stats_d2"])
        P.wait("v", "sd")
        P.step("v", lambda v: v.tensor_scalar(out=m_p[:], in0=sums_p[:], scalar1=1.0 / DH, scalar2=None,
                                              op0=ALU.mult), selfwait=False)
        P.step("v", lambda v: v.tensor_mul(out=msq_p[:], in0=m_p[:], in1=m_p[:]), selfwait=False)
        P.step("v", lambda v: v.tensor_scalar(out=var_p[:], in0=ssq_p[:], scalar1=1.0 / DH, scalar2=None,
                                              op0=ALU.mult), selfwait=False)
        P.step("v", lambda v: v.tensor_sub(out=var_p[:], in0=var_p[:], in1=msq_p[:]), selfwait=False)
        P.wait("a", "vs")
        P.inc("a", "as_", 1, lambda a: a.activation(out=rstd_p[:], in_=var_p[:], func=AF.Ln, bias=epsA[0:16, 0:1]))
        P.inc("a", "as_", 1, lambda a: a.activation(out=rstd_p[:], in_=rstd_p[:], func=AF.Exp, scale=-0.5))
        for oc in range(NOC):
            P.wait("s", "vs")
            P.wait("s", "as_")
            P.newbatch("s")
            P.dma("s", m_b[:], m_p[2 * oc:2 * oc + 2, :].unsqueeze(1).broadcast_to([2, 64, 512]))
            P.dma("s", rstd_b[:], rstd_p[2 * oc:2 * oc + 2, :].unsqueeze(1).broadcast_to([2, 64, 512]))
            P.wait("v", "sd")
            P.step("v", lambda v, oc=oc: v.tensor_sub(out=t1f[:], in0=kvrope[oc][:], in1=m_b[:]),
                   selfwait=False)
            P.step("v", lambda v: v.scalar_tensor_tensor(
                out=t2f[:], in0=t1f[:], scalar=g128_sb[:, 0:1], in1=rstd_b[:], op0=ALU.mult, op1=ALU.mult),
                   selfwait=False)
            P.wait("v", "gd")
            P.wait("v", "as_")
            P.step("v", lambda v: v.tensor_scalar(
                out=lkv_t[:], in0=t2f[:], scalar1=b128_sb[:, 0:1], scalar2=None, op0=ALU.add),
                   selfwait=False)
            P.wait("g", "vs")
            P.dma("g", io["lkvF_d"][128 * oc:128 * (oc + 1), 512 * tq:512 * (tq + 1)], lkv_t[:])
            if dbg_what == "lkv":
                P.dma("g", dbg_fm[128 * oc:128 * (oc + 1), 512 * tq:512 * (tq + 1)], lkv_t[:])
            P.wait("a", "vs")
            P.wait("a", "ts")
            P.inc("a", "as_", 1, lambda a: a.activation(out=sq_bf[:], in_=lkv_t[:], func=AF.Square))
            P.wait("t", "as_")
            P.step("t", lambda t: t.matmul(out=ps_sum[0:2, :], lhsT=ind2_sb[:], rhs=sq_bf[:], start=True, stop=True))
            P.wait("v", "ts")
            P.step("v", lambda v, oc=oc, tq=tq: v.tensor_reduce(
                out=knall[:, 8 * tq + oc:8 * tq + oc + 1], in_=ps_sum[0:2, :], axis=AX.X, op=ALU.max))
            bank = oc % 2
            pview = ps_a[:, 512 * bank:512 * (bank + 1)].bitcast(BF16)
            P.wait("t", "vs")
            P.wait("t", "gd")
            for i in range(4):
                P.step("t", lambda t, i=i, pv=pview:
                       t.transpose(out=pv[:, 128 * i:128 * (i + 1)],
                                   in_=lkv_t[:, 128 * i:128 * (i + 1)],
                                   identity=idb_sb[:]))
            P.wait("v", "ts")
            P.wait("v", "gd")
            P.step("v", lambda v, pv=pview: v.tensor_copy(out=lkvt_ev[:], in_=pv[:, 0:512]), selfwait=False)
            P.wait("g", "vs")
            for i in range(4):
                P.dma("g", io["lkvT_d"][512 * tq + 128 * i:512 * tq + 128 * (i + 1), 128 * oc:128 * (oc + 1)],
                      lkvt_ev[:, 128 * i:128 * (i + 1)])
    P.barrier()
    if "C" in stages:
        load_weights("wqT")
    P.barrier()

    if "C" in stages:
        P.step("v", lambda v: v.tensor_reduce(
            out=knacc[:].rearrange("r (one c) -> r one c", one=1),
            in_=knall[:].rearrange("r (t c) -> r c t", t=8), axis=AX.X, op=ALU.max))

    _qts = (c_qts if c_qts is not None else range(NQT)) if "C" in stages else []
    exp_tok = {0: 0, 1: 0}
    for qt in _qts:
        chunks, masked = qtile_chunks(qt)
        nch = len(chunks)
        G = nch // 2
        load_x_group(qt)
        P.waitall("s")
        P.dma("s", cs_sl[:], io["cos_t"][:, 512 * qt:512 * (qt + 1)])
        P.dma("s", sn_sl[:], io["sinp_t"][:, 512 * qt:512 * (qt + 1)])
        P.wait("v", "sd")
        P.wait("g", "sd")
        for oc in range(NOC):
            proj_rope(oc, t2f)
            P.step("v", lambda v, oc=oc: v.tensor_copy(out=q_bf[oc][:], in_=t2f[:]), selfwait=False)
            if dbg_what == "q":
                P.wait("g", "vs")
                P.dma("g", dbg_fm[128 * oc:128 * (oc + 1), 512 * qt:512 * (qt + 1)], t2f[:])
            P.wait("a", "vs")
            P.wait("a", "ts")
            P.inc("a", "as_", 1, lambda a, oc=oc: a.activation(out=sq_bf[:], in_=q_bf[oc][:], func=AF.Square))
            P.wait("t", "as_")
            P.step("t", lambda t: t.matmul(out=ps_sum[0:2, :], lhsT=ind2_sb[:], rhs=sq_bf[:], start=True, stop=True))
            P.wait("v", "ts")
            P.step("v", lambda v, oc=oc: v.tensor_reduce(out=qnacc[:, oc:oc + 1], in_=ps_sum[0:2, :],
                                                         axis=AX.X, op=ALU.max))
            P.wait("t", "vs")
        P.wait("a", "vs")
        P.inc("a", "as_", 1, lambda a: a.activation(out=cprod[:], in_=qnacc[:], func=AF.Ln, bias=epsB[0:2, 0:1]))
        P.inc("a", "as_", 1, lambda a: a.activation(out=negc[:], in_=knacc[:], func=AF.Ln, bias=epsB[0:2, 0:1]))
        P.wait("v", "as_")
        P.step("v", lambda v: v.tensor_add(out=cprod[:], in0=cprod[:], in1=negc[:]))
        P.wait("a", "vs")
        P.inc("a", "as_", 1, lambda a: a.activation(out=negc[:], in_=cprod[:], func=AF.Exp, scale=0.5))
        P.wait("v", "as_")
        P.step("v", lambda v: v.tensor_scalar(out=negc[:], in0=negc[:], scalar1=-0.125, scalar2=None, op0=ALU.mult))
        P.waitall("s")
        P.dma("s", negc16[:].rearrange("p (c r) -> p c r", r=2)[:, :, 0:1], negc[0:1, :].unsqueeze(2),
              allow_slow_non_contiguous=True)
        P.dma("s", negc16[:].rearrange("p (c r) -> p c r", r=2)[:, :, 1:2], negc[1:2, :].unsqueeze(2),
              allow_slow_non_contiguous=True)
        P.wait("s", "sd")
        P.dma("s", bias_all[:], negc16[:].unsqueeze(1).broadcast_to([1, 128, 16]))
        P.wait("a", "sd")

        def load_pair_slices(pair):
            buf = pair % 2
            P.newbatch("s")
            toks = [P.dma("s", lkF[buf][:, 0:128 * nch], io["lkvF_d"][128 * pair:128 * (pair + 1), 0:128 * nch])]
            for c4 in range(0, nch, 4):
                hi = min(c4 + 4, nch)
                toks.append(P.dma("s", lkT[buf][:, 128 * c4:128 * hi].rearrange("p (j c) -> p j c", c=128),
                      io["lkvT_d"][128 * c4:128 * hi, 128 * pair:128 * (pair + 1)]
                      .rearrange("(j p) c -> p j c", p=128)))
            return toks[-1]

        def scores_chunk(pair, cx, c_i):
            buf = pair % 2
            pss = ps_a if c_i % 2 == 0 else ps_b
            stop0 = cx not in masked
            for (base, half) in ((0, 0), (64, 1)):
                P.step("t", lambda t, cx=cx, base=base, half=half, pss=pss, pair=pair, stop0=stop0, buf=buf:
                       t.matmul(out=pss[:, 512 * half:512 * (half + 1)],
                                lhsT=lkF[buf][base:base + 64, 128 * cx:128 * (cx + 1)],
                                rhs=q_bf[pair][base:base + 64, :],
                                start=True, stop=stop0,
                                tile_position=(base, 0)))
                if not stop0:
                    m = masked[cx]
                    P.step("t", lambda t, m=m, half=half, pss=pss:
                           t.matmul(out=pss[:, 512 * half:512 * (half + 1)],
                                    lhsT=tri_sb[:],
                                    rhs=ems_sb[:, 512 * m:512 * (m + 1)],
                                    start=False, stop=True))

        def exp_chunk(pair, c_i):
            sl = c_i % 2
            pss = ps_a if sl == 0 else ps_b
            h0 = 2 * pair
            P.inc("a", "as_", 1, lambda a, h0=h0, sl=sl, pss=pss: a.activation(
                out=pA[:, 512 * sl:512 * (sl + 1)], in_=pss[:, 0:512],
                func=AF.Exp, scale=0.125, bias=bias_all[:, h0:h0 + 1]), selfwait=False)
            P.inc("a", "as_", 1, lambda a, h0=h0, sl=sl, pss=pss: a.activation(
                out=pB[:, 512 * sl:512 * (sl + 1)], in_=pss[:, 512:1024],
                func=AF.Exp, scale=0.125, bias=bias_all[:, h0 + 1:h0 + 2]), selfwait=False)

        def av_chunk(pair, cx, c_i, st, sp):
            buf = pair % 2
            sl = c_i % 2
            P.step("t", lambda t, cx=cx, sl=sl, st=st, sp=sp, buf=buf:
                   t.matmul(out=ps_av[0:64, :], lhsT=lkT[buf][:, 128 * cx:128 * cx + 64],
                            rhs=pA[:, 512 * sl:512 * (sl + 1)],
                            start=st, stop=sp, tile_position=(0, 0), skip_group_check=True))
            P.step("t", lambda t, cx=cx, sl=sl, st=st, sp=sp, buf=buf:
                   t.matmul(out=ps_av[64:128, :], lhsT=lkT[buf][:, 128 * cx + 64:128 * (cx + 1)],
                            rhs=pB[:, 512 * sl:512 * (sl + 1)],
                            start=st, stop=sp, tile_position=(0, 64), skip_group_check=True))
            P.step("t", lambda t, sl=sl, st=st, sp=sp:
                   t.matmul(out=ps_sum[0:1, :], lhsT=ones_bf[:],
                            rhs=pA[:, 512 * sl:512 * (sl + 1)],
                            start=st, stop=sp, tile_position=(0, 0), skip_group_check=True))
            P.step("t", lambda t, sl=sl, st=st, sp=sp:
                   t.matmul(out=ps_sum[64:65, :], lhsT=ones_bf[:],
                            rhs=pB[:, 512 * sl:512 * (sl + 1)],
                            start=st, stop=sp, tile_position=(0, 64), skip_group_check=True))

        if not c_chunks:
            continue
        _pairs = list(c_pairs if c_pairs is not None else range(8))
        ts_end_pair = []
        load_toks = [load_pair_slices(_pairs[0])]
        for pi, pair in enumerate(_pairs):
            if pi + 1 < len(_pairs):
                if pi >= 1:
                    P.wait("s", "ts", ts_end_pair[pi - 1])
                load_toks.append(load_pair_slices(_pairs[pi + 1]))
            P.wait_tok("t", load_toks[pi])
            P.wait("t", "vs")
            P.wait("t", "as_")
            score_tok = {}
            scores_chunk(pair, chunks[0], 0)
            score_tok[0] = P.cnt["ts"]
            for c_i in range(nch):
                P.wait("a", "ts", score_tok[c_i])
                exp_chunk(pair, c_i)
                exp_tok[c_i % 2] = P.cnt["as_"]
                if c_i + 1 < nch:
                    if exp_tok[(c_i + 1) % 2] > 0:
                        P.wait("t", "as_", exp_tok[(c_i + 1) % 2])
                    scores_chunk(pair, chunks[c_i + 1], c_i + 1)
                    score_tok[c_i + 1] = P.cnt["ts"]
                P.wait("t", "as_", exp_tok[c_i % 2])
                av_chunk(pair, chunks[c_i], c_i, st=(c_i == 0), sp=(c_i == nch - 1))
            P.wait("v", "ts")
            P.step("v", lambda v: v.tensor_copy(out=sum_ev[0:1, :], in_=ps_sum[0:1, :]), selfwait=False)
            P.step("v", lambda v: v.tensor_copy(out=sum_ev[64:65, :], in_=ps_sum[64:65, :]), selfwait=False)
            P.step("v", lambda v, pair=pair: v.tensor_copy(out=av_sb[pair][:], in_=ps_av[:]), selfwait=False)
            P.wait("g", "vs")
            P.dma("g", l_all[2 * pair:2 * pair + 1, :], sum_ev[0:1, :])
            P.dma("g", l_all[2 * pair + 1:2 * pair + 2, :], sum_ev[64:65, :])
            P.wait("t", "vs")
            ts_end_pair.append(P.cnt["ts"])
        P.wait("v", "gd")
        P.step("v", lambda v: v.reciprocal(out=rl_all[:], in_=l_all[:]))
        for pair in _pairs:
            P.waitall("s")
            P.dma("s", rl_b[:], rl_all[2 * pair:2 * pair + 2, :].unsqueeze(1).broadcast_to([2, 64, 512]))
            P.wait("v", "sd")
            P.wait("v", "gd")
            P.step("v", lambda v, pair=pair: v.tensor_mul(out=outn_t[:], in0=av_sb[pair][:], in1=rl_b[:]),
                   selfwait=False)
            P.wait("g", "vs")
            P.dma("g", io["outn_d"][128 * pair:128 * (pair + 1), 512 * TPERM[qt]:512 * (TPERM[qt] + 1)], outn_t[:])
            if dbg_what == "outn":
                P.dma("g", dbg_fm[128 * pair:128 * (pair + 1), 512 * TPERM[qt]:512 * (TPERM[qt] + 1)], outn_t[:])
    P.barrier()

    if dbg_what == "outnd" and "D" in stages:
        for dc in range(NOC):
            for tq2 in range(8):
                P.waitall("s")
                P.dma("s", outn_t[:], io["outn_d"][128 * dc:128 * (dc + 1), 512 * tq2:512 * (tq2 + 1)])
                P.wait("v", "sd")
                P.step("v", lambda v: v.tensor_copy(out=sum_ev[:], in_=outn_t[:]))
                P.wait("g", "vs")
                P.dma("g", dbg_fm[128 * dc:128 * (dc + 1), 512 * tq2:512 * (tq2 + 1)], sum_ev[:])
    if "D" in stages:
        load_weights("woT")
    P.barrier()
    d_mm_tok = 0
    d_z_tok = 0
    for tt in (range(NTT) if "D" in stages else []):
        if tt == 0:
            P.waitall("s")
        else:
            P.wait("s", "ts", d_mm_tok)
            P.wait("s", "vs", d_z_tok)
        P.newbatch("s")
        for dc in range(NOC):
            P.dma("s", on_t[dc][:], io["outn_d"][128 * dc:128 * (dc + 1), 128 * tt:128 * (tt + 1)])
        d_ld_tok = P.dma("s", xt_d[:], io["x"][128 * tt:128 * (tt + 1), :])
        if dbg_what == "onread":
            P.wait("v", "sd")
            P.wait("v", "gd")
            for dc in range(NOC):
                P.step("v", lambda v, dc=dc: v.tensor_copy(out=y1t[:, 128 * dc:128 * (dc + 1)], in_=on_t[dc][:]))
            P.wait("g", "vs")
            P.dma("g", io["dbg"][128 * tt:128 * (tt + 1), :], y1t[:])
        P.wait_tok("t", d_ld_tok)
        if d_z_tok > 0:
            P.wait("t", "vs", d_z_tok)
        for half in range(2):
            psh = ps_a[:, 512 * half:512 * (half + 1)]
            for dc in range(NOC):
                P.step("t", lambda t, dc=dc, half=half, psh=psh:
                       t.matmul(out=psh[:], lhsT=on_t[dc][:], rhs=w_bf[dc][:, 512 * half:512 * (half + 1)],
                                start=(dc == 0), stop=False))
            P.step("t", lambda t, half=half, psh=psh:
                   t.matmul(out=psh[:], lhsT=ones_f1[:], rhs=bo_sb[0:1, 512 * half:512 * (half + 1)],
                            start=False, stop=True))
        d_mm_tok = P.cnt["ts"]
        P.wait("v", "ts", d_mm_tok)
        P.wait("v", "gd")
        P.wait("v", "as_")
        for half in range(2):
            P.step("v", lambda v, half=half: v.tensor_add(
                out=z_sb[:, 512 * half:512 * (half + 1)], in0=ps_a[:, 512 * half:512 * (half + 1)],
                in1=xt_d[:, 512 * half:512 * (half + 1)]), selfwait=False)
        d_z_tok = P.cnt["vs"]
        if dbg_what == "z":
            P.wait("g", "vs")
            P.dma("g", io["dbg"][128 * tt:128 * (tt + 1), :], z_sb[:])
        if d_level < 2:
            continue
        P.step("v", lambda v: v.tensor_reduce(out=rsum[:], in_=z_sb[:], axis=AX.X, op=ALU.add))
        P.wait("a", "vs")
        P.inc("a", "as_", 1, lambda a: a.activation(out=zsq[:], in_=z_sb[:], func=AF.Square), selfwait=False)
        P.wait("v", "as_")
        P.step("v", lambda v: v.tensor_reduce(out=ssum[:], in_=zsq[:], axis=AX.X, op=ALU.add))
        P.wait("g", "vs")
        P.wait("g", "as_")
        P.step("g", lambda g: g.tensor_scalar(out=mvec[:], in0=rsum[:], scalar1=1.0 / D, scalar2=None, op0=ALU.mult))
        P.step("g", lambda g: g.tensor_mul(out=rstdv[:], in0=mvec[:], in1=mvec[:]))
        P.step("g", lambda g: g.tensor_scalar(out=varv[:], in0=ssum[:], scalar1=1.0 / D, scalar2=None, op0=ALU.mult))
        P.step("g", lambda g: g.tensor_sub(out=varv[:], in0=varv[:], in1=rstdv[:]))
        P.wait("a", "gs")
        P.inc("a", "as_", 1, lambda a: a.activation(out=rstdv[:], in_=varv[:], func=AF.Ln, bias=epsA[:, 0:1]))
        P.inc("a", "as_", 1, lambda a: a.activation(out=rstdv[:], in_=rstdv[:], func=AF.Exp, scale=-0.5))
        if dbg_what == "ln1":
            P.wait("g", "vs")
            P.wait("g", "as_")
            P.dma("g", io["dbg"][128 * tt:128 * (tt + 1), 0:1], mvec[:], allow_slow_non_contiguous=True)
            P.dma("g", io["dbg"][128 * tt:128 * (tt + 1), 1:2], varv[:], allow_slow_non_contiguous=True)
            P.dma("g", io["dbg"][128 * tt:128 * (tt + 1), 2:3], rstdv[:], allow_slow_non_contiguous=True)
            P.dma("g", io["dbg"][128 * tt:128 * (tt + 1), 3:4], rsum[:], allow_slow_non_contiguous=True)
            P.dma("g", io["dbg"][128 * tt:128 * (tt + 1), 4:5], ssum[:], allow_slow_non_contiguous=True)
        P.wait("v", "as_")
        P.wait("v", "gs")
        P.step("v", lambda v: v.tensor_scalar(out=y1t[:], in0=z_sb[:], scalar1=mvec[:, 0:1],
                                              scalar2=rstdv[:, 0:1], op0=ALU.subtract, op1=ALU.mult),
               selfwait=False)
        P.step("v", lambda v, tt=tt: v.tensor_reduce(out=mxcol[:, tt:tt + 1], in_=y1t[:], axis=AX.X,
                                                     op=ALU.max, apply_absolute_value=True))
        if d_level < 3:
            continue
        P.wait("g", "vs")
        P.dma("g", io["y1_d"][128 * tt:128 * (tt + 1), :], y1t[:])
        if io["dbg_on"] and dbg_what in (None, "y1"):
            P.dma("g", io["dbg"][128 * tt:128 * (tt + 1), :], y1t[:])
    P.step("v", lambda v: v.tensor_reduce(out=mxfin[:], in_=mxcol[:], axis=AX.X, op=ALU.max))
    P.wait("a", "vs")
    P.dma("a", io["mx_d"], mxfin[:])
    P.wait("a", "ad")
    P.dma("a", mxrow[:], io["mx_d"].rearrange("p one -> one p"))
    P.wait("v", "ad")
    P.step("v", lambda v: v.memset(mx8[:], 0.0))
    P.step("v", lambda v: v.tensor_reduce(out=mx8[:, 0:1], in_=mxrow[:], axis=AX.X, op=ALU.max))
    P.wait("g", "vs")
    P.dma("g", io["mx_out"], mx8[:])
    P.wait("g", "gd")


def peer_reduce(nc, P, io, E, sb, name):
    pm = E(sb(f"{name}_pm", [1, 64], F32))
    gmax = E(sb(f"{name}_gmax", [1, 8], F32))
    P.dma("s", pm[:], io["peer_mx"])
    P.wait("v", "sd")
    P.step("v", lambda v: v.tensor_reduce(
        out=gmax[:].rearrange("p (s one) -> p s one", one=1),
        in_=pm[:].rearrange("p (s r) -> p s r", r=8), axis=AX.X, op=ALU.max))
    return gmax


def build_phase1(nc, P, io, ctx, parts='sdxmer'):
    E = ctx.enter_context
    sb = nc.sbuf_tensor
    gmax = peer_reduce(nc, P, io, E, sb, "p1")
    s1v = E(sb("p1_s1", [1, 1], F32))
    rs1v = E(sb("p1_rs1", [1, 1], F32))
    s1_pp = E(sb("p1_s1pp", [128, 1], F32))
    rs1_pp = E(sb("p1_rs1pp", [128, 1], F32))
    P.step("v", lambda v: v.tensor_scalar(out=s1v[:], in0=gmax[:, 0:1], scalar1=1.0 / QMAX,
                                          scalar2=1e-12, op0=ALU.mult, op1=ALU.max))
    P.wait("a", "vs")
    P.inc("a", "as_", 1, lambda a: a.activation(out=rs1v[:], in_=s1v[:], func=AF.Copy))
    P.wait("v", "as_")
    P.step("v", lambda v: v.reciprocal(out=rs1v[:], in_=rs1v[:]))
    P.waitall("s")
    P.dma("s", s1_pp[:], s1v[:].unsqueeze(1).broadcast_to([1, 128, 1]))
    P.dma("s", rs1_pp[:], rs1v[:].unsqueeze(1).broadcast_to([1, 128, 1]))

    nat = E(sb("p1_nat", [128, NI], F32))
    bm = E(sb("p1_bm", [128, 96], F32))
    s_c = E(sb("p1_sc", [128, 96], F32))
    rs_c = E(sb("p1_rsc", [128, 96], F32))
    sT_up = E(sb("p1_sTup", [32, NI], F32))
    rT_up = E(sb("p1_rTup", [32, NI], F32))
    sT_gt = E(sb("p1_sTgt", [32, NI], F32))
    rT_gt = E(sb("p1_rTgt", [32, NI], F32))
    sT_dn = E(sb("p1_sTdn", [96, D], F32))
    rT_dn = E(sb("p1_rTdn", [96, D], F32))
    wt_f = [E(sb(f"p1_wtf{i}", [128, 512], F32)) for i in range(2)]
    sb_b = E(sb("p1_sbb", [128, 512], F32))
    rb_b = E(sb("p1_rbb", [128, 512], F32))
    ti32 = E(sb("p1_ti32", [128, 512], I32))
    tf32 = E(sb("p1_tf32", [128, 512], F32))
    wq_b = E(sb("p1_wqb", [128, 512], BF16))
    ps_t = E(nc.psum_tensor("p1_pst", [128, 512], F32))
    ps_c0 = E(nc.psum_tensor("p1_psc0", [128, 512], F32))
    ps_c1 = E(nc.psum_tensor("p1_psc1", [128, 512], F32))

    def dequant(natname, tname, dqname, K, O, sT, rT):
        nblk = K // 32
        for c in range(O // 128):
            P.waitall("s")
            P.dma("s", nat[:, 0:K], io[natname][128 * c:128 * (c + 1), :])
            P.wait("v", "sd")
            P.step("v", lambda v, K=K, nblk=nblk: v.tensor_reduce(
                out=bm[:, 0:nblk].rearrange("p (b one) -> p b one", one=1),
                in_=nat[:, 0:K].rearrange("p (b w) -> p b w", w=32),
                axis=AX.X, op=ALU.max, apply_absolute_value=True))
            P.wait("g", "vs")
            P.step("g", lambda g, nblk=nblk: g.tensor_scalar(
                out=s_c[:, 0:nblk], in0=bm[:, 0:nblk], scalar1=1.0 / 7.0, scalar2=1e-12,
                op0=ALU.mult, op1=ALU.max))
            P.wait("v", "gs")
            P.step("v", lambda v, nblk=nblk: v.reciprocal(out=rs_c[:, 0:nblk], in_=s_c[:, 0:nblk]))
            P.wait("t", "vs")
            P.wait("t", "gs")
            P.step("t", lambda t, nblk=nblk: t.transpose(out=ps_t[0:nblk, 0:128], in_=s_c[:, 0:nblk],
                                                         identity=idf_sb[:]))
            P.wait("v", "ts")
            P.step("v", lambda v, c=c, nblk=nblk, sT=sT: v.tensor_copy(
                out=sT[0:nblk, 128 * c:128 * (c + 1)], in_=ps_t[0:nblk, 0:128]))
            P.wait("t", "vs")
            P.step("t", lambda t, nblk=nblk: t.transpose(out=ps_t[0:nblk, 0:128], in_=rs_c[:, 0:nblk],
                                                         identity=idf_sb[:]))
            P.wait("v", "ts")
            P.step("v", lambda v, c=c, nblk=nblk, rT=rT: v.tensor_copy(
                out=rT[0:nblk, 128 * c:128 * (c + 1)], in_=ps_t[0:nblk, 0:128]))
            P.wait("t", "vs")
        for hc in range(K // 128):
            for os_ in range(O // 512):
                buf = (hc * (O // 512) + os_) % 2
                P.waitall("s")
                P.dma("s", wt_f[buf][:], io[tname][128 * hc:128 * (hc + 1), 512 * os_:512 * (os_ + 1)])
                P.dma("s", sb_b[:], sT[4 * hc:4 * hc + 4, 512 * os_:512 * (os_ + 1)]
                      .unsqueeze(1).broadcast_to([4, 32, 512]))
                P.dma("s", rb_b[:], rT[4 * hc:4 * hc + 4, 512 * os_:512 * (os_ + 1)]
                      .unsqueeze(1).broadcast_to([4, 32, 512]))
                P.wait("v", "sd")
                P.wait("v", "gd")
                P.step("v", lambda v, buf=buf: v.tensor_mul(out=tf32[:], in0=wt_f[buf][:], in1=rb_b[:]),
                       selfwait=False)
                P.step("v", lambda v: v.tensor_copy(out=ti32[:], in_=tf32[:]), selfwait=False)
                P.wait("v", "gd")
                P.step("v", lambda v: v.tensor_mul(out=wq_b[:], in0=ti32[:], in1=sb_b[:]), selfwait=False)
                P.wait("g", "vs")
                P.dma("g", io[dqname][128 * hc:128 * (hc + 1), 512 * os_:512 * (os_ + 1)], wq_b[:])

    idf_sb = E(sb("p1_idf", [128, 128], F32))
    P.dma("s", idf_sb[:], io["idf"])
    if "d" in parts:
        dequant("w_up", "wupT", "wupT_dq", D, NI, sT_up, rT_up)
        dequant("w_gate", "wgateT", "wgateT_dq", D, NI, sT_gt, rT_gt)
        dequant("w_down", "wdownT", "wdownT_dq", NI, D, sT_dn, rT_dn)
    P.barrier()

    y1t = E(sb("p1_y1t", [128, D], F32))
    xqi = E(sb("p1_xqi", [128, D], I32))
    xqf = E(sb("p1_xqf", [128, D], F32))
    xqb = E(sb("p1_xqb", [128, D], BF16))
    xq_fm = [E(sb(f"p1_xqfm{c}", [128, T], BF16)) for c in range(NOC)]
    idb_sb = E(sb("p1_idb", [128, 128], BF16))
    P.dma("s", idb_sb[:], io["idb"])
    for tt in (range(NTT) if "x" in parts else []):
        P.waitall("s")
        P.dma("s", y1t[:], io["y1_d"][128 * tt:128 * (tt + 1), :])
        P.wait("v", "sd")
        P.wait("v", "gs")
        P.step("v", lambda v: v.tensor_scalar(out=xqf[:], in0=y1t[:], scalar1=rs1_pp[:, 0:1],
                                              scalar2=None, op0=ALU.mult), selfwait=False)
        P.step("v", lambda v: v.tensor_copy(out=xqi[:], in_=xqf[:]), selfwait=False)
        P.wait("v", "ts")
        P.step("v", lambda v: v.tensor_scalar(out=xqb[:], in0=xqi[:], scalar1=s1_pp[:, 0:1],
                                              scalar2=None, op0=ALU.mult), selfwait=False)
        P.wait("t", "vs")
        for hc in range(NOC):
            bank = hc % 2
            pview = (ps_c0 if bank == 0 else ps_c1)[:, 0:128].bitcast(BF16)
            P.wait("t", "vs")
            P.step("t", lambda t, hc=hc, pv=pview:
                   t.transpose(out=pv[:, 0:128], in_=xqb[:, 128 * hc:128 * (hc + 1)], identity=idb_sb[:]))
            P.wait("v", "ts")
            P.step("v", lambda v, hc=hc, tt=tt, pv=pview:
                   v.tensor_copy(out=xq_fm[hc][:, 128 * tt:128 * (tt + 1)], in_=pv[:, 0:128]),
                   selfwait=False)
    P.barrier()

    wq_l = [[E(sb(f"p1_wql{b}_{h}", [128, 128], BF16)) for h in range(NOC)] for b in range(2)]
    upt = [E(sb(f"p1_upt{b}", [128, 512], F32)) for b in range(2)]
    ucol = E(sb("p1_ucol", [128, 2 * NIC * NQT], F32))
    gcol = E(sb("p1_gcol", [128, NIC * NQT], F32))
    mxf = E(sb("p1_mxf", [128, 4], F32))
    mxrow = E(sb("p1_mxrow", [1, 128], F32))
    mx8 = E(sb("p1_mx8", [1, 8], F32))
    _wl = ()
    if "m" in parts:
        _wl = ((0, "wupT_dq", "up_d"), (1, "wgateT_dq", "gate_d"))
    elif "G" in parts:
        _wl = ((1, "wgateT_dq", "gate_d"),)
    elif "u" in parts:
        _wl = ((0, "wupT_dq", "up_d"),)
    _iters = [(wi, dqname, dram, oc) for (wi, dqname, dram) in _wl
              for oc in range(4 if "u4" in parts else NIC)]
    mm_tok = {}
    evac_tok = {0: 0, 1: 0}
    vred_tok = {0: 0, 1: 0}
    gdma_tok = {0: None, 1: None}
    for k, (wi, dqname, dram, oc) in enumerate(_iters):
        wb = k % 2
        if k - 2 in mm_tok:
            P.wait("s", "ts", mm_tok[k - 2])
        P.newbatch("s")
        ld_tok = None
        for hc in range(NOC):
            ld_tok = P.dma("s", wq_l[wb][hc][:],
                           io[dqname][128 * hc:128 * (hc + 1), 128 * oc:128 * (oc + 1)])
        P.wait_tok("t", ld_tok)
        for tc in range(NQT):
            pb = tc % 2
            ps = ps_c0 if pb == 0 else ps_c1
            if evac_tok[pb] > 0:
                P.wait("t", "as_", evac_tok[pb])
            for hc in range(NOC):
                P.step("t", lambda t, hc=hc, tc=tc, ps=ps, wb=wb:
                       t.matmul(out=ps[:], lhsT=wq_l[wb][hc][:], rhs=xq_fm[hc][:, 512 * tc:512 * (tc + 1)],
                                start=(hc == 0), stop=(hc == NOC - 1)))
            P.wait("a", "ts", P.cnt["ts"])
            if vred_tok[pb] > 0:
                P.wait("a", "vs", vred_tok[pb])
            if gdma_tok[pb] is not None:
                P.wait_tok("a", gdma_tok[pb])
            P.inc("a", "as_", 1, lambda a, ps=ps, pb=pb:
                  a.activation(out=upt[pb][:], in_=ps[:], func=AF.Copy), selfwait=False)
            evac_tok[pb] = P.cnt["as_"]
            P.wait("g", "as_", evac_tok[pb])
            gdma_tok[pb] = P.dma("g", io[dram][128 * oc:128 * (oc + 1), 512 * tc:512 * (tc + 1)],
                                 upt[pb][:])
            idx = oc * NQT + tc
            P.wait("v", "as_", evac_tok[pb])
            P.step("v", lambda v, idx=idx, wi=wi, pb=pb: v.tensor_reduce(
                out=ucol[:, (wi * NIC * NQT + idx):(wi * NIC * NQT + idx) + 1],
                in_=upt[pb][:], axis=AX.X, op=ALU.max, apply_absolute_value=True))
            if wi == 1:
                P.step("v", lambda v, idx=idx, pb=pb: v.tensor_reduce(
                    out=gcol[:, idx:idx + 1], in_=upt[pb][:], axis=AX.X, op=ALU.max))
            vred_tok[pb] = P.cnt["vs"]
        mm_tok[k] = P.cnt["ts"]
    P.barrier()
    P.step("v", lambda v: v.tensor_reduce(out=mxf[:, 0:1], in_=ucol[:, 0:NIC * NQT], axis=AX.X, op=ALU.max))
    P.step("v", lambda v: v.tensor_reduce(out=mxf[:, 1:2], in_=ucol[:, NIC * NQT:], axis=AX.X, op=ALU.max))
    P.step("v", lambda v: v.tensor_reduce(out=mxf[:, 2:3], in_=gcol[:], axis=AX.X, op=ALU.max))
    P.wait("a", "vs")
    P.dma("a", io["mx_d"].rearrange("p one -> p one"), mxf[:, 0:1])
    P.wait("a", "ad")
    P.dma("a", mxrow[:], io["mx_d"].rearrange("p one -> one p"))
    P.wait("v", "ad")
    P.step("v", lambda v: v.memset(mx8[:], 0.0))
    P.step("v", lambda v: v.tensor_reduce(out=mx8[:, 1:2], in_=mxrow[:], axis=AX.X, op=ALU.max))
    P.wait("a", "vs")
    P.dma("a", io["mx_d"], mxf[:, 1:2])
    P.wait("a", "ad")
    P.dma("a", mxrow[:], io["mx_d"].rearrange("p one -> one p"))
    P.wait("v", "ad")
    P.step("v", lambda v: v.tensor_reduce(out=mx8[:, 2:3], in_=mxrow[:], axis=AX.X, op=ALU.max))
    P.wait("a", "vs")
    P.dma("a", io["mx_d"], mxf[:, 2:3])
    P.wait("a", "ad")
    P.dma("a", mxrow[:], io["mx_d"].rearrange("p one -> one p"))
    P.wait("v", "ad")
    P.step("v", lambda v: v.tensor_reduce(out=mx8[:, 3:4], in_=mxrow[:], axis=AX.X, op=ALU.max))
    P.wait("g", "vs")
    P.dma("g", io["mx_out"], mx8[:])
    P.wait("g", "gd")


def build_phase2(nc, P, io, ctx):
    E = ctx.enter_context
    sb = nc.sbuf_tensor
    gmax = peer_reduce(nc, P, io, E, sb, "p2")
    scl = E(sb("p2_scl", [1, 8], F32))
    rscl = E(sb("p2_rscl", [1, 8], F32))
    t11a = E(sb("p2_t11a", [1, 1], F32))
    t11b = E(sb("p2_t11b", [1, 1], I32))
    t11c = E(sb("p2_t11c", [1, 1], I32))
    t11d = E(sb("p2_t11d", [1, 1], F32))
    t11e = E(sb("p2_t11e", [1, 1], F32))
    sb16 = E(sb("p2_sb16", [1, 1], F32))
    scraw = E(sb("p2_scraw", [1, 1], F32))
    scv = E(sb("p2_scv", [1, 1], F32))
    rscv = E(sb("p2_rscv", [1, 1], F32))
    sbrc = E(sb("p2_sbrc", [1, 1], F32))
    csa = E(sb("p2_csa", [1, 1], F32))
    rsa_pp = E(sb("p2_rsapp", [128, 1], F32))
    rsb_pp = E(sb("p2_rsbpp", [128, 1], F32))
    sb_pp = E(sb("p2_sbpp", [128, 1], F32))
    sbrc_pp = E(sb("p2_sbrcpp", [128, 1], F32))
    P.step("v", lambda v: v.tensor_scalar(out=scl[:], in0=gmax[:], scalar1=1.0 / QMAX,
                                          scalar2=1e-12, op0=ALU.mult, op1=ALU.max))
    P.wait("a", "vs")
    P.inc("a", "as_", 1, lambda a: a.activation(out=rscl[:], in_=scl[:], func=AF.Copy))
    P.wait("v", "as_")
    P.step("v", lambda v: v.reciprocal(out=rscl[:], in_=rscl[:]))
    P.wait("a", "vs")
    P.inc("a", "as_", 1, lambda a: a.activation(out=t11a[:], in_=gmax[:, 3:4], func=AF.Copy))
    P.wait("v", "as_")
    P.step("v", lambda v: v.tensor_scalar(out=t11d[:], in0=t11a[:], scalar1=rscl[0:1, 2:3],
                                          scalar2=None, op0=ALU.mult))
    P.step("v", lambda v: v.tensor_copy(out=t11b[:], in_=t11d[:]))
    P.wait("a", "vs")
    P.inc("a", "as_", 1, lambda a: a.activation(out=t11d[:], in_=t11b[:], func=AF.Sigmoid,
                                                scale=scl[0:1, 2:3]))
    P.wait("v", "as_")
    P.step("v", lambda v: v.tensor_scalar(out=t11e[:], in0=t11d[:], scalar1=65536.0,
                                          scalar2=None, op0=ALU.mult))
    P.step("v", lambda v: v.tensor_copy(out=t11c[:], in_=t11e[:]))
    P.wait("g", "vs")
    P.step("g", lambda g: g.tensor_scalar(out=t11c[:], in0=t11c[:], scalar1=65535.0,
                                          scalar2=None, op0=ALU.min)) if False else None
    P.wait("a", "vs")
    P.inc("a", "as_", 1, lambda a: a.activation(out=t11d[:], in_=t11c[:], func=AF.Copy))
    P.wait("v", "as_")
    P.step("v", lambda v: v.tensor_scalar(out=t11c[:], in0=t11d[:], scalar1=65535.0,
                                          scalar2=None, op0=ALU.min))
    P.step("v", lambda v: v.tensor_mul(out=scraw[:], in0=t11b[:], in1=t11c[:]))
    P.step("v", lambda v: v.tensor_scalar(out=sb16[:], in0=scl[0:1, 2:3], scalar1=1.0 / 65536.0,
                                          scalar2=None, op0=ALU.mult))
    P.wait("a", "vs")
    P.inc("a", "as_", 1, lambda a: a.activation(out=scraw[:], in_=scraw[:], func=AF.Copy,
                                                scale=sb16[0:1, 0:1]))
    P.wait("v", "as_")
    P.step("v", lambda v: v.tensor_scalar(out=scv[:], in0=scraw[:], scalar1=1.0 / QMAX,
                                          scalar2=1e-12, op0=ALU.mult, op1=ALU.max))
    P.wait("a", "vs")
    P.inc("a", "as_", 1, lambda a: a.activation(out=rscv[:], in_=scv[:], func=AF.Copy))
    P.wait("v", "as_")
    P.step("v", lambda v: v.reciprocal(out=rscv[:], in_=rscv[:]))
    P.step("v", lambda v: v.tensor_scalar(out=sbrc[:], in0=rscv[:], scalar1=sb16[0:1, 0:1],
                                          scalar2=None, op0=ALU.mult))
    P.step("v", lambda v: v.tensor_scalar(out=csa[:], in0=scv[:], scalar1=scl[0:1, 1:2],
                                          scalar2=None, op0=ALU.mult))
    scrow = E(sb("p2_scrow", [1, 16], F32))
    P.step("v", lambda v: v.memset(scrow[:], 0.0))
    P.step("v", lambda v: v.tensor_copy(out=scrow[:, 0:1], in_=scl[:, 1:2]))
    P.step("v", lambda v: v.tensor_copy(out=scrow[:, 1:2], in_=scl[:, 2:3]))
    P.step("v", lambda v: v.tensor_copy(out=scrow[:, 2:3], in_=scv[:]))
    P.step("v", lambda v: v.tensor_copy(out=scrow[:, 3:4], in_=csa[:]))
    P.step("v", lambda v: v.tensor_copy(out=scrow[:, 4:5], in_=sbrc[:]))
    P.wait("g", "vs")
    P.dma("g", io["scal_d"], scrow[:])
    P.waitall("s")
    P.dma("s", rsa_pp[:], rscl[:, 1:2].unsqueeze(1).broadcast_to([1, 128, 1]))
    P.dma("s", rsb_pp[:], rscl[:, 2:3].unsqueeze(1).broadcast_to([1, 128, 1]))
    P.dma("s", sb_pp[:], scl[:, 2:3].unsqueeze(1).broadcast_to([1, 128, 1]))
    P.dma("s", sbrc_pp[:], sbrc[:].unsqueeze(1).broadcast_to([1, 128, 1]))
    P.barrier()

    up_t = [E(sb(f"p2_upt{i}", [128, 512], F32)) for i in range(2)]
    gt_t = [E(sb(f"p2_gtt{i}", [128, 512], F32)) for i in range(2)]
    upi = E(sb("p2_upi", [128, 512], I32))
    tmpf = E(sb("p2_tmpf", [128, 512], F32))
    gti = E(sb("p2_gti", [128, 512], I32))
    sig = E(sb("p2_sig", [128, 512], F32))
    sqi = E(sb("p2_sqi", [128, 512], I32))
    sq2 = E(sb("p2_sq2", [128, 512], I32))
    g2p = E(sb("p2_g2p", [128, 512], F32))
    g2i = E(sb("p2_g2i", [128, 512], I32))
    opr = E(sb("p2_opr", [128, 512], F32))
    ocol = E(sb("p2_ocol", [128, NIC * NQT], F32))
    mxrow = E(sb("p2_mxrow", [1, 128], F32))
    mx8 = E(sb("p2_mx8", [1, 8], F32))
    mxf = E(sb("p2_mxf", [128, 1], F32))
    ntile = NIC * NQT
    rd_tok = {0: 0, 1: 0}
    for idx in range(ntile):
        oc, tc = divmod(idx, NQT)
        buf = idx % 2
        if rd_tok[buf] > 0:
            P.wait("s", "vs", rd_tok[buf])
        P.newbatch("s")
        P.dma("s", up_t[buf][:], io["up_d"][128 * oc:128 * (oc + 1), 512 * tc:512 * (tc + 1)])
        ld_tok = P.dma("s", gt_t[buf][:], io["gate_d"][128 * oc:128 * (oc + 1), 512 * tc:512 * (tc + 1)])
        P.wait_tok("v", ld_tok)
        P.wait("v", "as_")
        P.step("v", lambda v, buf=buf: v.tensor_scalar(out=tmpf[:], in0=up_t[buf][:], scalar1=rsa_pp[:, 0:1],
                                                       scalar2=None, op0=ALU.mult), selfwait=False)
        P.step("v", lambda v: v.tensor_copy(out=upi[:], in_=tmpf[:]), selfwait=False)
        P.step("v", lambda v, buf=buf: v.tensor_scalar(out=tmpf[:], in0=gt_t[buf][:], scalar1=rsb_pp[:, 0:1],
                                                       scalar2=None, op0=ALU.mult), selfwait=False)
        P.step("v", lambda v: v.tensor_copy(out=gti[:], in_=tmpf[:]), selfwait=False)
        rd_tok[buf] = P.cnt["vs"]
        P.wait("a", "vs")
        P.inc("a", "as_", 1, lambda a: a.activation(out=sig[:], in_=gti[:], func=AF.Sigmoid,
                                                    scale=sb_pp[:, 0:1]), selfwait=False)
        P.wait("v", "as_")
        P.step("v", lambda v: v.tensor_scalar(out=tmpf[:], in0=sig[:], scalar1=65536.0,
                                              scalar2=None, op0=ALU.mult), selfwait=False)
        P.step("v", lambda v: v.tensor_copy(out=sqi[:], in_=tmpf[:]), selfwait=False)
        P.step("v", lambda v: v.scalar_tensor_tensor(out=g2p[:], in0=sqi[:], scalar=65535.0,
                                                     in1=gti[:], op0=ALU.min, op1=ALU.mult),
               selfwait=False)
        P.step("v", lambda v: v.tensor_scalar(out=tmpf[:], in0=g2p[:], scalar1=sbrc_pp[:, 0:1],
                                              scalar2=None, op0=ALU.mult), selfwait=False)
        P.step("v", lambda v: v.tensor_copy(out=g2i[:], in_=tmpf[:]), selfwait=False)
        P.wait("v", "ad")
        P.step("v", lambda v: v.tensor_mul(out=opr[:], in0=g2i[:], in1=upi[:]), selfwait=False)
        P.step("v", lambda v, idx=idx: v.tensor_reduce(out=ocol[:, idx:idx + 1], in_=opr[:],
                                                       axis=AX.X, op=ALU.max, apply_absolute_value=True))
        P.wait("a", "vs")
        P.wait("a", "ad")
        P.dma("a", io["o_d"][128 * oc:128 * (oc + 1), 512 * tc:512 * (tc + 1)], opr[:])
    P.barrier()
    P.step("v", lambda v: v.tensor_reduce(out=mxf[:], in_=ocol[:], axis=AX.X, op=ALU.max))
    P.wait("a", "vs")
    P.dma("a", io["mx_d"], mxf[:])
    P.wait("a", "ad")
    P.dma("a", mxrow[:], io["mx_d"].rearrange("p one -> one p"))
    P.wait("v", "ad")
    P.step("v", lambda v: v.memset(mx8[:], 0.0))
    P.step("v", lambda v: v.tensor_reduce(out=mx8[:, 4:5], in_=mxrow[:], axis=AX.X, op=ALU.max))
    P.wait("g", "vs")
    P.dma("g", io["mx_out"], mx8[:])
    P.wait("g", "gd")


def build_phase3(nc, P, io, ctx, dbg_what=None):
    E = ctx.enter_context
    sb = nc.sbuf_tensor
    gmax = peer_reduce(nc, P, io, E, sb, "p3")
    scrow = E(sb("p3_scrow", [1, 16], F32))
    sdv = E(sb("p3_sdv", [1, 1], F32))
    rsdv = E(sb("p3_rsdv", [1, 1], F32))
    crd = E(sb("p3_crd", [1, 1], F32))
    crd_pp = E(sb("p3_crdpp", [128, 1], F32))
    sd_pp = E(sb("p3_sdpp", [128, 1], F32))
    P.wait("s", "vs")
    P.dma("s", scrow[:], io["scal_d"])
    P.wait("v", "sd")
    P.step("v", lambda v: v.tensor_scalar(out=sdv[:], in0=gmax[:, 4:5], scalar1=scrow[0:1, 3:4],
                                          scalar2=None, op0=ALU.mult))
    P.wait("a", "vs")
    P.inc("a", "as_", 1, lambda a: a.activation(out=rsdv[:], in_=sdv[:], func=AF.Copy))
    P.wait("v", "as_")
    P.step("v", lambda v: v.tensor_scalar(out=sdv[:], in0=rsdv[:], scalar1=1.0 / QMAX,
                                          scalar2=1e-12, op0=ALU.mult, op1=ALU.max))
    P.wait("a", "vs")
    P.inc("a", "as_", 1, lambda a: a.activation(out=rsdv[:], in_=sdv[:], func=AF.Copy))
    P.wait("v", "as_")
    P.step("v", lambda v: v.reciprocal(out=rsdv[:], in_=rsdv[:]))
    P.step("v", lambda v: v.tensor_scalar(out=crd[:], in0=rsdv[:], scalar1=scrow[0:1, 3:4],
                                          scalar2=None, op0=ALU.mult))
    P.waitall("s")
    P.dma("s", crd_pp[:], crd[:].unsqueeze(1).broadcast_to([1, 128, 1]))
    P.dma("s", sd_pp[:], sdv[:].unsqueeze(1).broadcast_to([1, 128, 1]))
    P.barrier()

    opr = [E(sb(f"p3_opr{i}", [128, 512], F32)) for i in range(2)]
    oqi = E(sb("p3_oqi", [128, 512], I32))
    oqf = E(sb("p3_oqf", [128, 512], F32))
    oqb = E(sb("p3_oqb", [128, 512], BF16))
    oq_tok = {0: 0, 1: 0}
    for idx in range(NIC * NQT):
        oc, tc = divmod(idx, NQT)
        buf = idx % 2
        if oq_tok[buf] > 0:
            P.wait("s", "vs", oq_tok[buf])
        P.newbatch("s")
        ld_tok = P.dma("s", opr[buf][:], io["o_d"][128 * oc:128 * (oc + 1), 512 * tc:512 * (tc + 1)])
        P.wait_tok("v", ld_tok)
        P.wait("v", "gd")
        P.step("v", lambda v, buf=buf: v.tensor_scalar(out=oqf[:], in0=opr[buf][:], scalar1=crd_pp[:, 0:1],
                                                       scalar2=None, op0=ALU.mult), selfwait=False)
        oq_tok[buf] = P.cnt["vs"]
        P.step("v", lambda v: v.tensor_copy(out=oqi[:], in_=oqf[:]), selfwait=False)
        P.step("v", lambda v: v.tensor_scalar(out=oqb[:], in0=oqi[:], scalar1=sd_pp[:, 0:1],
                                              scalar2=None, op0=ALU.mult), selfwait=False)
        P.wait("g", "vs")
        P.dma("g", io["gate_d"][128 * oc:128 * (oc + 1), 512 * tc:512 * (tc + 1)].bitcast(BF16)[:, 0:512],
              oqb[:])
    P.barrier()

    wd_l = [E(sb(f"p3_wdl{c}", [128, D], BF16)) for c in range(NIC)]
    wd_tok = None
    for ic in range(NIC):
        wd_tok = P.dma("s", wd_l[ic][:], io["wdownT_dq"][128 * ic:128 * (ic + 1), :])
    on_t = [E(sb(f"p3_ont{c}", [128, 128], BF16)) for c in range(NIC)]
    y1t = E(sb("p3_y1t", [128, D], F32))
    z_sb = E(sb("p3_z", [128, D], F32))
    zsq = E(sb("p3_zsq", [128, D], F32))
    rsum = E(sb("p3_rsum", [128, 1], F32))
    mvec = E(sb("p3_mvec", [128, 1], F32))
    ssum = E(sb("p3_ssum", [128, 1], F32))
    varv = E(sb("p3_varv", [128, 1], F32))
    rstdv = E(sb("p3_rstdv", [128, 1], F32))
    y2t = E(sb("p3_y2t", [128, D], F32))
    epsA = E(sb("p3_epsA", [128, 1], F32))
    P.step("v", lambda v: v.memset(epsA[:], EPS))
    ps_a = E(nc.psum_tensor("p3_psa", [128, 1024], F32))
    P.wait_tok("t", wd_tok)
    d_mm_tok = 0
    d_z_tok = 0
    for tt in range(NTT):
        if tt == 0:
            P.waitall("s")
        else:
            P.wait("s", "ts", d_mm_tok)
            P.wait("s", "vs", d_z_tok)
        P.newbatch("s")
        for ic in range(NIC):
            P.dma("s", on_t[ic][:],
                  io["gate_d"][128 * ic:128 * (ic + 1), 512 * (tt // 4):512 * (tt // 4) + 512]
                  .bitcast(BF16)[:, 128 * (tt % 4):128 * (tt % 4) + 128])
        d_ld_tok = P.dma("s", y1t[:], io["y1_d"][128 * tt:128 * (tt + 1), :])
        P.wait_tok("t", d_ld_tok)
        if d_z_tok > 0:
            P.wait("t", "vs", d_z_tok)
        for half in range(2):
            psh = ps_a[:, 512 * half:512 * (half + 1)]
            for ic in range(NIC):
                P.step("t", lambda t, ic=ic, half=half, psh=psh:
                       t.matmul(out=psh[:], lhsT=on_t[ic][:], rhs=wd_l[ic][:, 512 * half:512 * (half + 1)],
                                start=(ic == 0), stop=(ic == NIC - 1)))
        d_mm_tok = P.cnt["ts"]
        P.wait("v", "ts", d_mm_tok)
        P.wait("v", "gd")
        P.wait("v", "as_")
        for half in range(2):
            P.step("v", lambda v, half=half: v.tensor_add(
                out=z_sb[:, 512 * half:512 * (half + 1)], in0=ps_a[:, 512 * half:512 * (half + 1)],
                in1=y1t[:, 512 * half:512 * (half + 1)]), selfwait=False)
        d_z_tok = P.cnt["vs"]
        P.step("v", lambda v: v.tensor_reduce(out=rsum[:], in_=z_sb[:], axis=AX.X, op=ALU.add))
        P.wait("a", "vs")
        P.inc("a", "as_", 1, lambda a: a.activation(out=zsq[:], in_=z_sb[:], func=AF.Square), selfwait=False)
        P.wait("v", "as_")
        P.step("v", lambda v: v.tensor_reduce(out=ssum[:], in_=zsq[:], axis=AX.X, op=ALU.add))
        P.wait("g", "vs")
        P.wait("g", "as_")
        P.step("g", lambda g: g.tensor_scalar(out=mvec[:], in0=rsum[:], scalar1=1.0 / D, scalar2=None, op0=ALU.mult))
        P.step("g", lambda g: g.tensor_mul(out=rstdv[:], in0=mvec[:], in1=mvec[:]))
        P.step("g", lambda g: g.tensor_scalar(out=varv[:], in0=ssum[:], scalar1=1.0 / D, scalar2=None, op0=ALU.mult))
        P.step("g", lambda g: g.tensor_sub(out=varv[:], in0=varv[:], in1=rstdv[:]))
        P.wait("a", "gs")
        P.inc("a", "as_", 1, lambda a: a.activation(out=rstdv[:], in_=varv[:], func=AF.Ln, bias=epsA[:, 0:1]))
        P.inc("a", "as_", 1, lambda a: a.activation(out=rstdv[:], in_=rstdv[:], func=AF.Exp, scale=-0.5))
        P.wait("v", "as_")
        P.wait("v", "gs")
        P.step("v", lambda v: v.tensor_scalar(out=y2t[:], in0=z_sb[:], scalar1=mvec[:, 0:1],
                                              scalar2=rstdv[:, 0:1], op0=ALU.subtract, op1=ALU.mult))
        P.wait("g", "vs")
        P.dma("g", io["out"][128 * tt:128 * (tt + 1), :], y2t[:])
    P.wait("g", "gd")


ALLSEMS2 = ALLSEMS + ["ph"]


def build_full(dbg=False, only=None, p1_parts="sdxmer"):
    nc = bass.Bass("TRN2", target_bir_lowering=False)
    io = declare_io(nc, dbg=dbg)
    plans = []
    from contextlib import ExitStack
    with ExitStack() as octx:
        sems = {k: octx.enter_context(nc.semaphore(k)) for k in ALLSEMS2}
        ph_sb = octx.enter_context(nc.sbuf_tensor("ph_sb", [1, 1], I32))
        builders = [build_phase0,
                    lambda nc, P, io, c: build_phase1(nc, P, io, c, parts=p1_parts),
                    build_phase2, build_phase3]
        for i, bf in enumerate(builders):
            P = Plan()
            if only is None or i == only:
                with ExitStack() as pctx:
                    bf(nc, P, io, pctx)
            plans.append(P)
        block = octx.enter_context(nc.Block())

        def mk_emit(engname):
            def f(eng):
                if engname == "s":
                    eng.dma_start(out=ph_sb[:], in_=io["phase"]).then_inc(sems["ph"], 16)
                eng.wait_ge(sems["ph"], 16)
                p = eng.value_load(ph_sb[0:1, 0:1])
                with nc.If(p == 0):
                    plans[0].emit(engname, eng, sems)
                with nc.Else():
                    with nc.If(p == 1):
                        plans[1].emit(engname, eng, sems)
                    with nc.Else():
                        with nc.If(p == 2):
                            plans[2].emit(engname, eng, sems)
                        with nc.Else():
                            plans[3].emit(engname, eng, sems)
            return f

        block.sync(mk_emit("s"))
        block.tensor(mk_emit("t"))
        block.vector(mk_emit("v"))
        block.scalar(mk_emit("a"))
        block.gpsimd(mk_emit("g"))
    return nc, io


def host_inputs(x_core, Wq, Wkv, Wo, bo, ln_g, ln_b, w_up, w_gate, w_down):
    cos_t, sinp_t = rope_tables()
    tri, ems = mask_consts()
    return {
        "x": np.ascontiguousarray(x_core),
        "wqT": np.ascontiguousarray(Wq.T), "wkvT": np.ascontiguousarray(Wkv.T),
        "woT": np.ascontiguousarray(Wo.T), "bo": np.ascontiguousarray(bo.reshape(1, -1)),
        "g128": np.ascontiguousarray(np.tile(ln_g, 2).reshape(128, 1)),
        "b128": np.ascontiguousarray(np.tile(ln_b, 2).reshape(128, 1)),
        "cos_t": cos_t, "sinp_t": sinp_t, "tri": tri, "ems": ems,
        "ind2": ind2_const(),
        "idb": np.ascontiguousarray(np.eye(128).astype(ml_dtypes.bfloat16)),
        "idf": np.ascontiguousarray(np.eye(128, dtype=np.float32)),
        "w_up": np.ascontiguousarray(w_up), "w_gate": np.ascontiguousarray(w_gate),
        "w_down": np.ascontiguousarray(w_down),
        "wupT": np.ascontiguousarray(w_up.T), "wgateT": np.ascontiguousarray(w_gate.T),
        "wdownT": np.ascontiguousarray(w_down.T),
        "phase": np.array([[0]], np.int32),
        "peer_mx": np.zeros((1, 64), np.float32),
    }
_CACHE = {}


def kernel(L0_attn_x, L0_attn_Wq, L0_attn_Wkv, L0_attn_Wo, L0_attn_bo,
           L0_attn_ln_g, L0_attn_ln_b, L0_ffn_w_gate, L0_ffn_w_up, L0_ffn_w_down):
    from concourse.bass_utils import run_bass_kernel_spmd
    n = 8
    if "nc" not in _CACHE:
        _CACHE["nc"], _CACHE["io"] = build_full(dbg=False)
    nc = _CACHE["nc"]
    x = np.asarray(L0_attn_x)
    base = [host_inputs(x[c], np.asarray(L0_attn_Wq), np.asarray(L0_attn_Wkv),
                        np.asarray(L0_attn_Wo), np.asarray(L0_attn_bo),
                        np.asarray(L0_attn_ln_g), np.asarray(L0_attn_ln_b),
                        np.asarray(L0_ffn_w_up), np.asarray(L0_ffn_w_gate),
                        np.asarray(L0_ffn_w_down)) for c in range(n)]
    peer = np.zeros((1, 64), np.float32)
    slot_of_phase = {0: [0], 1: [1, 2, 3], 2: [4], 3: []}
    outs = None
    for phase in range(4):
        for c in range(n):
            base[c]["phase"] = np.array([[phase]], np.int32)
            base[c]["peer_mx"] = peer.copy()
        res = run_bass_kernel_spmd(nc, base, core_ids=list(range(n)))
        mx = np.stack([np.asarray(res.results[c]["mx_out"])[0] for c in range(n)])
        for s in slot_of_phase[phase]:
            for r in range(n):
                peer[0, s * 8 + r] = mx[r, s]
        outs = res
    return np.stack([np.asarray(outs.results[c]["out"]) for c in range(n)]).astype(np.float32)


# version 05 (final)
# speedup vs baseline: 1.7864x; 1.0278x over previous
import numpy as np
import ml_dtypes
import concourse.bass as bass
import concourse.mybir as mybir

F32 = mybir.dt.float32
BF16 = mybir.dt.bfloat16
I32 = mybir.dt.int32
AF = mybir.ActivationFunctionType
ALU = mybir.AluOpType
AX = mybir.AxisListType

T = 4096
D = 1024
H = 16
DH = 64
CTX = 1536
NI = 3072
NTT = T // 128
NQT = 8
NOC = D // 128
NIC = NI // 128
QMAX = 32767.0
EPS = 1e-5

ENGS = ("s", "t", "v", "a", "g")
KQ = 4
QUEUES = ("sd", "ad", "gd", "vd")
QSEMS = {q: [f"{q}{k}" for k in range(KQ)] for q in QUEUES}
ESEM = {"s": "ss", "t": "ts", "v": "vs", "a": "as_", "g": "gs"}
ENG2Q = {"s": "sd", "a": "ad", "g": "gd", "v": "vd"}
ALLSEMS = [s for q in QUEUES for s in QSEMS[q]] + list(ESEM.values())


class Plan:
    def __init__(self):
        self.cnt = {k: 0 for k in ALLSEMS}
        self.ops = {e: [] for e in ENGS}
        self.qidx = {q: 0 for q in QUEUES}
        self.qgate = {q: False for q in QUEUES}
        self.qwaited = {q: False for q in QUEUES}

    def op(self, eng, fn):
        self.ops[eng].append(("op", fn))

    def inc(self, eng, sem, by, fn, selfwait=None):
        self.cnt[sem] += by
        self.ops[eng].append(("inc", sem, by, fn))
        if selfwait is None:
            selfwait = sem == ESEM.get(eng)
        if selfwait and sem in ESEM.values() and eng != "t":
            self.ops[eng].append(("wait", sem, self.cnt[sem]))
        return self.cnt[sem]

    def step(self, eng, fn, selfwait=True):
        return self.inc(eng, ESEM[eng], 1, fn, selfwait=selfwait)

    def newbatch(self, eng):
        q = ENG2Q[eng]
        self.qidx[q] = (self.qidx[q] + 1) % KQ
        self.qgate[q] = True

    def dma(self, eng, out, in_, **kw):
        q = ENG2Q[eng]
        if self.qwaited[q]:
            self.qidx[q] = (self.qidx[q] + 1) % KQ
            self.qgate[q] = True
            self.qwaited[q] = False
        sem = QSEMS[q][self.qidx[q]]
        if self.qgate[q]:
            if self.cnt[sem] > 0:
                self.ops[eng].append(("wait", sem, self.cnt[sem]))
            self.qgate[q] = False
        self.cnt[sem] += 16
        self.ops[eng].append(("inc", sem, 16,
                              lambda e, o=out, i=in_, kw=kw: e.dma_start(out=o, in_=i, **kw)))
        return (sem, self.cnt[sem])

    def wait(self, eng, sem, val=None):
        if sem in QUEUES:
            self.qwaited[sem] = True
            for s in QSEMS[sem]:
                if self.cnt[s] > 0:
                    self.ops[eng].append(("wait", s, self.cnt[s]))
            return
        v = self.cnt[sem] if val is None else val
        if v <= 0:
            return
        self.ops[eng].append(("wait", sem, v))

    def wait_tok(self, eng, tok):
        sem, v = tok
        for q in QUEUES:
            if sem in QSEMS[q]:
                self.qwaited[q] = True
        self.ops[eng].append(("wait", sem, v))

    def waitall(self, eng):
        for q in QUEUES:
            self.wait(eng, q)
        for e, s in ESEM.items():
            if e != eng:
                self.wait(eng, s)

    def fence(self, eng):
        self.wait(eng, ESEM[eng])

    def barrier(self):
        vals = {}
        for e in ENGS:
            vals[e] = self.inc(e, ESEM[e], 1, lambda eng: eng.nop(), selfwait=False)
        for e in ENGS:
            for o in ENGS:
                if o != e:
                    self.wait(e, ESEM[o], vals[o])
            for q in QUEUES:
                self.wait(e, q)

    def emit(self, eng_name, eng, sems):
        for item in self.ops[eng_name]:
            if item[0] == "op":
                item[1](eng)
            elif item[0] == "inc":
                _, sem, by, fn = item
                fn(eng).then_inc(sems[sem], by)
            else:
                _, sem, v = item
                eng.wait_ge(sems[sem], v)


def rope_tables():
    inv_freq = 1.0 / (10000.0 ** (np.arange(0, DH, 2, dtype=np.float64) / DH))
    freqs = np.arange(T, dtype=np.float64)[:, None] * inv_freq[None, :]
    emb = np.concatenate([freqs, freqs], axis=-1)
    cos = np.cos(emb).astype(np.float32).T
    sin = np.sin(emb).astype(np.float32).T
    sinp = sin.copy()
    sinp[:32, :] *= -1.0
    return (np.ascontiguousarray(np.concatenate([cos, cos], axis=0)),
            np.ascontiguousarray(np.concatenate([sinp, sinp], axis=0)))


def mask_consts():
    tri = np.ascontiguousarray(np.triu(np.ones((128, 128), np.float32)).astype(ml_dtypes.bfloat16))
    ems = np.zeros((4, 128, 512), np.float32)
    for m in range(4):
        for q in range(512):
            j = max(q - 128 * m + 1, 0)
            if j <= 127:
                ems[m, j, q] = -32768.0
    return tri, np.ascontiguousarray(ems.astype(ml_dtypes.bfloat16))


def ind2_const():
    a = np.zeros((128, 2), np.float32)
    a[:64, 0] = 1.0
    a[64:, 1] = 1.0
    return np.ascontiguousarray(a.astype(ml_dtypes.bfloat16))


def qtile_chunks(qt):
    if qt < 3:
        return list(range(12)), {}
    chunks = list(range(4 * qt + 4))
    masked = {4 * qt + m: m for m in range(4)}
    return chunks, masked


TPERM = [5, 6, 7, 0, 1, 2, 3, 4]


def declare_io(nc, dbg=True):
    io = {"dbg_on": dbg}
    d = nc.dram_tensor
    io["x"] = d("x", [T, D], F32, kind="ExternalInput").ap()
    io["wqT"] = d("wqT", [D, D], F32, kind="ExternalInput").ap()
    io["wkvT"] = d("wkvT", [D, D], F32, kind="ExternalInput").ap()
    io["woT"] = d("woT", [D, D], F32, kind="ExternalInput").ap()
    io["bo"] = d("bo", [1, D], F32, kind="ExternalInput").ap()
    io["g128"] = d("g128", [128, 1], F32, kind="ExternalInput").ap()
    io["b128"] = d("b128", [128, 1], F32, kind="ExternalInput").ap()
    io["cos_t"] = d("cos_t", [128, T], F32, kind="ExternalInput").ap()
    io["sinp_t"] = d("sinp_t", [128, T], F32, kind="ExternalInput").ap()
    io["tri"] = d("tri", [128, 128], BF16, kind="ExternalInput").ap()
    io["ems"] = d("ems", [4, 128, 512], BF16, kind="ExternalInput").ap()
    io["ind2"] = d("ind2", [128, 2], BF16, kind="ExternalInput").ap()
    io["idb"] = d("idb", [128, 128], BF16, kind="ExternalInput").ap()
    io["idf"] = d("idf", [128, 128], F32, kind="ExternalInput").ap()
    io["w_up"] = d("w_up", [NI, D], F32, kind="ExternalInput").ap()
    io["w_gate"] = d("w_gate", [NI, D], F32, kind="ExternalInput").ap()
    io["w_down"] = d("w_down", [D, NI], F32, kind="ExternalInput").ap()
    io["wupT"] = d("wupT", [D, NI], F32, kind="ExternalInput").ap()
    io["wgateT"] = d("wgateT", [D, NI], F32, kind="ExternalInput").ap()
    io["wdownT"] = d("wdownT", [NI, D], F32, kind="ExternalInput").ap()
    io["phase"] = d("phase", [1, 1], I32, kind="ExternalInput").ap()
    io["peer_mx"] = d("peer_mx", [1, 64], F32, kind="ExternalInput").ap()
    io["out"] = d("out", [T, D], F32, kind="ExternalOutput").ap()
    io["mx_out"] = d("mx_out", [1, 8], F32, kind="ExternalOutput").ap()
    if dbg:
        io["dbg"] = d("dbg", [T, D], F32, kind="ExternalOutput").ap()
    io["y1_d"] = d("y1_d", [T, D], F32, kind="Internal").ap()
    io["up_d"] = d("up_d", [NI, T], F32, kind="Internal").ap()
    io["gate_d"] = d("gate_d", [NI, T], F32, kind="Internal").ap()
    io["o_d"] = d("o_d", [NI, T], F32, kind="Internal").ap()
    io["outn_d"] = d("outn_d", [D, T], BF16, kind="Internal").ap()
    io["wupT_dq"] = d("wupT_dq", [D, NI], BF16, kind="Internal").ap()
    io["wgateT_dq"] = d("wgateT_dq", [D, NI], BF16, kind="Internal").ap()
    io["wdownT_dq"] = d("wdownT_dq", [NI, D], BF16, kind="Internal").ap()
    io["lkvF_d"] = d("lkvF_d", [D, T], BF16, kind="Internal").ap()
    io["lkvT_d"] = d("lkvT_d", [T, D], BF16, kind="Internal").ap()
    io["stats_d1"] = d("stats_d1", [16, 512], F32, kind="Internal").ap()
    io["stats_d2"] = d("stats_d2", [16, 512], F32, kind="Internal").ap()
    io["mx_d"] = d("mx_d", [128, 1], F32, kind="Internal").ap()
    io["scal_d"] = d("scal_d", [1, 16], F32, kind="Internal").ap()
    return io




def build_phase0(nc, P, io, ctx, stages='ABCD', dbg_what=None, c_qts=None, c_pairs=None, c_chunks=True, d_level=3):
    E = ctx.enter_context
    sb = nc.sbuf_tensor
    w_bf = [E(sb(f"p0_w{c}", [128, D], BF16)) for c in range(NOC)]
    g128_sb = E(sb("p0_g", [128, 1], F32))
    b128_sb = E(sb("p0_b", [128, 1], F32))
    ind2_sb = E(sb("p0_ind2", [128, 2], BF16))
    tri_sb = E(sb("p0_tri", [128, 128], BF16))
    ems_sb = E(sb("p0_ems", [128, 4 * 512], BF16))
    idb_sb = E(sb("p0_idb", [128, 128], BF16))
    ones_bf = E(sb("p0_ones", [128, 1], BF16))
    ones_f1 = E(sb("p0_onesf", [1, 128], F32))
    bo_sb = E(sb("p0_bo", [1, D], F32))
    knacc = E(sb("p0_knacc", [2, 8], F32))
    knall = E(sb("p0_knall", [2, 64], F32))
    tmp21 = E(sb("p0_tmp21", [2, 1], F32))
    xtm_f = [E(sb(f"p0_xtf{i}", [128, D], F32)) for i in range(4)]
    xtm_b = [E(sb(f"p0_xtb{i}", [128, D], BF16)) for i in range(4)]
    x_fm = [E(sb(f"p0_xfm{c}", [128, 512], BF16)) for c in range(NOC)]
    kvraw = E(sb("p0_kvr", [128, 512], F32))
    kvrot = E(sb("p0_kvrot", [128, 512], F32))
    kvc = E(sb("p0_kvc", [128, 512], F32))
    kvs = E(sb("p0_kvs", [128, 512], F32))
    kvrope = [E(sb(f"p0_kvro{c}", [128, 512], F32)) for c in range(NOC)]
    cs_sl = E(sb("p0_cs", [128, 512], F32))
    sn_sl = E(sb("p0_sn", [128, 512], F32))
    sq_bf = E(sb("p0_sq", [128, 512], BF16))
    v_bf = E(sb("p0_vbf", [128, 512], BF16))
    t1f = E(sb("p0_t1", [128, 512], F32))
    t2f = E(sb("p0_t2", [128, 512], F32))
    lkv_t = E(sb("p0_lkvt", [128, 512], BF16))
    lkvt_ev = E(sb("p0_lkvtev", [128, 512], BF16))
    stat2 = E(sb("p0_stat2", [2, 1024], F32))
    sums_p = E(sb("p0_sump", [16, 512], F32))
    ssq_p = E(sb("p0_ssqp", [16, 512], F32))
    m_p = E(sb("p0_mp", [16, 512], F32))
    msq_p = E(sb("p0_msqp", [16, 512], F32))
    var_p = E(sb("p0_varp", [16, 512], F32))
    rstd_p = E(sb("p0_rstdp", [16, 512], F32))
    m_b = E(sb("p0_mb", [128, 512], F32))
    rstd_b = E(sb("p0_rstdb", [128, 512], F32))
    q_bf = [E(sb(f"p0_q{c}", [128, 512], BF16)) for c in range(NOC)]
    qnacc = E(sb("p0_qnacc", [2, 8], F32))
    cprod = E(sb("p0_cprod", [2, 8], F32))
    negc = E(sb("p0_negc", [2, 8], F32))
    negc16 = E(sb("p0_negc16", [1, 16], F32))
    bias_all = E(sb("p0_bias", [128, 16], F32))
    pA = E(sb("p0_pA", [128, 1024], BF16))
    pB = E(sb("p0_pB", [128, 1024], BF16))
    lkF = [E(sb(f"p0_lkF{i}", [128, 4096], BF16)) for i in range(2)]
    lkT = [E(sb(f"p0_lkT{i}", [128, 4096], BF16)) for i in range(2)]
    av_sb = [E(sb(f"p0_av{p}", [128, 512], F32)) for p in range(8)]
    sum_ev = E(sb("p0_sumev", [128, 512], F32))
    l_all = E(sb("p0_lall", [16, 512], F32))
    rl_all = E(sb("p0_rlall", [16, 512], F32))
    rl_b = E(sb("p0_rlb", [128, 512], F32))
    outn_t = E(sb("p0_outn", [128, 512], BF16))
    on_t = [E(sb(f"p0_ont{c}", [128, 128], BF16)) for c in range(NOC)]
    xt_d = E(sb("p0_xtd", [128, D], F32))
    z_sb = E(sb("p0_z", [128, D], F32))
    zsq = E(sb("p0_zsq", [128, D], F32))
    rsum = E(sb("p0_rsum", [128, 1], F32))
    mvec = E(sb("p0_mvec", [128, 1], F32))
    ssum = E(sb("p0_ssum", [128, 1], F32))
    varv = E(sb("p0_varv", [128, 1], F32))
    rstdv = E(sb("p0_rstdv", [128, 1], F32))
    y1t = E(sb("p0_y1t", [128, D], F32))
    mxcol = E(sb("p0_mxcol", [128, NTT], F32))
    mxfin = E(sb("p0_mxfin", [128, 1], F32))
    mxrow = E(sb("p0_mxrow", [1, 128], F32))
    mx8 = E(sb("p0_mx8", [1, 8], F32))
    epsA = E(sb("p0_epsA", [128, 1], F32))
    epsB = E(sb("p0_epsB", [128, 1], F32))
    ps_a = E(nc.psum_tensor("p0_psa", [128, 1024], F32))
    ps_b = E(nc.psum_tensor("p0_psb", [128, 1024], F32))
    ps_av = E(nc.psum_tensor("p0_psav", [128, 512], F32))
    ps_sum = E(nc.psum_tensor("p0_pssum", [128, 512], F32))
    ps_c0 = E(nc.psum_tensor("p0_psc0", [128, 512], F32))
    ps_c1 = E(nc.psum_tensor("p0_psc1", [128, 512], F32))

    dbgv = io.get("dbg")
    dbg_fm = dbgv.rearrange("(r s) c -> r (s c)", s=4) if dbgv is not None else None

    P.step("v", lambda v: v.memset(ones_bf[:], 1.0))
    P.step("v", lambda v: v.memset(epsA[:], EPS))
    P.step("v", lambda v: v.memset(epsB[:], 1e-30))
    P.step("v", lambda v: v.memset(ones_f1[:], 1.0))
    P.step("v", lambda v: v.memset(knacc[:], 0.0))
    P.step("v", lambda v: v.memset(qnacc[:], 0.0))
    for (dst, src) in ((g128_sb, io["g128"]), (b128_sb, io["b128"]),
                       (ind2_sb, io["ind2"]), (tri_sb, io["tri"]),
                       (idb_sb, io["idb"]), (bo_sb, io["bo"])):
        P.dma("s", dst[:], src)
    P.dma("s", ems_sb[:].rearrange("p (m q) -> p m q", m=4), io["ems"].rearrange("m p q -> p m q"))

    def load_weights(name):
        for c in range(NOC):
            P.wait("s", "vs")
            P.newbatch("s")
            P.dma("s", xtm_f[0][:], io[name][128 * c:128 * (c + 1), :])
            P.wait("v", "sd")
            P.step("v", lambda v, c=c: v.tensor_copy(out=w_bf[c][:], in_=xtm_f[0][:]), selfwait=False)

    def load_x_group(tq):
        P.wait("s", "vs")
        P.newbatch("s")
        for i in range(4):
            P.dma("s", xtm_f[i][:], io["x"][512 * tq + 128 * i:512 * tq + 128 * (i + 1), :])
        P.wait("v", "sd")
        P.waitall("v")
        for i in range(4):
            P.step("v", lambda v, i=i: v.tensor_copy(out=xtm_b[i][:], in_=xtm_f[i][:]), selfwait=False)
        for hc in range(NOC):
            bank = hc % 2
            pview = ps_a[:, 512 * bank:512 * (bank + 1)].bitcast(BF16)
            P.wait("t", "vs")
            for i in range(4):
                P.step("t", lambda t, hc=hc, i=i, pv=pview:
                       t.transpose(out=pv[:, 128 * i:128 * (i + 1)],
                                   in_=xtm_b[i][:, 128 * hc:128 * (hc + 1)],
                                   identity=idb_sb[:]))
            P.wait("v", "ts")
            P.step("v", lambda v, hc=hc, pv=pview: v.tensor_copy(out=x_fm[hc][:], in_=pv[:, 0:512]),
                   selfwait=False)
        P.wait("t", "vs")

    def proj_rope(oc, outbuf):
        ps = ps_c0 if oc % 2 == 0 else ps_c1
        P.wait("t", "vs")
        for hc in range(NOC):
            P.step("t", lambda t, hc=hc, oc=oc, ps=ps:
                   t.matmul(out=ps[:], lhsT=w_bf[hc][:, 128 * oc:128 * (oc + 1)],
                            rhs=x_fm[hc][:], start=(hc == 0), stop=(hc == NOC - 1)))
        P.wait("v", "ts")
        P.wait("v", "ad")
        P.step("v", lambda v, ps=ps: v.tensor_copy(out=kvraw[:], in_=ps[:]), selfwait=False)
        P.wait("a", "vs")
        P.wait("a", "gs")
        for (d0, s0) in ((0, 32), (32, 0), (64, 96), (96, 64)):
            P.dma("a", kvrot[d0:d0 + 32, :], kvraw[s0:s0 + 32, :])
        P.wait("g", "ad")
        P.wait("g", "vs")
        P.step("g", lambda g: g.tensor_mul(out=kvs[:], in0=kvrot[:], in1=sn_sl[:]), selfwait=False)
        P.step("v", lambda v: v.tensor_mul(out=kvc[:], in0=kvraw[:], in1=cs_sl[:]), selfwait=False)
        P.wait("v", "gs")
        P.wait("v", "as_")
        P.step("v", lambda v, outbuf=outbuf: v.tensor_add(out=outbuf[:], in0=kvc[:], in1=kvs[:]),
               selfwait=False)

    if "B" in stages:
        load_weights("wkvT")
    P.barrier()

    for tq in (range(NQT) if "B" in stages else []):
        load_x_group(tq)
        P.wait("s", "vs")
        P.wait("s", "gs")
        P.newbatch("s")
        P.dma("s", cs_sl[:], io["cos_t"][:, 512 * tq:512 * (tq + 1)])
        P.dma("s", sn_sl[:], io["sinp_t"][:, 512 * tq:512 * (tq + 1)])
        P.wait("v", "sd")
        P.wait("g", "sd")
        for oc in range(NOC):
            proj_rope(oc, kvrope[oc])
            if dbg_what == "kvrope":
                P.wait("g", "vs")
                P.dma("g", dbg_fm[128 * oc:128 * (oc + 1), 512 * tq:512 * (tq + 1)], kvrope[oc][:])
            P.wait("a", "vs")
            P.wait("a", "ts")
            P.inc("a", "as_", 1, lambda a, oc=oc: a.activation(out=sq_bf[:], in_=kvrope[oc][:], func=AF.Square))
            P.step("v", lambda v, oc=oc: v.tensor_copy(out=v_bf[:], in_=kvrope[oc][:]), selfwait=False)
            P.wait("t", "as_")
            P.wait("t", "vs")
            P.step("t", lambda t: t.matmul(out=ps_sum[0:2, :], lhsT=ind2_sb[:], rhs=v_bf[:], start=True, stop=True))
            P.step("t", lambda t: t.matmul(out=ps_av[0:2, :], lhsT=ind2_sb[:], rhs=sq_bf[:], start=True, stop=True))
            P.wait("v", "ts")
            P.wait("v", "gd")
            P.step("v", lambda v: v.tensor_copy(out=stat2[:, 0:512], in_=ps_sum[0:2, :]), selfwait=False)
            P.step("v", lambda v: v.tensor_copy(out=stat2[:, 512:1024], in_=ps_av[0:2, :]), selfwait=False)
            P.wait("g", "vs")
            P.dma("g", io["stats_d1"][2 * oc:2 * oc + 2, :], stat2[:, 0:512])
            P.dma("g", io["stats_d2"][2 * oc:2 * oc + 2, :], stat2[:, 512:1024])
            P.wait("t", "vs")
        P.wait("s", "gd")
        P.wait("s", "vs")
        P.newbatch("s")
        P.dma("s", sums_p[:], io["stats_d1"])
        P.dma("s", ssq_p[:], io["stats_d2"])
        P.wait("v", "sd")
        P.step("v", lambda v: v.tensor_scalar(out=m_p[:], in0=sums_p[:], scalar1=1.0 / DH, scalar2=None,
                                              op0=ALU.mult), selfwait=False)
        P.step("v", lambda v: v.tensor_mul(out=msq_p[:], in0=m_p[:], in1=m_p[:]), selfwait=False)
        P.step("v", lambda v: v.tensor_scalar(out=var_p[:], in0=ssq_p[:], scalar1=1.0 / DH, scalar2=None,
                                              op0=ALU.mult), selfwait=False)
        P.step("v", lambda v: v.tensor_sub(out=var_p[:], in0=var_p[:], in1=msq_p[:]), selfwait=False)
        P.wait("a", "vs")
        P.inc("a", "as_", 1, lambda a: a.activation(out=rstd_p[:], in_=var_p[:], func=AF.Ln, bias=epsA[0:16, 0:1]))
        P.inc("a", "as_", 1, lambda a: a.activation(out=rstd_p[:], in_=rstd_p[:], func=AF.Exp, scale=-0.5))
        for oc in range(NOC):
            P.wait("s", "vs")
            P.wait("s", "as_")
            P.newbatch("s")
            P.dma("s", m_b[:], m_p[2 * oc:2 * oc + 2, :].unsqueeze(1).broadcast_to([2, 64, 512]))
            P.dma("s", rstd_b[:], rstd_p[2 * oc:2 * oc + 2, :].unsqueeze(1).broadcast_to([2, 64, 512]))
            P.wait("v", "sd")
            P.step("v", lambda v, oc=oc: v.tensor_sub(out=t1f[:], in0=kvrope[oc][:], in1=m_b[:]),
                   selfwait=False)
            P.step("v", lambda v: v.scalar_tensor_tensor(
                out=t2f[:], in0=t1f[:], scalar=g128_sb[:, 0:1], in1=rstd_b[:], op0=ALU.mult, op1=ALU.mult),
                   selfwait=False)
            P.wait("v", "gd")
            P.wait("v", "as_")
            P.step("v", lambda v: v.tensor_scalar(
                out=lkv_t[:], in0=t2f[:], scalar1=b128_sb[:, 0:1], scalar2=None, op0=ALU.add),
                   selfwait=False)
            P.wait("g", "vs")
            P.dma("g", io["lkvF_d"][128 * oc:128 * (oc + 1), 512 * tq:512 * (tq + 1)], lkv_t[:])
            if dbg_what == "lkv":
                P.dma("g", dbg_fm[128 * oc:128 * (oc + 1), 512 * tq:512 * (tq + 1)], lkv_t[:])
            P.wait("a", "vs")
            P.wait("a", "ts")
            P.inc("a", "as_", 1, lambda a: a.activation(out=sq_bf[:], in_=lkv_t[:], func=AF.Square))
            P.wait("t", "as_")
            P.step("t", lambda t: t.matmul(out=ps_sum[0:2, :], lhsT=ind2_sb[:], rhs=sq_bf[:], start=True, stop=True))
            P.wait("v", "ts")
            P.step("v", lambda v, oc=oc, tq=tq: v.tensor_reduce(
                out=knall[:, 8 * tq + oc:8 * tq + oc + 1], in_=ps_sum[0:2, :], axis=AX.X, op=ALU.max))
            bank = oc % 2
            pview = ps_a[:, 512 * bank:512 * (bank + 1)].bitcast(BF16)
            P.wait("t", "vs")
            P.wait("t", "gd")
            for i in range(4):
                P.step("t", lambda t, i=i, pv=pview:
                       t.transpose(out=pv[:, 128 * i:128 * (i + 1)],
                                   in_=lkv_t[:, 128 * i:128 * (i + 1)],
                                   identity=idb_sb[:]))
            P.wait("v", "ts")
            P.wait("v", "gd")
            P.step("v", lambda v, pv=pview: v.tensor_copy(out=lkvt_ev[:], in_=pv[:, 0:512]), selfwait=False)
            P.wait("g", "vs")
            for i in range(4):
                P.dma("g", io["lkvT_d"][512 * tq + 128 * i:512 * tq + 128 * (i + 1), 128 * oc:128 * (oc + 1)],
                      lkvt_ev[:, 128 * i:128 * (i + 1)])
    P.barrier()
    if "C" in stages:
        load_weights("wqT")
    P.barrier()

    if "C" in stages:
        P.step("v", lambda v: v.tensor_reduce(
            out=knacc[:].rearrange("r (one c) -> r one c", one=1),
            in_=knall[:].rearrange("r (t c) -> r c t", t=8), axis=AX.X, op=ALU.max))

    _qts = (c_qts if c_qts is not None else range(NQT)) if "C" in stages else []
    exp_tok = {0: 0, 1: 0}
    for qt in _qts:
        chunks, masked = qtile_chunks(qt)
        nch = len(chunks)
        load_x_group(qt)
        P.wait("s", "vs")
        P.wait("s", "gs")
        P.newbatch("s")
        P.dma("s", cs_sl[:], io["cos_t"][:, 512 * qt:512 * (qt + 1)])
        P.dma("s", sn_sl[:], io["sinp_t"][:, 512 * qt:512 * (qt + 1)])
        P.wait("v", "sd")
        P.wait("g", "sd")
        for oc in range(NOC):
            proj_rope(oc, t2f)
            P.step("v", lambda v, oc=oc: v.tensor_copy(out=q_bf[oc][:], in_=t2f[:]), selfwait=False)
            if dbg_what == "q":
                P.wait("g", "vs")
                P.dma("g", dbg_fm[128 * oc:128 * (oc + 1), 512 * qt:512 * (qt + 1)], t2f[:])
            P.wait("a", "vs")
            P.wait("a", "ts")
            P.inc("a", "as_", 1, lambda a, oc=oc: a.activation(out=sq_bf[:], in_=q_bf[oc][:], func=AF.Square))
            P.wait("t", "as_")
            P.step("t", lambda t: t.matmul(out=ps_sum[0:2, :], lhsT=ind2_sb[:], rhs=sq_bf[:], start=True, stop=True))
            P.wait("v", "ts")
            P.step("v", lambda v, oc=oc: v.tensor_reduce(out=qnacc[:, oc:oc + 1], in_=ps_sum[0:2, :],
                                                         axis=AX.X, op=ALU.max))
            P.wait("t", "vs")
        P.wait("a", "vs")
        P.inc("a", "as_", 1, lambda a: a.activation(out=cprod[:], in_=qnacc[:], func=AF.Ln, bias=epsB[0:2, 0:1]))
        P.inc("a", "as_", 1, lambda a: a.activation(out=negc[:], in_=knacc[:], func=AF.Ln, bias=epsB[0:2, 0:1]))
        P.wait("v", "as_")
        P.step("v", lambda v: v.tensor_add(out=cprod[:], in0=cprod[:], in1=negc[:]))
        P.wait("a", "vs")
        P.inc("a", "as_", 1, lambda a: a.activation(out=negc[:], in_=cprod[:], func=AF.Exp, scale=0.5))
        P.wait("v", "as_")
        P.step("v", lambda v: v.tensor_scalar(out=negc[:], in0=negc[:], scalar1=-0.125, scalar2=None, op0=ALU.mult))
        P.waitall("s")
        P.dma("s", negc16[:].rearrange("p (c r) -> p c r", r=2)[:, :, 0:1], negc[0:1, :].unsqueeze(2),
              allow_slow_non_contiguous=True)
        P.dma("s", negc16[:].rearrange("p (c r) -> p c r", r=2)[:, :, 1:2], negc[1:2, :].unsqueeze(2),
              allow_slow_non_contiguous=True)
        P.wait("s", "sd")
        P.dma("s", bias_all[:], negc16[:].unsqueeze(1).broadcast_to([1, 128, 16]))
        P.wait("a", "sd")

        def load_pair_slices(pair):
            buf = pair % 2
            P.newbatch("s")
            toks = [P.dma("s", lkF[buf][:, 0:128 * nch], io["lkvF_d"][128 * pair:128 * (pair + 1), 0:128 * nch])]
            for c4 in range(0, nch, 4):
                hi = min(c4 + 4, nch)
                toks.append(P.dma("s", lkT[buf][:, 128 * c4:128 * hi].rearrange("p (j c) -> p j c", c=128),
                      io["lkvT_d"][128 * c4:128 * hi, 128 * pair:128 * (pair + 1)]
                      .rearrange("(j p) c -> p j c", p=128)))
            return toks[-1]

        def scores_chunk(pair, cx, c_i):
            buf = pair % 2
            pss = ps_a if c_i % 2 == 0 else ps_b
            stop0 = cx not in masked
            for (base, half) in ((0, 0), (64, 1)):
                P.step("t", lambda t, cx=cx, base=base, half=half, pss=pss, pair=pair, stop0=stop0, buf=buf:
                       t.matmul(out=pss[:, 512 * half:512 * (half + 1)],
                                lhsT=lkF[buf][base:base + 64, 128 * cx:128 * (cx + 1)],
                                rhs=q_bf[pair][base:base + 64, :],
                                start=True, stop=stop0,
                                tile_position=(base, 0)))
                if not stop0:
                    m = masked[cx]
                    P.step("t", lambda t, m=m, half=half, pss=pss:
                           t.matmul(out=pss[:, 512 * half:512 * (half + 1)],
                                    lhsT=tri_sb[:],
                                    rhs=ems_sb[:, 512 * m:512 * (m + 1)],
                                    start=False, stop=True))

        def exp_chunk(pair, c_i):
            sl = c_i % 2
            pss = ps_a if sl == 0 else ps_b
            h0 = 2 * pair
            P.inc("a", "as_", 1, lambda a, h0=h0, sl=sl, pss=pss: a.activation(
                out=pA[:, 512 * sl:512 * (sl + 1)], in_=pss[:, 0:512],
                func=AF.Exp, scale=0.125, bias=bias_all[:, h0:h0 + 1]), selfwait=False)
            P.inc("a", "as_", 1, lambda a, h0=h0, sl=sl, pss=pss: a.activation(
                out=pB[:, 512 * sl:512 * (sl + 1)], in_=pss[:, 512:1024],
                func=AF.Exp, scale=0.125, bias=bias_all[:, h0 + 1:h0 + 2]), selfwait=False)

        def av_chunk(pair, cx, c_i, st, sp):
            buf = pair % 2
            sl = c_i % 2
            P.step("t", lambda t, cx=cx, sl=sl, st=st, sp=sp, buf=buf:
                   t.matmul(out=ps_av[0:64, :], lhsT=lkT[buf][:, 128 * cx:128 * cx + 64],
                            rhs=pA[:, 512 * sl:512 * (sl + 1)],
                            start=st, stop=sp, tile_position=(0, 0), skip_group_check=True))
            P.step("t", lambda t, cx=cx, sl=sl, st=st, sp=sp, buf=buf:
                   t.matmul(out=ps_av[64:128, :], lhsT=lkT[buf][:, 128 * cx + 64:128 * (cx + 1)],
                            rhs=pB[:, 512 * sl:512 * (sl + 1)],
                            start=st, stop=sp, tile_position=(0, 64), skip_group_check=True))
            P.step("t", lambda t, sl=sl, st=st, sp=sp:
                   t.matmul(out=ps_sum[0:1, :], lhsT=ones_bf[:],
                            rhs=pA[:, 512 * sl:512 * (sl + 1)],
                            start=st, stop=sp, tile_position=(0, 0), skip_group_check=True))
            P.step("t", lambda t, sl=sl, st=st, sp=sp:
                   t.matmul(out=ps_sum[64:65, :], lhsT=ones_bf[:],
                            rhs=pB[:, 512 * sl:512 * (sl + 1)],
                            start=st, stop=sp, tile_position=(0, 64), skip_group_check=True))

        if not c_chunks:
            continue
        _pairs = list(c_pairs if c_pairs is not None else range(8))
        ts_end_pair = []
        load_toks = [load_pair_slices(_pairs[0])]
        for pi, pair in enumerate(_pairs):
            if pi + 1 < len(_pairs):
                if pi >= 1:
                    P.wait("s", "ts", ts_end_pair[pi - 1])
                load_toks.append(load_pair_slices(_pairs[pi + 1]))
            P.wait_tok("t", load_toks[pi])
            P.wait("t", "vs")
            P.wait("t", "as_")
            score_tok = {}
            scores_chunk(pair, chunks[0], 0)
            score_tok[0] = P.cnt["ts"]
            for c_i in range(nch):
                P.wait("a", "ts", score_tok[c_i])
                exp_chunk(pair, c_i)
                exp_tok[c_i % 2] = P.cnt["as_"]
                if c_i + 1 < nch:
                    if exp_tok[(c_i + 1) % 2] > 0:
                        P.wait("t", "as_", exp_tok[(c_i + 1) % 2])
                    scores_chunk(pair, chunks[c_i + 1], c_i + 1)
                    score_tok[c_i + 1] = P.cnt["ts"]
                P.wait("t", "as_", exp_tok[c_i % 2])
                av_chunk(pair, chunks[c_i], c_i, st=(c_i == 0), sp=(c_i == nch - 1))
            P.wait("v", "ts")
            P.step("v", lambda v: v.tensor_copy(out=sum_ev[0:1, :], in_=ps_sum[0:1, :]), selfwait=False)
            P.step("v", lambda v: v.tensor_copy(out=sum_ev[64:65, :], in_=ps_sum[64:65, :]), selfwait=False)
            P.step("v", lambda v, pair=pair: v.tensor_copy(out=av_sb[pair][:], in_=ps_av[:]), selfwait=False)
            P.wait("g", "vs")
            P.dma("g", l_all[2 * pair:2 * pair + 1, :], sum_ev[0:1, :])
            P.dma("g", l_all[2 * pair + 1:2 * pair + 2, :], sum_ev[64:65, :])
            P.wait("t", "vs")
            ts_end_pair.append(P.cnt["ts"])
        P.wait("v", "gd")
        P.step("v", lambda v: v.reciprocal(out=rl_all[:], in_=l_all[:]))
        for pair in _pairs:
            P.waitall("s")
            P.dma("s", rl_b[:], rl_all[2 * pair:2 * pair + 2, :].unsqueeze(1).broadcast_to([2, 64, 512]))
            P.wait("v", "sd")
            P.wait("v", "gd")
            P.step("v", lambda v, pair=pair: v.tensor_mul(out=outn_t[:], in0=av_sb[pair][:], in1=rl_b[:]),
                   selfwait=False)
            P.wait("g", "vs")
            P.dma("g", io["outn_d"][128 * pair:128 * (pair + 1), 512 * TPERM[qt]:512 * (TPERM[qt] + 1)], outn_t[:])
            if dbg_what == "outn":
                P.dma("g", dbg_fm[128 * pair:128 * (pair + 1), 512 * TPERM[qt]:512 * (TPERM[qt] + 1)], outn_t[:])
    P.barrier()

    if dbg_what == "outnd" and "D" in stages:
        for dc in range(NOC):
            for tq2 in range(8):
                P.waitall("s")
                P.dma("s", outn_t[:], io["outn_d"][128 * dc:128 * (dc + 1), 512 * tq2:512 * (tq2 + 1)])
                P.wait("v", "sd")
                P.step("v", lambda v: v.tensor_copy(out=sum_ev[:], in_=outn_t[:]))
                P.wait("g", "vs")
                P.dma("g", dbg_fm[128 * dc:128 * (dc + 1), 512 * tq2:512 * (tq2 + 1)], sum_ev[:])
    if "D" in stages:
        load_weights("woT")
    P.barrier()
    d_mm_tok = 0
    d_z_tok = 0
    for tt in (range(NTT) if "D" in stages else []):
        if tt == 0:
            P.waitall("s")
        else:
            P.wait("s", "ts", d_mm_tok)
            P.wait("s", "vs", d_z_tok)
        P.newbatch("s")
        for dc in range(NOC):
            P.dma("s", on_t[dc][:], io["outn_d"][128 * dc:128 * (dc + 1), 128 * tt:128 * (tt + 1)])
        d_ld_tok = P.dma("s", xt_d[:], io["x"][128 * tt:128 * (tt + 1), :])
        if dbg_what == "onread":
            P.wait("v", "sd")
            P.wait("v", "gd")
            for dc in range(NOC):
                P.step("v", lambda v, dc=dc: v.tensor_copy(out=y1t[:, 128 * dc:128 * (dc + 1)], in_=on_t[dc][:]))
            P.wait("g", "vs")
            P.dma("g", io["dbg"][128 * tt:128 * (tt + 1), :], y1t[:])
        P.wait_tok("t", d_ld_tok)
        if d_z_tok > 0:
            P.wait("t", "vs", d_z_tok)
        for half in range(2):
            psh = ps_a[:, 512 * half:512 * (half + 1)]
            for dc in range(NOC):
                P.step("t", lambda t, dc=dc, half=half, psh=psh:
                       t.matmul(out=psh[:], lhsT=on_t[dc][:], rhs=w_bf[dc][:, 512 * half:512 * (half + 1)],
                                start=(dc == 0), stop=False))
            P.step("t", lambda t, half=half, psh=psh:
                   t.matmul(out=psh[:], lhsT=ones_f1[:], rhs=bo_sb[0:1, 512 * half:512 * (half + 1)],
                            start=False, stop=True))
        d_mm_tok = P.cnt["ts"]
        P.wait("v", "ts", d_mm_tok)
        P.wait("v", "gd")
        P.wait("v", "as_")
        for half in range(2):
            P.step("v", lambda v, half=half: v.tensor_add(
                out=z_sb[:, 512 * half:512 * (half + 1)], in0=ps_a[:, 512 * half:512 * (half + 1)],
                in1=xt_d[:, 512 * half:512 * (half + 1)]), selfwait=False)
        d_z_tok = P.cnt["vs"]
        if dbg_what == "z":
            P.wait("g", "vs")
            P.dma("g", io["dbg"][128 * tt:128 * (tt + 1), :], z_sb[:])
        if d_level < 2:
            continue
        P.step("v", lambda v: v.tensor_reduce(out=rsum[:], in_=z_sb[:], axis=AX.X, op=ALU.add))
        P.wait("a", "vs")
        P.inc("a", "as_", 1, lambda a: a.activation(out=zsq[:], in_=z_sb[:], func=AF.Square), selfwait=False)
        P.wait("v", "as_")
        P.step("v", lambda v: v.tensor_reduce(out=ssum[:], in_=zsq[:], axis=AX.X, op=ALU.add))
        P.wait("g", "vs")
        P.wait("g", "as_")
        P.step("g", lambda g: g.tensor_scalar(out=mvec[:], in0=rsum[:], scalar1=1.0 / D, scalar2=None, op0=ALU.mult))
        P.step("g", lambda g: g.tensor_mul(out=rstdv[:], in0=mvec[:], in1=mvec[:]))
        P.step("g", lambda g: g.tensor_scalar(out=varv[:], in0=ssum[:], scalar1=1.0 / D, scalar2=None, op0=ALU.mult))
        P.step("g", lambda g: g.tensor_sub(out=varv[:], in0=varv[:], in1=rstdv[:]))
        P.wait("a", "gs")
        P.inc("a", "as_", 1, lambda a: a.activation(out=rstdv[:], in_=varv[:], func=AF.Ln, bias=epsA[:, 0:1]))
        P.inc("a", "as_", 1, lambda a: a.activation(out=rstdv[:], in_=rstdv[:], func=AF.Exp, scale=-0.5))
        if dbg_what == "ln1":
            P.wait("g", "vs")
            P.wait("g", "as_")
            P.dma("g", io["dbg"][128 * tt:128 * (tt + 1), 0:1], mvec[:], allow_slow_non_contiguous=True)
            P.dma("g", io["dbg"][128 * tt:128 * (tt + 1), 1:2], varv[:], allow_slow_non_contiguous=True)
            P.dma("g", io["dbg"][128 * tt:128 * (tt + 1), 2:3], rstdv[:], allow_slow_non_contiguous=True)
            P.dma("g", io["dbg"][128 * tt:128 * (tt + 1), 3:4], rsum[:], allow_slow_non_contiguous=True)
            P.dma("g", io["dbg"][128 * tt:128 * (tt + 1), 4:5], ssum[:], allow_slow_non_contiguous=True)
        P.wait("v", "as_")
        P.wait("v", "gs")
        P.step("v", lambda v: v.tensor_scalar(out=y1t[:], in0=z_sb[:], scalar1=mvec[:, 0:1],
                                              scalar2=rstdv[:, 0:1], op0=ALU.subtract, op1=ALU.mult),
               selfwait=False)
        P.step("v", lambda v, tt=tt: v.tensor_reduce(out=mxcol[:, tt:tt + 1], in_=y1t[:], axis=AX.X,
                                                     op=ALU.max, apply_absolute_value=True))
        if d_level < 3:
            continue
        P.wait("g", "vs")
        P.dma("g", io["y1_d"][128 * tt:128 * (tt + 1), :], y1t[:])
        if io["dbg_on"] and dbg_what in (None, "y1"):
            P.dma("g", io["dbg"][128 * tt:128 * (tt + 1), :], y1t[:])
    P.step("v", lambda v: v.tensor_reduce(out=mxfin[:], in_=mxcol[:], axis=AX.X, op=ALU.max))
    P.wait("a", "vs")
    P.dma("a", io["mx_d"], mxfin[:])
    P.wait("a", "ad")
    P.dma("a", mxrow[:], io["mx_d"].rearrange("p one -> one p"))
    P.wait("v", "ad")
    P.step("v", lambda v: v.memset(mx8[:], 0.0))
    P.step("v", lambda v: v.tensor_reduce(out=mx8[:, 0:1], in_=mxrow[:], axis=AX.X, op=ALU.max))
    P.wait("g", "vs")
    P.dma("g", io["mx_out"], mx8[:])
    P.wait("g", "gd")


def peer_reduce(nc, P, io, E, sb, name):
    pm = E(sb(f"{name}_pm", [1, 64], F32))
    gmax = E(sb(f"{name}_gmax", [1, 8], F32))
    P.dma("s", pm[:], io["peer_mx"])
    P.wait("v", "sd")
    P.step("v", lambda v: v.tensor_reduce(
        out=gmax[:].rearrange("p (s one) -> p s one", one=1),
        in_=pm[:].rearrange("p (s r) -> p s r", r=8), axis=AX.X, op=ALU.max))
    return gmax


def build_phase1(nc, P, io, ctx, parts='sdxmer'):
    E = ctx.enter_context
    sb = nc.sbuf_tensor
    gmax = peer_reduce(nc, P, io, E, sb, "p1")
    s1v = E(sb("p1_s1", [1, 1], F32))
    rs1v = E(sb("p1_rs1", [1, 1], F32))
    s1_pp = E(sb("p1_s1pp", [128, 1], F32))
    rs1_pp = E(sb("p1_rs1pp", [128, 1], F32))
    P.step("v", lambda v: v.tensor_scalar(out=s1v[:], in0=gmax[:, 0:1], scalar1=1.0 / QMAX,
                                          scalar2=1e-12, op0=ALU.mult, op1=ALU.max))
    P.wait("a", "vs")
    P.inc("a", "as_", 1, lambda a: a.activation(out=rs1v[:], in_=s1v[:], func=AF.Copy))
    P.wait("v", "as_")
    P.step("v", lambda v: v.reciprocal(out=rs1v[:], in_=rs1v[:]))
    P.waitall("s")
    P.dma("s", s1_pp[:], s1v[:].unsqueeze(1).broadcast_to([1, 128, 1]))
    P.dma("s", rs1_pp[:], rs1v[:].unsqueeze(1).broadcast_to([1, 128, 1]))

    nat = E(sb("p1_nat", [128, NI], F32))
    bm = E(sb("p1_bm", [128, 96], F32))
    s_c = E(sb("p1_sc", [128, 96], F32))
    rs_c = E(sb("p1_rsc", [128, 96], F32))
    sT_up = E(sb("p1_sTup", [32, NI], F32))
    rT_up = E(sb("p1_rTup", [32, NI], F32))
    sT_gt = E(sb("p1_sTgt", [32, NI], F32))
    rT_gt = E(sb("p1_rTgt", [32, NI], F32))
    sT_dn = E(sb("p1_sTdn", [96, D], F32))
    rT_dn = E(sb("p1_rTdn", [96, D], F32))
    wt_f = [E(sb(f"p1_wtf{i}", [128, 512], F32)) for i in range(2)]
    sb_b = E(sb("p1_sbb", [128, 512], F32))
    rb_b = E(sb("p1_rbb", [128, 512], F32))
    ti32 = E(sb("p1_ti32", [128, 512], I32))
    tf32 = E(sb("p1_tf32", [128, 512], F32))
    wq_b = E(sb("p1_wqb", [128, 512], BF16))
    ps_t = E(nc.psum_tensor("p1_pst", [128, 512], F32))
    ps_c0 = E(nc.psum_tensor("p1_psc0", [128, 512], F32))
    ps_c1 = E(nc.psum_tensor("p1_psc1", [128, 512], F32))

    def dequant(natname, tname, dqname, K, O, sT, rT):
        nblk = K // 32
        for c in range(O // 128):
            P.wait("s", "vs")
            P.newbatch("s")
            P.dma("s", nat[:, 0:K], io[natname][128 * c:128 * (c + 1), :])
            P.wait("v", "sd")
            P.step("v", lambda v, K=K, nblk=nblk: v.tensor_reduce(
                out=bm[:, 0:nblk].rearrange("p (b one) -> p b one", one=1),
                in_=nat[:, 0:K].rearrange("p (b w) -> p b w", w=32),
                axis=AX.X, op=ALU.max, apply_absolute_value=True))
            P.wait("g", "vs")
            P.step("g", lambda g, nblk=nblk: g.tensor_scalar(
                out=s_c[:, 0:nblk], in0=bm[:, 0:nblk], scalar1=1.0 / 7.0, scalar2=1e-12,
                op0=ALU.mult, op1=ALU.max))
            P.wait("v", "gs")
            P.step("v", lambda v, nblk=nblk: v.reciprocal(out=rs_c[:, 0:nblk], in_=s_c[:, 0:nblk]))
            P.wait("t", "vs")
            P.wait("t", "gs")
            P.step("t", lambda t, nblk=nblk: t.transpose(out=ps_t[0:nblk, 0:128], in_=s_c[:, 0:nblk],
                                                         identity=idf_sb[:]))
            P.wait("v", "ts")
            P.step("v", lambda v, c=c, nblk=nblk, sT=sT: v.tensor_copy(
                out=sT[0:nblk, 128 * c:128 * (c + 1)], in_=ps_t[0:nblk, 0:128]))
            P.wait("t", "vs")
            P.step("t", lambda t, nblk=nblk: t.transpose(out=ps_t[0:nblk, 0:128], in_=rs_c[:, 0:nblk],
                                                         identity=idf_sb[:]))
            P.wait("v", "ts")
            P.step("v", lambda v, c=c, nblk=nblk, rT=rT: v.tensor_copy(
                out=rT[0:nblk, 128 * c:128 * (c + 1)], in_=ps_t[0:nblk, 0:128]))
            P.wait("t", "vs")
        for hc in range(K // 128):
            for os_ in range(O // 512):
                buf = (hc * (O // 512) + os_) % 2
                P.wait("s", "vs")
                P.newbatch("s")
                P.dma("s", wt_f[buf][:], io[tname][128 * hc:128 * (hc + 1), 512 * os_:512 * (os_ + 1)])
                P.dma("s", sb_b[:], sT[4 * hc:4 * hc + 4, 512 * os_:512 * (os_ + 1)]
                      .unsqueeze(1).broadcast_to([4, 32, 512]))
                P.dma("s", rb_b[:], rT[4 * hc:4 * hc + 4, 512 * os_:512 * (os_ + 1)]
                      .unsqueeze(1).broadcast_to([4, 32, 512]))
                P.wait("v", "sd")
                P.wait("v", "gd")
                P.step("v", lambda v, buf=buf: v.tensor_mul(out=tf32[:], in0=wt_f[buf][:], in1=rb_b[:]),
                       selfwait=False)
                P.step("v", lambda v: v.tensor_copy(out=ti32[:], in_=tf32[:]), selfwait=False)
                P.wait("v", "gd")
                P.step("v", lambda v: v.tensor_mul(out=wq_b[:], in0=ti32[:], in1=sb_b[:]), selfwait=False)
                P.wait("g", "vs")
                P.dma("g", io[dqname][128 * hc:128 * (hc + 1), 512 * os_:512 * (os_ + 1)], wq_b[:])

    idf_sb = E(sb("p1_idf", [128, 128], F32))
    P.dma("s", idf_sb[:], io["idf"])
    if "d" in parts:
        dequant("w_up", "wupT", "wupT_dq", D, NI, sT_up, rT_up)
        dequant("w_gate", "wgateT", "wgateT_dq", D, NI, sT_gt, rT_gt)
        dequant("w_down", "wdownT", "wdownT_dq", NI, D, sT_dn, rT_dn)
    P.barrier()

    y1t = E(sb("p1_y1t", [128, D], F32))
    xqi = E(sb("p1_xqi", [128, D], I32))
    xqf = E(sb("p1_xqf", [128, D], F32))
    xqb = E(sb("p1_xqb", [128, D], BF16))
    xq_fm = [E(sb(f"p1_xqfm{c}", [128, T], BF16)) for c in range(NOC)]
    idb_sb = E(sb("p1_idb", [128, 128], BF16))
    P.dma("s", idb_sb[:], io["idb"])
    for tt in (range(NTT) if "x" in parts else []):
        P.waitall("s")
        P.dma("s", y1t[:], io["y1_d"][128 * tt:128 * (tt + 1), :])
        P.wait("v", "sd")
        P.wait("v", "gs")
        P.step("v", lambda v: v.tensor_scalar(out=xqf[:], in0=y1t[:], scalar1=rs1_pp[:, 0:1],
                                              scalar2=None, op0=ALU.mult), selfwait=False)
        P.step("v", lambda v: v.tensor_copy(out=xqi[:], in_=xqf[:]), selfwait=False)
        P.wait("v", "ts")
        P.step("v", lambda v: v.tensor_scalar(out=xqb[:], in0=xqi[:], scalar1=s1_pp[:, 0:1],
                                              scalar2=None, op0=ALU.mult), selfwait=False)
        P.wait("t", "vs")
        for hc in range(NOC):
            bank = hc % 2
            pview = (ps_c0 if bank == 0 else ps_c1)[:, 0:128].bitcast(BF16)
            P.wait("t", "vs")
            P.step("t", lambda t, hc=hc, pv=pview:
                   t.transpose(out=pv[:, 0:128], in_=xqb[:, 128 * hc:128 * (hc + 1)], identity=idb_sb[:]))
            P.wait("v", "ts")
            P.step("v", lambda v, hc=hc, tt=tt, pv=pview:
                   v.tensor_copy(out=xq_fm[hc][:, 128 * tt:128 * (tt + 1)], in_=pv[:, 0:128]),
                   selfwait=False)
    P.barrier()

    wq_l = [[E(sb(f"p1_wql{b}_{h}", [128, 128], BF16)) for h in range(NOC)] for b in range(2)]
    upt = [E(sb(f"p1_upt{b}", [128, 512], F32)) for b in range(2)]
    ucol = E(sb("p1_ucol", [128, 2 * NIC * NQT], F32))
    gcol = E(sb("p1_gcol", [128, NIC * NQT], F32))
    mxf = E(sb("p1_mxf", [128, 4], F32))
    mxrow = E(sb("p1_mxrow", [1, 128], F32))
    mx8 = E(sb("p1_mx8", [1, 8], F32))
    _wl = ()
    if "m" in parts:
        _wl = ((0, "wupT_dq", "up_d"), (1, "wgateT_dq", "gate_d"))
    elif "G" in parts:
        _wl = ((1, "wgateT_dq", "gate_d"),)
    elif "u" in parts:
        _wl = ((0, "wupT_dq", "up_d"),)
    _iters = [(wi, dqname, dram, oc) for (wi, dqname, dram) in _wl
              for oc in range(4 if "u4" in parts else NIC)]
    mm_tok = {}
    evac_tok = {0: 0, 1: 0}
    vred_tok = {0: 0, 1: 0}
    gdma_tok = {0: None, 1: None}
    for k, (wi, dqname, dram, oc) in enumerate(_iters):
        wb = k % 2
        if k - 2 in mm_tok:
            P.wait("s", "ts", mm_tok[k - 2])
        P.newbatch("s")
        ld_tok = None
        for hc in range(NOC):
            ld_tok = P.dma("s", wq_l[wb][hc][:],
                           io[dqname][128 * hc:128 * (hc + 1), 128 * oc:128 * (oc + 1)])
        P.wait_tok("t", ld_tok)
        for tc in range(NQT):
            pb = tc % 2
            ps = ps_c0 if pb == 0 else ps_c1
            if evac_tok[pb] > 0:
                P.wait("t", "as_", evac_tok[pb])
            for hc in range(NOC):
                P.step("t", lambda t, hc=hc, tc=tc, ps=ps, wb=wb:
                       t.matmul(out=ps[:], lhsT=wq_l[wb][hc][:], rhs=xq_fm[hc][:, 512 * tc:512 * (tc + 1)],
                                start=(hc == 0), stop=(hc == NOC - 1)))
            P.wait("a", "ts", P.cnt["ts"])
            if vred_tok[pb] > 0:
                P.wait("a", "vs", vred_tok[pb])
            if gdma_tok[pb] is not None:
                P.wait_tok("a", gdma_tok[pb])
            P.inc("a", "as_", 1, lambda a, ps=ps, pb=pb:
                  a.activation(out=upt[pb][:], in_=ps[:], func=AF.Copy), selfwait=False)
            evac_tok[pb] = P.cnt["as_"]
            P.wait("g", "as_", evac_tok[pb])
            gdma_tok[pb] = P.dma("g", io[dram][128 * oc:128 * (oc + 1), 512 * tc:512 * (tc + 1)],
                                 upt[pb][:])
            idx = oc * NQT + tc
            P.wait("v", "as_", evac_tok[pb])
            P.step("v", lambda v, idx=idx, wi=wi, pb=pb: v.tensor_reduce(
                out=ucol[:, (wi * NIC * NQT + idx):(wi * NIC * NQT + idx) + 1],
                in_=upt[pb][:], axis=AX.X, op=ALU.max, apply_absolute_value=True))
            if wi == 1:
                P.step("v", lambda v, idx=idx, pb=pb: v.tensor_reduce(
                    out=gcol[:, idx:idx + 1], in_=upt[pb][:], axis=AX.X, op=ALU.max))
            vred_tok[pb] = P.cnt["vs"]
        mm_tok[k] = P.cnt["ts"]
    P.barrier()
    P.step("v", lambda v: v.tensor_reduce(out=mxf[:, 0:1], in_=ucol[:, 0:NIC * NQT], axis=AX.X, op=ALU.max))
    P.step("v", lambda v: v.tensor_reduce(out=mxf[:, 1:2], in_=ucol[:, NIC * NQT:], axis=AX.X, op=ALU.max))
    P.step("v", lambda v: v.tensor_reduce(out=mxf[:, 2:3], in_=gcol[:], axis=AX.X, op=ALU.max))
    P.wait("a", "vs")
    P.dma("a", io["mx_d"].rearrange("p one -> p one"), mxf[:, 0:1])
    P.wait("a", "ad")
    P.dma("a", mxrow[:], io["mx_d"].rearrange("p one -> one p"))
    P.wait("v", "ad")
    P.step("v", lambda v: v.memset(mx8[:], 0.0))
    P.step("v", lambda v: v.tensor_reduce(out=mx8[:, 1:2], in_=mxrow[:], axis=AX.X, op=ALU.max))
    P.wait("a", "vs")
    P.dma("a", io["mx_d"], mxf[:, 1:2])
    P.wait("a", "ad")
    P.dma("a", mxrow[:], io["mx_d"].rearrange("p one -> one p"))
    P.wait("v", "ad")
    P.step("v", lambda v: v.tensor_reduce(out=mx8[:, 2:3], in_=mxrow[:], axis=AX.X, op=ALU.max))
    P.wait("a", "vs")
    P.dma("a", io["mx_d"], mxf[:, 2:3])
    P.wait("a", "ad")
    P.dma("a", mxrow[:], io["mx_d"].rearrange("p one -> one p"))
    P.wait("v", "ad")
    P.step("v", lambda v: v.tensor_reduce(out=mx8[:, 3:4], in_=mxrow[:], axis=AX.X, op=ALU.max))
    P.wait("g", "vs")
    P.dma("g", io["mx_out"], mx8[:])
    P.wait("g", "gd")


def build_phase2(nc, P, io, ctx):
    E = ctx.enter_context
    sb = nc.sbuf_tensor
    gmax = peer_reduce(nc, P, io, E, sb, "p2")
    scl = E(sb("p2_scl", [1, 8], F32))
    rscl = E(sb("p2_rscl", [1, 8], F32))
    t11a = E(sb("p2_t11a", [1, 1], F32))
    t11b = E(sb("p2_t11b", [1, 1], I32))
    t11c = E(sb("p2_t11c", [1, 1], I32))
    t11d = E(sb("p2_t11d", [1, 1], F32))
    t11e = E(sb("p2_t11e", [1, 1], F32))
    sb16 = E(sb("p2_sb16", [1, 1], F32))
    scraw = E(sb("p2_scraw", [1, 1], F32))
    scv = E(sb("p2_scv", [1, 1], F32))
    rscv = E(sb("p2_rscv", [1, 1], F32))
    sbrc = E(sb("p2_sbrc", [1, 1], F32))
    csa = E(sb("p2_csa", [1, 1], F32))
    rsa_pp = E(sb("p2_rsapp", [128, 1], F32))
    rsb_pp = E(sb("p2_rsbpp", [128, 1], F32))
    sb_pp = E(sb("p2_sbpp", [128, 1], F32))
    sbrc_pp = E(sb("p2_sbrcpp", [128, 1], F32))
    P.step("v", lambda v: v.tensor_scalar(out=scl[:], in0=gmax[:], scalar1=1.0 / QMAX,
                                          scalar2=1e-12, op0=ALU.mult, op1=ALU.max))
    P.wait("a", "vs")
    P.inc("a", "as_", 1, lambda a: a.activation(out=rscl[:], in_=scl[:], func=AF.Copy))
    P.wait("v", "as_")
    P.step("v", lambda v: v.reciprocal(out=rscl[:], in_=rscl[:]))
    P.wait("a", "vs")
    P.inc("a", "as_", 1, lambda a: a.activation(out=t11a[:], in_=gmax[:, 3:4], func=AF.Copy))
    P.wait("v", "as_")
    P.step("v", lambda v: v.tensor_scalar(out=t11d[:], in0=t11a[:], scalar1=rscl[0:1, 2:3],
                                          scalar2=None, op0=ALU.mult))
    P.step("v", lambda v: v.tensor_copy(out=t11b[:], in_=t11d[:]))
    P.wait("a", "vs")
    P.inc("a", "as_", 1, lambda a: a.activation(out=t11d[:], in_=t11b[:], func=AF.Sigmoid,
                                                scale=scl[0:1, 2:3]))
    P.wait("v", "as_")
    P.step("v", lambda v: v.tensor_scalar(out=t11e[:], in0=t11d[:], scalar1=65536.0,
                                          scalar2=None, op0=ALU.mult))
    P.step("v", lambda v: v.tensor_copy(out=t11c[:], in_=t11e[:]))
    P.wait("g", "vs")
    P.step("g", lambda g: g.tensor_scalar(out=t11c[:], in0=t11c[:], scalar1=65535.0,
                                          scalar2=None, op0=ALU.min)) if False else None
    P.wait("a", "vs")
    P.inc("a", "as_", 1, lambda a: a.activation(out=t11d[:], in_=t11c[:], func=AF.Copy))
    P.wait("v", "as_")
    P.step("v", lambda v: v.tensor_scalar(out=t11c[:], in0=t11d[:], scalar1=65535.0,
                                          scalar2=None, op0=ALU.min))
    P.step("v", lambda v: v.tensor_mul(out=scraw[:], in0=t11b[:], in1=t11c[:]))
    P.step("v", lambda v: v.tensor_scalar(out=sb16[:], in0=scl[0:1, 2:3], scalar1=1.0 / 65536.0,
                                          scalar2=None, op0=ALU.mult))
    P.wait("a", "vs")
    P.inc("a", "as_", 1, lambda a: a.activation(out=scraw[:], in_=scraw[:], func=AF.Copy,
                                                scale=sb16[0:1, 0:1]))
    P.wait("v", "as_")
    P.step("v", lambda v: v.tensor_scalar(out=scv[:], in0=scraw[:], scalar1=1.0 / QMAX,
                                          scalar2=1e-12, op0=ALU.mult, op1=ALU.max))
    P.wait("a", "vs")
    P.inc("a", "as_", 1, lambda a: a.activation(out=rscv[:], in_=scv[:], func=AF.Copy))
    P.wait("v", "as_")
    P.step("v", lambda v: v.reciprocal(out=rscv[:], in_=rscv[:]))
    P.step("v", lambda v: v.tensor_scalar(out=sbrc[:], in0=rscv[:], scalar1=sb16[0:1, 0:1],
                                          scalar2=None, op0=ALU.mult))
    P.step("v", lambda v: v.tensor_scalar(out=csa[:], in0=scv[:], scalar1=scl[0:1, 1:2],
                                          scalar2=None, op0=ALU.mult))
    scrow = E(sb("p2_scrow", [1, 16], F32))
    P.step("v", lambda v: v.memset(scrow[:], 0.0))
    P.step("v", lambda v: v.tensor_copy(out=scrow[:, 0:1], in_=scl[:, 1:2]))
    P.step("v", lambda v: v.tensor_copy(out=scrow[:, 1:2], in_=scl[:, 2:3]))
    P.step("v", lambda v: v.tensor_copy(out=scrow[:, 2:3], in_=scv[:]))
    P.step("v", lambda v: v.tensor_copy(out=scrow[:, 3:4], in_=csa[:]))
    P.step("v", lambda v: v.tensor_copy(out=scrow[:, 4:5], in_=sbrc[:]))
    P.wait("g", "vs")
    P.dma("g", io["scal_d"], scrow[:])
    P.waitall("s")
    P.dma("s", rsa_pp[:], rscl[:, 1:2].unsqueeze(1).broadcast_to([1, 128, 1]))
    P.dma("s", rsb_pp[:], rscl[:, 2:3].unsqueeze(1).broadcast_to([1, 128, 1]))
    P.dma("s", sb_pp[:], scl[:, 2:3].unsqueeze(1).broadcast_to([1, 128, 1]))
    P.dma("s", sbrc_pp[:], sbrc[:].unsqueeze(1).broadcast_to([1, 128, 1]))
    P.barrier()

    up_t = [E(sb(f"p2_upt{i}", [128, 512], F32)) for i in range(2)]
    gt_t = [E(sb(f"p2_gtt{i}", [128, 512], F32)) for i in range(2)]
    upi = E(sb("p2_upi", [128, 512], I32))
    tmpf = E(sb("p2_tmpf", [128, 512], F32))
    gti = E(sb("p2_gti", [128, 512], I32))
    sig = E(sb("p2_sig", [128, 512], F32))
    sqi = E(sb("p2_sqi", [128, 512], I32))
    sq2 = E(sb("p2_sq2", [128, 512], I32))
    g2p = E(sb("p2_g2p", [128, 512], F32))
    g2i = E(sb("p2_g2i", [128, 512], I32))
    opr = E(sb("p2_opr", [128, 512], F32))
    ocol = E(sb("p2_ocol", [128, NIC * NQT], F32))
    mxrow = E(sb("p2_mxrow", [1, 128], F32))
    mx8 = E(sb("p2_mx8", [1, 8], F32))
    mxf = E(sb("p2_mxf", [128, 1], F32))
    ntile = NIC * NQT
    rd_tok = {0: 0, 1: 0}
    for idx in range(ntile):
        oc, tc = divmod(idx, NQT)
        buf = idx % 2
        if rd_tok[buf] > 0:
            P.wait("s", "vs", rd_tok[buf])
        P.newbatch("s")
        P.dma("s", up_t[buf][:], io["up_d"][128 * oc:128 * (oc + 1), 512 * tc:512 * (tc + 1)])
        ld_tok = P.dma("s", gt_t[buf][:], io["gate_d"][128 * oc:128 * (oc + 1), 512 * tc:512 * (tc + 1)])
        P.wait_tok("v", ld_tok)
        P.wait("v", "as_")
        P.step("v", lambda v, buf=buf: v.tensor_scalar(out=tmpf[:], in0=up_t[buf][:], scalar1=rsa_pp[:, 0:1],
                                                       scalar2=None, op0=ALU.mult), selfwait=False)
        P.step("v", lambda v: v.tensor_copy(out=upi[:], in_=tmpf[:]), selfwait=False)
        P.step("v", lambda v, buf=buf: v.tensor_scalar(out=tmpf[:], in0=gt_t[buf][:], scalar1=rsb_pp[:, 0:1],
                                                       scalar2=None, op0=ALU.mult), selfwait=False)
        P.step("v", lambda v: v.tensor_copy(out=gti[:], in_=tmpf[:]), selfwait=False)
        rd_tok[buf] = P.cnt["vs"]
        P.wait("a", "vs")
        P.inc("a", "as_", 1, lambda a: a.activation(out=sig[:], in_=gti[:], func=AF.Sigmoid,
                                                    scale=sb_pp[:, 0:1]), selfwait=False)
        P.wait("v", "as_")
        P.step("v", lambda v: v.tensor_scalar(out=tmpf[:], in0=sig[:], scalar1=65536.0,
                                              scalar2=None, op0=ALU.mult), selfwait=False)
        P.step("v", lambda v: v.tensor_copy(out=sqi[:], in_=tmpf[:]), selfwait=False)
        P.step("v", lambda v: v.scalar_tensor_tensor(out=g2p[:], in0=sqi[:], scalar=65535.0,
                                                     in1=gti[:], op0=ALU.min, op1=ALU.mult),
               selfwait=False)
        P.step("v", lambda v: v.tensor_scalar(out=tmpf[:], in0=g2p[:], scalar1=sbrc_pp[:, 0:1],
                                              scalar2=None, op0=ALU.mult), selfwait=False)
        P.step("v", lambda v: v.tensor_copy(out=g2i[:], in_=tmpf[:]), selfwait=False)
        P.wait("v", "ad")
        P.step("v", lambda v: v.tensor_mul(out=opr[:], in0=g2i[:], in1=upi[:]), selfwait=False)
        P.step("v", lambda v, idx=idx: v.tensor_reduce(out=ocol[:, idx:idx + 1], in_=opr[:],
                                                       axis=AX.X, op=ALU.max, apply_absolute_value=True))
        P.wait("a", "vs")
        P.wait("a", "ad")
        P.dma("a", io["o_d"][128 * oc:128 * (oc + 1), 512 * tc:512 * (tc + 1)], opr[:])
    P.barrier()
    P.step("v", lambda v: v.tensor_reduce(out=mxf[:], in_=ocol[:], axis=AX.X, op=ALU.max))
    P.wait("a", "vs")
    P.dma("a", io["mx_d"], mxf[:])
    P.wait("a", "ad")
    P.dma("a", mxrow[:], io["mx_d"].rearrange("p one -> one p"))
    P.wait("v", "ad")
    P.step("v", lambda v: v.memset(mx8[:], 0.0))
    P.step("v", lambda v: v.tensor_reduce(out=mx8[:, 4:5], in_=mxrow[:], axis=AX.X, op=ALU.max))
    P.wait("g", "vs")
    P.dma("g", io["mx_out"], mx8[:])
    P.wait("g", "gd")


def build_phase3(nc, P, io, ctx, dbg_what=None):
    E = ctx.enter_context
    sb = nc.sbuf_tensor
    gmax = peer_reduce(nc, P, io, E, sb, "p3")
    scrow = E(sb("p3_scrow", [1, 16], F32))
    sdv = E(sb("p3_sdv", [1, 1], F32))
    rsdv = E(sb("p3_rsdv", [1, 1], F32))
    crd = E(sb("p3_crd", [1, 1], F32))
    crd_pp = E(sb("p3_crdpp", [128, 1], F32))
    sd_pp = E(sb("p3_sdpp", [128, 1], F32))
    P.wait("s", "vs")
    P.dma("s", scrow[:], io["scal_d"])
    P.wait("v", "sd")
    P.step("v", lambda v: v.tensor_scalar(out=sdv[:], in0=gmax[:, 4:5], scalar1=scrow[0:1, 3:4],
                                          scalar2=None, op0=ALU.mult))
    P.wait("a", "vs")
    P.inc("a", "as_", 1, lambda a: a.activation(out=rsdv[:], in_=sdv[:], func=AF.Copy))
    P.wait("v", "as_")
    P.step("v", lambda v: v.tensor_scalar(out=sdv[:], in0=rsdv[:], scalar1=1.0 / QMAX,
                                          scalar2=1e-12, op0=ALU.mult, op1=ALU.max))
    P.wait("a", "vs")
    P.inc("a", "as_", 1, lambda a: a.activation(out=rsdv[:], in_=sdv[:], func=AF.Copy))
    P.wait("v", "as_")
    P.step("v", lambda v: v.reciprocal(out=rsdv[:], in_=rsdv[:]))
    P.step("v", lambda v: v.tensor_scalar(out=crd[:], in0=rsdv[:], scalar1=scrow[0:1, 3:4],
                                          scalar2=None, op0=ALU.mult))
    P.waitall("s")
    P.dma("s", crd_pp[:], crd[:].unsqueeze(1).broadcast_to([1, 128, 1]))
    P.dma("s", sd_pp[:], sdv[:].unsqueeze(1).broadcast_to([1, 128, 1]))
    P.barrier()

    opr = [E(sb(f"p3_opr{i}", [128, 512], F32)) for i in range(2)]
    oqi = E(sb("p3_oqi", [128, 512], I32))
    oqf = E(sb("p3_oqf", [128, 512], F32))
    oqb = E(sb("p3_oqb", [128, 512], BF16))
    oq_tok = {0: 0, 1: 0}
    for idx in range(NIC * NQT):
        oc, tc = divmod(idx, NQT)
        buf = idx % 2
        if oq_tok[buf] > 0:
            P.wait("s", "vs", oq_tok[buf])
        P.newbatch("s")
        ld_tok = P.dma("s", opr[buf][:], io["o_d"][128 * oc:128 * (oc + 1), 512 * tc:512 * (tc + 1)])
        P.wait_tok("v", ld_tok)
        P.wait("v", "gd")
        P.step("v", lambda v, buf=buf: v.tensor_scalar(out=oqf[:], in0=opr[buf][:], scalar1=crd_pp[:, 0:1],
                                                       scalar2=None, op0=ALU.mult), selfwait=False)
        oq_tok[buf] = P.cnt["vs"]
        P.step("v", lambda v: v.tensor_copy(out=oqi[:], in_=oqf[:]), selfwait=False)
        P.step("v", lambda v: v.tensor_scalar(out=oqb[:], in0=oqi[:], scalar1=sd_pp[:, 0:1],
                                              scalar2=None, op0=ALU.mult), selfwait=False)
        P.wait("g", "vs")
        P.dma("g", io["gate_d"][128 * oc:128 * (oc + 1), 512 * tc:512 * (tc + 1)].bitcast(BF16)[:, 0:512],
              oqb[:])
    P.barrier()

    wd_l = [E(sb(f"p3_wdl{c}", [128, D], BF16)) for c in range(NIC)]
    wd_tok = None
    for ic in range(NIC):
        wd_tok = P.dma("s", wd_l[ic][:], io["wdownT_dq"][128 * ic:128 * (ic + 1), :])
    on_t = [E(sb(f"p3_ont{c}", [128, 128], BF16)) for c in range(NIC)]
    y1t = E(sb("p3_y1t", [128, D], F32))
    z_sb = E(sb("p3_z", [128, D], F32))
    zsq = E(sb("p3_zsq", [128, D], F32))
    rsum = E(sb("p3_rsum", [128, 1], F32))
    mvec = E(sb("p3_mvec", [128, 1], F32))
    ssum = E(sb("p3_ssum", [128, 1], F32))
    varv = E(sb("p3_varv", [128, 1], F32))
    rstdv = E(sb("p3_rstdv", [128, 1], F32))
    y2t = E(sb("p3_y2t", [128, D], F32))
    epsA = E(sb("p3_epsA", [128, 1], F32))
    P.step("v", lambda v: v.memset(epsA[:], EPS))
    ps_a = E(nc.psum_tensor("p3_psa", [128, 1024], F32))
    P.wait_tok("t", wd_tok)
    d_mm_tok = 0
    d_z_tok = 0
    for tt in range(NTT):
        if tt == 0:
            P.waitall("s")
        else:
            P.wait("s", "ts", d_mm_tok)
            P.wait("s", "vs", d_z_tok)
        P.newbatch("s")
        for ic in range(NIC):
            P.dma("s", on_t[ic][:],
                  io["gate_d"][128 * ic:128 * (ic + 1), 512 * (tt // 4):512 * (tt // 4) + 512]
                  .bitcast(BF16)[:, 128 * (tt % 4):128 * (tt % 4) + 128])
        d_ld_tok = P.dma("s", y1t[:], io["y1_d"][128 * tt:128 * (tt + 1), :])
        P.wait_tok("t", d_ld_tok)
        if d_z_tok > 0:
            P.wait("t", "vs", d_z_tok)
        for half in range(2):
            psh = ps_a[:, 512 * half:512 * (half + 1)]
            for ic in range(NIC):
                P.step("t", lambda t, ic=ic, half=half, psh=psh:
                       t.matmul(out=psh[:], lhsT=on_t[ic][:], rhs=wd_l[ic][:, 512 * half:512 * (half + 1)],
                                start=(ic == 0), stop=(ic == NIC - 1)))
        d_mm_tok = P.cnt["ts"]
        P.wait("v", "ts", d_mm_tok)
        P.wait("v", "gd")
        P.wait("v", "as_")
        for half in range(2):
            P.step("v", lambda v, half=half: v.tensor_add(
                out=z_sb[:, 512 * half:512 * (half + 1)], in0=ps_a[:, 512 * half:512 * (half + 1)],
                in1=y1t[:, 512 * half:512 * (half + 1)]), selfwait=False)
        d_z_tok = P.cnt["vs"]
        P.step("v", lambda v: v.tensor_reduce(out=rsum[:], in_=z_sb[:], axis=AX.X, op=ALU.add))
        P.wait("a", "vs")
        P.inc("a", "as_", 1, lambda a: a.activation(out=zsq[:], in_=z_sb[:], func=AF.Square), selfwait=False)
        P.wait("v", "as_")
        P.step("v", lambda v: v.tensor_reduce(out=ssum[:], in_=zsq[:], axis=AX.X, op=ALU.add))
        P.wait("g", "vs")
        P.wait("g", "as_")
        P.step("g", lambda g: g.tensor_scalar(out=mvec[:], in0=rsum[:], scalar1=1.0 / D, scalar2=None, op0=ALU.mult))
        P.step("g", lambda g: g.tensor_mul(out=rstdv[:], in0=mvec[:], in1=mvec[:]))
        P.step("g", lambda g: g.tensor_scalar(out=varv[:], in0=ssum[:], scalar1=1.0 / D, scalar2=None, op0=ALU.mult))
        P.step("g", lambda g: g.tensor_sub(out=varv[:], in0=varv[:], in1=rstdv[:]))
        P.wait("a", "gs")
        P.inc("a", "as_", 1, lambda a: a.activation(out=rstdv[:], in_=varv[:], func=AF.Ln, bias=epsA[:, 0:1]))
        P.inc("a", "as_", 1, lambda a: a.activation(out=rstdv[:], in_=rstdv[:], func=AF.Exp, scale=-0.5))
        P.wait("v", "as_")
        P.wait("v", "gs")
        P.step("v", lambda v: v.tensor_scalar(out=y2t[:], in0=z_sb[:], scalar1=mvec[:, 0:1],
                                              scalar2=rstdv[:, 0:1], op0=ALU.subtract, op1=ALU.mult))
        P.wait("g", "vs")
        P.dma("g", io["out"][128 * tt:128 * (tt + 1), :], y2t[:])
    P.wait("g", "gd")


ALLSEMS2 = ALLSEMS + ["ph"]


def build_full(dbg=False, only=None, p1_parts="sdxmer"):
    nc = bass.Bass("TRN2", target_bir_lowering=False)
    io = declare_io(nc, dbg=dbg)
    plans = []
    from contextlib import ExitStack
    with ExitStack() as octx:
        sems = {k: octx.enter_context(nc.semaphore(k)) for k in ALLSEMS2}
        ph_sb = octx.enter_context(nc.sbuf_tensor("ph_sb", [1, 1], I32))
        builders = [build_phase0,
                    lambda nc, P, io, c: build_phase1(nc, P, io, c, parts=p1_parts),
                    build_phase2, build_phase3]
        for i, bf in enumerate(builders):
            P = Plan()
            if only is None or i == only:
                with ExitStack() as pctx:
                    bf(nc, P, io, pctx)
            plans.append(P)
        block = octx.enter_context(nc.Block())

        def mk_emit(engname):
            def f(eng):
                if engname == "s":
                    eng.dma_start(out=ph_sb[:], in_=io["phase"]).then_inc(sems["ph"], 16)
                eng.wait_ge(sems["ph"], 16)
                p = eng.value_load(ph_sb[0:1, 0:1])
                with nc.If(p == 0):
                    plans[0].emit(engname, eng, sems)
                with nc.Else():
                    with nc.If(p == 1):
                        plans[1].emit(engname, eng, sems)
                    with nc.Else():
                        with nc.If(p == 2):
                            plans[2].emit(engname, eng, sems)
                        with nc.Else():
                            plans[3].emit(engname, eng, sems)
            return f

        block.sync(mk_emit("s"))
        block.tensor(mk_emit("t"))
        block.vector(mk_emit("v"))
        block.scalar(mk_emit("a"))
        block.gpsimd(mk_emit("g"))
    return nc, io


def host_inputs(x_core, Wq, Wkv, Wo, bo, ln_g, ln_b, w_up, w_gate, w_down):
    cos_t, sinp_t = rope_tables()
    tri, ems = mask_consts()
    return {
        "x": np.ascontiguousarray(x_core),
        "wqT": np.ascontiguousarray(Wq.T), "wkvT": np.ascontiguousarray(Wkv.T),
        "woT": np.ascontiguousarray(Wo.T), "bo": np.ascontiguousarray(bo.reshape(1, -1)),
        "g128": np.ascontiguousarray(np.tile(ln_g, 2).reshape(128, 1)),
        "b128": np.ascontiguousarray(np.tile(ln_b, 2).reshape(128, 1)),
        "cos_t": cos_t, "sinp_t": sinp_t, "tri": tri, "ems": ems,
        "ind2": ind2_const(),
        "idb": np.ascontiguousarray(np.eye(128).astype(ml_dtypes.bfloat16)),
        "idf": np.ascontiguousarray(np.eye(128, dtype=np.float32)),
        "w_up": np.ascontiguousarray(w_up), "w_gate": np.ascontiguousarray(w_gate),
        "w_down": np.ascontiguousarray(w_down),
        "wupT": np.ascontiguousarray(w_up.T), "wgateT": np.ascontiguousarray(w_gate.T),
        "wdownT": np.ascontiguousarray(w_down.T),
        "phase": np.array([[0]], np.int32),
        "peer_mx": np.zeros((1, 64), np.float32),
    }
_CACHE = {}


def kernel(L0_attn_x, L0_attn_Wq, L0_attn_Wkv, L0_attn_Wo, L0_attn_bo,
           L0_attn_ln_g, L0_attn_ln_b, L0_ffn_w_gate, L0_ffn_w_up, L0_ffn_w_down):
    from concourse.bass_utils import run_bass_kernel_spmd
    n = 8
    if "nc" not in _CACHE:
        _CACHE["nc"], _CACHE["io"] = build_full(dbg=False)
    nc = _CACHE["nc"]
    x = np.asarray(L0_attn_x)
    base = [host_inputs(x[c], np.asarray(L0_attn_Wq), np.asarray(L0_attn_Wkv),
                        np.asarray(L0_attn_Wo), np.asarray(L0_attn_bo),
                        np.asarray(L0_attn_ln_g), np.asarray(L0_attn_ln_b),
                        np.asarray(L0_ffn_w_up), np.asarray(L0_ffn_w_gate),
                        np.asarray(L0_ffn_w_down)) for c in range(n)]
    peer = np.zeros((1, 64), np.float32)
    slot_of_phase = {0: [0], 1: [1, 2, 3], 2: [4], 3: []}
    outs = None
    for phase in range(4):
        for c in range(n):
            base[c]["phase"] = np.array([[phase]], np.int32)
            base[c]["peer_mx"] = peer.copy()
        res = run_bass_kernel_spmd(nc, base, core_ids=list(range(n)))
        mx = np.stack([np.asarray(res.results[c]["mx_out"])[0] for c in range(n)])
        for s in slot_of_phase[phase]:
            for r in range(n):
                peer[0, s * 8 + r] = mx[r, s]
        outs = res
    return np.stack([np.asarray(outs.results[c]["out"]) for c in range(n)]).astype(np.float32)
